# version 01
# baseline (speedup 1.0000x reference)
import math
import dataclasses
from contextlib import ExitStack

import numpy as np
import ml_dtypes

import concourse.bass as bass
import concourse.mybir as mybir
from concourse.bass_utils import run_bass_kernel_spmd

F32 = mybir.dt.float32
BF16 = mybir.dt.bfloat16
I32 = mybir.dt.int32
U8 = mybir.dt.uint8
AF = mybir.ActivationFunctionType
ALU = mybir.AluOpType
AX = mybir.AxisListType

BFNP = ml_dtypes.bfloat16

B_, T_, D_ = 4, 4096, 1024
NCORE = 8
TOK = 2048
NH, HD = 16, 64
EXP_N, EHID = 8, 2048
L1HID = 3072
EPS = 1e-5
S8 = 1.0 / 127.0
KTAYLOR = 12
BCONST = 30.0
SLOTS = [8, 16, 24, 32]

def _slopes():
    start = 2.0 ** (-(2.0 ** -(math.log2(NH) - 3)))
    return [start * start ** i for i in range(NH)]

SLOPES = _slopes()

QB_ABS = {0: [512, 1536, 2560, 3584],
          1: [0, 1024, 2048, 3072]}

def own_token_abs(half):
    out = []
    for qa in QB_ABS[half]:
        out.extend(range(qa, qa + 512))
    return out


class Buf:
    __slots__ = ("w", "rs", "name")
    def __init__(self, name):
        self.name = name
        self.w = None
        self.rs = []

class Plan:
    ENGS = ("t", "v", "a", "g", "sy")

    def __init__(self):
        self.q = {e: [] for e in self.ENGS}
        self.cnt = {e: 0 for e in ("t", "v", "a", "g", "sy", "cc")}
        self.hw = {e: {} for e in self.ENGS}

    def _emit(self, eng, fn, reads, writes, sem, inc):
        waits = {}
        deps = []
        for b in reads:
            if b.w is not None:
                deps.append(b.w)
        for b in writes:
            if b.w is not None:
                deps.append(b.w)
            deps.extend(b.rs)
        for (s, v) in deps:
            if s == "t" and eng == "t":
                continue
            if waits.get(s, -1) < v:
                waits[s] = v
        hw = self.hw[eng]
        wl = []
        for s, v in sorted(waits.items()):
            if hw.get(s, -1) >= v:
                continue
            hw[s] = v
            wl.append((s, v))
        self.cnt[sem] += inc
        tok = (sem, self.cnt[sem])
        self.q[eng].append((wl, fn, (sem, inc)))
        for b in reads:
            b.rs.append(tok)
        for b in writes:
            b.w = tok
            b.rs = []
        return tok

    def op(self, eng, fn, reads=(), writes=()):
        return self._emit(eng, fn, list(reads), list(writes), eng, 1)

    def dma(self, queue, fn, reads=(), writes=()):
        return self._emit(queue, fn, list(reads), list(writes), queue, 16)

    def coll(self, fn, reads=(), writes=()):
        return self._emit("g", fn, list(reads), list(writes), "cc", 1)

    def wait_tokens(self, eng, toks):
        wl = []
        hw = self.hw[eng]
        for (s, v) in toks:
            if hw.get(s, -1) < v:
                hw[s] = v
                wl.append((s, v))
        if wl:
            self.q[eng].append((wl, None, None))

    def replay(self, eng, handle, sems):
        for waits, fn, inc in self.q[eng]:
            for (s, v) in waits:
                handle.wait_ge(sems[s], v)
            if fn is not None:
                ins = fn(handle)
                if inc is not None:
                    ins.then_inc(sems[inc[0]], inc[1])


def mm_group(plan, terms, bank_buf, read_bufs):
    def fn(h, terms=terms):
        last = None
        for (o, l, r, st, sp) in terms:
            last = h.matmul(o, l, r, start=st, stop=sp)
        return last
    return plan.op("t", fn, reads=list(read_bufs), writes=[bank_buf])



def _bf(x):
    return np.ascontiguousarray(np.asarray(x).astype(BFNP))

def _f32(x):
    return np.ascontiguousarray(np.asarray(x), dtype=np.float32)

def prep_consts():
    c = {}
    c["ident_f"] = _f32(np.eye(128))
    c["iota_row"] = _f32(np.broadcast_to(np.arange(512, dtype=np.float32), (128, 512)))
    al = np.zeros((128, NH * 32), dtype=np.float32)
    p = np.arange(128, dtype=np.float32)
    for h in range(NH):
        for cc in range(32):
            al[:, h * 32 + cc] = SLOPES[h] * (128 * cc + p)
    c["alibi_cols"] = al
    return c

def prep_core(core, x_full):
    half = core % 2
    seq = core // 2
    toks = own_token_abs(half)
    d = {}
    d["x0"] = _f32(x_full[seq][toks])
    qabs = np.array(toks, dtype=np.float32)
    brow0 = np.zeros((NH, TOK), dtype=np.float32)
    for h in range(NH):
        brow0[h] = -8.0 * (BCONST + SLOPES[h] * qabs)
    d["brow0"] = _bf(brow0)
    thr = np.zeros((128, 32), dtype=np.float32)
    pp = np.arange(128, dtype=np.float32)
    for j in range(4):
        qa = QB_ABS[half][j]
        for i in range(8):
            cch = SLOTS[j] - 8 + i
            thr[:, 8 * j + i] = pp + 128.0 * cch - qa
    d["thr"] = thr
    ey = np.zeros((1024, 128), dtype=np.float32)
    ey[128 * core:128 * (core + 1), :] = np.eye(128)
    d["eyecol"] = ey
    return d

def prep_weights(inp):
    w = {}
    w["w_attn_T"] = _f32(np.asarray(inp["L0_attn_w_attn"]).T)
    w["w_proj_T"] = _f32(np.asarray(inp["L0_attn_w_proj"]).T)
    w["abel_w"] = _f32(inp["L0_ffn_abel_w"])
    w["abel_b"] = _f32(np.asarray(inp["L0_ffn_abel_b"]).reshape(8, 128).T)
    w["gate_w"] = _f32(inp["L0_ffn_moe_gate_w"])
    w["gate_b"] = _f32(np.asarray(inp["L0_ffn_moe_gate_b"]).reshape(8, 1))
    w["ew1"] = _f32(inp["L0_ffn_ew1"])
    eb1 = np.asarray(inp["L0_ffn_eb1"])
    w["eb1c"] = _f32(np.concatenate([np.asarray(eb1[e]).reshape(16, 128).T for e in range(8)], axis=1))
    w["ew2"] = _f32(inp["L0_ffn_ew2"])
    w["eb2"] = _f32(inp["L0_ffn_eb2"])
    w["gensT"] = _f32(np.transpose(np.asarray(inp["L0_ffn_gens"]), (0, 2, 1)))
    w["gc_w"] = _f32(np.asarray(inp["L0_ffn_gc_w"]).reshape(16, 128).T)
    w["gc_b"] = _f32(np.asarray(inp["L0_ffn_gc_b"]).reshape(1, 1))
    w["wq"] = _f32(inp["L1_attn_Wq"])
    w["wk"] = _f32(inp["L1_attn_Wk"])
    w["wv"] = _f32(inp["L1_attn_Wv"])
    w["wo"] = _f32(inp["L1_attn_Wo"])
    w["wg_T"] = np.ascontiguousarray(np.asarray(inp["L1_ffn_w_gate"]).T)
    w["wu_T"] = np.ascontiguousarray(np.asarray(inp["L1_ffn_w_up"]).T)
    w["wd_T"] = np.ascontiguousarray(np.asarray(inp["L1_ffn_w_down"]).T)
    return w

L0_W_KEYS = ["w_attn_T", "w_proj_T", "abel_w", "abel_b", "gate_w", "gate_b",
             "ew1", "eb1c", "ew2", "eb2", "gc_w", "gc_b"]
L1_W_KEYS = ["wq", "wk", "wv", "wo", "wg_T", "wu_T", "wd_T"]



class Bufs(dict):
    def buf(self, name):
        if name not in self:
            self[name] = Buf(name)
        return self[name]

@dataclasses.dataclass
class Ctx:
    nc: object
    plan: Plan
    sb: dict
    ps: object
    dram: dict
    bufs: Bufs
    mode: str

    def bank(self, i, rows=128):
        return self.ps.ap()[0:rows, 512 * i:512 * (i + 1)]

    def bankbuf(self, i):
        return self.bufs.buf(f"ps{i}")

    def stq(self, i):
        return (self.sb["stage"].ap()[:, 1024 * i:1024 * (i + 1)],
                self.bufs.buf(f"stage{i}"))

    def stq5(self, i):
        return (self.sb["stage"].ap()[:, 1024 * i:1024 * i + 512],
                self.bufs.buf(f"stage{i}"))


def flat_ap(ap2d, offset, dims):
    return dataclasses.replace(ap2d, offset=offset, ap=[list(x) for x in dims])


def chunked(src2d, p=128):
    return src2d.rearrange("(c p) n -> p c n", p=p)


DEBUG_OUTPUTS = ()

def declare_dram(nc, mode):
    d = {}
    def ein(name, shape, dt):
        d[name] = nc.dram_tensor(name, list(shape), dt, kind="ExternalInput").ap()
    def eout(name, shape, dt):
        d[name] = nc.dram_tensor(name, list(shape), dt, kind="ExternalOutput").ap()
    def internal(name, shape, dt):
        if name in DEBUG_OUTPUTS:
            eout(name, shape, dt)
        else:
            d[name] = nc.dram_tensor(name, list(shape), dt).ap()

    ein("ident_f", (128, 128), F32)
    ein("iota_row", (128, 512), F32)
    ein("thr", (128, 32), F32)

    if mode in ("expm", "fused"):
        ein("gensT", (3, 1024, 1024), F32)
        ein("eyecol", (1024, 128), F32)
    if mode == "expm":
        eout("eslice", (3 * 1024, 128), BF16)
    if mode in ("l0", "fused"):
        ein("alibi_cols", (128, NH * 32), F32)
        ein("brow0", (NH, TOK), BF16)
        ein("ln_gamma", (128, 8), F32)
        ein("ln_beta", (128, 8), F32)
        for k, shape, dt in [
            ("w_attn_T", (1024, 3072), F32), ("w_proj_T", (1024, 1024), F32),
            ("abel_w", (1024, 1024), F32), ("abel_b", (128, 8), F32),
            ("gate_w", (1024, 8), F32), ("gate_b", (8, 1), F32),
            ("ew1", (8, 1024, 2048), F32), ("eb1c", (128, 128), F32),
            ("ew2", (8, 2048, 1024), F32), ("eb2", (8, 1024), F32),
            ("gc_w", (128, 16), F32), ("gc_b", (1, 1), F32),
        ]:
            ein(k, shape, dt)
    if mode == "l0":
        ein("E", (3, 1024, 1024), BF16)
    if mode in ("l1", "fused"):
        for k, shape, dt in [
            ("wq", (1024, 1024), F32), ("wk", (1024, 1024), F32),
            ("wv", (1024, 1024), F32), ("wo", (1024, 1024), F32),
            ("wg_T", (1024, 3072), I32), ("wu_T", (1024, 3072), I32),
            ("wd_T", (3072, 1024), I32),
        ]:
            ein(k, shape, dt)

    if mode in ("l0", "l1"):
        ein("x_in", (TOK, 1024), F32)
        eout("x_out", (TOK, 1024), F32)
    if mode == "fused":
        ein("x0", (TOK, 1024), F32)
        eout("x_out", (TOK, 1024), F32)
        internal("x_mid", (TOK, 1024), F32)
        internal("eg_in", (3 * 1024, 128), BF16)
        internal("eg_out", (NCORE * 3 * 1024, 128), BF16)

    if mode in ("l0", "l1", "fused"):
        internal("xn_dram", (TOK, 1024), BF16)
        internal("q_dram", (1024, TOK), BF16)
        internal("ag_in", (2048, 2048), BF16)
        internal("ag_out", (4096, 2048), BF16)
        internal("kabs", (1024, 4096), BF16)
        internal("vabs", (4096, 1024), BF16)
        internal("attn_dram", (TOK, 1024), BF16)
    if mode in ("l0", "fused"):
        internal("xn2p_dram", (TOK, 1024), BF16)
        internal("mixed_dram", (1024, TOK), BF16)
        internal("moe_tok", (1024, 1024), BF16)
    if mode in ("l1", "fused"):
        internal("ffn_dram", (3, TOK, 1024), F32)
        internal("wg_bf", (1024, 3072), BF16)
        internal("wu_bf", (1024, 3072), BF16)
        internal("wd_bf", (3072, 1024), BF16)
    return d


def alloc_sbuf(nc, stack, mode):
    en = stack.enter_context
    sb = {}
    def t(name, shape, dt):
        sb[name] = en(nc.sbuf_tensor("sb_" + name, list(shape), dt))
    t("W1", (128, 16384), BF16)
    t("W2", (128, 16384), BF16)
    t("A1", (128, 16384), BF16)
    t("A3", (128, 16384), BF16)
    t("kslot", (64, 4096), BF16)
    t("vslot", (128, 32 * 65), BF16)
    t("qslot", (64, 2 * 512), BF16)
    t("xstage", (128, 2 * 1024), F32)
    t("stage", (128, 4 * 1024), BF16)
    t("scr", (128, 2048), F32)
    t("pbuf", (128, 4 * 512), BF16)
    t("rb", (128, 512), F32)
    t("maskq", (128, 8 * 512), mybir.dt.uint16)
    t("iota", (128, 512), F32)
    t("thrT", (128, 32), F32)
    t("identf", (128, 128), F32)
    t("zeros", (128, 512), BF16)
    t("onesb", (1, 512), BF16)
    t("onesf", (1, 128), F32)
    t("browst", (1, 512), BF16)
    t("ebrow", (1, 1024), BF16)
    t("statc", (128, 16), F32)
    t("epsc", (128, 1), F32)
    t("lgt", (128, 8), F32)
    t("negbig", (128, 8), F32)
    t("e1t", (128, 8), F32)
    t("lmt", (128, 8), F32)
    t("ge2f", (128, 8), F32)
    t("eqm", (128, 8), U8)
    t("wall", (128, 64), F32)
    t("reciprow", (1, 512), F32)
    t("lnc", (128, 16), F32)
    if mode in ("l0", "fused"):
        t("alibi", (128, NH * 32), F32)
        t("eb1c", (128, 128), F32)
        t("gatew", (128, 64), BF16)
        t("gatewf", (128, 64), F32)
        t("gateb", (8, 1), F32)
        t("gcw", (128, 16), BF16)
        t("gcb", (1, 1), F32)
        t("abelb", (128, 8), F32)
        t("grow", (1, 1024), F32)
    return sb



def load_const_sbuf(ctx):
    p, sb, dr, bf = ctx.plan, ctx.sb, ctx.dram, ctx.bufs
    def ld(dst_name, src_name, queue="sy"):
        dst = sb[dst_name]
        src = dr[src_name]
        p.dma(queue, lambda h, d=dst, s=src: h.dma_start(out=d.ap()[:], in_=s),
              writes=[bf.buf(dst_name)])
    ld("iota", "iota_row")
    ld("thrT", "thr")
    ld("identf", "ident_f")
    if ctx.mode in ("l0", "fused"):
        ld("alibi", "alibi_cols")
        ld("eb1c", "eb1c")
        ld("gateb", "gate_b")
        ld("gcb", "gc_b")
        ld("abelb", "abel_b")
        gw = dr["gate_w"].rearrange("(c p) e -> p c e", p=128)
        p.dma("g", lambda h, gw=gw: h.dma_start(
            out=ctx.sb["gatew"].ap().rearrange("p (c e) -> p c e", c=8), in_=gw),
            writes=[bf.buf("gatew")])
        p.dma("sy", lambda h, gw=gw: h.dma_start(
            out=ctx.sb["gatewf"].ap().rearrange("p (c e) -> p c e", c=8), in_=gw),
            writes=[bf.buf("gatewf")])
        p.dma("g", lambda h: h.dma_start(out=ctx.sb["gcw"].ap()[:], in_=ctx.dram["gc_w"]),
              writes=[bf.buf("gcw")])
        p.dma("sy", lambda h: h.dma_start(out=sb["lnc"].ap()[:, 0:8], in_=dr["ln_gamma"]),
              writes=[bf.buf("lnc")])
        p.dma("sy", lambda h: h.dma_start(out=sb["lnc"].ap()[:, 8:16], in_=dr["ln_beta"]),
              writes=[bf.buf("lnc")])
    p.op("v", lambda h: h.memset(sb["zeros"].ap()[:], 0.0), writes=[bf.buf("zeros")])
    p.op("v", lambda h: h.memset(sb["epsc"].ap()[:], EPS), writes=[bf.buf("epsc")])
    p.op("v", lambda h: h.memset(sb["onesb"].ap()[:], 1.0), writes=[bf.buf("onesb")])
    p.op("v", lambda h: h.memset(sb["onesf"].ap()[:], 1.0), writes=[bf.buf("onesf")])
    p.op("v", lambda h: h.memset(sb["negbig"].ap()[:], -1e30), writes=[bf.buf("negbig")])



def emit_ln(ctx, li, x_in, x_in_buf):
    p, sb, dr, bf = ctx.plan, ctx.sb, ctx.dram, ctx.bufs
    st = sb["statc"].ap()
    stb = bf.buf("statc")
    scrb = bf.buf("scr")
    xnb = bf.buf("xn_dram")
    xn2b = bf.buf("xn2p_dram")
    for t in range(16):
        slot = t % 2
        xs = sb["xstage"].ap()[:, 1024 * slot:1024 * (slot + 1)]
        xsb = bf.buf(f"xstage{slot}")
        p.dma("sy", lambda h, o=xs, i=x_in[128 * t:128 * (t + 1), :]: h.dma_start(out=o, in_=i),
              reads=[x_in_buf], writes=[xsb])
        p.op("v", lambda h, o=st[:, 0:1], i=xs: h.reduce_sum(o, i, axis=AX.X),
             reads=[xsb], writes=[stb])
        p.op("a", lambda h, o=sb["scr"].ap()[:, 0:1024], i=xs, a=st[:, 1:2]:
             h.activation(o, i, AF.Square, accum_out=a),
             reads=[xsb], writes=[scrb, stb])
        p.op("v", lambda h: h.tensor_scalar(st[:, 2:3], st[:, 0:1], 1.0 / 1024, None, ALU.mult),
             reads=[stb], writes=[stb])
        p.op("v", lambda h: h.tensor_scalar(st[:, 3:4], st[:, 1:2], 1.0 / 1024, None, ALU.mult),
             reads=[stb], writes=[stb])
        p.op("v", lambda h: h.tensor_tensor(st[:, 4:5], st[:, 2:3], st[:, 2:3], ALU.mult),
             reads=[stb], writes=[stb])
        p.op("v", lambda h: h.tensor_tensor(st[:, 5:6], st[:, 3:4], st[:, 4:5], ALU.subtract),
             reads=[stb], writes=[stb])
        p.op("a", lambda h: h.activation(st[:, 6:7], st[:, 5:6], AF.Sqrt, bias=sb["epsc"].ap()),
             reads=[stb, bf.buf("epsc")], writes=[stb])
        p.op("v", lambda h: h.reciprocal(st[:, 7:8], st[:, 6:7]),
             reads=[stb], writes=[stb])
        p.op("v", lambda h: h.tensor_scalar(st[:, 8:9], st[:, 2:3], st[:, 7:8], -1.0, ALU.mult, ALU.mult),
             reads=[stb], writes=[stb])
        xnt, xntb = ctx.stq(slot)
        p.op("a", lambda h, o=xnt, i=xs: h.activation(o, i, AF.Identity, bias=st[:, 8:9], scale=st[:, 7:8]),
             reads=[xsb, stb], writes=[xntb])
        p.dma("sy", lambda h, o=dr["xn_dram"][128 * t:128 * (t + 1), :], i=xnt: h.dma_start(out=o, in_=i),
              reads=[xntb], writes=[xnb])
        if li == 0:
            p.op("v", lambda h: h.tensor_tensor(st[:, 9:10], st[:, 5:6], st[:, 7:8], ALU.mult),
                 reads=[stb], writes=[stb])
            p.op("v", lambda h: h.tensor_tensor(st[:, 10:11], st[:, 9:10], st[:, 7:8], ALU.mult),
                 reads=[stb], writes=[stb])
            p.op("a", lambda h: h.activation(st[:, 11:12], st[:, 10:11], AF.Sqrt, bias=sb["epsc"].ap()),
                 reads=[stb, bf.buf("epsc")], writes=[stb])
            p.op("v", lambda h: h.reciprocal(st[:, 12:13], st[:, 11:12]),
                 reads=[stb], writes=[stb])
            p.op("v", lambda h: h.tensor_tensor(st[:, 13:14], st[:, 7:8], st[:, 12:13], ALU.mult),
                 reads=[stb], writes=[stb])
            p.op("v", lambda h: h.tensor_scalar(st[:, 14:15], st[:, 2:3], st[:, 13:14], -1.0, ALU.mult, ALU.mult),
                 reads=[stb], writes=[stb])
            xn2t, xn2tb = ctx.stq(2 + slot)
            p.op("a", lambda h, o=xn2t, i=xs: h.activation(o, i, AF.Identity, bias=st[:, 14:15], scale=st[:, 13:14]),
                 reads=[xsb, stb], writes=[xn2tb])
            p.dma("sy", lambda h, o=dr["xn2p_dram"][128 * t:128 * (t + 1), :], i=xn2t: h.dma_start(out=o, in_=i),
                  reads=[xn2tb], writes=[xn2b])


def emit_xnT(ctx):
    p, sb, dr, bf = ctx.plan, ctx.sb, ctx.dram, ctx.bufs
    for c in range(8):
        dst = sb["A1"].ap()[:, 2048 * c:2048 * (c + 1)]
        src = dr["xn_dram"][:, 128 * c:128 * (c + 1)]
        p.dma("sy", lambda h, o=dst, i=src: h.dma_start_transpose(o, i),
              reads=[bf.buf("xn_dram")], writes=[bf.buf(f"A1_{c // 4}")])



def emit_qkv(ctx, li):
    p, sb, dr, bf = ctx.plan, ctx.sb, ctx.dram, ctx.bufs
    a1 = sb["A1"].ap()
    a1r = [bf.buf("A1_0"), bf.buf("A1_1")]
    w1b, w2b = bf.buf("W1"), bf.buf("W2")
    if li == 0:
        wsrc = dr["w_attn_T"]
        qs, ks, vs = (chunked(wsrc[:, 0:1024]), chunked(wsrc[:, 1024:2048]),
                      chunked(wsrc[:, 2048:3072]))
    else:
        qs, ks, vs = chunked(dr["wq"]), chunked(dr["wk"]), chunked(dr["wv"])
    def wload(c0, c1, src, arena, buf):
        dst = sb[arena].ap()[:, c0:c1].rearrange("p (c n) -> p c n", c=8)
        p.dma("g", lambda h, o=dst, i=src: h.dma_start(out=o, in_=i), writes=[buf])
    wload(0, 8192, qs, "W1", w1b)
    wload(8192, 16384, ks, "W1", w1b)
    wload(0, 8192, vs, "W2", w2b)

    for which, wc0 in (("q", 0), ("k", 8192)):
        for m in range(8):
            for n in range(4):
                i = (m * 4 + n) % 2
                bank, bb = ctx.bank(i), ctx.bankbuf(i)
                terms = []
                for k in range(8):
                    lhs = sb["W1"].ap()[:, wc0 + 1024 * k + 128 * m: wc0 + 1024 * k + 128 * (m + 1)]
                    rhs = a1[:, 2048 * k + 512 * n: 2048 * k + 512 * (n + 1)]
                    terms.append((bank, lhs, rhs, k == 0, k == 7))
                mm_group(p, terms, bb, [w1b] + a1r)
                ss, ssb = ctx.stq5(i)
                p.op("v", lambda h, o=ss, b=bank: h.tensor_copy(o, b), reads=[bb], writes=[ssb])
                if which == "q":
                    dst = dr["q_dram"][128 * m:128 * (m + 1), 512 * n:512 * (n + 1)]
                    p.dma("sy", lambda h, o=dst, i2=ss: h.dma_start(out=o, in_=i2),
                          reads=[ssb], writes=[bf.buf("q_dram")])
                else:
                    dst = dr["ag_in"][128 * m:128 * (m + 1), 512 * n:512 * (n + 1)]
                    p.dma("sy", lambda h, o=dst, i2=ss: h.dma_start(out=o, in_=i2),
                          reads=[ssb], writes=[bf.buf("ag_in")])

    for tq in range(16):
        for n in range(2):
            i = (tq * 2 + n) % 2
            bank, bb = ctx.bank(i), ctx.bankbuf(i)
            terms = []
            for k in range(8):
                lhs = a1[:, 2048 * k + 128 * tq: 2048 * k + 128 * (tq + 1)]
                rhs = sb["W2"].ap()[:, 1024 * k + 512 * n: 1024 * k + 512 * (n + 1)]
                terms.append((bank, lhs, rhs, k == 0, k == 7))
            mm_group(p, terms, bb, [w2b] + a1r)
            ss, ssb = ctx.stq5(i)
            p.op("v", lambda h, o=ss, b=bank: h.tensor_copy(o, b), reads=[bb], writes=[ssb])
            off = 1024 * 2048 + (128 * tq) * 1024 + 512 * n
            dst = flat_ap(dr["ag_in"], off, [[1024, 128], [1, 512]])
            p.dma("sy", lambda h, o=dst, i2=ss: h.dma_start(out=o, in_=i2),
                  reads=[ssb], writes=[bf.buf("ag_in")])


AG_SERIAL = False

def emit_precast_l1(ctx):
    p, dr, bf = ctx.plan, ctx.dram, ctx.bufs
    for name in ("wg_bf", "wu_bf", "wd_bf"):
        srcn = {"wg_bf": "wg_T", "wu_bf": "wu_T", "wd_bf": "wd_T"}[name]
        p.dma("g", lambda h, o=dr[name], i=dr[srcn]: h.dma_start(out=o, in_=i),
              writes=[bf.buf(name)])
    ctx.precast_done = [bf.buf("wg_bf"), bf.buf("wu_bf"), bf.buf("wd_bf")]


def emit_ag(ctx):
    p, dr, bf = ctx.plan, ctx.dram, ctx.bufs
    extra = getattr(ctx, "precast_done", None) or []
    tok = p.coll(lambda h: h.collective_compute(
        "AllGather", ALU.bypass,
        replica_groups=[[0, 1], [2, 3], [4, 5], [6, 7]],
        ins=[dr["ag_in"]], outs=[dr["ag_out"]]),
        reads=[bf.buf("ag_in")] + list(extra), writes=[bf.buf("ag_out")])
    if AG_SERIAL:
        for nm in ("W1", "W2", "A1_0", "A1_1", "A3a", "A3b", "xstage0", "xstage1",
                   "pbuf0", "pbuf1", "scr", "statc", "ffn_dram", "stage0", "stage1",
                   "stage2", "stage3"):
            b = bf.buf(nm)
            b.rs.append(tok)


def emit_kv_reorder(ctx):
    p, dr, bf = ctx.plan, ctx.dram, ctx.bufs
    blocks0 = [a // 128 for a in own_token_abs(0)[::128]]
    blocks1 = [a // 128 for a in own_token_abs(1)[::128]]
    src_of_abs = {}
    for loc, ab in enumerate(blocks0):
        src_of_abs[ab] = (0, loc)
    for loc, ab in enumerate(blocks1):
        src_of_abs[ab] = (1, loc)
    agb, kb, vb = bf.buf("ag_out"), bf.buf("kabs"), bf.buf("vabs")
    for ab in range(32):
        rank, loc = src_of_abs[ab]
        src = dr["ag_out"][2048 * rank: 2048 * rank + 1024, 128 * loc:128 * (loc + 1)]
        dst = dr["kabs"][:, 128 * ab:128 * (ab + 1)]
        p.dma("sy", lambda h, o=dst, i=src: h.dma_start(out=o, in_=i), reads=[agb], writes=[kb])
        off = (2048 * rank + 1024) * 2048 + (128 * loc) * 1024
        src = flat_ap(dr["ag_out"], off, [[1024, 128], [1, 1024]])
        dst = dr["vabs"][128 * ab:128 * (ab + 1), :]
        p.dma("sy", lambda h, o=dst, i=src: h.dma_start(out=o, in_=i), reads=[agb], writes=[vb])



def emit_flash(ctx, li):
    p, sb, dr, bf = ctx.plan, ctx.sb, ctx.dram, ctx.bufs
    a1w = [bf.buf("A1_0"), bf.buf("A1_1")]
    kbuf, vbuf, qbuf = bf.buf("kslot"), bf.buf("vslot"), bf.buf("qslot")
    brb = bf.buf("browst")
    mqb = bf.buf("maskq")
    iob, thb = bf.buf("iota"), bf.buf("thrT")
    kabs_b, vabs_b, qdr_b = bf.buf("kabs"), bf.buf("vabs"), bf.buf("q_dram")
    alb = bf.buf("alibi") if li == 0 else None
    zb = bf.buf("zeros")
    onb = bf.buf("onesb")
    ofb = bf.buf("onesf")

    ps_y, ps_y_buf = ctx.ps.ap()[0:65, 1024:1536], ctx.bankbuf(2)
    ps_bc, ps_bc_buf = ctx.ps.ap()[0:64, 1536:2048], ctx.bankbuf(3)
    vsl = sb["vslot"].ap().rearrange("p (c w) -> p c w", w=65)

    if li == 1:
        p.op("v", lambda h: h.memset(sb["browst"].ap()[:], -8.0 * BCONST), writes=[brb])

    for j in range(4):
        s = SLOTS[j]
        for i8 in range(8):
            dst = sb["maskq"].ap()[:, 512 * i8:512 * (i8 + 1)]
            thr = sb["thrT"].ap()[:, 8 * j + i8: 8 * j + i8 + 1]
            p.op("v", lambda h, o=dst, t=thr: h.tensor_scalar(o, sb["iota"].ap()[:], t, None, ALU.is_ge),
                 reads=[iob, thb], writes=[mqb])
        for hd in range(NH):
            p.dma("sy", lambda h, o=sb["kslot"].ap()[:, 0:128 * s],
                  i=dr["kabs"][64 * hd:64 * (hd + 1), 0:128 * s]: h.dma_start(out=o, in_=i),
                  reads=[kabs_b], writes=[kbuf])
            src = flat_ap(dr["vabs"], 64 * hd, [[1024, 128], [128 * 1024, s], [1, 64]])
            p.dma("sy", lambda h, o=vsl[:, 0:s, 0:64], i=src: h.dma_start(out=o, in_=i),
                  reads=[vabs_b], writes=[vbuf])
            p.op("v", lambda h, o=vsl[:, 0:s, 64:65]: h.memset(o, 1.0), writes=[vbuf])
            qsl = sb["qslot"].ap()[:, 512 * (hd % 2):512 * (hd % 2 + 1)]
            p.dma("sy", lambda h, o=qsl,
                  i=dr["q_dram"][64 * hd:64 * (hd + 1), 512 * j:512 * (j + 1)]: h.dma_start(out=o, in_=i),
                  reads=[qdr_b], writes=[qbuf])
            if li == 0:
                p.dma("sy", lambda h, o=sb["browst"].ap()[:],
                      i=dr["brow0"][hd:hd + 1, 512 * j:512 * (j + 1)]: h.dma_start(out=o, in_=i),
                      reads=[bf.buf("brow0_src")], writes=[brb])

            for c in range(s):
                i = c % 2
                bank, bb = ctx.bank(i), ctx.bankbuf(i)
                terms = [
                    (bank, sb["kslot"].ap()[:, 128 * c:128 * (c + 1)], qsl, True, False),
                    (bank, sb["onesb"].ap()[:, 0:128], sb["browst"].ap()[:], False, True),
                ]
                mm_group(p, terms, bb, [kbuf, qbuf, onb, brb])
                pb = sb["pbuf"].ap()[:, 512 * i:512 * (i + 1)]
                pbb = bf.buf(f"pbuf{i}")
                if li == 0:
                    bias = sb["alibi"].ap()[:, hd * 32 + c: hd * 32 + c + 1]
                    p.op("a", lambda h, o=pb, b=bank, bi=bias: h.activation(o, b, AF.Exp, bias=bi, scale=0.125),
                         reads=[bb, alb], writes=[pbb])
                else:
                    p.op("a", lambda h, o=pb, b=bank: h.activation(o, b, AF.Exp, scale=0.125),
                         reads=[bb], writes=[pbb])
                if c >= s - 8:
                    msk = sb["maskq"].ap()[:, 512 * (c - (s - 8)):512 * (c - (s - 8) + 1)]
                    pm = sb["pbuf"].ap()[:, 512 * (2 + i):512 * (3 + i)]
                    pmb = bf.buf(f"pbuf{2 + i}")
                    p.op("v", lambda h, o=pm, m=msk, src=pb: h.select(o, m, src, sb["zeros"].ap()[:]),
                         reads=[mqb, zb, pbb], writes=[pmb])
                    pv_src, pv_srcb = pm, pmb
                else:
                    pv_src, pv_srcb = pb, pbb
                lhs = vsl[:, c, 0:65]
                p.op("t", lambda h, o=ps_y, l=lhs, r=pv_src, st=(c == 0), sp=(c == s - 1):
                     h.matmul(o, l, r, start=st, stop=sp),
                     reads=[vbuf, pv_srcb], writes=[ps_y_buf])

            rr = sb["reciprow"].ap()
            rrb = bf.buf("reciprow")
            p.op("v", lambda h, o=rr: h.reciprocal(o, ps_y[64:65, 0:512]),
                 reads=[ps_y_buf], writes=[rrb])
            p.op("t", lambda h, o=ps_bc: h.matmul(o, sb["onesf"].ap()[:, 0:64], rr, start=True, stop=True),
                 reads=[ofb, rrb], writes=[ps_bc_buf])
            rbb = bf.buf("rb")
            p.op("v", lambda h: h.tensor_copy(sb["rb"].ap()[0:64, :], ps_bc),
                 reads=[ps_bc_buf], writes=[rbb])
            dst = sb["A1"].ap()[64 * (hd % 2):64 * (hd % 2) + 64,
                                2048 * (hd // 2) + 512 * j: 2048 * (hd // 2) + 512 * (j + 1)]
            p.op("v", lambda h, o=dst: h.tensor_tensor(o, ps_y[0:64, 0:512], sb["rb"].ap()[0:64, :], ALU.mult),
                 reads=[ps_y_buf, rbb], writes=[a1w[(hd // 2) // 4]])


def emit_attn_proj(ctx, li):
    p, sb, dr, bf = ctx.plan, ctx.sb, ctx.dram, ctx.bufs
    w1b = bf.buf("W1")
    a1r = [bf.buf("A1_0"), bf.buf("A1_1")]
    src = chunked(dr["w_proj_T"] if li == 0 else dr["wo"])
    dst = sb["W1"].ap()[:, 0:8192].rearrange("p (c n) -> p c n", c=8)
    p.dma("g", lambda h, o=dst, i=src: h.dma_start(out=o, in_=i), writes=[w1b])
    for tq in range(16):
        for n in range(2):
            i = (tq * 2 + n) % 2
            bank, bb = ctx.bank(i), ctx.bankbuf(i)
            terms = []
            for k in range(8):
                lhs = sb["A1"].ap()[:, 2048 * k + 128 * tq:2048 * k + 128 * (tq + 1)]
                rhs = sb["W1"].ap()[:, 1024 * k + 512 * n:1024 * k + 512 * (n + 1)]
                terms.append((bank, lhs, rhs, k == 0, k == 7))
            mm_group(p, terms, bb, [w1b] + a1r)
            ss, ssb = ctx.stq5(i)
            p.op("v", lambda h, o=ss, b=bank: h.tensor_copy(o, b), reads=[bb], writes=[ssb])
            d2 = dr["attn_dram"][128 * tq:128 * (tq + 1), 512 * n:512 * (n + 1)]
            p.dma("sy", lambda h, o=d2, i2=ss: h.dma_start(out=o, in_=i2),
                  reads=[ssb], writes=[bf.buf("attn_dram")])



def emit_gate0(ctx):
    p, sb, dr, bf = ctx.plan, ctx.sb, ctx.dram, ctx.bufs
    ps_lb = ctx.bankbuf(4)
    ps_t, ps_tb = ctx.ps.ap()[0:128, 2560:3072], ctx.bankbuf(5)
    scrb = bf.buf("scr")
    st = sb["statc"].ap()
    stb = bf.buf("statc")
    for n in range(2):
        psl = ctx.ps.ap()[0:8, 2048 + 512 * n:2048 + 512 * (n + 1)]
        dst = sb["scr"].ap()[0:8, 1024 + 512 * n:1024 + 512 * (n + 1)]
        p.op("a", lambda h, o=dst, i=psl: h.activation(o, i, AF.Identity, bias=sb["gateb"].ap()[:]),
             reads=[ps_lb, bf.buf("gateb")], writes=[scrb])
    for s in range(8):
        src = sb["scr"].ap()[0:8, 1024 + 128 * s:1024 + 128 * (s + 1)]
        p.op("t", lambda h, i=src: h.transpose(ps_t[0:128, 0:8], i, sb["identf"].ap()[0:8, 0:8]),
             reads=[scrb, bf.buf("identf")], writes=[ps_tb])
        lg = sb["lgt"].ap()
        lgb = bf.buf("lgt")
        p.op("v", lambda h: h.tensor_copy(lg, ps_t[0:128, 0:8]), reads=[ps_tb], writes=[lgb])
        p.op("v", lambda h: h.reduce_max(st[:, 0:1], lg, axis=AX.X), reads=[lgb], writes=[stb])
        p.op("v", lambda h: h.tensor_scalar(st[:, 1:2], st[:, 0:1], -1.0, None, ALU.mult),
             reads=[stb], writes=[stb])
        p.op("v", lambda h: h.tensor_scalar(sb["eqm"].ap(), lg, st[:, 0:1], None, ALU.is_ge),
             reads=[lgb, stb], writes=[bf.buf("eqm")])
        p.op("v", lambda h: h.select(sb["lmt"].ap(), sb["eqm"].ap(), sb["negbig"].ap(), lg),
             reads=[bf.buf("eqm"), bf.buf("negbig"), lgb], writes=[bf.buf("lmt")])
        p.op("v", lambda h: h.reduce_max(st[:, 2:3], sb["lmt"].ap(), axis=AX.X),
             reads=[bf.buf("lmt")], writes=[stb])
        p.op("a", lambda h: h.activation(sb["e1t"].ap(), lg, AF.Exp, bias=st[:, 1:2]),
             reads=[lgb, stb], writes=[bf.buf("e1t")])
        p.op("a", lambda h: h.activation(st[:, 3:4], st[:, 2:3], AF.Exp, bias=st[:, 1:2]),
             reads=[stb], writes=[stb])
        p.op("v", lambda h: h.tensor_scalar(st[:, 4:5], st[:, 3:4], 1.0, None, ALU.add),
             reads=[stb], writes=[stb])
        p.op("v", lambda h: h.reciprocal(st[:, 5:6], st[:, 4:5]), reads=[stb], writes=[stb])
        p.op("v", lambda h: h.tensor_scalar(sb["ge2f"].ap(), lg, st[:, 2:3], None, ALU.is_ge),
             reads=[lgb, stb], writes=[bf.buf("ge2f")])
        dst = sb["wall"].ap()[:, 8 * s:8 * (s + 1)]
        p.op("v", lambda h, o=dst: h.scalar_tensor_tensor(o, sb["e1t"].ap(), st[:, 5:6],
                                                          sb["ge2f"].ap(), ALU.mult, ALU.mult),
             reads=[bf.buf("e1t"), stb, bf.buf("ge2f")], writes=[bf.buf("wall")])


def emit_E_load(ctx, r, arena, wb):
    p, sb, dr, bf = ctx.plan, ctx.sb, ctx.dram, ctx.bufs
    if ctx.mode == "l0":
        src = chunked(dr["E"][r])
        dst = sb[arena].ap()[:, 0:8192].rearrange("p (c n) -> p c n", c=8)
        p.dma("sy", lambda h, o=dst, i=src: h.dma_start(out=o, in_=i),
              reads=[bf.buf("E_src")], writes=[wb])
    else:
        dst = sb[arena].ap()[:, 0:8192].rearrange("p (c j) -> p c j", c=8).rearrange(
            "p c (rk cc) -> p c rk cc", rk=8)
        src = flat_ap(dr["eg_out"], (1024 * r) * 128,
                      [[128, 128], [128 * 128, 8], [3072 * 128, 8], [1, 128]])
        p.dma("sy", lambda h, o=dst, i=src: h.dma_start(out=o, in_=i),
              reads=[bf.buf("eg_out")], writes=[wb])


def emit_ffn0(ctx):
    p, sb, dr, bf = ctx.plan, ctx.sb, ctx.dram, ctx.bufs
    w1b, w2b = bf.buf("W1"), bf.buf("W2")
    a3ab, a3bb = bf.buf("A3a"), bf.buf("A3b")
    a1ab, a1bb = bf.buf("A1_0"), bf.buf("A1_1")
    onb = bf.buf("onesb")
    for th in range(2):
        for c in range(8):
            dst = sb["A3"].ap()[:, 1024 * c:1024 * (c + 1)]
            src = dr["xn2p_dram"][1024 * th:1024 * (th + 1), 128 * c:128 * (c + 1)]
            p.dma("sy", lambda h, o=dst, i=src: h.dma_start_transpose(o, i),
                  reads=[bf.buf("xn2p_dram")], writes=[a3ab])
        ps_l, ps_lb = ctx.ps.ap()[0:8, 2048:2560], ctx.bankbuf(4)
        scrb = bf.buf("scr")
        for c in range(8):
            sl = sb["A3"].ap()[:, 1024 * c:1024 * (c + 1)]
            fs = sb["scr"].ap()[:, 0:1024]
            p.op("a", lambda h, o=fs, i=sl, g=sb["lnc"].ap()[:, c:c + 1], b2=sb["lnc"].ap()[:, 8 + c:9 + c]:
                 h.activation(o, i, AF.Identity, bias=b2, scale=g),
                 reads=[bf.buf("lnc"), a3ab, scrb], writes=[scrb])
            for nb in range(2):
                p.op("t", lambda h, c=c, nb=nb: h.matmul(
                    ps_l[:, 512 * nb:512 * (nb + 1)] if False else ctx.ps.ap()[0:8, 2048 + 512 * nb:2048 + 512 * (nb + 1)],
                    sb["gatewf"].ap()[:, 8 * c:8 * (c + 1)],
                    sb["scr"].ap()[:, 512 * nb:512 * (nb + 1)],
                    start=(c == 0), stop=(c == 7)),
                    reads=[bf.buf("gatewf"), scrb], writes=[ps_lb])
            p.op("v", lambda h, o=sl, i=fs: h.tensor_copy(o, i),
                 reads=[scrb], writes=[a3ab])

        emit_gate0(ctx)

        for e in range(EXP_N):
            src = chunked(dr["ew1"][e])
            dst = sb["W1"].ap().rearrange("p (c n) -> p c n", c=8)
            p.dma("g", lambda h, o=dst, i=src: h.dma_start(out=o, in_=i), writes=[w1b])
            src = chunked(dr["ew2"][e])
            dst = sb["W2"].ap().rearrange("p (c n) -> p c n", c=16)
            p.dma("g", lambda h, o=dst, i=src: h.dma_start(out=o, in_=i), writes=[w2b])
            p.dma("g", lambda h, o=sb["ebrow"].ap()[:], i=dr["eb2"][e:e + 1, :]: h.dma_start(out=o, in_=i),
                  writes=[bf.buf("ebrow")])

            for tb in range(2):
                for m in range(16):
                    i = m % 2
                    bank, bb = ctx.bank(i), ctx.bankbuf(i)
                    terms = []
                    for k in range(8):
                        lhs = sb["W1"].ap()[:, 2048 * k + 128 * m:2048 * k + 128 * (m + 1)]
                        rhs = sb["A3"].ap()[:, 1024 * k + 512 * tb:1024 * k + 512 * (tb + 1)]
                        terms.append((bank, lhs, rhs, k == 0, k == 7))
                    mm_group(p, terms, bb, [w1b, a3ab])
                    hdst = sb["A1"].ap()[:, 512 * m:512 * (m + 1)]
                    bias = sb["eb1c"].ap()[:, 16 * e + m:16 * e + m + 1]
                    p.op("a", lambda h, o=hdst, b=bank, bi=bias: h.activation(o, b, AF.Gelu, bias=bi),
                         reads=[bb, bf.buf("eb1c")], writes=[a1ab])
                for t4 in range(4):
                    for dn in range(2):
                        ib = t4 * 2 + dn
                        bank, bb = ctx.bank(ib), ctx.bankbuf(ib)
                        terms = []
                        for k in range(16):
                            lhs = sb["A1"].ap()[:, 512 * k + 128 * t4:512 * k + 128 * (t4 + 1)]
                            rhs = sb["W2"].ap()[:, 1024 * k + 512 * dn:1024 * k + 512 * (dn + 1)]
                            terms.append((bank, lhs, rhs, k == 0, False))
                        terms.append((bank, sb["onesb"].ap()[:, 0:128],
                                      sb["ebrow"].ap()[:, 512 * dn:512 * (dn + 1)], False, True))
                        mm_group(p, terms, bb, [a1ab, w2b, onb, bf.buf("ebrow")])
                        tt = tb * 4 + t4
                        acc = sb["A3"].ap()[:, 8192 + 1024 * tt + 512 * dn:8192 + 1024 * tt + 512 * (dn + 1)]
                        wcol = sb["wall"].ap()[:, 8 * tt + e:8 * tt + e + 1]
                        if e == 0:
                            p.op("v", lambda h, o=acc, b=bank, w=wcol: h.tensor_scalar(o, b, w, None, ALU.mult),
                                 reads=[bb, bf.buf("wall")], writes=[a3bb])
                        else:
                            p.op("v", lambda h, o=acc, b=bank, w=wcol: h.scalar_tensor_tensor(
                                o, b, w, o, ALU.mult, ALU.add),
                                reads=[bb, bf.buf("wall")], writes=[a3bb])

        for tt in range(8):
            src2 = sb["A3"].ap()[:, 8192 + 1024 * tt:8192 + 1024 * (tt + 1)]
            p.dma("sy", lambda h, o=dr["moe_tok"][128 * tt:128 * (tt + 1), :], i=src2: h.dma_start(out=o, in_=i),
                  reads=[a3bb], writes=[bf.buf("moe_tok")])
        for c in range(8):
            dst = sb["A1"].ap()[:, 1024 * c:1024 * (c + 1)]
            src2 = dr["moe_tok"][:, 128 * c:128 * (c + 1)]
            p.dma("sy", lambda h, o=dst, i=src2: h.dma_start_transpose(o, i),
                  reads=[bf.buf("moe_tok")], writes=[a1ab])
        for r in range(3):
            warena = "W1" if r % 2 == 0 else "W2"
            wb = w1b if r % 2 == 0 else w2b
            emit_E_load(ctx, r, warena, wb)
            src_cols = 0 if r % 2 == 0 else 8192
            dst_cols = 8192 if r % 2 == 0 else 0
            srcb = a1ab if r % 2 == 0 else a1bb
            dstb = a1bb if r % 2 == 0 else a1ab
            for m in range(8):
                for n in range(2):
                    i = (m * 2 + n) % 2
                    bank, bb = ctx.bank(i), ctx.bankbuf(i)
                    terms = []
                    for k in range(8):
                        lhs = sb[warena].ap()[:, 1024 * k + 128 * m:1024 * k + 128 * (m + 1)]
                        rhs = sb["A1"].ap()[:, src_cols + 1024 * k + 512 * n:src_cols + 1024 * k + 512 * (n + 1)]
                        terms.append((bank, lhs, rhs, k == 0, k == 7))
                    mm_group(p, terms, bb, [wb, srcb])
                    dst = sb["A1"].ap()[:, dst_cols + 1024 * m + 512 * n:dst_cols + 1024 * m + 512 * (n + 1)]
                    p.op("v", lambda h, o=dst, b=bank: h.tensor_copy(o, b), reads=[bb], writes=[dstb])

        src = chunked(dr["abel_w"])
        dst = sb["W1"].ap()[:, 0:8192].rearrange("p (c n) -> p c n", c=8)
        p.dma("g", lambda h, o=dst, i=src: h.dma_start(out=o, in_=i), writes=[w1b])
        for m in range(8):
            for n in range(2):
                i = (m * 2 + n) % 2
                bank, bb = ctx.bank(i), ctx.bankbuf(i)
                terms = []
                for k in range(8):
                    lhs = sb["W1"].ap()[:, 1024 * k + 128 * m:1024 * k + 128 * (m + 1)]
                    rhs = sb["A3"].ap()[:, 1024 * k + 512 * n:1024 * k + 512 * (n + 1)]
                    terms.append((bank, lhs, rhs, k == 0, k == 7))
                mm_group(p, terms, bb, [w1b, a3ab])
                dst = sb["A3"].ap()[:, 8192 + 1024 * m + 512 * n:8192 + 1024 * m + 512 * (n + 1)]
                bias = sb["abelb"].ap()[:, m:m + 1]
                p.op("a", lambda h, o=dst, b=bank, bi=bias: h.activation(o, b, AF.Gelu, bias=bi),
                     reads=[bb, bf.buf("abelb")], writes=[a3bb])

        ps_g, ps_gb = ctx.ps.ap()[0:1, 2048:2560], ctx.bankbuf(4)
        for n in range(2):
            terms = []
            for k in range(8):
                lhs = sb["gcw"].ap()[:, k:k + 1]
                rhs = sb["A3"].ap()[:, 8192 + 1024 * k + 512 * n:8192 + 1024 * k + 512 * (n + 1)]
                terms.append((ps_g, lhs, rhs, k == 0, False))
            for k in range(8):
                lhs = sb["gcw"].ap()[:, 8 + k:9 + k]
                rhs = sb["A1"].ap()[:, 8192 + 1024 * k + 512 * n:8192 + 1024 * k + 512 * (n + 1)]
                terms.append((ps_g, lhs, rhs, False, k == 7))
            mm_group(p, terms, ps_gb, [bf.buf("gcw"), a3bb, a1bb])
            dst = sb["grow"].ap()[:, 512 * n:512 * (n + 1)]
            p.op("a", lambda h, o=dst: h.activation(o, ps_g, AF.Sigmoid, bias=sb["gcb"].ap()[:]),
                 reads=[ps_gb, bf.buf("gcb")], writes=[bf.buf("grow")])

        ps_bc, ps_bc_buf = ctx.ps.ap()[0:128, 1536:2048], ctx.bankbuf(3)
        for n in range(2):
            p.op("t", lambda h, r2=sb["grow"].ap()[:, 512 * n:512 * (n + 1)]:
                 h.matmul(ps_bc, sb["onesf"].ap()[:, 0:128], r2, start=True, stop=True),
                 reads=[bf.buf("grow"), bf.buf("onesf")], writes=[ps_bc_buf])
            p.op("v", lambda h: h.tensor_copy(sb["rb"].ap()[:], ps_bc),
                 reads=[ps_bc_buf], writes=[bf.buf("rb")])
            for m in range(8):
                ab = sb["A3"].ap()[:, 8192 + 1024 * m + 512 * n:8192 + 1024 * m + 512 * (n + 1)]
                mo = sb["A1"].ap()[:, 8192 + 1024 * m + 512 * n:8192 + 1024 * m + 512 * (n + 1)]
                df = sb["pbuf"].ap()[:, 0:512]
                p.op("v", lambda h, o=df, a=mo, b2=ab: h.tensor_tensor(o, a, b2, ALU.subtract),
                     reads=[a1bb, a3bb], writes=[bf.buf("pbuf0")])
                p.op("v", lambda h, o=df: h.tensor_tensor(o, o, sb["rb"].ap()[:], ALU.mult),
                     reads=[bf.buf("rb"), bf.buf("pbuf0")], writes=[bf.buf("pbuf0")])
                mx = sb["pbuf"].ap()[:, 512:1024]
                p.op("v", lambda h, o=mx, a=df, b2=ab: h.tensor_tensor(o, a, b2, ALU.add),
                     reads=[bf.buf("pbuf0"), a3bb], writes=[bf.buf("pbuf1")])
                dst = dr["mixed_dram"][128 * m:128 * (m + 1), 1024 * th + 512 * n:1024 * th + 512 * (n + 1)]
                p.dma("sy", lambda h, o=dst, i=mx: h.dma_start(out=o, in_=i),
                      reads=[bf.buf("pbuf1")], writes=[bf.buf("mixed_dram")])



def emit_ffn1(ctx):
    p, sb, dr, bf = ctx.plan, ctx.sb, ctx.dram, ctx.bufs
    w1b, w2b = bf.buf("W1"), bf.buf("W2")
    a1r = [bf.buf("A1_0"), bf.buf("A1_1")]
    a3b = bf.buf("A3a")
    pc = getattr(ctx, "precast_done", None) or []
    for third in range(3):
        for (name, c0, c1) in (("wg_bf", 0, 8192), ("wu_bf", 8192, 16384)):
            src = chunked(dr[name][:, 1024 * third:1024 * (third + 1)])
            dst = sb["W1"].ap()[:, c0:c1].rearrange("p (c n) -> p c n", c=8)
            p.dma("sy", lambda h, o=dst, i=src: h.dma_start(out=o, in_=i),
                  reads=list(pc), writes=[w1b])
        src = chunked(dr["wd_bf"][1024 * third:1024 * (third + 1), :])
        dst = sb["W2"].ap()[:, 0:8192].rearrange("p (c n) -> p c n", c=8)
        p.dma("sy", lambda h, o=dst, i=src: h.dma_start(out=o, in_=i),
              reads=list(pc), writes=[w2b])

        for m in range(8):
            for n in range(4):
                ig = (m * 4 + n) % 2
                bankg, bbg = ctx.bank(2 * ig), ctx.bankbuf(2 * ig)
                banku, bbu = ctx.bank(2 * ig + 1), ctx.bankbuf(2 * ig + 1)
                termsg, termsu = [], []
                for k in range(8):
                    rhs = sb["A1"].ap()[:, 2048 * k + 512 * n:2048 * k + 512 * (n + 1)]
                    lg = sb["W1"].ap()[:, 1024 * k + 128 * m:1024 * k + 128 * (m + 1)]
                    lu = sb["W1"].ap()[:, 8192 + 1024 * k + 128 * m:8192 + 1024 * k + 128 * (m + 1)]
                    termsg.append((bankg, lg, rhs, k == 0, k == 7))
                    termsu.append((banku, lu, rhs, k == 0, k == 7))
                mm_group(p, termsg, bbg, [w1b] + a1r)
                mm_group(p, termsu, bbu, [w1b] + a1r)
                sil = sb["pbuf"].ap()[:, 512 * ig:512 * (ig + 1)]
                silb = bf.buf(f"pbuf{ig}")
                p.op("a", lambda h, o=sil, b=bankg: h.activation(o, b, AF.Silu, scale=S8),
                     reads=[bbg], writes=[silb])
                dst = sb["A3"].ap()[:, 2048 * m + 512 * n:2048 * m + 512 * (n + 1)]
                p.op("v", lambda h, o=dst, b=banku, s=sil: h.scalar_tensor_tensor(
                    o, b, S8, s, ALU.mult, ALU.mult),
                    reads=[bbu, silb], writes=[a3b])
        for tq in range(16):
            for dn in range(2):
                i = (tq * 2 + dn) % 2
                bank, bb = ctx.bank(4 + i), ctx.bankbuf(4 + i)
                terms = []
                for k in range(8):
                    lhs = sb["A3"].ap()[:, 2048 * k + 128 * tq:2048 * k + 128 * (tq + 1)]
                    rhs = sb["W2"].ap()[:, 1024 * k + 512 * dn:1024 * k + 512 * (dn + 1)]
                    terms.append((bank, lhs, rhs, k == 0, k == 7))
                mm_group(p, terms, bb, [a3b, w2b])
                xs = sb["xstage"].ap()[:, 1024 * i:1024 * i + 512]
                xsb = bf.buf(f"xstage{i}")
                p.op("v", lambda h, o=xs, b=bank: h.tensor_scalar(o, b, S8, None, ALU.mult),
                     reads=[bb], writes=[xsb])
                dst = dr["ffn_dram"][third, 128 * tq:128 * (tq + 1), 512 * dn:512 * (dn + 1)]
                p.dma("sy", lambda h, o=dst, i2=xs: h.dma_start(out=o, in_=i2),
                      reads=[xsb], writes=[bf.buf("ffn_dram")])



def emit_xpass(ctx, li, x_in, x_out, x_in_buf, x_out_buf):
    p, sb, dr, bf = ctx.plan, ctx.sb, ctx.dram, ctx.bufs
    for t in range(16):
        slot = t % 2
        xs = sb["xstage"].ap()[:, 1024 * slot:1024 * (slot + 1)]
        xsb = bf.buf(f"xstage{slot}")
        p.dma("sy", lambda h, o=xs, i=x_in[128 * t:128 * (t + 1), :]: h.dma_start(out=o, in_=i),
              reads=[x_in_buf], writes=[xsb])
        at, atb = ctx.stq(slot)
        p.dma("sy", lambda h, o=at, i=dr["attn_dram"][128 * t:128 * (t + 1), :]: h.dma_start(out=o, in_=i),
              reads=[bf.buf("attn_dram")], writes=[atb])
        p.op("v", lambda h, o=xs, a=at: h.tensor_tensor(o, o, a, ALU.add),
             reads=[atb, xsb], writes=[xsb])
        if li == 0:
            xn, xnb2 = ctx.stq(2 + slot)
            p.dma("sy", lambda h, o=xn, i=dr["xn_dram"][128 * t:128 * (t + 1), :]: h.dma_start(out=o, in_=i),
                  reads=[bf.buf("xn_dram")], writes=[xnb2])
            p.op("v", lambda h, o=xs, a=xn: h.tensor_tensor(o, o, a, ALU.add),
                 reads=[xnb2, xsb], writes=[xsb])
            p.dma("sy", lambda h, o=xn, i=dr["mixed_dram"][:, 128 * t:128 * (t + 1)]: h.dma_start_transpose(o, i),
                  reads=[bf.buf("mixed_dram")], writes=[xnb2])
            p.op("v", lambda h, o=xs, a=xn: h.tensor_tensor(o, o, a, ALU.add),
                 reads=[xnb2, xsb], writes=[xsb])
        else:
            ff = sb["scr"].ap()[:, 1024 * slot:1024 * (slot + 1)]
            ffb = bf.buf("scr")
            for third in range(3):
                p.dma("sy", lambda h, o=ff, i=dr["ffn_dram"][third, 128 * t:128 * (t + 1), :]: h.dma_start(out=o, in_=i),
                      reads=[bf.buf("ffn_dram")], writes=[ffb])
                p.op("v", lambda h, o=xs, a=ff: h.tensor_tensor(o, o, a, ALU.add),
                     reads=[ffb, xsb], writes=[xsb])
        p.dma("sy", lambda h, o=x_out[128 * t:128 * (t + 1), :], i=xs: h.dma_start(out=o, in_=i),
              reads=[xsb], writes=[x_out_buf])



def emit_expm(ctx):
    p, sb, dr, bf = ctx.plan, ctx.sb, ctx.dram, ctx.bufs
    w1b = bf.buf("W1")
    xa = lambda c: ctx.sb["A1"].ap()[:, 0 + 128 * c:0 + 128 * (c + 1)]
    xbq = lambda c: ctx.sb["A1"].ap()[:, 1024 + 128 * c:1024 + 128 * (c + 1)]
    x0 = lambda c: ctx.sb["A1"].ap()[:, 2048 + 128 * c:2048 + 128 * (c + 1)]
    xab, xbb, x0b = bf.buf("eXa"), bf.buf("eXb"), bf.buf("eX0")
    for c in range(8):
        p.dma("g", lambda h, o=x0(c), i=dr["eyecol"][128 * c:128 * (c + 1), :]: h.dma_start(out=o, in_=i),
              writes=[x0b])
    out_dram = dr["eslice"] if ctx.mode == "expm" else dr["eg_in"]
    outb = bf.buf("eslice" if ctx.mode == "expm" else "eg_in")
    for gi in range(3):
        src = chunked(dr["gensT"][gi])
        dst = sb["W1"].ap()[:, 0:8192].rearrange("p (c n) -> p c n", c=8)
        p.dma("g", lambda h, o=dst, i=src: h.dma_start(out=o, in_=i), writes=[w1b])
        for c in range(8):
            p.op("v", lambda h, o=xa(c), i=x0(c): h.tensor_copy(o, i), reads=[x0b], writes=[xab])
        cur, curb = xa, xab
        nxt, nxtb = xbq, xbb
        for kk in range(KTAYLOR, 0, -1):
            for m in range(8):
                i = m % 2
                bank, bb = ctx.bank(i), ctx.bankbuf(i)
                terms = []
                for k in range(8):
                    lhs = sb["W1"].ap()[:, 1024 * k + 128 * m:1024 * k + 128 * (m + 1)]
                    terms.append((bank[0:128, 0:128], lhs, cur(k), k == 0, k == 7))
                mm_group(p, terms, bb, [w1b, curb])
                p.op("v", lambda h, o=nxt(m), b=bank[0:128, 0:128], i0=x0(m), s=1.0 / kk:
                     h.scalar_tensor_tensor(o, b, s, i0, ALU.mult, ALU.add),
                     reads=[bb, x0b], writes=[nxtb])
            cur, curb, nxt, nxtb = nxt, nxtb, cur, curb
        for c in range(8):
            dst = out_dram[1024 * gi + 128 * c:1024 * gi + 128 * (c + 1), :]
            p.dma("sy", lambda h, o=dst, i=cur(c): h.dma_start(out=o, in_=i),
                  reads=[curb], writes=[outb])
    if ctx.mode == "fused":
        p.coll(lambda h: h.collective_compute(
            "AllGather", ALU.bypass, replica_groups=[list(range(8))],
            ins=[dr["eg_in"]], outs=[dr["eg_out"]]),
            reads=[bf.buf("eg_in")], writes=[bf.buf("eg_out")])
    tok = outb.w
    for nm in ("A1_0", "A1_1"):
        b = bf.buf(nm)
        b.w = tok
        b.rs = []



def emit_layer(ctx, li, x_in, x_out, x_in_buf, x_out_buf):
    emit_ln(ctx, li, x_in, x_in_buf)
    emit_xnT(ctx)
    emit_qkv(ctx, li)
    emit_ag(ctx)
    if li == 0:
        emit_ffn0(ctx)
    else:
        emit_ffn1(ctx)
    emit_kv_reorder(ctx)
    emit_flash(ctx, li)
    emit_attn_proj(ctx, li)
    emit_xpass(ctx, li, x_in, x_out, x_in_buf, x_out_buf)


def build(mode):
    nc = bass.Bass("TRN2", target_bir_lowering=False)
    dram = declare_dram(nc, mode)
    with ExitStack() as stack:
        sb = alloc_sbuf(nc, stack, mode)
        ps = stack.enter_context(nc.psum_tensor("ps_all", [128, 4096], F32))
        sems = {}
        for s in ("t", "v", "a", "g", "sy", "cc"):
            sems[s] = stack.enter_context(nc.semaphore(f"sem_{s}"))
        block = stack.enter_context(nc.Block())
        plan = Plan()
        ctx = Ctx(nc=nc, plan=plan, sb=sb, ps=ps, dram=dram, bufs=Bufs(), mode=mode)
        load_const_sbuf(ctx)
        if mode == "expm":
            emit_expm(ctx)
        elif mode == "l0":
            emit_layer(ctx, 0, dram["x_in"], dram["x_out"],
                       ctx.bufs.buf("x_in_src"), ctx.bufs.buf("x_out_dst"))
        elif mode == "l1":
            emit_precast_l1(ctx)
            emit_layer(ctx, 1, dram["x_in"], dram["x_out"],
                       ctx.bufs.buf("x_in_src"), ctx.bufs.buf("x_out_dst"))
        elif mode == "fused":
            emit_precast_l1(ctx)
            emit_expm(ctx)
            emit_layer(ctx, 0, dram["x0"], dram["x_mid"],
                       ctx.bufs.buf("x_in_src"), ctx.bufs.buf("x_mid_buf"))
            emit_layer(ctx, 1, dram["x_mid"], dram["x_out"],
                       ctx.bufs.buf("x_mid_buf"), ctx.bufs.buf("x_out_dst"))
        plan.wait_tokens("sy", [("sy", plan.cnt["sy"]), ("g", plan.cnt["g"]),
                                ("v", plan.cnt["v"]), ("a", plan.cnt["a"]),
                                ("t", plan.cnt["t"])])
        block.tensor(lambda h: plan.replay("t", h, sems))
        block.vector(lambda h: plan.replay("v", h, sems))
        block.scalar(lambda h: plan.replay("a", h, sems))
        block.gpsimd(lambda h: plan.replay("g", h, sems))
        block.sync(lambda h: plan.replay("sy", h, sems))
    return nc



def kernel(**inputs):
    x_full = np.asarray(inputs["L0_attn_x"], dtype=np.float32)
    w = prep_weights(inputs)
    consts = prep_consts()
    lng = _f32(np.asarray(inputs["L0_ffn_ln_gamma"]).reshape(8, 128).T)
    lnb = _f32(np.asarray(inputs["L0_ffn_ln_beta"]).reshape(8, 128).T)

    cores = list(range(NCORE))
    per_core = [prep_core(c, x_full) for c in cores]

    nc = build("expm")
    in_maps = []
    for c in cores:
        in_maps.append({"gensT": w["gensT"], "eyecol": per_core[c]["eyecol"],
                        "ident_f": consts["ident_f"], "iota_row": consts["iota_row"],
                        "thr": per_core[c]["thr"]})
    res = run_bass_kernel_spmd(nc, in_maps, core_ids=cores)
    E = np.concatenate([np.asarray(r["eslice"]).reshape(3, 1024, 128) for r in res.results], axis=2)
    E = np.ascontiguousarray(E)

    nc = build("l0")
    in_maps = []
    for c in cores:
        m = {k: w[k] for k in L0_W_KEYS}
        m.update({"ident_f": consts["ident_f"], "iota_row": consts["iota_row"],
                  "alibi_cols": consts["alibi_cols"], "thr": per_core[c]["thr"],
                  "brow0": per_core[c]["brow0"], "x_in": per_core[c]["x0"],
                  "E": E, "ln_gamma": lng, "ln_beta": lnb})
        in_maps.append(m)
    res = run_bass_kernel_spmd(nc, in_maps, core_ids=cores)
    x_mid = [np.asarray(r["x_out"], dtype=np.float32) for r in res.results]

    nc = build("l1")
    in_maps = []
    for c in cores:
        m = {k: w[k] for k in L1_W_KEYS}
        m.update({"ident_f": consts["ident_f"], "iota_row": consts["iota_row"],
                  "thr": per_core[c]["thr"], "x_in": x_mid[c]})
        in_maps.append(m)
    res = run_bass_kernel_spmd(nc, in_maps, core_ids=cores)

    out = np.zeros((B_, T_, D_), dtype=np.float32)
    for c in cores:
        toks = own_token_abs(c % 2)
        out[c // 2][toks] = np.asarray(res.results[c]["x_out"], dtype=np.float32)
    return out


# version 04
# speedup vs baseline: 1.4222x; 1.4222x over previous
import math
import dataclasses
from contextlib import ExitStack

import numpy as np
import ml_dtypes

import concourse.bass as bass
import concourse.mybir as mybir
from concourse.bass_utils import run_bass_kernel_spmd

F32 = mybir.dt.float32
BF16 = mybir.dt.bfloat16
I32 = mybir.dt.int32
U8 = mybir.dt.uint8
AF = mybir.ActivationFunctionType
ALU = mybir.AluOpType
AX = mybir.AxisListType

BFNP = ml_dtypes.bfloat16

B_, T_, D_ = 4, 4096, 1024
NCORE = 8
TOK = 2048
NH, HD = 16, 64
EXP_N, EHID = 8, 2048
L1HID = 3072
EPS = 1e-5
S8 = 1.0 / 127.0
KTAYLOR = 12
BCONST = 30.0
SLOTS = [8, 16, 24, 32]

def _slopes():
    start = 2.0 ** (-(2.0 ** -(math.log2(NH) - 3)))
    return [start * start ** i for i in range(NH)]

SLOPES = _slopes()

QB_ABS = {0: [512, 1536, 2560, 3584],
          1: [0, 1024, 2048, 3072]}

def own_token_abs(half):
    out = []
    for qa in QB_ABS[half]:
        out.extend(range(qa, qa + 512))
    return out


class Buf:
    __slots__ = ("w", "rs", "name")
    def __init__(self, name):
        self.name = name
        self.w = None
        self.rs = []

class Plan:
    ENGS = ("t", "v", "a", "g", "sy")

    def __init__(self):
        self.q = {e: [] for e in self.ENGS}
        self.cnt = {e: 0 for e in ("t", "v", "a", "g", "sy", "cc")}
        self.hw = {e: {} for e in self.ENGS}

    def _emit(self, eng, fn, reads, writes, sem, inc):
        waits = {}
        deps = []
        for b in reads:
            if b.w is not None:
                deps.append(b.w)
        for b in writes:
            if b.w is not None:
                deps.append(b.w)
            deps.extend(b.rs)
        for (s, v) in deps:
            if s == "t" and eng == "t":
                continue
            if waits.get(s, -1) < v:
                waits[s] = v
        hw = self.hw[eng]
        wl = []
        for s, v in sorted(waits.items()):
            if hw.get(s, -1) >= v:
                continue
            hw[s] = v
            wl.append((s, v))
        self.cnt[sem] += inc
        tok = (sem, self.cnt[sem])
        self.q[eng].append((wl, fn, (sem, inc)))
        for b in reads:
            b.rs.append(tok)
        for b in writes:
            b.w = tok
            b.rs = []
        return tok

    def op(self, eng, fn, reads=(), writes=()):
        return self._emit(eng, fn, list(reads), list(writes), eng, 1)

    def dma(self, queue, fn, reads=(), writes=()):
        return self._emit(queue, fn, list(reads), list(writes), queue, 16)

    def coll(self, fn, reads=(), writes=()):
        return self._emit("g", fn, list(reads), list(writes), "cc", 1)

    def wait_tokens(self, eng, toks):
        wl = []
        hw = self.hw[eng]
        for (s, v) in toks:
            if hw.get(s, -1) < v:
                hw[s] = v
                wl.append((s, v))
        if wl:
            self.q[eng].append((wl, None, None))

    def replay(self, eng, handle, sems):
        for waits, fn, inc in self.q[eng]:
            for (s, v) in waits:
                handle.wait_ge(sems[s], v)
            if fn is not None:
                ins = fn(handle)
                if inc is not None:
                    ins.then_inc(sems[inc[0]], inc[1])


def mm_group(plan, terms, bank_buf, read_bufs):
    def fn(h, terms=terms):
        last = None
        for (o, l, r, st, sp) in terms:
            last = h.matmul(o, l, r, start=st, stop=sp)
        return last
    return plan.op("t", fn, reads=list(read_bufs), writes=[bank_buf])



def _bf(x):
    return np.ascontiguousarray(np.asarray(x).astype(BFNP))

def _f32(x):
    return np.ascontiguousarray(np.asarray(x), dtype=np.float32)

def prep_consts():
    c = {}
    c["ident_f"] = _f32(np.eye(128))
    c["iota_row"] = _f32(np.broadcast_to(np.arange(512, dtype=np.float32), (128, 512)))
    al = np.zeros((128, NH * 32), dtype=np.float32)
    p = np.arange(128, dtype=np.float32)
    for h in range(NH):
        for cc in range(32):
            al[:, h * 32 + cc] = SLOPES[h] * (128 * cc + p)
    c["alibi_cols"] = al
    return c

def prep_core(core, x_full):
    half = core % 2
    seq = core // 2
    toks = own_token_abs(half)
    d = {}
    d["x0"] = _f32(x_full[seq][toks])
    qabs = np.array(toks, dtype=np.float32)
    brow0 = np.zeros((NH, TOK), dtype=np.float32)
    for h in range(NH):
        brow0[h] = -8.0 * (BCONST + SLOPES[h] * qabs)
    d["brow0"] = _bf(brow0)
    thr = np.zeros((128, 32), dtype=np.float32)
    pp = np.arange(128, dtype=np.float32)
    for j in range(4):
        qa = QB_ABS[half][j]
        for i in range(8):
            cch = SLOTS[j] - 8 + i
            thr[:, 8 * j + i] = pp + 128.0 * cch - qa
    d["thr"] = thr
    ey = np.zeros((1024, 128), dtype=np.float32)
    ey[128 * core:128 * (core + 1), :] = np.eye(128)
    d["eyecol"] = ey
    return d

def prep_weights(inp):
    w = {}
    w["w_attn_T"] = _f32(np.asarray(inp["L0_attn_w_attn"]).T)
    w["w_proj_T"] = _f32(np.asarray(inp["L0_attn_w_proj"]).T)
    w["abel_w"] = _f32(inp["L0_ffn_abel_w"])
    w["abel_b"] = _f32(np.asarray(inp["L0_ffn_abel_b"]).reshape(8, 128).T)
    w["gate_w"] = _f32(inp["L0_ffn_moe_gate_w"])
    w["gate_b"] = _f32(np.asarray(inp["L0_ffn_moe_gate_b"]).reshape(8, 1))
    w["ew1"] = _f32(inp["L0_ffn_ew1"])
    eb1 = np.asarray(inp["L0_ffn_eb1"])
    w["eb1c"] = _f32(np.concatenate([np.asarray(eb1[e]).reshape(16, 128).T for e in range(8)], axis=1))
    w["ew2"] = _f32(inp["L0_ffn_ew2"])
    w["eb2"] = _f32(inp["L0_ffn_eb2"])
    w["gensT"] = _f32(np.transpose(np.asarray(inp["L0_ffn_gens"]), (0, 2, 1)))
    w["gc_w"] = _f32(np.asarray(inp["L0_ffn_gc_w"]).reshape(16, 128).T)
    w["gc_b"] = _f32(np.asarray(inp["L0_ffn_gc_b"]).reshape(1, 1))
    w["wq"] = _f32(inp["L1_attn_Wq"])
    w["wk"] = _f32(inp["L1_attn_Wk"])
    w["wv"] = _f32(inp["L1_attn_Wv"])
    w["wo"] = _f32(inp["L1_attn_Wo"])
    w["wg_T"] = np.ascontiguousarray(np.asarray(inp["L1_ffn_w_gate"]).T)
    w["wu_T"] = np.ascontiguousarray(np.asarray(inp["L1_ffn_w_up"]).T)
    w["wd_T"] = np.ascontiguousarray(np.asarray(inp["L1_ffn_w_down"]).T)
    return w

L0_W_KEYS = ["w_attn_T", "w_proj_T", "abel_w", "abel_b", "gate_w", "gate_b",
             "ew1", "eb1c", "ew2", "eb2", "gc_w", "gc_b"]
L1_W_KEYS = ["wq", "wk", "wv", "wo", "wg_T", "wu_T", "wd_T"]



class Bufs(dict):
    def buf(self, name):
        if name not in self:
            self[name] = Buf(name)
        return self[name]

@dataclasses.dataclass
class Ctx:
    nc: object
    plan: Plan
    sb: dict
    ps: object
    dram: dict
    bufs: Bufs
    mode: str

    def bank(self, i, rows=128):
        return self.ps.ap()[0:rows, 512 * i:512 * (i + 1)]

    def bankbuf(self, i):
        return self.bufs.buf(f"ps{i}")

    def stq(self, i):
        return (self.sb["stage"].ap()[:, 1024 * i:1024 * (i + 1)],
                self.bufs.buf(f"stage{i}"))

    def stq5(self, i):
        return (self.sb["stage"].ap()[:, 1024 * i:1024 * i + 512],
                self.bufs.buf(f"stage{i}"))


def flat_ap(ap2d, offset, dims):
    return dataclasses.replace(ap2d, offset=offset, ap=[list(x) for x in dims])


def chunked(src2d, p=128):
    return src2d.rearrange("(c p) n -> p c n", p=p)


DEBUG_OUTPUTS = ()

def declare_dram(nc, mode):
    d = {}
    def ein(name, shape, dt):
        d[name] = nc.dram_tensor(name, list(shape), dt, kind="ExternalInput").ap()
    def eout(name, shape, dt):
        d[name] = nc.dram_tensor(name, list(shape), dt, kind="ExternalOutput").ap()
    def internal(name, shape, dt):
        if name in DEBUG_OUTPUTS:
            eout(name, shape, dt)
        else:
            d[name] = nc.dram_tensor(name, list(shape), dt).ap()

    ein("ident_f", (128, 128), F32)
    ein("iota_row", (128, 512), F32)
    ein("thr", (128, 32), F32)

    if mode in ("expm", "fused"):
        ein("gensT", (3, 1024, 1024), F32)
        ein("eyecol", (1024, 128), F32)
    if mode == "expm":
        eout("eslice", (3 * 1024, 128), BF16)
    if mode in ("l0", "fused"):
        ein("alibi_cols", (128, NH * 32), F32)
        ein("brow0", (NH, TOK), BF16)
        ein("ln_gamma", (128, 8), F32)
        ein("ln_beta", (128, 8), F32)
        for k, shape, dt in [
            ("w_attn_T", (1024, 3072), F32), ("w_proj_T", (1024, 1024), F32),
            ("abel_w", (1024, 1024), F32), ("abel_b", (128, 8), F32),
            ("gate_w", (1024, 8), F32), ("gate_b", (8, 1), F32),
            ("ew1", (8, 1024, 2048), F32), ("eb1c", (128, 128), F32),
            ("ew2", (8, 2048, 1024), F32), ("eb2", (8, 1024), F32),
            ("gc_w", (128, 16), F32), ("gc_b", (1, 1), F32),
        ]:
            ein(k, shape, dt)
    if mode == "l0":
        ein("E", (3, 1024, 1024), BF16)
    if mode in ("l1", "fused"):
        for k, shape, dt in [
            ("wq", (1024, 1024), F32), ("wk", (1024, 1024), F32),
            ("wv", (1024, 1024), F32), ("wo", (1024, 1024), F32),
            ("wg_T", (1024, 3072), I32), ("wu_T", (1024, 3072), I32),
            ("wd_T", (3072, 1024), I32),
        ]:
            ein(k, shape, dt)

    if mode in ("l0", "l1"):
        ein("x_in", (TOK, 1024), F32)
        eout("x_out", (TOK, 1024), F32)
    if mode == "fused":
        ein("x0", (TOK, 1024), F32)
        eout("x_out", (TOK, 1024), F32)
        internal("x_mid", (TOK, 1024), F32)
        internal("eg_in", (3 * 1024, 128), BF16)
        internal("eg_out", (NCORE * 3 * 1024, 128), BF16)

    if mode in ("l0", "l1", "fused"):
        internal("xn_dram", (TOK, 1024), BF16)
        internal("q_dram", (1024, TOK), BF16)
        internal("ag_in", (2048, 2048), BF16)
        internal("ag_out", (4096, 2048), BF16)
        internal("kabs", (1024, 4096), BF16)
        internal("vabs", (4096, 1024), BF16)
        internal("attn_dram", (TOK, 1024), BF16)
    if mode in ("l0", "fused"):
        internal("xn2p_dram", (TOK, 1024), BF16)
        internal("mixed_dram", (1024, TOK), BF16)
        internal("moe_tok", (1024, 1024), BF16)
    if mode in ("l1", "fused"):
        internal("ffn_dram", (3, TOK, 1024), F32)
        internal("wg_bf", (1024, 3072), BF16)
        internal("wu_bf", (1024, 3072), BF16)
        internal("wd_bf", (3072, 1024), BF16)
    return d


def alloc_sbuf(nc, stack, mode):
    en = stack.enter_context
    sb = {}
    def t(name, shape, dt):
        sb[name] = en(nc.sbuf_tensor("sb_" + name, list(shape), dt))
    t("W1", (128, 16384), BF16)
    t("W2", (128, 16384), BF16)
    t("A1", (128, 16384), BF16)
    t("A3", (128, 16384), BF16)
    t("kslot", (65, 4096), BF16)
    t("vslot", (128, 32 * 65), BF16)
    t("qslot", (65, 2 * 512), BF16)
    t("xstage", (128, 2 * 1024), F32)
    t("stage", (128, 4 * 1024), BF16)
    t("scr", (128, 2048), F32)
    t("pbuf", (128, 4 * 512), BF16)
    t("rb", (128, 1024), F32)
    t("maskq", (128, 8 * 512), mybir.dt.uint16)
    t("iota", (128, 512), F32)
    t("thrT", (128, 32), F32)
    t("identf", (128, 128), F32)
    t("zeros", (128, 512), BF16)
    t("onesb", (1, 512), BF16)
    t("onesf", (1, 128), F32)
    t("browst", (1, 512), BF16)
    t("ebrow", (1, 1024), BF16)
    t("statc", (128, 16), F32)
    t("epsc", (128, 1), F32)
    t("lgt", (128, 8), F32)
    t("negbig", (128, 8), F32)
    t("e1t", (128, 8), F32)
    t("lmt", (128, 8), F32)
    t("ge2f", (128, 8), F32)
    t("eqm", (128, 8), U8)
    t("wall", (128, 64), F32)
    t("reciprow", (1, 512), F32)
    t("lnc", (128, 16), F32)
    if mode in ("l0", "fused"):
        t("alibi", (128, NH * 32), F32)
        t("eb1c", (128, 128), F32)
        t("gatew", (128, 64), BF16)
        t("gatewf", (128, 64), F32)
        t("gateb", (8, 1), F32)
        t("gcw", (128, 16), BF16)
        t("gcb", (1, 1), F32)
        t("abelb", (128, 8), F32)
        t("grow", (1, 1024), F32)
    return sb



def load_const_sbuf(ctx):
    p, sb, dr, bf = ctx.plan, ctx.sb, ctx.dram, ctx.bufs
    def ld(dst_name, src_name, queue="sy"):
        dst = sb[dst_name]
        src = dr[src_name]
        p.dma(queue, lambda h, d=dst, s=src: h.dma_start(out=d.ap()[:], in_=s),
              writes=[bf.buf(dst_name)])
    ld("iota", "iota_row")
    ld("thrT", "thr")
    ld("identf", "ident_f")
    if ctx.mode in ("l0", "fused"):
        ld("alibi", "alibi_cols")
        ld("eb1c", "eb1c")
        ld("gateb", "gate_b")
        ld("gcb", "gc_b")
        ld("abelb", "abel_b")
        gw = dr["gate_w"].rearrange("(c p) e -> p c e", p=128)
        p.dma("g", lambda h, gw=gw: h.dma_start(
            out=ctx.sb["gatew"].ap().rearrange("p (c e) -> p c e", c=8), in_=gw),
            writes=[bf.buf("gatew")])
        p.dma("sy", lambda h, gw=gw: h.dma_start(
            out=ctx.sb["gatewf"].ap().rearrange("p (c e) -> p c e", c=8), in_=gw),
            writes=[bf.buf("gatewf")])
        p.dma("g", lambda h: h.dma_start(out=ctx.sb["gcw"].ap()[:], in_=ctx.dram["gc_w"]),
              writes=[bf.buf("gcw")])
        p.dma("sy", lambda h: h.dma_start(out=sb["lnc"].ap()[:, 0:8], in_=dr["ln_gamma"]),
              writes=[bf.buf("lnc")])
        p.dma("sy", lambda h: h.dma_start(out=sb["lnc"].ap()[:, 8:16], in_=dr["ln_beta"]),
              writes=[bf.buf("lnc")])
    p.op("v", lambda h: h.memset(sb["zeros"].ap()[:], 0.0), writes=[bf.buf("zeros")])
    p.op("v", lambda h: h.memset(sb["epsc"].ap()[:], EPS), writes=[bf.buf("epsc")])
    p.op("v", lambda h: h.memset(sb["onesb"].ap()[:], 1.0), writes=[bf.buf("onesb")])
    p.op("v", lambda h: h.memset(sb["onesf"].ap()[:], 1.0), writes=[bf.buf("onesf")])
    p.op("v", lambda h: h.memset(sb["negbig"].ap()[:], -1e30), writes=[bf.buf("negbig")])



def emit_ln(ctx, li, x_in, x_in_buf):
    p, sb, dr, bf = ctx.plan, ctx.sb, ctx.dram, ctx.bufs
    st = sb["statc"].ap()
    stb = bf.buf("statc")
    scrb = bf.buf("scr")
    xnb = bf.buf("xn_dram")
    xn2b = bf.buf("xn2p_dram")
    for t in range(16):
        slot = t % 2
        xs = sb["xstage"].ap()[:, 1024 * slot:1024 * (slot + 1)]
        xsb = bf.buf(f"xstage{slot}")
        p.dma("sy", lambda h, o=xs, i=x_in[128 * t:128 * (t + 1), :]: h.dma_start(out=o, in_=i),
              reads=[x_in_buf], writes=[xsb])
        p.op("v", lambda h, o=st[:, 0:1], i=xs: h.reduce_sum(o, i, axis=AX.X),
             reads=[xsb], writes=[stb])
        p.op("a", lambda h, o=sb["scr"].ap()[:, 0:1024], i=xs, a=st[:, 1:2]:
             h.activation(o, i, AF.Square, accum_out=a),
             reads=[xsb], writes=[scrb, stb])
        p.op("v", lambda h: h.tensor_scalar(st[:, 2:3], st[:, 0:1], 1.0 / 1024, None, ALU.mult),
             reads=[stb], writes=[stb])
        p.op("v", lambda h: h.tensor_scalar(st[:, 3:4], st[:, 1:2], 1.0 / 1024, None, ALU.mult),
             reads=[stb], writes=[stb])
        p.op("v", lambda h: h.tensor_tensor(st[:, 4:5], st[:, 2:3], st[:, 2:3], ALU.mult),
             reads=[stb], writes=[stb])
        p.op("v", lambda h: h.tensor_tensor(st[:, 5:6], st[:, 3:4], st[:, 4:5], ALU.subtract),
             reads=[stb], writes=[stb])
        p.op("a", lambda h: h.activation(st[:, 6:7], st[:, 5:6], AF.Sqrt, bias=sb["epsc"].ap()),
             reads=[stb, bf.buf("epsc")], writes=[stb])
        p.op("v", lambda h: h.reciprocal(st[:, 7:8], st[:, 6:7]),
             reads=[stb], writes=[stb])
        p.op("v", lambda h: h.tensor_scalar(st[:, 8:9], st[:, 2:3], st[:, 7:8], -1.0, ALU.mult, ALU.mult),
             reads=[stb], writes=[stb])
        xnt, xntb = ctx.stq(slot)
        p.op("a", lambda h, o=xnt, i=xs: h.activation(o, i, AF.Identity, bias=st[:, 8:9], scale=st[:, 7:8]),
             reads=[xsb, stb], writes=[xntb])
        p.dma("sy", lambda h, o=dr["xn_dram"][128 * t:128 * (t + 1), :], i=xnt: h.dma_start(out=o, in_=i),
              reads=[xntb], writes=[xnb])
        if li == 0:
            p.op("v", lambda h: h.tensor_tensor(st[:, 9:10], st[:, 5:6], st[:, 7:8], ALU.mult),
                 reads=[stb], writes=[stb])
            p.op("v", lambda h: h.tensor_tensor(st[:, 10:11], st[:, 9:10], st[:, 7:8], ALU.mult),
                 reads=[stb], writes=[stb])
            p.op("a", lambda h: h.activation(st[:, 11:12], st[:, 10:11], AF.Sqrt, bias=sb["epsc"].ap()),
                 reads=[stb, bf.buf("epsc")], writes=[stb])
            p.op("v", lambda h: h.reciprocal(st[:, 12:13], st[:, 11:12]),
                 reads=[stb], writes=[stb])
            p.op("v", lambda h: h.tensor_tensor(st[:, 13:14], st[:, 7:8], st[:, 12:13], ALU.mult),
                 reads=[stb], writes=[stb])
            p.op("v", lambda h: h.tensor_scalar(st[:, 14:15], st[:, 2:3], st[:, 13:14], -1.0, ALU.mult, ALU.mult),
                 reads=[stb], writes=[stb])
            xn2t, xn2tb = ctx.stq(2 + slot)
            p.op("a", lambda h, o=xn2t, i=xs: h.activation(o, i, AF.Identity, bias=st[:, 14:15], scale=st[:, 13:14]),
                 reads=[xsb, stb], writes=[xn2tb])
            p.dma("sy", lambda h, o=dr["xn2p_dram"][128 * t:128 * (t + 1), :], i=xn2t: h.dma_start(out=o, in_=i),
                  reads=[xn2tb], writes=[xn2b])


def emit_xnT(ctx):
    p, sb, dr, bf = ctx.plan, ctx.sb, ctx.dram, ctx.bufs
    for c in range(8):
        dst = sb["A1"].ap()[:, 2048 * c:2048 * (c + 1)]
        src = dr["xn_dram"][:, 128 * c:128 * (c + 1)]
        p.dma("sy", lambda h, o=dst, i=src: h.dma_start_transpose(o, i),
              reads=[bf.buf("xn_dram")], writes=[bf.buf(f"A1_{c // 4}")])



def emit_qkv(ctx, li):
    p, sb, dr, bf = ctx.plan, ctx.sb, ctx.dram, ctx.bufs
    a1 = sb["A1"].ap()
    a1r = [bf.buf("A1_0"), bf.buf("A1_1")]
    w1b, w2b = bf.buf("W1"), bf.buf("W2")
    if li == 0:
        wsrc = dr["w_attn_T"]
        qs, ks, vs = (chunked(wsrc[:, 0:1024]), chunked(wsrc[:, 1024:2048]),
                      chunked(wsrc[:, 2048:3072]))
    else:
        qs, ks, vs = chunked(dr["wq"]), chunked(dr["wk"]), chunked(dr["wv"])
    exb = []
    tok = getattr(ctx, "expm_cc_tok", None)
    if tok is not None:
        eb = Buf("_expm_cc")
        eb.w = tok
        exb = [eb]
    def wload(c0, c1, src, arena, buf):
        dst = sb[arena].ap()[:, c0:c1].rearrange("p (c n) -> p c n", c=8)
        p.dma("g", lambda h, o=dst, i=src: h.dma_start(out=o, in_=i), reads=exb, writes=[buf])
    wload(0, 8192, qs, "W1", w1b)
    wload(8192, 16384, ks, "W1", w1b)
    wload(0, 8192, vs, "W2", w2b)

    for which, wc0 in (("q", 0), ("k", 8192)):
        for m in range(8):
            for n in range(4):
                i = (m * 4 + n) % 2
                bank, bb = ctx.bank(i), ctx.bankbuf(i)
                terms = []
                for k in range(8):
                    lhs = sb["W1"].ap()[:, wc0 + 1024 * k + 128 * m: wc0 + 1024 * k + 128 * (m + 1)]
                    rhs = a1[:, 2048 * k + 512 * n: 2048 * k + 512 * (n + 1)]
                    terms.append((bank, lhs, rhs, k == 0, k == 7))
                mm_group(p, terms, bb, [w1b] + a1r)
                ss, ssb = ctx.stq5(i)
                p.op("v", lambda h, o=ss, b=bank: h.tensor_copy(o, b), reads=[bb], writes=[ssb])
                if which == "q":
                    dst = dr["q_dram"][128 * m:128 * (m + 1), 512 * n:512 * (n + 1)]
                    p.dma("sy", lambda h, o=dst, i2=ss: h.dma_start(out=o, in_=i2),
                          reads=[ssb], writes=[bf.buf("q_dram")])
                else:
                    dst = dr["ag_in"][128 * m:128 * (m + 1), 512 * n:512 * (n + 1)]
                    p.dma("sy", lambda h, o=dst, i2=ss: h.dma_start(out=o, in_=i2),
                          reads=[ssb], writes=[bf.buf("ag_in")])

    for tq in range(16):
        for n in range(2):
            i = (tq * 2 + n) % 2
            bank, bb = ctx.bank(i), ctx.bankbuf(i)
            terms = []
            for k in range(8):
                lhs = a1[:, 2048 * k + 128 * tq: 2048 * k + 128 * (tq + 1)]
                rhs = sb["W2"].ap()[:, 1024 * k + 512 * n: 1024 * k + 512 * (n + 1)]
                terms.append((bank, lhs, rhs, k == 0, k == 7))
            mm_group(p, terms, bb, [w2b] + a1r)
            ss, ssb = ctx.stq5(i)
            p.op("v", lambda h, o=ss, b=bank: h.tensor_copy(o, b), reads=[bb], writes=[ssb])
            off = 1024 * 2048 + (128 * tq) * 1024 + 512 * n
            dst = flat_ap(dr["ag_in"], off, [[1024, 128], [1, 512]])
            p.dma("sy", lambda h, o=dst, i2=ss: h.dma_start(out=o, in_=i2),
                  reads=[ssb], writes=[bf.buf("ag_in")])


AG_SERIAL = False
FLASH_PIPE = True
FLASH_YPP = False

def emit_precast_l1(ctx):
    p, dr, bf = ctx.plan, ctx.dram, ctx.bufs
    for name in ("wg_bf", "wu_bf", "wd_bf"):
        srcn = {"wg_bf": "wg_T", "wu_bf": "wu_T", "wd_bf": "wd_T"}[name]
        p.dma("g", lambda h, o=dr[name], i=dr[srcn]: h.dma_start(out=o, in_=i),
              writes=[bf.buf(name)])
    ctx.precast_done = [bf.buf("wg_bf"), bf.buf("wu_bf"), bf.buf("wd_bf")]


def emit_ag(ctx):
    p, dr, bf = ctx.plan, ctx.dram, ctx.bufs
    extra = getattr(ctx, "precast_done", None) or []
    tok = p.coll(lambda h: h.collective_compute(
        "AllGather", ALU.bypass,
        replica_groups=[[0, 1], [2, 3], [4, 5], [6, 7]],
        ins=[dr["ag_in"]], outs=[dr["ag_out"]]),
        reads=[bf.buf("ag_in")] + list(extra), writes=[bf.buf("ag_out")])
    ctx.last_ag_tok = tok
    if AG_SERIAL:
        for nm in ("W1", "W2", "A1_0", "A1_1", "A3a", "A3b", "xstage0", "xstage1",
                   "pbuf0", "pbuf1", "scr", "statc", "ffn_dram", "stage0", "stage1",
                   "stage2", "stage3"):
            b = bf.buf(nm)
            b.rs.append(tok)


def emit_kv_reorder(ctx):
    p, dr, bf = ctx.plan, ctx.dram, ctx.bufs
    blocks0 = [a // 128 for a in own_token_abs(0)[::128]]
    blocks1 = [a // 128 for a in own_token_abs(1)[::128]]
    src_of_abs = {}
    for loc, ab in enumerate(blocks0):
        src_of_abs[ab] = (0, loc)
    for loc, ab in enumerate(blocks1):
        src_of_abs[ab] = (1, loc)
    agb, kb, vb = bf.buf("ag_out"), bf.buf("kabs"), bf.buf("vabs")
    for ab in range(32):
        rank, loc = src_of_abs[ab]
        src = dr["ag_out"][2048 * rank: 2048 * rank + 1024, 128 * loc:128 * (loc + 1)]
        dst = dr["kabs"][:, 128 * ab:128 * (ab + 1)]
        p.dma("sy", lambda h, o=dst, i=src: h.dma_start(out=o, in_=i), reads=[agb], writes=[kb])
        off = (2048 * rank + 1024) * 2048 + (128 * loc) * 1024
        src = flat_ap(dr["ag_out"], off, [[1024, 128], [1, 1024]])
        dst = dr["vabs"][128 * ab:128 * (ab + 1), :]
        p.dma("sy", lambda h, o=dst, i=src: h.dma_start(out=o, in_=i), reads=[agb], writes=[vb])



def emit_flash(ctx, li):
    p, sb, dr, bf = ctx.plan, ctx.sb, ctx.dram, ctx.bufs
    a1w = [bf.buf("A1_0"), bf.buf("A1_1")]
    kbuf, vbuf, qbuf = bf.buf("kslot"), bf.buf("vslot"), bf.buf("qslot")
    mqb = bf.buf("maskq")
    iob, thb = bf.buf("iota"), bf.buf("thrT")
    kabs_b, vabs_b, qdr_b = bf.buf("kabs"), bf.buf("vabs"), bf.buf("q_dram")
    alb = bf.buf("alibi") if li == 0 else None
    zb = bf.buf("zeros")
    ofb = bf.buf("onesf")
    vsl = sb["vslot"].ap().rearrange("p (c w) -> p c w", w=65)

    p.op("v", lambda h: h.memset(sb["kslot"].ap()[64:65, :], 1.0), writes=[kbuf])
    if li == 1:
        p.op("v", lambda h: h.memset(sb["qslot"].ap()[64:65, :], -8.0 * BCONST), writes=[qbuf])

    for j in range(4):
        s = SLOTS[j]
        for i8 in range(8):
            dst = sb["maskq"].ap()[:, 512 * i8:512 * (i8 + 1)]
            thr = sb["thrT"].ap()[:, 8 * j + i8: 8 * j + i8 + 1]
            p.op("v", lambda h, o=dst, t=thr: h.tensor_scalar(o, sb["iota"].ap()[:], t, None, ALU.is_ge),
                 reads=[iob, thb], writes=[mqb])
        for hd in range(NH):
            if li == 0:
                qa_min = QB_ABS[1][j]
                cs = [c for c in range(s)
                      if c >= s - 8 or SLOPES[hd] * ((128 * c + 127) - qa_min) >= -93.0]
            else:
                cs = list(range(s))
            p.dma("sy", lambda h, o=sb["kslot"].ap()[0:64, 0:128 * s],
                  i=dr["kabs"][64 * hd:64 * (hd + 1), 0:128 * s]: h.dma_start(out=o, in_=i),
                  reads=[kabs_b], writes=[kbuf])
            src = flat_ap(dr["vabs"], 64 * hd, [[1024, 128], [128 * 1024, s], [1, 64]])
            p.dma("sy", lambda h, o=vsl[:, 0:s, 0:64], i=src: h.dma_start(out=o, in_=i),
                  reads=[vabs_b], writes=[vbuf])
            p.op("v", lambda h, o=vsl[:, 0:s, 64:65]: h.memset(o, 1.0), writes=[vbuf])
            qs0 = 512 * (hd % 2)
            qsl = sb["qslot"].ap()[0:65, qs0:qs0 + 512]
            p.dma("sy", lambda h, o=sb["qslot"].ap()[0:64, qs0:qs0 + 512],
                  i=dr["q_dram"][64 * hd:64 * (hd + 1), 512 * j:512 * (j + 1)]: h.dma_start(out=o, in_=i),
                  reads=[qdr_b], writes=[qbuf])
            if li == 0:
                p.dma("sy", lambda h, o=sb["qslot"].ap()[64:65, qs0:qs0 + 512],
                      i=dr["brow0"][hd:hd + 1, 512 * j:512 * (j + 1)]: h.dma_start(out=o, in_=i),
                      reads=[bf.buf("brow0_src")], writes=[qbuf])

            yb = 2 if (hd % 2 == 0 or not FLASH_YPP) else 4
            bcb = 3 if (hd % 2 == 0 or not FLASH_YPP) else 5
            ps_y, ps_y_buf = ctx.ps.ap()[0:65, 512 * yb:512 * (yb + 1)], ctx.bankbuf(yb)
            ps_bc, ps_bc_buf = ctx.ps.ap()[0:64, 512 * bcb:512 * (bcb + 1)], ctx.bankbuf(bcb)

            def s_mm(idx):
                c = cs[idx]
                i = idx % 2
                bank, bb = ctx.bank(i), ctx.bankbuf(i)
                p.op("t", lambda h, o=bank, l=sb["kslot"].ap()[0:65, 128 * c:128 * (c + 1)], r=qsl:
                     h.matmul(o, l, r, start=True, stop=True),
                     reads=[kbuf, qbuf], writes=[bb])

            s_mm(0)
            for idx, c in enumerate(cs):
                i = idx % 2
                bank, bb = ctx.bank(i), ctx.bankbuf(i)
                pb = sb["pbuf"].ap()[:, 512 * i:512 * (i + 1)]
                pbb = bf.buf(f"pbuf{i}")
                if li == 0:
                    bias = sb["alibi"].ap()[:, hd * 32 + c: hd * 32 + c + 1]
                    p.op("a", lambda h, o=pb, b=bank, bi=bias: h.activation(o, b, AF.Exp, bias=bi, scale=0.125),
                         reads=[bb, alb], writes=[pbb])
                else:
                    p.op("a", lambda h, o=pb, b=bank: h.activation(o, b, AF.Exp, scale=0.125),
                         reads=[bb], writes=[pbb])
                if FLASH_PIPE and idx + 1 < len(cs):
                    s_mm(idx + 1)
                if c >= s - 8:
                    msk = sb["maskq"].ap()[:, 512 * (c - (s - 8)):512 * (c - (s - 8) + 1)]
                    pm = sb["pbuf"].ap()[:, 512 * (2 + i):512 * (3 + i)]
                    pmb = bf.buf(f"pbuf{2 + i}")
                    p.op("v", lambda h, o=pm, m=msk, src=pb: h.select(o, m, src, sb["zeros"].ap()[:]),
                         reads=[mqb, zb, pbb], writes=[pmb])
                    pv_src, pv_srcb = pm, pmb
                else:
                    pv_src, pv_srcb = pb, pbb
                lhs = vsl[:, c, 0:65]
                p.op("t", lambda h, o=ps_y, l=lhs, r=pv_src, st=(idx == 0), sp=(idx == len(cs) - 1):
                     h.matmul(o, l, r, start=st, stop=sp),
                     reads=[vbuf, pv_srcb], writes=[ps_y_buf])
                if (not FLASH_PIPE) and idx + 1 < len(cs):
                    s_mm(idx + 1)

            rr = sb["reciprow"].ap()
            rrb = bf.buf("reciprow")
            p.op("v", lambda h, o=rr: h.reciprocal(o, ps_y[64:65, 0:512]),
                 reads=[ps_y_buf], writes=[rrb])
            p.op("t", lambda h, o=ps_bc: h.matmul(o, sb["onesf"].ap()[:, 0:64], rr, start=True, stop=True),
                 reads=[ofb, rrb], writes=[ps_bc_buf])
            rbb = bf.buf(f"rb{hd % 2}")
            rbap = sb["rb"].ap()[0:64, 512 * (hd % 2):512 * (hd % 2) + 512]
            p.op("v", lambda h, o=rbap: h.tensor_copy(o, ps_bc),
                 reads=[ps_bc_buf], writes=[rbb])
            dst = sb["A1"].ap()[64 * (hd % 2):64 * (hd % 2) + 64,
                                2048 * (hd // 2) + 512 * j: 2048 * (hd // 2) + 512 * (j + 1)]
            p.op("v", lambda h, o=dst, r2=rbap: h.tensor_tensor(o, ps_y[0:64, 0:512], r2, ALU.mult),
                 reads=[ps_y_buf, rbb], writes=[a1w[(hd // 2) // 4]])


def emit_attn_proj(ctx, li):
    p, sb, dr, bf = ctx.plan, ctx.sb, ctx.dram, ctx.bufs
    w1b = bf.buf("W1")
    a1r = [bf.buf("A1_0"), bf.buf("A1_1")]
    src = chunked(dr["w_proj_T"] if li == 0 else dr["wo"])
    dst = sb["W1"].ap()[:, 0:8192].rearrange("p (c n) -> p c n", c=8)
    p.dma("g", lambda h, o=dst, i=src: h.dma_start(out=o, in_=i), writes=[w1b])
    for tq in range(16):
        for n in range(2):
            i = (tq * 2 + n) % 2
            bank, bb = ctx.bank(i), ctx.bankbuf(i)
            terms = []
            for k in range(8):
                lhs = sb["A1"].ap()[:, 2048 * k + 128 * tq:2048 * k + 128 * (tq + 1)]
                rhs = sb["W1"].ap()[:, 1024 * k + 512 * n:1024 * k + 512 * (n + 1)]
                terms.append((bank, lhs, rhs, k == 0, k == 7))
            mm_group(p, terms, bb, [w1b] + a1r)
            ss, ssb = ctx.stq5(i)
            p.op("v", lambda h, o=ss, b=bank: h.tensor_copy(o, b), reads=[bb], writes=[ssb])
            d2 = dr["attn_dram"][128 * tq:128 * (tq + 1), 512 * n:512 * (n + 1)]
            p.dma("sy", lambda h, o=d2, i2=ss: h.dma_start(out=o, in_=i2),
                  reads=[ssb], writes=[bf.buf("attn_dram")])



def emit_gate0(ctx):
    p, sb, dr, bf = ctx.plan, ctx.sb, ctx.dram, ctx.bufs
    ps_lb = ctx.bankbuf(4)
    ps_t, ps_tb = ctx.ps.ap()[0:128, 2560:3072], ctx.bankbuf(5)
    scrb = bf.buf("scr")
    st = sb["statc"].ap()
    stb = bf.buf("statc")
    for n in range(2):
        psl = ctx.ps.ap()[0:8, 2048 + 512 * n:2048 + 512 * (n + 1)]
        dst = sb["scr"].ap()[0:8, 1024 + 512 * n:1024 + 512 * (n + 1)]
        p.op("a", lambda h, o=dst, i=psl: h.activation(o, i, AF.Identity, bias=sb["gateb"].ap()[:]),
             reads=[ps_lb, bf.buf("gateb")], writes=[scrb])
    for s in range(8):
        src = sb["scr"].ap()[0:8, 1024 + 128 * s:1024 + 128 * (s + 1)]
        p.op("t", lambda h, i=src: h.transpose(ps_t[0:128, 0:8], i, sb["identf"].ap()[0:8, 0:8]),
             reads=[scrb, bf.buf("identf")], writes=[ps_tb])
        lg = sb["lgt"].ap()
        lgb = bf.buf("lgt")
        p.op("v", lambda h: h.tensor_copy(lg, ps_t[0:128, 0:8]), reads=[ps_tb], writes=[lgb])
        p.op("v", lambda h: h.reduce_max(st[:, 0:1], lg, axis=AX.X), reads=[lgb], writes=[stb])
        p.op("v", lambda h: h.tensor_scalar(st[:, 1:2], st[:, 0:1], -1.0, None, ALU.mult),
             reads=[stb], writes=[stb])
        p.op("v", lambda h: h.tensor_scalar(sb["eqm"].ap(), lg, st[:, 0:1], None, ALU.is_ge),
             reads=[lgb, stb], writes=[bf.buf("eqm")])
        p.op("v", lambda h: h.select(sb["lmt"].ap(), sb["eqm"].ap(), sb["negbig"].ap(), lg),
             reads=[bf.buf("eqm"), bf.buf("negbig"), lgb], writes=[bf.buf("lmt")])
        p.op("v", lambda h: h.reduce_max(st[:, 2:3], sb["lmt"].ap(), axis=AX.X),
             reads=[bf.buf("lmt")], writes=[stb])
        p.op("a", lambda h: h.activation(sb["e1t"].ap(), lg, AF.Exp, bias=st[:, 1:2]),
             reads=[lgb, stb], writes=[bf.buf("e1t")])
        p.op("a", lambda h: h.activation(st[:, 3:4], st[:, 2:3], AF.Exp, bias=st[:, 1:2]),
             reads=[stb], writes=[stb])
        p.op("v", lambda h: h.tensor_scalar(st[:, 4:5], st[:, 3:4], 1.0, None, ALU.add),
             reads=[stb], writes=[stb])
        p.op("v", lambda h: h.reciprocal(st[:, 5:6], st[:, 4:5]), reads=[stb], writes=[stb])
        p.op("v", lambda h: h.tensor_scalar(sb["ge2f"].ap(), lg, st[:, 2:3], None, ALU.is_ge),
             reads=[lgb, stb], writes=[bf.buf("ge2f")])
        dst = sb["wall"].ap()[:, 8 * s:8 * (s + 1)]
        p.op("v", lambda h, o=dst: h.scalar_tensor_tensor(o, sb["e1t"].ap(), st[:, 5:6],
                                                          sb["ge2f"].ap(), ALU.mult, ALU.mult),
             reads=[bf.buf("e1t"), stb, bf.buf("ge2f")], writes=[bf.buf("wall")])


def emit_E_load(ctx, r, arena, wb):
    p, sb, dr, bf = ctx.plan, ctx.sb, ctx.dram, ctx.bufs
    if ctx.mode == "l0":
        src = chunked(dr["E"][r])
        dst = sb[arena].ap()[:, 0:8192].rearrange("p (c n) -> p c n", c=8)
        p.dma("sy", lambda h, o=dst, i=src: h.dma_start(out=o, in_=i),
              reads=[bf.buf("E_src")], writes=[wb])
    else:
        for rk in range(8):
            dst = dataclasses.replace(sb[arena].ap(), offset=128 * rk,
                                      ap=[[16384, 128], [1024, 8], [1, 128]])
            src = flat_ap(dr["eg_out"], (3072 * rk + 1024 * r) * 128,
                          [[128, 128], [128 * 128, 8], [1, 128]])
            p.dma("sy", lambda h, o=dst, i=src: h.dma_start(out=o, in_=i),
                  reads=[bf.buf("eg_out")], writes=[wb])


def emit_ffn0(ctx):
    p, sb, dr, bf = ctx.plan, ctx.sb, ctx.dram, ctx.bufs
    w1b, w2b = bf.buf("W1"), bf.buf("W2")
    a3ab, a3bb = bf.buf("A3a"), bf.buf("A3b")
    a1ab, a1bb = bf.buf("A1_0"), bf.buf("A1_1")
    onb = bf.buf("onesb")
    for th in range(2):
        for c in range(8):
            dst = sb["A3"].ap()[:, 1024 * c:1024 * (c + 1)]
            src = dr["xn2p_dram"][1024 * th:1024 * (th + 1), 128 * c:128 * (c + 1)]
            p.dma("sy", lambda h, o=dst, i=src: h.dma_start_transpose(o, i),
                  reads=[bf.buf("xn2p_dram")], writes=[a3ab])
        ps_l, ps_lb = ctx.ps.ap()[0:8, 2048:2560], ctx.bankbuf(4)
        scrb = bf.buf("scr")
        for c in range(8):
            sl = sb["A3"].ap()[:, 1024 * c:1024 * (c + 1)]
            fs = sb["scr"].ap()[:, 0:1024]
            p.op("a", lambda h, o=fs, i=sl, g=sb["lnc"].ap()[:, c:c + 1], b2=sb["lnc"].ap()[:, 8 + c:9 + c]:
                 h.activation(o, i, AF.Identity, bias=b2, scale=g),
                 reads=[bf.buf("lnc"), a3ab, scrb], writes=[scrb])
            for nb in range(2):
                p.op("t", lambda h, c=c, nb=nb: h.matmul(
                    ps_l[:, 512 * nb:512 * (nb + 1)] if False else ctx.ps.ap()[0:8, 2048 + 512 * nb:2048 + 512 * (nb + 1)],
                    sb["gatewf"].ap()[:, 8 * c:8 * (c + 1)],
                    sb["scr"].ap()[:, 512 * nb:512 * (nb + 1)],
                    start=(c == 0), stop=(c == 7)),
                    reads=[bf.buf("gatewf"), scrb], writes=[ps_lb])
            p.op("v", lambda h, o=sl, i=fs: h.tensor_copy(o, i),
                 reads=[scrb], writes=[a3ab])

        emit_gate0(ctx)

        for e in range(EXP_N):
            src = chunked(dr["ew1"][e])
            dst = sb["W1"].ap().rearrange("p (c n) -> p c n", c=8)
            p.dma("g", lambda h, o=dst, i=src: h.dma_start(out=o, in_=i), writes=[w1b])
            src = chunked(dr["ew2"][e])
            dst = sb["W2"].ap().rearrange("p (c n) -> p c n", c=16)
            p.dma("g", lambda h, o=dst, i=src: h.dma_start(out=o, in_=i), writes=[w2b])
            p.dma("g", lambda h, o=sb["ebrow"].ap()[:], i=dr["eb2"][e:e + 1, :]: h.dma_start(out=o, in_=i),
                  writes=[bf.buf("ebrow")])

            for tb in range(2):
                for m in range(16):
                    i = m % 2
                    bank, bb = ctx.bank(i), ctx.bankbuf(i)
                    terms = []
                    for k in range(8):
                        lhs = sb["W1"].ap()[:, 2048 * k + 128 * m:2048 * k + 128 * (m + 1)]
                        rhs = sb["A3"].ap()[:, 1024 * k + 512 * tb:1024 * k + 512 * (tb + 1)]
                        terms.append((bank, lhs, rhs, k == 0, k == 7))
                    mm_group(p, terms, bb, [w1b, a3ab])
                    hdst = sb["A1"].ap()[:, 512 * m:512 * (m + 1)]
                    bias = sb["eb1c"].ap()[:, 16 * e + m:16 * e + m + 1]
                    p.op("a", lambda h, o=hdst, b=bank, bi=bias: h.activation(o, b, AF.Gelu, bias=bi),
                         reads=[bb, bf.buf("eb1c")], writes=[a1ab])
                for t4 in range(4):
                    for dn in range(2):
                        ib = t4 * 2 + dn
                        bank, bb = ctx.bank(ib), ctx.bankbuf(ib)
                        terms = []
                        for k in range(16):
                            lhs = sb["A1"].ap()[:, 512 * k + 128 * t4:512 * k + 128 * (t4 + 1)]
                            rhs = sb["W2"].ap()[:, 1024 * k + 512 * dn:1024 * k + 512 * (dn + 1)]
                            terms.append((bank, lhs, rhs, k == 0, False))
                        terms.append((bank, sb["onesb"].ap()[:, 0:128],
                                      sb["ebrow"].ap()[:, 512 * dn:512 * (dn + 1)], False, True))
                        mm_group(p, terms, bb, [a1ab, w2b, onb, bf.buf("ebrow")])
                        tt = tb * 4 + t4
                        acc = sb["A3"].ap()[:, 8192 + 1024 * tt + 512 * dn:8192 + 1024 * tt + 512 * (dn + 1)]
                        wcol = sb["wall"].ap()[:, 8 * tt + e:8 * tt + e + 1]
                        if e == 0:
                            p.op("v", lambda h, o=acc, b=bank, w=wcol: h.tensor_scalar(o, b, w, None, ALU.mult),
                                 reads=[bb, bf.buf("wall")], writes=[a3bb])
                        else:
                            p.op("v", lambda h, o=acc, b=bank, w=wcol: h.scalar_tensor_tensor(
                                o, b, w, o, ALU.mult, ALU.add),
                                reads=[bb, bf.buf("wall")], writes=[a3bb])

        for tt in range(8):
            src2 = sb["A3"].ap()[:, 8192 + 1024 * tt:8192 + 1024 * (tt + 1)]
            p.dma("sy", lambda h, o=dr["moe_tok"][128 * tt:128 * (tt + 1), :], i=src2: h.dma_start(out=o, in_=i),
                  reads=[a3bb], writes=[bf.buf("moe_tok")])
        for c in range(8):
            dst = sb["A1"].ap()[:, 1024 * c:1024 * (c + 1)]
            src2 = dr["moe_tok"][:, 128 * c:128 * (c + 1)]
            p.dma("sy", lambda h, o=dst, i=src2: h.dma_start_transpose(o, i),
                  reads=[bf.buf("moe_tok")], writes=[a1ab])
        for r in range(3):
            warena = "W1" if r % 2 == 0 else "W2"
            wb = w1b if r % 2 == 0 else w2b
            emit_E_load(ctx, r, warena, wb)
            src_cols = 0 if r % 2 == 0 else 8192
            dst_cols = 8192 if r % 2 == 0 else 0
            srcb = a1ab if r % 2 == 0 else a1bb
            dstb = a1bb if r % 2 == 0 else a1ab
            for m in range(8):
                for n in range(2):
                    i = (m * 2 + n) % 2
                    bank, bb = ctx.bank(i), ctx.bankbuf(i)
                    terms = []
                    for k in range(8):
                        lhs = sb[warena].ap()[:, 1024 * k + 128 * m:1024 * k + 128 * (m + 1)]
                        rhs = sb["A1"].ap()[:, src_cols + 1024 * k + 512 * n:src_cols + 1024 * k + 512 * (n + 1)]
                        terms.append((bank, lhs, rhs, k == 0, k == 7))
                    mm_group(p, terms, bb, [wb, srcb])
                    dst = sb["A1"].ap()[:, dst_cols + 1024 * m + 512 * n:dst_cols + 1024 * m + 512 * (n + 1)]
                    p.op("v", lambda h, o=dst, b=bank: h.tensor_copy(o, b), reads=[bb], writes=[dstb])

        src = chunked(dr["abel_w"])
        dst = sb["W1"].ap()[:, 0:8192].rearrange("p (c n) -> p c n", c=8)
        p.dma("g", lambda h, o=dst, i=src: h.dma_start(out=o, in_=i), writes=[w1b])
        for m in range(8):
            for n in range(2):
                i = (m * 2 + n) % 2
                bank, bb = ctx.bank(i), ctx.bankbuf(i)
                terms = []
                for k in range(8):
                    lhs = sb["W1"].ap()[:, 1024 * k + 128 * m:1024 * k + 128 * (m + 1)]
                    rhs = sb["A3"].ap()[:, 1024 * k + 512 * n:1024 * k + 512 * (n + 1)]
                    terms.append((bank, lhs, rhs, k == 0, k == 7))
                mm_group(p, terms, bb, [w1b, a3ab])
                dst = sb["A3"].ap()[:, 8192 + 1024 * m + 512 * n:8192 + 1024 * m + 512 * (n + 1)]
                bias = sb["abelb"].ap()[:, m:m + 1]
                p.op("a", lambda h, o=dst, b=bank, bi=bias: h.activation(o, b, AF.Gelu, bias=bi),
                     reads=[bb, bf.buf("abelb")], writes=[a3bb])

        ps_g, ps_gb = ctx.ps.ap()[0:1, 2048:2560], ctx.bankbuf(4)
        for n in range(2):
            terms = []
            for k in range(8):
                lhs = sb["gcw"].ap()[:, k:k + 1]
                rhs = sb["A3"].ap()[:, 8192 + 1024 * k + 512 * n:8192 + 1024 * k + 512 * (n + 1)]
                terms.append((ps_g, lhs, rhs, k == 0, False))
            for k in range(8):
                lhs = sb["gcw"].ap()[:, 8 + k:9 + k]
                rhs = sb["A1"].ap()[:, 8192 + 1024 * k + 512 * n:8192 + 1024 * k + 512 * (n + 1)]
                terms.append((ps_g, lhs, rhs, False, k == 7))
            mm_group(p, terms, ps_gb, [bf.buf("gcw"), a3bb, a1bb])
            dst = sb["grow"].ap()[:, 512 * n:512 * (n + 1)]
            p.op("a", lambda h, o=dst: h.activation(o, ps_g, AF.Sigmoid, bias=sb["gcb"].ap()[:]),
                 reads=[ps_gb, bf.buf("gcb")], writes=[bf.buf("grow")])

        ps_bc, ps_bc_buf = ctx.ps.ap()[0:128, 1536:2048], ctx.bankbuf(3)
        for n in range(2):
            p.op("t", lambda h, r2=sb["grow"].ap()[:, 512 * n:512 * (n + 1)]:
                 h.matmul(ps_bc, sb["onesf"].ap()[:, 0:128], r2, start=True, stop=True),
                 reads=[bf.buf("grow"), bf.buf("onesf")], writes=[ps_bc_buf])
            p.op("v", lambda h: h.tensor_copy(sb["rb"].ap()[:, 0:512], ps_bc),
                 reads=[ps_bc_buf], writes=[bf.buf("rb")])
            for m in range(8):
                ab = sb["A3"].ap()[:, 8192 + 1024 * m + 512 * n:8192 + 1024 * m + 512 * (n + 1)]
                mo = sb["A1"].ap()[:, 8192 + 1024 * m + 512 * n:8192 + 1024 * m + 512 * (n + 1)]
                df = sb["pbuf"].ap()[:, 0:512]
                p.op("v", lambda h, o=df, a=mo, b2=ab: h.tensor_tensor(o, a, b2, ALU.subtract),
                     reads=[a1bb, a3bb], writes=[bf.buf("pbuf0")])
                p.op("v", lambda h, o=df: h.tensor_tensor(o, o, sb["rb"].ap()[:, 0:512], ALU.mult),
                     reads=[bf.buf("rb"), bf.buf("pbuf0")], writes=[bf.buf("pbuf0")])
                mx = sb["pbuf"].ap()[:, 512:1024]
                p.op("v", lambda h, o=mx, a=df, b2=ab: h.tensor_tensor(o, a, b2, ALU.add),
                     reads=[bf.buf("pbuf0"), a3bb], writes=[bf.buf("pbuf1")])
                dst = dr["mixed_dram"][128 * m:128 * (m + 1), 1024 * th + 512 * n:1024 * th + 512 * (n + 1)]
                p.dma("sy", lambda h, o=dst, i=mx: h.dma_start(out=o, in_=i),
                      reads=[bf.buf("pbuf1")], writes=[bf.buf("mixed_dram")])



def emit_ffn1(ctx):
    p, sb, dr, bf = ctx.plan, ctx.sb, ctx.dram, ctx.bufs
    w1b, w2b = bf.buf("W1"), bf.buf("W2")
    a1r = [bf.buf("A1_0"), bf.buf("A1_1")]
    a3b = bf.buf("A3a")
    pc = list(getattr(ctx, "precast_done", None) or [])
    agtok = getattr(ctx, "last_ag_tok", None)
    if agtok is not None:
        b = Buf("_ag_done")
        b.w = agtok
        pc.append(b)
    for third in range(3):
        for (name, c0, c1) in (("wg_bf", 0, 8192), ("wu_bf", 8192, 16384)):
            src = chunked(dr[name][:, 1024 * third:1024 * (third + 1)])
            dst = sb["W1"].ap()[:, c0:c1].rearrange("p (c n) -> p c n", c=8)
            p.dma("sy", lambda h, o=dst, i=src: h.dma_start(out=o, in_=i),
                  reads=list(pc), writes=[w1b])
        src = chunked(dr["wd_bf"][1024 * third:1024 * (third + 1), :])
        dst = sb["W2"].ap()[:, 0:8192].rearrange("p (c n) -> p c n", c=8)
        p.dma("sy", lambda h, o=dst, i=src: h.dma_start(out=o, in_=i),
              reads=list(pc), writes=[w2b])

        for m in range(8):
            for n in range(4):
                ig = (m * 4 + n) % 2
                bankg, bbg = ctx.bank(2 * ig), ctx.bankbuf(2 * ig)
                banku, bbu = ctx.bank(2 * ig + 1), ctx.bankbuf(2 * ig + 1)
                termsg, termsu = [], []
                for k in range(8):
                    rhs = sb["A1"].ap()[:, 2048 * k + 512 * n:2048 * k + 512 * (n + 1)]
                    lg = sb["W1"].ap()[:, 1024 * k + 128 * m:1024 * k + 128 * (m + 1)]
                    lu = sb["W1"].ap()[:, 8192 + 1024 * k + 128 * m:8192 + 1024 * k + 128 * (m + 1)]
                    termsg.append((bankg, lg, rhs, k == 0, k == 7))
                    termsu.append((banku, lu, rhs, k == 0, k == 7))
                mm_group(p, termsg, bbg, [w1b] + a1r)
                mm_group(p, termsu, bbu, [w1b] + a1r)
                sil = sb["pbuf"].ap()[:, 512 * ig:512 * (ig + 1)]
                silb = bf.buf(f"pbuf{ig}")
                p.op("a", lambda h, o=sil, b=bankg: h.activation(o, b, AF.Silu, scale=S8),
                     reads=[bbg], writes=[silb])
                dst = sb["A3"].ap()[:, 2048 * m + 512 * n:2048 * m + 512 * (n + 1)]
                p.op("v", lambda h, o=dst, b=banku, s=sil: h.scalar_tensor_tensor(
                    o, b, S8, s, ALU.mult, ALU.mult),
                    reads=[bbu, silb], writes=[a3b])
        for tq in range(16):
            for dn in range(2):
                i = (tq * 2 + dn) % 2
                bank, bb = ctx.bank(4 + i), ctx.bankbuf(4 + i)
                terms = []
                for k in range(8):
                    lhs = sb["A3"].ap()[:, 2048 * k + 128 * tq:2048 * k + 128 * (tq + 1)]
                    rhs = sb["W2"].ap()[:, 1024 * k + 512 * dn:1024 * k + 512 * (dn + 1)]
                    terms.append((bank, lhs, rhs, k == 0, k == 7))
                mm_group(p, terms, bb, [a3b, w2b])
                xs = sb["xstage"].ap()[:, 1024 * i:1024 * i + 512]
                xsb = bf.buf(f"xstage{i}")
                p.op("v", lambda h, o=xs, b=bank: h.tensor_scalar(o, b, S8, None, ALU.mult),
                     reads=[bb], writes=[xsb])
                dst = dr["ffn_dram"][third, 128 * tq:128 * (tq + 1), 512 * dn:512 * (dn + 1)]
                p.dma("sy", lambda h, o=dst, i2=xs: h.dma_start(out=o, in_=i2),
                      reads=[xsb], writes=[bf.buf("ffn_dram")])



def emit_xpass(ctx, li, x_in, x_out, x_in_buf, x_out_buf):
    p, sb, dr, bf = ctx.plan, ctx.sb, ctx.dram, ctx.bufs
    for t in range(16):
        slot = t % 2
        xs = sb["xstage"].ap()[:, 1024 * slot:1024 * (slot + 1)]
        xsb = bf.buf(f"xstage{slot}")
        p.dma("sy", lambda h, o=xs, i=x_in[128 * t:128 * (t + 1), :]: h.dma_start(out=o, in_=i),
              reads=[x_in_buf], writes=[xsb])
        at, atb = ctx.stq(slot)
        p.dma("sy", lambda h, o=at, i=dr["attn_dram"][128 * t:128 * (t + 1), :]: h.dma_start(out=o, in_=i),
              reads=[bf.buf("attn_dram")], writes=[atb])
        p.op("v", lambda h, o=xs, a=at: h.tensor_tensor(o, o, a, ALU.add),
             reads=[atb, xsb], writes=[xsb])
        if li == 0:
            xn, xnb2 = ctx.stq(2 + slot)
            p.dma("sy", lambda h, o=xn, i=dr["xn_dram"][128 * t:128 * (t + 1), :]: h.dma_start(out=o, in_=i),
                  reads=[bf.buf("xn_dram")], writes=[xnb2])
            p.op("v", lambda h, o=xs, a=xn: h.tensor_tensor(o, o, a, ALU.add),
                 reads=[xnb2, xsb], writes=[xsb])
            p.dma("sy", lambda h, o=xn, i=dr["mixed_dram"][:, 128 * t:128 * (t + 1)]: h.dma_start_transpose(o, i),
                  reads=[bf.buf("mixed_dram")], writes=[xnb2])
            p.op("v", lambda h, o=xs, a=xn: h.tensor_tensor(o, o, a, ALU.add),
                 reads=[xnb2, xsb], writes=[xsb])
        else:
            ff = sb["scr"].ap()[:, 1024 * slot:1024 * (slot + 1)]
            ffb = bf.buf("scr")
            for third in range(3):
                p.dma("sy", lambda h, o=ff, i=dr["ffn_dram"][third, 128 * t:128 * (t + 1), :]: h.dma_start(out=o, in_=i),
                      reads=[bf.buf("ffn_dram")], writes=[ffb])
                p.op("v", lambda h, o=xs, a=ff: h.tensor_tensor(o, o, a, ALU.add),
                     reads=[ffb, xsb], writes=[xsb])
        p.dma("sy", lambda h, o=x_out[128 * t:128 * (t + 1), :], i=xs: h.dma_start(out=o, in_=i),
              reads=[xsb], writes=[x_out_buf])



def emit_expm(ctx):
    p, sb, dr, bf = ctx.plan, ctx.sb, ctx.dram, ctx.bufs
    w1b = bf.buf("W1")
    xa = lambda c: ctx.sb["A1"].ap()[:, 0 + 128 * c:0 + 128 * (c + 1)]
    xbq = lambda c: ctx.sb["A1"].ap()[:, 1024 + 128 * c:1024 + 128 * (c + 1)]
    x0 = lambda c: ctx.sb["A1"].ap()[:, 2048 + 128 * c:2048 + 128 * (c + 1)]
    xab, xbb, x0b = bf.buf("eXa"), bf.buf("eXb"), bf.buf("eX0")
    for c in range(8):
        p.dma("g", lambda h, o=x0(c), i=dr["eyecol"][128 * c:128 * (c + 1), :]: h.dma_start(out=o, in_=i),
              writes=[x0b])
    out_dram = dr["eslice"] if ctx.mode == "expm" else dr["eg_in"]
    outb = bf.buf("eslice" if ctx.mode == "expm" else "eg_in")
    for gi in range(3):
        src = chunked(dr["gensT"][gi])
        dst = sb["W1"].ap()[:, 0:8192].rearrange("p (c n) -> p c n", c=8)
        p.dma("g", lambda h, o=dst, i=src: h.dma_start(out=o, in_=i), writes=[w1b])
        for c in range(8):
            p.op("v", lambda h, o=xa(c), i=x0(c): h.tensor_copy(o, i), reads=[x0b], writes=[xab])
        cur, curb = xa, xab
        nxt, nxtb = xbq, xbb
        for kk in range(KTAYLOR, 0, -1):
            for m in range(8):
                i = m % 2
                bank, bb = ctx.bank(i), ctx.bankbuf(i)
                terms = []
                for k in range(8):
                    lhs = sb["W1"].ap()[:, 1024 * k + 128 * m:1024 * k + 128 * (m + 1)]
                    terms.append((bank[0:128, 0:128], lhs, cur(k), k == 0, k == 7))
                mm_group(p, terms, bb, [w1b, curb])
                p.op("v", lambda h, o=nxt(m), b=bank[0:128, 0:128], i0=x0(m), s=1.0 / kk:
                     h.scalar_tensor_tensor(o, b, s, i0, ALU.mult, ALU.add),
                     reads=[bb, x0b], writes=[nxtb])
            cur, curb, nxt, nxtb = nxt, nxtb, cur, curb
        for c in range(8):
            dst = out_dram[1024 * gi + 128 * c:1024 * gi + 128 * (c + 1), :]
            p.dma("sy", lambda h, o=dst, i=cur(c): h.dma_start(out=o, in_=i),
                  reads=[curb], writes=[outb])
    if ctx.mode == "fused":
        extra = getattr(ctx, "precast_done", None) or []
        tok = p.coll(lambda h: h.collective_compute(
            "AllGather", ALU.bypass, replica_groups=[list(range(8))],
            ins=[dr["eg_in"]], outs=[dr["eg_out"]]),
            reads=[bf.buf("eg_in")] + list(extra), writes=[bf.buf("eg_out")])
        ctx.expm_cc_tok = tok
    tok = outb.w
    for nm in ("A1_0", "A1_1"):
        b = bf.buf(nm)
        b.w = tok
        b.rs = []



def emit_layer(ctx, li, x_in, x_out, x_in_buf, x_out_buf):
    emit_ln(ctx, li, x_in, x_in_buf)
    emit_xnT(ctx)
    emit_qkv(ctx, li)
    emit_ag(ctx)
    if li == 0:
        emit_ffn0(ctx)
    else:
        emit_ffn1(ctx)
    emit_kv_reorder(ctx)
    emit_flash(ctx, li)
    emit_attn_proj(ctx, li)
    emit_xpass(ctx, li, x_in, x_out, x_in_buf, x_out_buf)


def build(mode):
    nc = bass.Bass("TRN2", target_bir_lowering=False)
    dram = declare_dram(nc, mode)
    with ExitStack() as stack:
        sb = alloc_sbuf(nc, stack, mode)
        ps = stack.enter_context(nc.psum_tensor("ps_all", [128, 4096], F32))
        sems = {}
        for s in ("t", "v", "a", "g", "sy", "cc"):
            sems[s] = stack.enter_context(nc.semaphore(f"sem_{s}"))
        block = stack.enter_context(nc.Block())
        plan = Plan()
        ctx = Ctx(nc=nc, plan=plan, sb=sb, ps=ps, dram=dram, bufs=Bufs(), mode=mode)
        load_const_sbuf(ctx)
        if mode == "expm":
            emit_expm(ctx)
        elif mode == "l0":
            emit_layer(ctx, 0, dram["x_in"], dram["x_out"],
                       ctx.bufs.buf("x_in_src"), ctx.bufs.buf("x_out_dst"))
        elif mode == "l1":
            emit_precast_l1(ctx)
            emit_layer(ctx, 1, dram["x_in"], dram["x_out"],
                       ctx.bufs.buf("x_in_src"), ctx.bufs.buf("x_out_dst"))
        elif mode == "fused":
            emit_precast_l1(ctx)
            emit_expm(ctx)
            emit_layer(ctx, 0, dram["x0"], dram["x_mid"],
                       ctx.bufs.buf("x_in_src"), ctx.bufs.buf("x_mid_buf"))
            emit_layer(ctx, 1, dram["x_mid"], dram["x_out"],
                       ctx.bufs.buf("x_mid_buf"), ctx.bufs.buf("x_out_dst"))
        plan.wait_tokens("sy", [("sy", plan.cnt["sy"]), ("g", plan.cnt["g"]),
                                ("v", plan.cnt["v"]), ("a", plan.cnt["a"]),
                                ("t", plan.cnt["t"])])
        block.tensor(lambda h: plan.replay("t", h, sems))
        block.vector(lambda h: plan.replay("v", h, sems))
        block.scalar(lambda h: plan.replay("a", h, sems))
        block.gpsimd(lambda h: plan.replay("g", h, sems))
        block.sync(lambda h: plan.replay("sy", h, sems))
    return nc



def kernel(**inputs):
    x_full = np.asarray(inputs["L0_attn_x"], dtype=np.float32)
    w = prep_weights(inputs)
    consts = prep_consts()
    lng = _f32(np.asarray(inputs["L0_ffn_ln_gamma"]).reshape(8, 128).T)
    lnb = _f32(np.asarray(inputs["L0_ffn_ln_beta"]).reshape(8, 128).T)

    cores = list(range(NCORE))
    per_core = [prep_core(c, x_full) for c in cores]

    nc = build("fused")
    in_maps = []
    for c in cores:
        m = {k: w[k] for k in L0_W_KEYS + L1_W_KEYS}
        m.update({"ident_f": consts["ident_f"], "iota_row": consts["iota_row"],
                  "alibi_cols": consts["alibi_cols"], "thr": per_core[c]["thr"],
                  "brow0": per_core[c]["brow0"], "x0": per_core[c]["x0"],
                  "gensT": w["gensT"], "eyecol": per_core[c]["eyecol"],
                  "ln_gamma": lng, "ln_beta": lnb})
        in_maps.append(m)
    res = run_bass_kernel_spmd(nc, in_maps, core_ids=cores)

    out = np.zeros((B_, T_, D_), dtype=np.float32)
    for c in cores:
        toks = own_token_abs(c % 2)
        out[c // 2][toks] = np.asarray(res.results[c]["x_out"], dtype=np.float32)
    return out


# version 05
# speedup vs baseline: 1.4277x; 1.0039x over previous
import math
import dataclasses
from contextlib import ExitStack

import numpy as np
import ml_dtypes

import concourse.bass as bass
import concourse.mybir as mybir
from concourse.bass_utils import run_bass_kernel_spmd

F32 = mybir.dt.float32
BF16 = mybir.dt.bfloat16
I32 = mybir.dt.int32
U8 = mybir.dt.uint8
AF = mybir.ActivationFunctionType
ALU = mybir.AluOpType
AX = mybir.AxisListType

BFNP = ml_dtypes.bfloat16

B_, T_, D_ = 4, 4096, 1024
NCORE = 8
TOK = 2048
NH, HD = 16, 64
EXP_N, EHID = 8, 2048
L1HID = 3072
EPS = 1e-5
S8 = 1.0 / 127.0
KTAYLOR = 12
BCONST = 30.0
SLOTS = [8, 16, 24, 32]

def _slopes():
    start = 2.0 ** (-(2.0 ** -(math.log2(NH) - 3)))
    return [start * start ** i for i in range(NH)]

SLOPES = _slopes()

QB_ABS = {0: [512, 1536, 2560, 3584],
          1: [0, 1024, 2048, 3072]}

def own_token_abs(half):
    out = []
    for qa in QB_ABS[half]:
        out.extend(range(qa, qa + 512))
    return out


class Buf:
    __slots__ = ("w", "rs", "name")
    def __init__(self, name):
        self.name = name
        self.w = None
        self.rs = []

class Plan:
    ENGS = ("t", "v", "a", "g", "sy")

    def __init__(self):
        self.q = {e: [] for e in self.ENGS}
        self.cnt = {e: 0 for e in ("t", "v", "a", "g", "sy", "cc")}
        self.hw = {e: {} for e in self.ENGS}

    def _emit(self, eng, fn, reads, writes, sem, inc):
        waits = {}
        deps = []
        for b in reads:
            if b.w is not None:
                deps.append(b.w)
        for b in writes:
            if b.w is not None:
                deps.append(b.w)
            deps.extend(b.rs)
        for (s, v) in deps:
            if s == "t" and eng == "t":
                continue
            if waits.get(s, -1) < v:
                waits[s] = v
        hw = self.hw[eng]
        wl = []
        for s, v in sorted(waits.items()):
            if hw.get(s, -1) >= v:
                continue
            hw[s] = v
            wl.append((s, v))
        self.cnt[sem] += inc
        tok = (sem, self.cnt[sem])
        self.q[eng].append((wl, fn, (sem, inc)))
        for b in reads:
            b.rs.append(tok)
        for b in writes:
            b.w = tok
            b.rs = []
        return tok

    def op(self, eng, fn, reads=(), writes=()):
        return self._emit(eng, fn, list(reads), list(writes), eng, 1)

    def dma(self, queue, fn, reads=(), writes=()):
        return self._emit(queue, fn, list(reads), list(writes), queue, 16)

    def coll(self, fn, reads=(), writes=()):
        return self._emit("g", fn, list(reads), list(writes), "cc", 1)

    def wait_tokens(self, eng, toks):
        wl = []
        hw = self.hw[eng]
        for (s, v) in toks:
            if hw.get(s, -1) < v:
                hw[s] = v
                wl.append((s, v))
        if wl:
            self.q[eng].append((wl, None, None))

    def replay(self, eng, handle, sems):
        for waits, fn, inc in self.q[eng]:
            for (s, v) in waits:
                handle.wait_ge(sems[s], v)
            if fn is not None:
                ins = fn(handle)
                if inc is not None:
                    ins.then_inc(sems[inc[0]], inc[1])


def mm_group(plan, terms, bank_buf, read_bufs):
    def fn(h, terms=terms):
        last = None
        for (o, l, r, st, sp) in terms:
            last = h.matmul(o, l, r, start=st, stop=sp)
        return last
    return plan.op("t", fn, reads=list(read_bufs), writes=[bank_buf])



def _bf(x):
    return np.ascontiguousarray(np.asarray(x).astype(BFNP))

def _f32(x):
    return np.ascontiguousarray(np.asarray(x), dtype=np.float32)

def prep_consts():
    c = {}
    c["ident_f"] = _f32(np.eye(128))
    c["iota_row"] = _f32(np.broadcast_to(np.arange(512, dtype=np.float32), (128, 512)))
    al = np.zeros((128, NH * 32), dtype=np.float32)
    p = np.arange(128, dtype=np.float32)
    for h in range(NH):
        for cc in range(32):
            al[:, h * 32 + cc] = SLOPES[h] * (128 * cc + p)
    c["alibi_cols"] = al
    return c

def prep_core(core, x_full):
    half = core % 2
    seq = core // 2
    toks = own_token_abs(half)
    d = {}
    d["x0"] = _f32(x_full[seq][toks])
    qabs = np.array(toks, dtype=np.float32)
    brow0 = np.zeros((NH, TOK), dtype=np.float32)
    for h in range(NH):
        brow0[h] = -8.0 * (BCONST + SLOPES[h] * qabs)
    d["brow0"] = _bf(brow0)
    thr = np.zeros((128, 32), dtype=np.float32)
    pp = np.arange(128, dtype=np.float32)
    for j in range(4):
        qa = QB_ABS[half][j]
        for i in range(8):
            cch = SLOTS[j] - 8 + i
            thr[:, 8 * j + i] = pp + 128.0 * cch - qa
    d["thr"] = thr
    ey = np.zeros((1024, 128), dtype=np.float32)
    ey[128 * core:128 * (core + 1), :] = np.eye(128)
    d["eyecol"] = ey
    return d

def prep_weights(inp):
    w = {}
    w["w_attn_T"] = _f32(np.asarray(inp["L0_attn_w_attn"]).T)
    w["w_proj_T"] = _f32(np.asarray(inp["L0_attn_w_proj"]).T)
    w["abel_w"] = _f32(inp["L0_ffn_abel_w"])
    w["abel_b"] = _f32(np.asarray(inp["L0_ffn_abel_b"]).reshape(8, 128).T)
    w["gate_w"] = _f32(inp["L0_ffn_moe_gate_w"])
    w["gate_b"] = _f32(np.asarray(inp["L0_ffn_moe_gate_b"]).reshape(8, 1))
    w["ew1"] = _f32(inp["L0_ffn_ew1"])
    eb1 = np.asarray(inp["L0_ffn_eb1"])
    w["eb1c"] = _f32(np.concatenate([np.asarray(eb1[e]).reshape(16, 128).T for e in range(8)], axis=1))
    w["ew2"] = _f32(inp["L0_ffn_ew2"])
    w["eb2"] = _f32(inp["L0_ffn_eb2"])
    w["gensT"] = _f32(np.transpose(np.asarray(inp["L0_ffn_gens"]), (0, 2, 1)))
    w["gc_w"] = _f32(np.asarray(inp["L0_ffn_gc_w"]).reshape(16, 128).T)
    w["gc_b"] = _f32(np.asarray(inp["L0_ffn_gc_b"]).reshape(1, 1))
    w["wq"] = _f32(inp["L1_attn_Wq"])
    w["wk"] = _f32(inp["L1_attn_Wk"])
    w["wv"] = _f32(inp["L1_attn_Wv"])
    w["wo"] = _f32(inp["L1_attn_Wo"])
    w["wg_T"] = np.ascontiguousarray(np.asarray(inp["L1_ffn_w_gate"]).T)
    w["wu_T"] = np.ascontiguousarray(np.asarray(inp["L1_ffn_w_up"]).T)
    w["wd_T"] = np.ascontiguousarray(np.asarray(inp["L1_ffn_w_down"]).T)
    return w

L0_W_KEYS = ["w_attn_T", "w_proj_T", "abel_w", "abel_b", "gate_w", "gate_b",
             "ew1", "eb1c", "ew2", "eb2", "gc_w", "gc_b"]
L1_W_KEYS = ["wq", "wk", "wv", "wo", "wg_T", "wu_T", "wd_T"]



class Bufs(dict):
    def buf(self, name):
        if name not in self:
            self[name] = Buf(name)
        return self[name]

@dataclasses.dataclass
class Ctx:
    nc: object
    plan: Plan
    sb: dict
    ps: object
    dram: dict
    bufs: Bufs
    mode: str

    def bank(self, i, rows=128):
        return self.ps.ap()[0:rows, 512 * i:512 * (i + 1)]

    def bankbuf(self, i):
        return self.bufs.buf(f"ps{i}")

    def stq(self, i):
        return (self.sb["stage"].ap()[:, 1024 * i:1024 * (i + 1)],
                self.bufs.buf(f"stage{i}"))

    def stq5(self, i):
        return (self.sb["stage"].ap()[:, 1024 * i:1024 * i + 512],
                self.bufs.buf(f"stage{i}"))


def flat_ap(ap2d, offset, dims):
    return dataclasses.replace(ap2d, offset=offset, ap=[list(x) for x in dims])


def chunked(src2d, p=128):
    return src2d.rearrange("(c p) n -> p c n", p=p)


DEBUG_OUTPUTS = ()

def declare_dram(nc, mode):
    d = {}
    def ein(name, shape, dt):
        d[name] = nc.dram_tensor(name, list(shape), dt, kind="ExternalInput").ap()
    def eout(name, shape, dt):
        d[name] = nc.dram_tensor(name, list(shape), dt, kind="ExternalOutput").ap()
    def internal(name, shape, dt):
        if name in DEBUG_OUTPUTS:
            eout(name, shape, dt)
        else:
            d[name] = nc.dram_tensor(name, list(shape), dt).ap()

    ein("ident_f", (128, 128), F32)
    ein("iota_row", (128, 512), F32)
    ein("thr", (128, 32), F32)

    if mode in ("expm", "fused"):
        ein("gensT", (3, 1024, 1024), F32)
        ein("eyecol", (1024, 128), F32)
    if mode == "expm":
        eout("eslice", (3 * 1024, 128), BF16)
    if mode in ("l0", "fused"):
        ein("alibi_cols", (128, NH * 32), F32)
        ein("brow0", (NH, TOK), BF16)
        ein("ln_gamma", (128, 8), F32)
        ein("ln_beta", (128, 8), F32)
        for k, shape, dt in [
            ("w_attn_T", (1024, 3072), F32), ("w_proj_T", (1024, 1024), F32),
            ("abel_w", (1024, 1024), F32), ("abel_b", (128, 8), F32),
            ("gate_w", (1024, 8), F32), ("gate_b", (8, 1), F32),
            ("ew1", (8, 1024, 2048), F32), ("eb1c", (128, 128), F32),
            ("ew2", (8, 2048, 1024), F32), ("eb2", (8, 1024), F32),
            ("gc_w", (128, 16), F32), ("gc_b", (1, 1), F32),
        ]:
            ein(k, shape, dt)
    if mode == "l0":
        ein("E", (3, 1024, 1024), BF16)
    if mode in ("l1", "fused"):
        for k, shape, dt in [
            ("wq", (1024, 1024), F32), ("wk", (1024, 1024), F32),
            ("wv", (1024, 1024), F32), ("wo", (1024, 1024), F32),
            ("wg_T", (1024, 3072), I32), ("wu_T", (1024, 3072), I32),
            ("wd_T", (3072, 1024), I32),
        ]:
            ein(k, shape, dt)

    if mode in ("l0", "l1"):
        ein("x_in", (TOK, 1024), F32)
        eout("x_out", (TOK, 1024), F32)
    if mode == "fused":
        ein("x0", (TOK, 1024), F32)
        eout("x_out", (TOK, 1024), F32)
        internal("x_mid", (TOK, 1024), F32)
        internal("eg_in", (3 * 1024, 128), BF16)
        internal("eg_out", (NCORE * 3 * 1024, 128), BF16)

    if mode in ("l0", "l1", "fused"):
        internal("xn_dram", (TOK, 1024), BF16)
        internal("q_dram", (1024, TOK), BF16)
        internal("ag_in", (2048, 2048), BF16)
        internal("ag_out", (4096, 2048), BF16)
        internal("kabs", (1024, 4096), BF16)
        internal("vabs", (4096, 1024), BF16)
        internal("attn_dram", (TOK, 1024), BF16)
    if mode in ("l0", "fused"):
        internal("xn2p_dram", (TOK, 1024), BF16)
        internal("mixed_dram", (1024, TOK), BF16)
        internal("moe_tok", (1024, 1024), BF16)
    if mode in ("l1", "fused"):
        internal("ffn_dram", (3, TOK, 1024), BF16)
        internal("wg_bf", (1024, 3072), BF16)
        internal("wu_bf", (1024, 3072), BF16)
        internal("wd_bf", (3072, 1024), BF16)
    return d


def alloc_sbuf(nc, stack, mode):
    en = stack.enter_context
    sb = {}
    def t(name, shape, dt):
        sb[name] = en(nc.sbuf_tensor("sb_" + name, list(shape), dt))
    t("W1", (128, 16384), BF16)
    t("W2", (128, 16384), BF16)
    t("A1", (128, 16384), BF16)
    t("A3", (128, 16384), BF16)
    t("kslot", (65, 4096), BF16)
    t("vslot", (128, 32 * 65), BF16)
    t("qslot", (65, 2 * 512), BF16)
    t("xstage", (128, 2 * 1024), F32)
    t("stage", (128, 4 * 1024), BF16)
    t("scr", (128, 2048), F32)
    t("pbuf", (128, 4 * 512), BF16)
    t("rb", (128, 1024), F32)
    t("maskq", (128, 8 * 512), mybir.dt.uint16)
    t("iota", (128, 512), F32)
    t("thrT", (128, 32), F32)
    t("identf", (128, 128), F32)
    t("zeros", (128, 512), BF16)
    t("onesb", (1, 512), BF16)
    t("onesf", (1, 128), F32)
    t("browst", (1, 512), BF16)
    t("ebrow", (1, 1024), BF16)
    t("statc", (128, 16), F32)
    t("epsc", (128, 1), F32)
    t("lgt", (128, 8), F32)
    t("negbig", (128, 8), F32)
    t("e1t", (128, 8), F32)
    t("lmt", (128, 8), F32)
    t("ge2f", (128, 8), F32)
    t("eqm", (128, 8), U8)
    t("wall", (128, 64), F32)
    t("reciprow", (1, 512), F32)
    t("lnc", (128, 16), F32)
    if mode in ("l0", "fused"):
        t("alibi", (128, NH * 32), F32)
        t("eb1c", (128, 128), F32)
        t("gatew", (128, 64), BF16)
        t("gatewf", (128, 64), F32)
        t("gateb", (8, 1), F32)
        t("gcw", (128, 16), BF16)
        t("gcb", (1, 1), F32)
        t("abelb", (128, 8), F32)
        t("grow", (1, 1024), F32)
    return sb



def load_const_sbuf(ctx):
    p, sb, dr, bf = ctx.plan, ctx.sb, ctx.dram, ctx.bufs
    def ld(dst_name, src_name, queue="sy"):
        dst = sb[dst_name]
        src = dr[src_name]
        p.dma(queue, lambda h, d=dst, s=src: h.dma_start(out=d.ap()[:], in_=s),
              writes=[bf.buf(dst_name)])
    ld("iota", "iota_row")
    ld("thrT", "thr")
    ld("identf", "ident_f")
    if ctx.mode in ("l0", "fused"):
        ld("alibi", "alibi_cols")
        ld("eb1c", "eb1c")
        ld("gateb", "gate_b")
        ld("gcb", "gc_b")
        ld("abelb", "abel_b")
        gw = dr["gate_w"].rearrange("(c p) e -> p c e", p=128)
        p.dma("g", lambda h, gw=gw: h.dma_start(
            out=ctx.sb["gatew"].ap().rearrange("p (c e) -> p c e", c=8), in_=gw),
            writes=[bf.buf("gatew")])
        p.dma("sy", lambda h, gw=gw: h.dma_start(
            out=ctx.sb["gatewf"].ap().rearrange("p (c e) -> p c e", c=8), in_=gw),
            writes=[bf.buf("gatewf")])
        p.dma("g", lambda h: h.dma_start(out=ctx.sb["gcw"].ap()[:], in_=ctx.dram["gc_w"]),
              writes=[bf.buf("gcw")])
        p.dma("sy", lambda h: h.dma_start(out=sb["lnc"].ap()[:, 0:8], in_=dr["ln_gamma"]),
              writes=[bf.buf("lnc")])
        p.dma("sy", lambda h: h.dma_start(out=sb["lnc"].ap()[:, 8:16], in_=dr["ln_beta"]),
              writes=[bf.buf("lnc")])
    p.op("v", lambda h: h.memset(sb["zeros"].ap()[:], 0.0), writes=[bf.buf("zeros")])
    p.op("v", lambda h: h.memset(sb["epsc"].ap()[:], EPS), writes=[bf.buf("epsc")])
    p.op("v", lambda h: h.memset(sb["onesb"].ap()[:], 1.0), writes=[bf.buf("onesb")])
    p.op("v", lambda h: h.memset(sb["onesf"].ap()[:], 1.0), writes=[bf.buf("onesf")])
    p.op("v", lambda h: h.memset(sb["negbig"].ap()[:], -1e30), writes=[bf.buf("negbig")])



def emit_ln(ctx, li, x_in, x_in_buf):
    p, sb, dr, bf = ctx.plan, ctx.sb, ctx.dram, ctx.bufs
    st = sb["statc"].ap()
    stb = bf.buf("statc")
    scrb = bf.buf("scr")
    xnb = bf.buf("xn_dram")
    xn2b = bf.buf("xn2p_dram")
    for t in range(16):
        slot = t % 2
        xs = sb["xstage"].ap()[:, 1024 * slot:1024 * (slot + 1)]
        xsb = bf.buf(f"xstage{slot}")
        p.dma("sy", lambda h, o=xs, i=x_in[128 * t:128 * (t + 1), :]: h.dma_start(out=o, in_=i),
              reads=[x_in_buf], writes=[xsb])
        p.op("v", lambda h, o=st[:, 0:1], i=xs: h.reduce_sum(o, i, axis=AX.X),
             reads=[xsb], writes=[stb])
        p.op("a", lambda h, o=sb["scr"].ap()[:, 0:1024], i=xs, a=st[:, 1:2]:
             h.activation(o, i, AF.Square, accum_out=a),
             reads=[xsb], writes=[scrb, stb])
        p.op("v", lambda h: h.tensor_scalar(st[:, 2:3], st[:, 0:1], 1.0 / 1024, None, ALU.mult),
             reads=[stb], writes=[stb])
        p.op("v", lambda h: h.tensor_scalar(st[:, 3:4], st[:, 1:2], 1.0 / 1024, None, ALU.mult),
             reads=[stb], writes=[stb])
        p.op("v", lambda h: h.tensor_tensor(st[:, 4:5], st[:, 2:3], st[:, 2:3], ALU.mult),
             reads=[stb], writes=[stb])
        p.op("v", lambda h: h.tensor_tensor(st[:, 5:6], st[:, 3:4], st[:, 4:5], ALU.subtract),
             reads=[stb], writes=[stb])
        p.op("a", lambda h: h.activation(st[:, 6:7], st[:, 5:6], AF.Sqrt, bias=sb["epsc"].ap()),
             reads=[stb, bf.buf("epsc")], writes=[stb])
        p.op("v", lambda h: h.reciprocal(st[:, 7:8], st[:, 6:7]),
             reads=[stb], writes=[stb])
        p.op("v", lambda h: h.tensor_scalar(st[:, 8:9], st[:, 2:3], st[:, 7:8], -1.0, ALU.mult, ALU.mult),
             reads=[stb], writes=[stb])
        xnt, xntb = ctx.stq(slot)
        p.op("a", lambda h, o=xnt, i=xs: h.activation(o, i, AF.Identity, bias=st[:, 8:9], scale=st[:, 7:8]),
             reads=[xsb, stb], writes=[xntb])
        p.dma("sy", lambda h, o=dr["xn_dram"][128 * t:128 * (t + 1), :], i=xnt: h.dma_start(out=o, in_=i),
              reads=[xntb], writes=[xnb])
        if li == 0:
            p.op("v", lambda h: h.tensor_tensor(st[:, 9:10], st[:, 5:6], st[:, 7:8], ALU.mult),
                 reads=[stb], writes=[stb])
            p.op("v", lambda h: h.tensor_tensor(st[:, 10:11], st[:, 9:10], st[:, 7:8], ALU.mult),
                 reads=[stb], writes=[stb])
            p.op("a", lambda h: h.activation(st[:, 11:12], st[:, 10:11], AF.Sqrt, bias=sb["epsc"].ap()),
                 reads=[stb, bf.buf("epsc")], writes=[stb])
            p.op("v", lambda h: h.reciprocal(st[:, 12:13], st[:, 11:12]),
                 reads=[stb], writes=[stb])
            p.op("v", lambda h: h.tensor_tensor(st[:, 13:14], st[:, 7:8], st[:, 12:13], ALU.mult),
                 reads=[stb], writes=[stb])
            p.op("v", lambda h: h.tensor_scalar(st[:, 14:15], st[:, 2:3], st[:, 13:14], -1.0, ALU.mult, ALU.mult),
                 reads=[stb], writes=[stb])
            xn2t, xn2tb = ctx.stq(2 + slot)
            p.op("a", lambda h, o=xn2t, i=xs: h.activation(o, i, AF.Identity, bias=st[:, 14:15], scale=st[:, 13:14]),
                 reads=[xsb, stb], writes=[xn2tb])
            p.dma("sy", lambda h, o=dr["xn2p_dram"][128 * t:128 * (t + 1), :], i=xn2t: h.dma_start(out=o, in_=i),
                  reads=[xn2tb], writes=[xn2b])


def emit_xnT(ctx):
    p, sb, dr, bf = ctx.plan, ctx.sb, ctx.dram, ctx.bufs
    for c in range(8):
        dst = sb["A1"].ap()[:, 2048 * c:2048 * (c + 1)]
        src = dr["xn_dram"][:, 128 * c:128 * (c + 1)]
        p.dma("sy", lambda h, o=dst, i=src: h.dma_start_transpose(o, i),
              reads=[bf.buf("xn_dram")], writes=[bf.buf(f"A1_{c // 4}")])



def emit_qkv(ctx, li):
    p, sb, dr, bf = ctx.plan, ctx.sb, ctx.dram, ctx.bufs
    a1 = sb["A1"].ap()
    a1r = [bf.buf("A1_0"), bf.buf("A1_1")]
    w1b, w2b = bf.buf("W1"), bf.buf("W2")
    if li == 0:
        wsrc = dr["w_attn_T"]
        qs, ks, vs = (chunked(wsrc[:, 0:1024]), chunked(wsrc[:, 1024:2048]),
                      chunked(wsrc[:, 2048:3072]))
    else:
        qs, ks, vs = chunked(dr["wq"]), chunked(dr["wk"]), chunked(dr["wv"])
    exb = []
    tok = getattr(ctx, "expm_cc_tok", None)
    if tok is not None:
        eb = Buf("_expm_cc")
        eb.w = tok
        exb = [eb]
    def wload(c0, c1, src, arena, buf):
        dst = sb[arena].ap()[:, c0:c1].rearrange("p (c n) -> p c n", c=8)
        p.dma("g", lambda h, o=dst, i=src: h.dma_start(out=o, in_=i), reads=exb, writes=[buf])
    wload(0, 8192, qs, "W1", w1b)
    wload(8192, 16384, ks, "W1", w1b)
    wload(0, 8192, vs, "W2", w2b)

    for which, wc0 in (("q", 0), ("k", 8192)):
        for m in range(8):
            for n in range(4):
                i = (m * 4 + n) % 2
                bank, bb = ctx.bank(i), ctx.bankbuf(i)
                terms = []
                for k in range(8):
                    lhs = sb["W1"].ap()[:, wc0 + 1024 * k + 128 * m: wc0 + 1024 * k + 128 * (m + 1)]
                    rhs = a1[:, 2048 * k + 512 * n: 2048 * k + 512 * (n + 1)]
                    terms.append((bank, lhs, rhs, k == 0, k == 7))
                mm_group(p, terms, bb, [w1b] + a1r)
                ss, ssb = ctx.stq5(i)
                p.op("v", lambda h, o=ss, b=bank: h.tensor_copy(o, b), reads=[bb], writes=[ssb])
                if which == "q":
                    dst = dr["q_dram"][128 * m:128 * (m + 1), 512 * n:512 * (n + 1)]
                    p.dma("sy", lambda h, o=dst, i2=ss: h.dma_start(out=o, in_=i2),
                          reads=[ssb], writes=[bf.buf("q_dram")])
                else:
                    dst = dr["ag_in"][128 * m:128 * (m + 1), 512 * n:512 * (n + 1)]
                    p.dma("sy", lambda h, o=dst, i2=ss: h.dma_start(out=o, in_=i2),
                          reads=[ssb], writes=[bf.buf("ag_in")])

    for tq in range(16):
        for n in range(2):
            i = (tq * 2 + n) % 2
            bank, bb = ctx.bank(i), ctx.bankbuf(i)
            terms = []
            for k in range(8):
                lhs = a1[:, 2048 * k + 128 * tq: 2048 * k + 128 * (tq + 1)]
                rhs = sb["W2"].ap()[:, 1024 * k + 512 * n: 1024 * k + 512 * (n + 1)]
                terms.append((bank, lhs, rhs, k == 0, k == 7))
            mm_group(p, terms, bb, [w2b] + a1r)
            ss, ssb = ctx.stq5(i)
            p.op("v", lambda h, o=ss, b=bank: h.tensor_copy(o, b), reads=[bb], writes=[ssb])
            off = 1024 * 2048 + (128 * tq) * 1024 + 512 * n
            dst = flat_ap(dr["ag_in"], off, [[1024, 128], [1, 512]])
            p.dma("sy", lambda h, o=dst, i2=ss: h.dma_start(out=o, in_=i2),
                  reads=[ssb], writes=[bf.buf("ag_in")])


AG_SERIAL = False
FLASH_PIPE = True
FLASH_YPP = False

def emit_precast_l1(ctx):
    p, dr, bf = ctx.plan, ctx.dram, ctx.bufs
    for name in ("wg_bf", "wu_bf", "wd_bf"):
        srcn = {"wg_bf": "wg_T", "wu_bf": "wu_T", "wd_bf": "wd_T"}[name]
        p.dma("g", lambda h, o=dr[name], i=dr[srcn]: h.dma_start(out=o, in_=i),
              writes=[bf.buf(name)])
    ctx.precast_done = [bf.buf("wg_bf"), bf.buf("wu_bf"), bf.buf("wd_bf")]


def emit_ag(ctx):
    p, dr, bf = ctx.plan, ctx.dram, ctx.bufs
    extra = getattr(ctx, "precast_done", None) or []
    tok = p.coll(lambda h: h.collective_compute(
        "AllGather", ALU.bypass,
        replica_groups=[[0, 1], [2, 3], [4, 5], [6, 7]],
        ins=[dr["ag_in"]], outs=[dr["ag_out"]]),
        reads=[bf.buf("ag_in")] + list(extra), writes=[bf.buf("ag_out")])
    ctx.last_ag_tok = tok
    if AG_SERIAL:
        for nm in ("W1", "W2", "A1_0", "A1_1", "A3a", "A3b", "xstage0", "xstage1",
                   "pbuf0", "pbuf1", "scr", "statc", "ffn_dram", "stage0", "stage1",
                   "stage2", "stage3"):
            b = bf.buf(nm)
            b.rs.append(tok)


def emit_kv_reorder(ctx):
    p, dr, bf = ctx.plan, ctx.dram, ctx.bufs
    blocks0 = [a // 128 for a in own_token_abs(0)[::128]]
    blocks1 = [a // 128 for a in own_token_abs(1)[::128]]
    src_of_abs = {}
    for loc, ab in enumerate(blocks0):
        src_of_abs[ab] = (0, loc)
    for loc, ab in enumerate(blocks1):
        src_of_abs[ab] = (1, loc)
    agb, kb, vb = bf.buf("ag_out"), bf.buf("kabs"), bf.buf("vabs")
    for ab in range(32):
        rank, loc = src_of_abs[ab]
        src = dr["ag_out"][2048 * rank: 2048 * rank + 1024, 128 * loc:128 * (loc + 1)]
        dst = dr["kabs"][:, 128 * ab:128 * (ab + 1)]
        p.dma("sy", lambda h, o=dst, i=src: h.dma_start(out=o, in_=i), reads=[agb], writes=[kb])
        off = (2048 * rank + 1024) * 2048 + (128 * loc) * 1024
        src = flat_ap(dr["ag_out"], off, [[1024, 128], [1, 1024]])
        dst = dr["vabs"][128 * ab:128 * (ab + 1), :]
        p.dma("sy", lambda h, o=dst, i=src: h.dma_start(out=o, in_=i), reads=[agb], writes=[vb])



def emit_flash(ctx, li):
    p, sb, dr, bf = ctx.plan, ctx.sb, ctx.dram, ctx.bufs
    a1w = [bf.buf("A1_0"), bf.buf("A1_1")]
    kbuf, vbuf, qbuf = bf.buf("kslot"), bf.buf("vslot"), bf.buf("qslot")
    mqb = bf.buf("maskq")
    iob, thb = bf.buf("iota"), bf.buf("thrT")
    kabs_b, vabs_b, qdr_b = bf.buf("kabs"), bf.buf("vabs"), bf.buf("q_dram")
    alb = bf.buf("alibi") if li == 0 else None
    zb = bf.buf("zeros")
    ofb = bf.buf("onesf")
    vsl = sb["vslot"].ap().rearrange("p (c w) -> p c w", w=65)

    p.op("v", lambda h: h.memset(sb["kslot"].ap()[64:65, :], 1.0), writes=[kbuf])
    if li == 1:
        p.op("v", lambda h: h.memset(sb["qslot"].ap()[64:65, :], -8.0 * BCONST), writes=[qbuf])

    for j in range(4):
        s = SLOTS[j]
        for i8 in range(8):
            dst = sb["maskq"].ap()[:, 512 * i8:512 * (i8 + 1)]
            thr = sb["thrT"].ap()[:, 8 * j + i8: 8 * j + i8 + 1]
            p.op("v", lambda h, o=dst, t=thr: h.tensor_scalar(o, sb["iota"].ap()[:], t, None, ALU.is_ge),
                 reads=[iob, thb], writes=[mqb])
        for hd in range(NH):
            if li == 0:
                qa_min = QB_ABS[1][j]
                cs = [c for c in range(s)
                      if c >= s - 8 or SLOPES[hd] * ((128 * c + 127) - qa_min) >= -93.0]
            else:
                cs = list(range(s))
            p.dma("sy", lambda h, o=sb["kslot"].ap()[0:64, 0:128 * s],
                  i=dr["kabs"][64 * hd:64 * (hd + 1), 0:128 * s]: h.dma_start(out=o, in_=i),
                  reads=[kabs_b], writes=[kbuf])
            src = flat_ap(dr["vabs"], 64 * hd, [[1024, 128], [128 * 1024, s], [1, 64]])
            p.dma("sy", lambda h, o=vsl[:, 0:s, 0:64], i=src: h.dma_start(out=o, in_=i),
                  reads=[vabs_b], writes=[vbuf])
            p.op("v", lambda h, o=vsl[:, 0:s, 64:65]: h.memset(o, 1.0), writes=[vbuf])
            qs0 = 512 * (hd % 2)
            qsl = sb["qslot"].ap()[0:65, qs0:qs0 + 512]
            p.dma("sy", lambda h, o=sb["qslot"].ap()[0:64, qs0:qs0 + 512],
                  i=dr["q_dram"][64 * hd:64 * (hd + 1), 512 * j:512 * (j + 1)]: h.dma_start(out=o, in_=i),
                  reads=[qdr_b], writes=[qbuf])
            if li == 0:
                p.dma("sy", lambda h, o=sb["qslot"].ap()[64:65, qs0:qs0 + 512],
                      i=dr["brow0"][hd:hd + 1, 512 * j:512 * (j + 1)]: h.dma_start(out=o, in_=i),
                      reads=[bf.buf("brow0_src")], writes=[qbuf])

            yb = 2 if (hd % 2 == 0 or not FLASH_YPP) else 4
            bcb = 3 if (hd % 2 == 0 or not FLASH_YPP) else 5
            ps_y, ps_y_buf = ctx.ps.ap()[0:65, 512 * yb:512 * (yb + 1)], ctx.bankbuf(yb)
            ps_bc, ps_bc_buf = ctx.ps.ap()[0:64, 512 * bcb:512 * (bcb + 1)], ctx.bankbuf(bcb)

            def s_mm(idx):
                c = cs[idx]
                i = idx % 2
                bank, bb = ctx.bank(i), ctx.bankbuf(i)
                p.op("t", lambda h, o=bank, l=sb["kslot"].ap()[0:65, 128 * c:128 * (c + 1)], r=qsl:
                     h.matmul(o, l, r, start=True, stop=True),
                     reads=[kbuf, qbuf], writes=[bb])

            s_mm(0)
            for idx, c in enumerate(cs):
                i = idx % 2
                bank, bb = ctx.bank(i), ctx.bankbuf(i)
                pb = sb["pbuf"].ap()[:, 512 * i:512 * (i + 1)]
                pbb = bf.buf(f"pbuf{i}")
                if li == 0:
                    bias = sb["alibi"].ap()[:, hd * 32 + c: hd * 32 + c + 1]
                    p.op("a", lambda h, o=pb, b=bank, bi=bias: h.activation(o, b, AF.Exp, bias=bi, scale=0.125),
                         reads=[bb, alb], writes=[pbb])
                else:
                    p.op("a", lambda h, o=pb, b=bank: h.activation(o, b, AF.Exp, scale=0.125),
                         reads=[bb], writes=[pbb])
                if FLASH_PIPE and idx + 1 < len(cs):
                    s_mm(idx + 1)
                if c >= s - 8:
                    msk = sb["maskq"].ap()[:, 512 * (c - (s - 8)):512 * (c - (s - 8) + 1)]
                    pm = sb["pbuf"].ap()[:, 512 * (2 + i):512 * (3 + i)]
                    pmb = bf.buf(f"pbuf{2 + i}")
                    p.op("v", lambda h, o=pm, m=msk, src=pb: h.select(o, m, src, sb["zeros"].ap()[:]),
                         reads=[mqb, zb, pbb], writes=[pmb])
                    pv_src, pv_srcb = pm, pmb
                else:
                    pv_src, pv_srcb = pb, pbb
                lhs = vsl[:, c, 0:65]
                p.op("t", lambda h, o=ps_y, l=lhs, r=pv_src, st=(idx == 0), sp=(idx == len(cs) - 1):
                     h.matmul(o, l, r, start=st, stop=sp),
                     reads=[vbuf, pv_srcb], writes=[ps_y_buf])
                if (not FLASH_PIPE) and idx + 1 < len(cs):
                    s_mm(idx + 1)

            rr = sb["reciprow"].ap()
            rrb = bf.buf("reciprow")
            p.op("v", lambda h, o=rr: h.reciprocal(o, ps_y[64:65, 0:512]),
                 reads=[ps_y_buf], writes=[rrb])
            p.op("t", lambda h, o=ps_bc: h.matmul(o, sb["onesf"].ap()[:, 0:64], rr, start=True, stop=True),
                 reads=[ofb, rrb], writes=[ps_bc_buf])
            rbb = bf.buf(f"rb{hd % 2}")
            rbap = sb["rb"].ap()[0:64, 512 * (hd % 2):512 * (hd % 2) + 512]
            p.op("v", lambda h, o=rbap: h.tensor_copy(o, ps_bc),
                 reads=[ps_bc_buf], writes=[rbb])
            dst = sb["A1"].ap()[64 * (hd % 2):64 * (hd % 2) + 64,
                                2048 * (hd // 2) + 512 * j: 2048 * (hd // 2) + 512 * (j + 1)]
            p.op("v", lambda h, o=dst, r2=rbap: h.tensor_tensor(o, ps_y[0:64, 0:512], r2, ALU.mult),
                 reads=[ps_y_buf, rbb], writes=[a1w[(hd // 2) // 4]])


def emit_attn_proj(ctx, li):
    p, sb, dr, bf = ctx.plan, ctx.sb, ctx.dram, ctx.bufs
    w1b = bf.buf("W1")
    a1r = [bf.buf("A1_0"), bf.buf("A1_1")]
    for fl in range(3):
        p.op("v", lambda h: h.tensor_copy(sb["statc"].ap()[:, 15:16], sb["statc"].ap()[:, 14:15]),
             reads=a1r, writes=[bf.buf("statc")])
    flb = bf.buf("statc")
    a1r = a1r + [flb]
    src = chunked(dr["w_proj_T"] if li == 0 else dr["wo"])
    dst = sb["W1"].ap()[:, 0:8192].rearrange("p (c n) -> p c n", c=8)
    p.dma("g", lambda h, o=dst, i=src: h.dma_start(out=o, in_=i), writes=[w1b])
    for tq in range(16):
        for n in range(2):
            i = (tq * 2 + n) % 2
            bank, bb = ctx.bank(i), ctx.bankbuf(i)
            terms = []
            for k in range(8):
                lhs = sb["A1"].ap()[:, 2048 * k + 128 * tq:2048 * k + 128 * (tq + 1)]
                rhs = sb["W1"].ap()[:, 1024 * k + 512 * n:1024 * k + 512 * (n + 1)]
                terms.append((bank, lhs, rhs, k == 0, k == 7))
            mm_group(p, terms, bb, [w1b] + a1r)
            ss, ssb = ctx.stq5(i)
            p.op("v", lambda h, o=ss, b=bank: h.tensor_copy(o, b), reads=[bb], writes=[ssb])
            d2 = dr["attn_dram"][128 * tq:128 * (tq + 1), 512 * n:512 * (n + 1)]
            p.dma("sy", lambda h, o=d2, i2=ss: h.dma_start(out=o, in_=i2),
                  reads=[ssb], writes=[bf.buf("attn_dram")])



def emit_gate0(ctx):
    p, sb, dr, bf = ctx.plan, ctx.sb, ctx.dram, ctx.bufs
    ps_lb = ctx.bankbuf(4)
    ps_t, ps_tb = ctx.ps.ap()[0:128, 2560:3072], ctx.bankbuf(5)
    scrb = bf.buf("scr")
    st = sb["statc"].ap()
    stb = bf.buf("statc")
    for n in range(2):
        psl = ctx.ps.ap()[0:8, 2048 + 512 * n:2048 + 512 * (n + 1)]
        dst = sb["scr"].ap()[0:8, 1024 + 512 * n:1024 + 512 * (n + 1)]
        p.op("a", lambda h, o=dst, i=psl: h.activation(o, i, AF.Identity, bias=sb["gateb"].ap()[:]),
             reads=[ps_lb, bf.buf("gateb")], writes=[scrb])
    for s in range(8):
        src = sb["scr"].ap()[0:8, 1024 + 128 * s:1024 + 128 * (s + 1)]
        p.op("t", lambda h, i=src: h.transpose(ps_t[0:128, 0:8], i, sb["identf"].ap()[0:8, 0:8]),
             reads=[scrb, bf.buf("identf")], writes=[ps_tb])
        lg = sb["lgt"].ap()
        lgb = bf.buf("lgt")
        p.op("v", lambda h: h.tensor_copy(lg, ps_t[0:128, 0:8]), reads=[ps_tb], writes=[lgb])
        p.op("v", lambda h: h.reduce_max(st[:, 0:1], lg, axis=AX.X), reads=[lgb], writes=[stb])
        p.op("v", lambda h: h.tensor_scalar(st[:, 1:2], st[:, 0:1], -1.0, None, ALU.mult),
             reads=[stb], writes=[stb])
        p.op("v", lambda h: h.tensor_scalar(sb["eqm"].ap(), lg, st[:, 0:1], None, ALU.is_ge),
             reads=[lgb, stb], writes=[bf.buf("eqm")])
        p.op("v", lambda h: h.select(sb["lmt"].ap(), sb["eqm"].ap(), sb["negbig"].ap(), lg),
             reads=[bf.buf("eqm"), bf.buf("negbig"), lgb], writes=[bf.buf("lmt")])
        p.op("v", lambda h: h.reduce_max(st[:, 2:3], sb["lmt"].ap(), axis=AX.X),
             reads=[bf.buf("lmt")], writes=[stb])
        p.op("a", lambda h: h.activation(sb["e1t"].ap(), lg, AF.Exp, bias=st[:, 1:2]),
             reads=[lgb, stb], writes=[bf.buf("e1t")])
        p.op("a", lambda h: h.activation(st[:, 3:4], st[:, 2:3], AF.Exp, bias=st[:, 1:2]),
             reads=[stb], writes=[stb])
        p.op("v", lambda h: h.tensor_scalar(st[:, 4:5], st[:, 3:4], 1.0, None, ALU.add),
             reads=[stb], writes=[stb])
        p.op("v", lambda h: h.reciprocal(st[:, 5:6], st[:, 4:5]), reads=[stb], writes=[stb])
        p.op("v", lambda h: h.tensor_scalar(sb["ge2f"].ap(), lg, st[:, 2:3], None, ALU.is_ge),
             reads=[lgb, stb], writes=[bf.buf("ge2f")])
        dst = sb["wall"].ap()[:, 8 * s:8 * (s + 1)]
        p.op("v", lambda h, o=dst: h.scalar_tensor_tensor(o, sb["e1t"].ap(), st[:, 5:6],
                                                          sb["ge2f"].ap(), ALU.mult, ALU.mult),
             reads=[bf.buf("e1t"), stb, bf.buf("ge2f")], writes=[bf.buf("wall")])


def emit_E_load(ctx, r, arena, wb):
    p, sb, dr, bf = ctx.plan, ctx.sb, ctx.dram, ctx.bufs
    if ctx.mode == "l0":
        src = chunked(dr["E"][r])
        dst = sb[arena].ap()[:, 0:8192].rearrange("p (c n) -> p c n", c=8)
        p.dma("sy", lambda h, o=dst, i=src: h.dma_start(out=o, in_=i),
              reads=[bf.buf("E_src")], writes=[wb])
    else:
        for rk in range(8):
            dst = dataclasses.replace(sb[arena].ap(), offset=128 * rk,
                                      ap=[[16384, 128], [1024, 8], [1, 128]])
            src = flat_ap(dr["eg_out"], (3072 * rk + 1024 * r) * 128,
                          [[128, 128], [128 * 128, 8], [1, 128]])
            p.dma("sy", lambda h, o=dst, i=src: h.dma_start(out=o, in_=i),
                  reads=[bf.buf("eg_out")], writes=[wb])


def emit_ffn0(ctx):
    p, sb, dr, bf = ctx.plan, ctx.sb, ctx.dram, ctx.bufs
    w1b, w2b = bf.buf("W1"), bf.buf("W2")
    a3ab, a3bb = bf.buf("A3a"), bf.buf("A3b")
    a1ab, a1bb = bf.buf("A1_0"), bf.buf("A1_1")
    onb = bf.buf("onesb")
    for th in range(2):
        for c in range(8):
            dst = sb["A3"].ap()[:, 1024 * c:1024 * (c + 1)]
            src = dr["xn2p_dram"][1024 * th:1024 * (th + 1), 128 * c:128 * (c + 1)]
            p.dma("sy", lambda h, o=dst, i=src: h.dma_start_transpose(o, i),
                  reads=[bf.buf("xn2p_dram")], writes=[a3ab])
        ps_l, ps_lb = ctx.ps.ap()[0:8, 2048:2560], ctx.bankbuf(4)
        scrb = bf.buf("scr")
        for c in range(8):
            sl = sb["A3"].ap()[:, 1024 * c:1024 * (c + 1)]
            fs = sb["scr"].ap()[:, 0:1024]
            p.op("a", lambda h, o=fs, i=sl, g=sb["lnc"].ap()[:, c:c + 1], b2=sb["lnc"].ap()[:, 8 + c:9 + c]:
                 h.activation(o, i, AF.Identity, bias=b2, scale=g),
                 reads=[bf.buf("lnc"), a3ab, scrb], writes=[scrb])
            for nb in range(2):
                p.op("t", lambda h, c=c, nb=nb: h.matmul(
                    ps_l[:, 512 * nb:512 * (nb + 1)] if False else ctx.ps.ap()[0:8, 2048 + 512 * nb:2048 + 512 * (nb + 1)],
                    sb["gatewf"].ap()[:, 8 * c:8 * (c + 1)],
                    sb["scr"].ap()[:, 512 * nb:512 * (nb + 1)],
                    start=(c == 0), stop=(c == 7)),
                    reads=[bf.buf("gatewf"), scrb], writes=[ps_lb])
            p.op("v", lambda h, o=sl, i=fs: h.tensor_copy(o, i),
                 reads=[scrb], writes=[a3ab])

        emit_gate0(ctx)

        for e in range(EXP_N):
            src = chunked(dr["ew1"][e])
            dst = sb["W1"].ap().rearrange("p (c n) -> p c n", c=8)
            p.dma("g", lambda h, o=dst, i=src: h.dma_start(out=o, in_=i), writes=[w1b])
            src = chunked(dr["ew2"][e])
            dst = sb["W2"].ap().rearrange("p (c n) -> p c n", c=16)
            p.dma("g", lambda h, o=dst, i=src: h.dma_start(out=o, in_=i), writes=[w2b])
            p.dma("g", lambda h, o=sb["ebrow"].ap()[:], i=dr["eb2"][e:e + 1, :]: h.dma_start(out=o, in_=i),
                  writes=[bf.buf("ebrow")])

            for tb in range(2):
                for m in range(16):
                    i = m % 2
                    bank, bb = ctx.bank(i), ctx.bankbuf(i)
                    terms = []
                    for k in range(8):
                        lhs = sb["W1"].ap()[:, 2048 * k + 128 * m:2048 * k + 128 * (m + 1)]
                        rhs = sb["A3"].ap()[:, 1024 * k + 512 * tb:1024 * k + 512 * (tb + 1)]
                        terms.append((bank, lhs, rhs, k == 0, k == 7))
                    mm_group(p, terms, bb, [w1b, a3ab])
                    hdst = sb["A1"].ap()[:, 512 * m:512 * (m + 1)]
                    bias = sb["eb1c"].ap()[:, 16 * e + m:16 * e + m + 1]
                    p.op("a", lambda h, o=hdst, b=bank, bi=bias: h.activation(o, b, AF.Gelu, bias=bi),
                         reads=[bb, bf.buf("eb1c")], writes=[a1ab])
                for t4 in range(4):
                    for dn in range(2):
                        ib = t4 * 2 + dn
                        bank, bb = ctx.bank(ib), ctx.bankbuf(ib)
                        terms = []
                        for k in range(16):
                            lhs = sb["A1"].ap()[:, 512 * k + 128 * t4:512 * k + 128 * (t4 + 1)]
                            rhs = sb["W2"].ap()[:, 1024 * k + 512 * dn:1024 * k + 512 * (dn + 1)]
                            terms.append((bank, lhs, rhs, k == 0, False))
                        terms.append((bank, sb["onesb"].ap()[:, 0:128],
                                      sb["ebrow"].ap()[:, 512 * dn:512 * (dn + 1)], False, True))
                        mm_group(p, terms, bb, [a1ab, w2b, onb, bf.buf("ebrow")])
                        tt = tb * 4 + t4
                        acc = sb["A3"].ap()[:, 8192 + 1024 * tt + 512 * dn:8192 + 1024 * tt + 512 * (dn + 1)]
                        wcol = sb["wall"].ap()[:, 8 * tt + e:8 * tt + e + 1]
                        if e == 0:
                            p.op("v", lambda h, o=acc, b=bank, w=wcol: h.tensor_scalar(o, b, w, None, ALU.mult),
                                 reads=[bb, bf.buf("wall")], writes=[a3bb])
                        else:
                            p.op("v", lambda h, o=acc, b=bank, w=wcol: h.scalar_tensor_tensor(
                                o, b, w, o, ALU.mult, ALU.add),
                                reads=[bb, bf.buf("wall")], writes=[a3bb])

        for tt in range(8):
            src2 = sb["A3"].ap()[:, 8192 + 1024 * tt:8192 + 1024 * (tt + 1)]
            p.dma("sy", lambda h, o=dr["moe_tok"][128 * tt:128 * (tt + 1), :], i=src2: h.dma_start(out=o, in_=i),
                  reads=[a3bb], writes=[bf.buf("moe_tok")])
        for c in range(8):
            dst = sb["A1"].ap()[:, 1024 * c:1024 * (c + 1)]
            src2 = dr["moe_tok"][:, 128 * c:128 * (c + 1)]
            p.dma("sy", lambda h, o=dst, i=src2: h.dma_start_transpose(o, i),
                  reads=[bf.buf("moe_tok")], writes=[a1ab])
        for r in range(3):
            warena = "W1" if r % 2 == 0 else "W2"
            wb = w1b if r % 2 == 0 else w2b
            emit_E_load(ctx, r, warena, wb)
            src_cols = 0 if r % 2 == 0 else 8192
            dst_cols = 8192 if r % 2 == 0 else 0
            srcb = a1ab if r % 2 == 0 else a1bb
            dstb = a1bb if r % 2 == 0 else a1ab
            for m in range(8):
                for n in range(2):
                    i = (m * 2 + n) % 2
                    bank, bb = ctx.bank(i), ctx.bankbuf(i)
                    terms = []
                    for k in range(8):
                        lhs = sb[warena].ap()[:, 1024 * k + 128 * m:1024 * k + 128 * (m + 1)]
                        rhs = sb["A1"].ap()[:, src_cols + 1024 * k + 512 * n:src_cols + 1024 * k + 512 * (n + 1)]
                        terms.append((bank, lhs, rhs, k == 0, k == 7))
                    mm_group(p, terms, bb, [wb, srcb])
                    dst = sb["A1"].ap()[:, dst_cols + 1024 * m + 512 * n:dst_cols + 1024 * m + 512 * (n + 1)]
                    p.op("v", lambda h, o=dst, b=bank: h.tensor_copy(o, b), reads=[bb], writes=[dstb])

        src = chunked(dr["abel_w"])
        dst = sb["W1"].ap()[:, 0:8192].rearrange("p (c n) -> p c n", c=8)
        p.dma("g", lambda h, o=dst, i=src: h.dma_start(out=o, in_=i), writes=[w1b])
        for m in range(8):
            for n in range(2):
                i = (m * 2 + n) % 2
                bank, bb = ctx.bank(i), ctx.bankbuf(i)
                terms = []
                for k in range(8):
                    lhs = sb["W1"].ap()[:, 1024 * k + 128 * m:1024 * k + 128 * (m + 1)]
                    rhs = sb["A3"].ap()[:, 1024 * k + 512 * n:1024 * k + 512 * (n + 1)]
                    terms.append((bank, lhs, rhs, k == 0, k == 7))
                mm_group(p, terms, bb, [w1b, a3ab])
                dst = sb["A3"].ap()[:, 8192 + 1024 * m + 512 * n:8192 + 1024 * m + 512 * (n + 1)]
                bias = sb["abelb"].ap()[:, m:m + 1]
                p.op("a", lambda h, o=dst, b=bank, bi=bias: h.activation(o, b, AF.Gelu, bias=bi),
                     reads=[bb, bf.buf("abelb")], writes=[a3bb])

        ps_g, ps_gb = ctx.ps.ap()[0:1, 2048:2560], ctx.bankbuf(4)
        for n in range(2):
            terms = []
            for k in range(8):
                lhs = sb["gcw"].ap()[:, k:k + 1]
                rhs = sb["A3"].ap()[:, 8192 + 1024 * k + 512 * n:8192 + 1024 * k + 512 * (n + 1)]
                terms.append((ps_g, lhs, rhs, k == 0, False))
            for k in range(8):
                lhs = sb["gcw"].ap()[:, 8 + k:9 + k]
                rhs = sb["A1"].ap()[:, 8192 + 1024 * k + 512 * n:8192 + 1024 * k + 512 * (n + 1)]
                terms.append((ps_g, lhs, rhs, False, k == 7))
            mm_group(p, terms, ps_gb, [bf.buf("gcw"), a3bb, a1bb])
            dst = sb["grow"].ap()[:, 512 * n:512 * (n + 1)]
            p.op("a", lambda h, o=dst: h.activation(o, ps_g, AF.Sigmoid, bias=sb["gcb"].ap()[:]),
                 reads=[ps_gb, bf.buf("gcb")], writes=[bf.buf("grow")])

        ps_bc, ps_bc_buf = ctx.ps.ap()[0:128, 1536:2048], ctx.bankbuf(3)
        for n in range(2):
            p.op("t", lambda h, r2=sb["grow"].ap()[:, 512 * n:512 * (n + 1)]:
                 h.matmul(ps_bc, sb["onesf"].ap()[:, 0:128], r2, start=True, stop=True),
                 reads=[bf.buf("grow"), bf.buf("onesf")], writes=[ps_bc_buf])
            p.op("v", lambda h: h.tensor_copy(sb["rb"].ap()[:, 0:512], ps_bc),
                 reads=[ps_bc_buf], writes=[bf.buf("rb")])
            for m in range(8):
                ab = sb["A3"].ap()[:, 8192 + 1024 * m + 512 * n:8192 + 1024 * m + 512 * (n + 1)]
                mo = sb["A1"].ap()[:, 8192 + 1024 * m + 512 * n:8192 + 1024 * m + 512 * (n + 1)]
                df = sb["pbuf"].ap()[:, 0:512]
                p.op("v", lambda h, o=df, a=mo, b2=ab: h.tensor_tensor(o, a, b2, ALU.subtract),
                     reads=[a1bb, a3bb], writes=[bf.buf("pbuf0")])
                p.op("v", lambda h, o=df: h.tensor_tensor(o, o, sb["rb"].ap()[:, 0:512], ALU.mult),
                     reads=[bf.buf("rb"), bf.buf("pbuf0")], writes=[bf.buf("pbuf0")])
                mx = sb["pbuf"].ap()[:, 512:1024]
                p.op("v", lambda h, o=mx, a=df, b2=ab: h.tensor_tensor(o, a, b2, ALU.add),
                     reads=[bf.buf("pbuf0"), a3bb], writes=[bf.buf("pbuf1")])
                dst = dr["mixed_dram"][128 * m:128 * (m + 1), 1024 * th + 512 * n:1024 * th + 512 * (n + 1)]
                p.dma("sy", lambda h, o=dst, i=mx: h.dma_start(out=o, in_=i),
                      reads=[bf.buf("pbuf1")], writes=[bf.buf("mixed_dram")])



def emit_ffn1(ctx):
    p, sb, dr, bf = ctx.plan, ctx.sb, ctx.dram, ctx.bufs
    w1b, w2b = bf.buf("W1"), bf.buf("W2")
    a1r = [bf.buf("A1_0"), bf.buf("A1_1")]
    a3b = bf.buf("A3a")
    pc = list(getattr(ctx, "precast_done", None) or [])
    agtok = getattr(ctx, "last_ag_tok", None)
    if agtok is not None:
        b = Buf("_ag_done")
        b.w = agtok
        pc.append(b)
    for third in range(3):
        for (name, c0, c1) in (("wg_bf", 0, 8192), ("wu_bf", 8192, 16384)):
            src = chunked(dr[name][:, 1024 * third:1024 * (third + 1)])
            dst = sb["W1"].ap()[:, c0:c1].rearrange("p (c n) -> p c n", c=8)
            p.dma("sy", lambda h, o=dst, i=src: h.dma_start(out=o, in_=i),
                  reads=list(pc), writes=[w1b])
        src = chunked(dr["wd_bf"][1024 * third:1024 * (third + 1), :])
        dst = sb["W2"].ap()[:, 0:8192].rearrange("p (c n) -> p c n", c=8)
        p.dma("sy", lambda h, o=dst, i=src: h.dma_start(out=o, in_=i),
              reads=list(pc), writes=[w2b])

        for m in range(8):
            for n in range(4):
                ig = (m * 4 + n) % 2
                bankg, bbg = ctx.bank(2 * ig), ctx.bankbuf(2 * ig)
                banku, bbu = ctx.bank(2 * ig + 1), ctx.bankbuf(2 * ig + 1)
                termsg, termsu = [], []
                for k in range(8):
                    rhs = sb["A1"].ap()[:, 2048 * k + 512 * n:2048 * k + 512 * (n + 1)]
                    lg = sb["W1"].ap()[:, 1024 * k + 128 * m:1024 * k + 128 * (m + 1)]
                    lu = sb["W1"].ap()[:, 8192 + 1024 * k + 128 * m:8192 + 1024 * k + 128 * (m + 1)]
                    termsg.append((bankg, lg, rhs, k == 0, k == 7))
                    termsu.append((banku, lu, rhs, k == 0, k == 7))
                mm_group(p, termsg, bbg, [w1b] + a1r)
                mm_group(p, termsu, bbu, [w1b] + a1r)
                sil = sb["pbuf"].ap()[:, 512 * ig:512 * (ig + 1)]
                silb = bf.buf(f"pbuf{ig}")
                p.op("a", lambda h, o=sil, b=bankg: h.activation(o, b, AF.Silu, scale=S8),
                     reads=[bbg], writes=[silb])
                dst = sb["A3"].ap()[:, 2048 * m + 512 * n:2048 * m + 512 * (n + 1)]
                p.op("v", lambda h, o=dst, b=banku, s=sil: h.scalar_tensor_tensor(
                    o, b, S8, s, ALU.mult, ALU.mult),
                    reads=[bbu, silb], writes=[a3b])
        for tq in range(16):
            for dn in range(2):
                i = (tq * 2 + dn) % 2
                bank, bb = ctx.bank(4 + i), ctx.bankbuf(4 + i)
                terms = []
                for k in range(8):
                    lhs = sb["A3"].ap()[:, 2048 * k + 128 * tq:2048 * k + 128 * (tq + 1)]
                    rhs = sb["W2"].ap()[:, 1024 * k + 512 * dn:1024 * k + 512 * (dn + 1)]
                    terms.append((bank, lhs, rhs, k == 0, k == 7))
                mm_group(p, terms, bb, [a3b, w2b])
                ss, ssb = ctx.stq5(i)
                p.op("v", lambda h, o=ss, b=bank: h.tensor_scalar(o, b, S8, None, ALU.mult),
                     reads=[bb], writes=[ssb])
                dst = dr["ffn_dram"][third, 128 * tq:128 * (tq + 1), 512 * dn:512 * (dn + 1)]
                p.dma("sy", lambda h, o=dst, i2=ss: h.dma_start(out=o, in_=i2),
                      reads=[ssb], writes=[bf.buf("ffn_dram")])



XPASS_BARRIER = True

def emit_xpass(ctx, li, x_in, x_out, x_in_buf, x_out_buf):
    p, sb, dr, bf = ctx.plan, ctx.sb, ctx.dram, ctx.bufs
    if XPASS_BARRIER:
        toks = [(s, p.cnt[s]) for s in ("t", "v", "a", "g", "sy")]
        for e in ("t", "v", "a", "g", "sy"):
            p.wait_tokens(e, toks)
    for t in range(16):
        slot = t % 2
        xs = sb["xstage"].ap()[:, 1024 * slot:1024 * (slot + 1)]
        xsb = bf.buf(f"xstage{slot}")
        sc = sb["scr"].ap()[:, 1024 * slot:1024 * (slot + 1)]
        scb = bf.buf(f"scrx{slot}")
        p.dma("sy", lambda h, o=xs, i=x_in[128 * t:128 * (t + 1), :]: h.dma_start(out=o, in_=i),
              reads=[x_in_buf], writes=[xsb])
        at, atb = ctx.stq(slot)
        p.dma("sy", lambda h, o=at, i=dr["attn_dram"][128 * t:128 * (t + 1), :]: h.dma_start(out=o, in_=i),
              reads=[bf.buf("attn_dram")], writes=[atb])
        p.op("v", lambda h, o=sc, a=xs, b=at: h.tensor_tensor(o, a, b, ALU.add),
             reads=[atb, xsb], writes=[scb])
        cur, curb, nxt, nxtb = sc, scb, xs, xsb
        if li == 0:
            xn, xnb2 = ctx.stq(2 + slot)
            p.dma("sy", lambda h, o=xn, i=dr["xn_dram"][128 * t:128 * (t + 1), :]: h.dma_start(out=o, in_=i),
                  reads=[bf.buf("xn_dram")], writes=[xnb2])
            p.op("v", lambda h, o=nxt, a=cur, b=xn: h.tensor_tensor(o, a, b, ALU.add),
                 reads=[xnb2, curb], writes=[nxtb])
            cur, curb, nxt, nxtb = nxt, nxtb, cur, curb
            p.wait_tokens("sy", [("sy", p.cnt["sy"])])
            p.dma("sy", lambda h, o=xn, i=dr["mixed_dram"][:, 128 * t:128 * (t + 1)]: h.dma_start_transpose(o, i),
                  reads=[bf.buf("mixed_dram")], writes=[xnb2])
            p.wait_tokens("sy", [("sy", p.cnt["sy"])])
            p.op("v", lambda h, o=nxt, a=cur, b=xn: h.tensor_tensor(o, a, b, ALU.add),
                 reads=[xnb2, curb], writes=[nxtb])
            cur, curb, nxt, nxtb = nxt, nxtb, cur, curb
        else:
            ff, ffb = ctx.stq(2 + slot)
            for third in range(3):
                p.dma("sy", lambda h, o=ff, i=dr["ffn_dram"][third, 128 * t:128 * (t + 1), :]: h.dma_start(out=o, in_=i),
                      reads=[bf.buf("ffn_dram")], writes=[ffb])
                p.op("v", lambda h, o=nxt, a=cur, b=ff: h.tensor_tensor(o, a, b, ALU.add),
                     reads=[ffb, curb], writes=[nxtb])
                cur, curb, nxt, nxtb = nxt, nxtb, cur, curb
        p.op("v", lambda h, a=cur: h.tensor_copy(sb["statc"].ap()[:, 15:16], a[0:128, 0:1]),
             reads=[curb], writes=[bf.buf("statc")])
        p.dma("sy", lambda h, o=x_out[128 * t:128 * (t + 1), :], i=cur: h.dma_start(out=o, in_=i),
              reads=[curb], writes=[x_out_buf])



def emit_expm(ctx):
    p, sb, dr, bf = ctx.plan, ctx.sb, ctx.dram, ctx.bufs
    w1b = bf.buf("W1")
    xa = lambda c: ctx.sb["A1"].ap()[:, 0 + 128 * c:0 + 128 * (c + 1)]
    xbq = lambda c: ctx.sb["A1"].ap()[:, 1024 + 128 * c:1024 + 128 * (c + 1)]
    x0 = lambda c: ctx.sb["A1"].ap()[:, 2048 + 128 * c:2048 + 128 * (c + 1)]
    xab, xbb, x0b = bf.buf("eXa"), bf.buf("eXb"), bf.buf("eX0")
    for c in range(8):
        p.dma("g", lambda h, o=x0(c), i=dr["eyecol"][128 * c:128 * (c + 1), :]: h.dma_start(out=o, in_=i),
              writes=[x0b])
    out_dram = dr["eslice"] if ctx.mode == "expm" else dr["eg_in"]
    outb = bf.buf("eslice" if ctx.mode == "expm" else "eg_in")
    for gi in range(3):
        src = chunked(dr["gensT"][gi])
        dst = sb["W1"].ap()[:, 0:8192].rearrange("p (c n) -> p c n", c=8)
        p.dma("g", lambda h, o=dst, i=src: h.dma_start(out=o, in_=i), writes=[w1b])
        for c in range(8):
            p.op("v", lambda h, o=xa(c), i=x0(c): h.tensor_copy(o, i), reads=[x0b], writes=[xab])
        cur, curb = xa, xab
        nxt, nxtb = xbq, xbb
        for kk in range(KTAYLOR, 0, -1):
            for m in range(8):
                i = m % 2
                bank, bb = ctx.bank(i), ctx.bankbuf(i)
                terms = []
                for k in range(8):
                    lhs = sb["W1"].ap()[:, 1024 * k + 128 * m:1024 * k + 128 * (m + 1)]
                    terms.append((bank[0:128, 0:128], lhs, cur(k), k == 0, k == 7))
                mm_group(p, terms, bb, [w1b, curb])
                p.op("v", lambda h, o=nxt(m), b=bank[0:128, 0:128], i0=x0(m), s=1.0 / kk:
                     h.scalar_tensor_tensor(o, b, s, i0, ALU.mult, ALU.add),
                     reads=[bb, x0b], writes=[nxtb])
            cur, curb, nxt, nxtb = nxt, nxtb, cur, curb
        for c in range(8):
            dst = out_dram[1024 * gi + 128 * c:1024 * gi + 128 * (c + 1), :]
            p.dma("sy", lambda h, o=dst, i=cur(c): h.dma_start(out=o, in_=i),
                  reads=[curb], writes=[outb])
    if ctx.mode == "fused":
        extra = getattr(ctx, "precast_done", None) or []
        tok = p.coll(lambda h: h.collective_compute(
            "AllGather", ALU.bypass, replica_groups=[list(range(8))],
            ins=[dr["eg_in"]], outs=[dr["eg_out"]]),
            reads=[bf.buf("eg_in")] + list(extra), writes=[bf.buf("eg_out")])
        ctx.expm_cc_tok = tok
    tok = outb.w
    for nm in ("A1_0", "A1_1"):
        b = bf.buf(nm)
        b.w = tok
        b.rs = []



def emit_layer(ctx, li, x_in, x_out, x_in_buf, x_out_buf):
    emit_ln(ctx, li, x_in, x_in_buf)
    emit_xnT(ctx)
    emit_qkv(ctx, li)
    emit_ag(ctx)
    if li == 0:
        emit_ffn0(ctx)
    else:
        emit_ffn1(ctx)
    emit_kv_reorder(ctx)
    emit_flash(ctx, li)
    emit_attn_proj(ctx, li)
    emit_xpass(ctx, li, x_in, x_out, x_in_buf, x_out_buf)


def build(mode):
    nc = bass.Bass("TRN2", target_bir_lowering=False)
    dram = declare_dram(nc, mode)
    with ExitStack() as stack:
        sb = alloc_sbuf(nc, stack, mode)
        ps = stack.enter_context(nc.psum_tensor("ps_all", [128, 4096], F32))
        sems = {}
        for s in ("t", "v", "a", "g", "sy", "cc"):
            sems[s] = stack.enter_context(nc.semaphore(f"sem_{s}"))
        block = stack.enter_context(nc.Block())
        plan = Plan()
        ctx = Ctx(nc=nc, plan=plan, sb=sb, ps=ps, dram=dram, bufs=Bufs(), mode=mode)
        load_const_sbuf(ctx)
        if mode == "expm":
            emit_expm(ctx)
        elif mode == "l0":
            emit_layer(ctx, 0, dram["x_in"], dram["x_out"],
                       ctx.bufs.buf("x_in_src"), ctx.bufs.buf("x_out_dst"))
        elif mode == "l1":
            emit_precast_l1(ctx)
            emit_layer(ctx, 1, dram["x_in"], dram["x_out"],
                       ctx.bufs.buf("x_in_src"), ctx.bufs.buf("x_out_dst"))
        elif mode == "fused":
            emit_precast_l1(ctx)
            emit_expm(ctx)
            emit_layer(ctx, 0, dram["x0"], dram["x_mid"],
                       ctx.bufs.buf("x_in_src"), ctx.bufs.buf("x_mid_buf"))
            emit_layer(ctx, 1, dram["x_mid"], dram["x_out"],
                       ctx.bufs.buf("x_mid_buf"), ctx.bufs.buf("x_out_dst"))
        plan.wait_tokens("sy", [("sy", plan.cnt["sy"]), ("g", plan.cnt["g"]),
                                ("v", plan.cnt["v"]), ("a", plan.cnt["a"]),
                                ("t", plan.cnt["t"])])
        block.tensor(lambda h: plan.replay("t", h, sems))
        block.vector(lambda h: plan.replay("v", h, sems))
        block.scalar(lambda h: plan.replay("a", h, sems))
        block.gpsimd(lambda h: plan.replay("g", h, sems))
        block.sync(lambda h: plan.replay("sy", h, sems))
    return nc



def kernel(**inputs):
    x_full = np.asarray(inputs["L0_attn_x"], dtype=np.float32)
    w = prep_weights(inputs)
    consts = prep_consts()
    lng = _f32(np.asarray(inputs["L0_ffn_ln_gamma"]).reshape(8, 128).T)
    lnb = _f32(np.asarray(inputs["L0_ffn_ln_beta"]).reshape(8, 128).T)

    cores = list(range(NCORE))
    per_core = [prep_core(c, x_full) for c in cores]

    nc = build("fused")
    in_maps = []
    for c in cores:
        m = {k: w[k] for k in L0_W_KEYS + L1_W_KEYS}
        m.update({"ident_f": consts["ident_f"], "iota_row": consts["iota_row"],
                  "alibi_cols": consts["alibi_cols"], "thr": per_core[c]["thr"],
                  "brow0": per_core[c]["brow0"], "x0": per_core[c]["x0"],
                  "gensT": w["gensT"], "eyecol": per_core[c]["eyecol"],
                  "ln_gamma": lng, "ln_beta": lnb})
        in_maps.append(m)
    res = run_bass_kernel_spmd(nc, in_maps, core_ids=cores)

    out = np.zeros((B_, T_, D_), dtype=np.float32)
    for c in cores:
        toks = own_token_abs(c % 2)
        out[c // 2][toks] = np.asarray(res.results[c]["x_out"], dtype=np.float32)
    return out


# version 08
# speedup vs baseline: 1.4412x; 1.0094x over previous
import math
import dataclasses
from contextlib import ExitStack

import numpy as np
import ml_dtypes

import concourse.bass as bass
import concourse.mybir as mybir
from concourse.bass_utils import run_bass_kernel_spmd

F32 = mybir.dt.float32
BF16 = mybir.dt.bfloat16
I32 = mybir.dt.int32
U8 = mybir.dt.uint8
AF = mybir.ActivationFunctionType
ALU = mybir.AluOpType
AX = mybir.AxisListType

BFNP = ml_dtypes.bfloat16

B_, T_, D_ = 4, 4096, 1024
NCORE = 8
TOK = 2048
NH, HD = 16, 64
EXP_N, EHID = 8, 2048
L1HID = 3072
EPS = 1e-5
S8 = 1.0 / 127.0
KTAYLOR = 12
BCONST = 30.0
SLOTS = [8, 16, 24, 32]

def _slopes():
    start = 2.0 ** (-(2.0 ** -(math.log2(NH) - 3)))
    return [start * start ** i for i in range(NH)]

SLOPES = _slopes()

QB_ABS = {0: [512, 1536, 2560, 3584],
          1: [0, 1024, 2048, 3072]}

def own_token_abs(half):
    out = []
    for qa in QB_ABS[half]:
        out.extend(range(qa, qa + 512))
    return out


class Buf:
    __slots__ = ("w", "rs", "name")
    def __init__(self, name):
        self.name = name
        self.w = None
        self.rs = []

class Plan:
    ENGS = ("t", "v", "a", "g", "sy")

    def __init__(self):
        self.q = {e: [] for e in self.ENGS}
        self.cnt = {e: 0 for e in ("t", "v", "a", "g", "sy", "cc")}
        self.hw = {e: {} for e in self.ENGS}

    def _emit(self, eng, fn, reads, writes, sem, inc):
        waits = {}
        deps = []
        for b in reads:
            if b.w is not None:
                deps.append(b.w)
        for b in writes:
            if b.w is not None:
                deps.append(b.w)
            deps.extend(b.rs)
        for (s, v) in deps:
            if s == "t" and eng == "t":
                continue
            if waits.get(s, -1) < v:
                waits[s] = v
        hw = self.hw[eng]
        wl = []
        for s, v in sorted(waits.items()):
            if hw.get(s, -1) >= v:
                continue
            hw[s] = v
            wl.append((s, v))
        self.cnt[sem] += inc
        tok = (sem, self.cnt[sem])
        self.q[eng].append((wl, fn, (sem, inc)))
        for b in reads:
            b.rs.append(tok)
        for b in writes:
            b.w = tok
            b.rs = []
        return tok

    def op(self, eng, fn, reads=(), writes=()):
        return self._emit(eng, fn, list(reads), list(writes), eng, 1)

    def dma(self, queue, fn, reads=(), writes=()):
        return self._emit(queue, fn, list(reads), list(writes), queue, 16)

    def coll(self, fn, reads=(), writes=()):
        return self._emit("g", fn, list(reads), list(writes), "cc", 1)

    def wait_tokens(self, eng, toks):
        wl = []
        hw = self.hw[eng]
        for (s, v) in toks:
            if hw.get(s, -1) < v:
                hw[s] = v
                wl.append((s, v))
        if wl:
            self.q[eng].append((wl, None, None))

    def replay(self, eng, handle, sems):
        for waits, fn, inc in self.q[eng]:
            for (s, v) in waits:
                handle.wait_ge(sems[s], v)
            if fn is not None:
                ins = fn(handle)
                if inc is not None:
                    ins.then_inc(sems[inc[0]], inc[1])


def mm_group(plan, terms, bank_buf, read_bufs):
    def fn(h, terms=terms):
        last = None
        for (o, l, r, st, sp) in terms:
            last = h.matmul(o, l, r, start=st, stop=sp)
        return last
    return plan.op("t", fn, reads=list(read_bufs), writes=[bank_buf])



def _bf(x):
    return np.ascontiguousarray(np.asarray(x).astype(BFNP))

def _f32(x):
    return np.ascontiguousarray(np.asarray(x), dtype=np.float32)

def prep_consts():
    c = {}
    c["ident_f"] = _f32(np.eye(128))
    c["iota_row"] = _f32(np.broadcast_to(np.arange(512, dtype=np.float32), (128, 512)))
    al = np.zeros((128, NH * 32), dtype=np.float32)
    p = np.arange(128, dtype=np.float32)
    for h in range(NH):
        for cc in range(32):
            al[:, h * 32 + cc] = SLOPES[h] * (128 * cc + p)
    c["alibi_cols"] = al
    return c

def prep_core(core, x_full):
    half = core % 2
    seq = core // 2
    toks = own_token_abs(half)
    d = {}
    d["x0"] = _f32(x_full[seq][toks])
    qabs = np.array(toks, dtype=np.float32)
    brow0 = np.zeros((NH, TOK), dtype=np.float32)
    for h in range(NH):
        brow0[h] = -8.0 * (BCONST + SLOPES[h] * qabs)
    d["brow0"] = _bf(brow0)
    thr = np.zeros((128, 32), dtype=np.float32)
    pp = np.arange(128, dtype=np.float32)
    for j in range(4):
        qa = QB_ABS[half][j]
        for i in range(8):
            cch = SLOTS[j] - 8 + i
            thr[:, 8 * j + i] = pp + 128.0 * cch - qa
    d["thr"] = thr
    ey = np.zeros((1024, 128), dtype=np.float32)
    ey[128 * core:128 * (core + 1), :] = np.eye(128)
    d["eyecol"] = ey
    return d

def prep_weights(inp):
    w = {}
    w["w_attn_T"] = _f32(np.asarray(inp["L0_attn_w_attn"]).T)
    w["w_proj_T"] = _f32(np.asarray(inp["L0_attn_w_proj"]).T)
    w["abel_w"] = _f32(inp["L0_ffn_abel_w"])
    w["abel_b"] = _f32(np.asarray(inp["L0_ffn_abel_b"]).reshape(8, 128).T)
    w["gate_w"] = _f32(inp["L0_ffn_moe_gate_w"])
    w["gate_b"] = _f32(np.asarray(inp["L0_ffn_moe_gate_b"]).reshape(8, 1))
    w["ew1"] = _f32(inp["L0_ffn_ew1"])
    eb1 = np.asarray(inp["L0_ffn_eb1"])
    w["eb1c"] = _f32(np.concatenate([np.asarray(eb1[e]).reshape(16, 128).T for e in range(8)], axis=1))
    w["ew2"] = _f32(inp["L0_ffn_ew2"])
    w["eb2"] = _f32(inp["L0_ffn_eb2"])
    w["gensT"] = _f32(np.transpose(np.asarray(inp["L0_ffn_gens"]), (0, 2, 1)))
    w["gc_w"] = _f32(np.asarray(inp["L0_ffn_gc_w"]).reshape(16, 128).T)
    w["gc_b"] = _f32(np.asarray(inp["L0_ffn_gc_b"]).reshape(1, 1))
    w["wq"] = _f32(inp["L1_attn_Wq"])
    w["wk"] = _f32(inp["L1_attn_Wk"])
    w["wv"] = _f32(inp["L1_attn_Wv"])
    w["wo"] = _f32(inp["L1_attn_Wo"])
    w["wg_T"] = np.ascontiguousarray(np.asarray(inp["L1_ffn_w_gate"]).T)
    w["wu_T"] = np.ascontiguousarray(np.asarray(inp["L1_ffn_w_up"]).T)
    w["wd_T"] = np.ascontiguousarray(np.asarray(inp["L1_ffn_w_down"]).T)
    return w

L0_W_KEYS = ["w_attn_T", "w_proj_T", "abel_w", "abel_b", "gate_w", "gate_b",
             "ew1", "eb1c", "ew2", "eb2", "gc_w", "gc_b"]
L1_W_KEYS = ["wq", "wk", "wv", "wo", "wg_T", "wu_T", "wd_T"]



class Bufs(dict):
    def buf(self, name):
        if name not in self:
            self[name] = Buf(name)
        return self[name]

@dataclasses.dataclass
class Ctx:
    nc: object
    plan: Plan
    sb: dict
    ps: object
    dram: dict
    bufs: Bufs
    mode: str

    def bank(self, i, rows=128):
        return self.ps.ap()[0:rows, 512 * i:512 * (i + 1)]

    def bankbuf(self, i):
        return self.bufs.buf(f"ps{i}")

    def stq(self, i):
        return (self.sb["stage"].ap()[:, 1024 * i:1024 * (i + 1)],
                self.bufs.buf(f"stage{i}"))

    def stq5(self, i):
        return (self.sb["stage"].ap()[:, 1024 * i:1024 * i + 512],
                self.bufs.buf(f"stage{i}"))


def flat_ap(ap2d, offset, dims):
    return dataclasses.replace(ap2d, offset=offset, ap=[list(x) for x in dims])


def chunked(src2d, p=128):
    return src2d.rearrange("(c p) n -> p c n", p=p)


DEBUG_OUTPUTS = ()

def declare_dram(nc, mode):
    d = {}
    def ein(name, shape, dt):
        d[name] = nc.dram_tensor(name, list(shape), dt, kind="ExternalInput").ap()
    def eout(name, shape, dt):
        d[name] = nc.dram_tensor(name, list(shape), dt, kind="ExternalOutput").ap()
    def internal(name, shape, dt):
        if name in DEBUG_OUTPUTS:
            eout(name, shape, dt)
        else:
            d[name] = nc.dram_tensor(name, list(shape), dt).ap()

    ein("ident_f", (128, 128), F32)
    ein("iota_row", (128, 512), F32)
    ein("thr", (128, 32), F32)

    if mode in ("expm", "fused"):
        ein("gensT", (3, 1024, 1024), F32)
        ein("eyecol", (1024, 128), F32)
    if mode == "expm":
        eout("eslice", (3 * 1024, 128), BF16)
    if mode in ("l0", "fused"):
        ein("alibi_cols", (128, NH * 32), F32)
        ein("brow0", (NH, TOK), BF16)
        ein("ln_gamma", (128, 8), F32)
        ein("ln_beta", (128, 8), F32)
        for k, shape, dt in [
            ("w_attn_T", (1024, 3072), F32), ("w_proj_T", (1024, 1024), F32),
            ("abel_w", (1024, 1024), F32), ("abel_b", (128, 8), F32),
            ("gate_w", (1024, 8), F32), ("gate_b", (8, 1), F32),
            ("ew1", (8, 1024, 2048), F32), ("eb1c", (128, 128), F32),
            ("ew2", (8, 2048, 1024), F32), ("eb2", (8, 1024), F32),
            ("gc_w", (128, 16), F32), ("gc_b", (1, 1), F32),
        ]:
            ein(k, shape, dt)
    if mode == "l0":
        ein("E", (3, 1024, 1024), BF16)
    if mode in ("l1", "fused"):
        for k, shape, dt in [
            ("wq", (1024, 1024), F32), ("wk", (1024, 1024), F32),
            ("wv", (1024, 1024), F32), ("wo", (1024, 1024), F32),
            ("wg_T", (1024, 3072), I32), ("wu_T", (1024, 3072), I32),
            ("wd_T", (3072, 1024), I32),
        ]:
            ein(k, shape, dt)

    if mode in ("l0", "l1"):
        ein("x_in", (TOK, 1024), F32)
        eout("x_out", (TOK, 1024), F32)
    if mode == "fused":
        ein("x0", (TOK, 1024), F32)
        eout("x_out", (TOK, 1024), F32)
        internal("x_mid", (TOK, 1024), F32)
        internal("eg_in", (3 * 1024, 128), BF16)
        internal("eg_out", (NCORE * 3 * 1024, 128), BF16)

    if mode in ("l0", "l1", "fused"):
        internal("xn_dram", (TOK, 1024), BF16)
        internal("q_dram", (1024, TOK), BF16)
        internal("ag_in", (2048, 2048), BF16)
        internal("ag_out", (4096, 2048), BF16)
        internal("kabs", (1024, 4096), BF16)
        internal("vabs", (4096, 1024), BF16)
        internal("attn_dram", (TOK, 1024), BF16)
    if mode in ("l0", "fused"):
        internal("xn2p_dram", (TOK, 1024), BF16)
        internal("mixed_dram", (1024, TOK), BF16)
        internal("moe_tok", (1024, 1024), BF16)
    if mode in ("l1", "fused"):
        internal("ffn_dram", (3, TOK, 1024), BF16)
        internal("wg_bf", (1024, 3072), BF16)
        internal("wu_bf", (1024, 3072), BF16)
        internal("wd_bf", (3072, 1024), BF16)
    return d


def alloc_sbuf(nc, stack, mode):
    en = stack.enter_context
    sb = {}
    def t(name, shape, dt):
        sb[name] = en(nc.sbuf_tensor("sb_" + name, list(shape), dt))
    t("W1", (128, 16384), BF16)
    t("W2", (128, 16384), BF16)
    t("A1", (128, 16384), BF16)
    t("A3", (128, 16384), BF16)
    t("kslot", (65, 4096), BF16)
    t("vslot", (128, 32 * 65), BF16)
    t("qslot", (65, 2 * 512), BF16)
    t("xstage", (128, 2 * 1024), F32)
    t("stage", (128, 4 * 1024), BF16)
    t("scr", (128, 2048), F32)
    t("pbuf", (128, 4 * 512), BF16)
    t("rb", (128, 1024), F32)
    t("maskq", (128, 8 * 512), mybir.dt.uint16)
    t("iota", (128, 512), F32)
    t("thrT", (128, 32), F32)
    t("identf", (128, 128), F32)
    t("zeros", (128, 512), BF16)
    t("onesb", (1, 512), BF16)
    t("onesf", (1, 128), F32)
    t("browst", (1, 512), BF16)
    t("ebrow", (1, 1024), BF16)
    t("statc", (128, 16), F32)
    t("epsc", (128, 1), F32)
    t("lgt", (128, 8), F32)
    t("negbig", (128, 8), F32)
    t("e1t", (128, 8), F32)
    t("lmt", (128, 8), F32)
    t("ge2f", (128, 8), F32)
    t("eqm", (128, 8), U8)
    t("wall", (128, 64), F32)
    t("reciprow", (1, 512), F32)
    t("lnc", (128, 16), F32)
    if mode in ("l0", "fused"):
        t("alibi", (128, NH * 32), F32)
        t("eb1c", (128, 128), F32)
        t("gatew", (128, 64), BF16)
        t("gatewf", (128, 64), F32)
        t("gateb", (8, 1), F32)
        t("gcw", (128, 16), BF16)
        t("gcb", (1, 1), F32)
        t("abelb", (128, 8), F32)
        t("grow", (1, 1024), F32)
    return sb



def load_const_sbuf(ctx):
    p, sb, dr, bf = ctx.plan, ctx.sb, ctx.dram, ctx.bufs
    def ld(dst_name, src_name, queue="sy"):
        dst = sb[dst_name]
        src = dr[src_name]
        p.dma(queue, lambda h, d=dst, s=src: h.dma_start(out=d.ap()[:], in_=s),
              writes=[bf.buf(dst_name)])
    ld("iota", "iota_row")
    ld("thrT", "thr")
    ld("identf", "ident_f")
    if ctx.mode in ("l0", "fused"):
        ld("alibi", "alibi_cols")
        ld("eb1c", "eb1c")
        ld("gateb", "gate_b")
        ld("gcb", "gc_b")
        ld("abelb", "abel_b")
        gw = dr["gate_w"].rearrange("(c p) e -> p c e", p=128)
        p.dma("g", lambda h, gw=gw: h.dma_start(
            out=ctx.sb["gatew"].ap().rearrange("p (c e) -> p c e", c=8), in_=gw),
            writes=[bf.buf("gatew")])
        p.dma("sy", lambda h, gw=gw: h.dma_start(
            out=ctx.sb["gatewf"].ap().rearrange("p (c e) -> p c e", c=8), in_=gw),
            writes=[bf.buf("gatewf")])
        p.dma("g", lambda h: h.dma_start(out=ctx.sb["gcw"].ap()[:], in_=ctx.dram["gc_w"]),
              writes=[bf.buf("gcw")])
        p.dma("sy", lambda h: h.dma_start(out=sb["lnc"].ap()[:, 0:8], in_=dr["ln_gamma"]),
              writes=[bf.buf("lnc")])
        p.dma("sy", lambda h: h.dma_start(out=sb["lnc"].ap()[:, 8:16], in_=dr["ln_beta"]),
              writes=[bf.buf("lnc")])
    p.op("v", lambda h: h.memset(sb["zeros"].ap()[:], 0.0), writes=[bf.buf("zeros")])
    p.op("v", lambda h: h.memset(sb["epsc"].ap()[:], EPS), writes=[bf.buf("epsc")])
    p.op("v", lambda h: h.memset(sb["onesb"].ap()[:], 1.0), writes=[bf.buf("onesb")])
    p.op("v", lambda h: h.memset(sb["onesf"].ap()[:], 1.0), writes=[bf.buf("onesf")])
    p.op("v", lambda h: h.memset(sb["negbig"].ap()[:], -1e30), writes=[bf.buf("negbig")])



def emit_ln(ctx, li, x_in, x_in_buf):
    p, sb, dr, bf = ctx.plan, ctx.sb, ctx.dram, ctx.bufs
    st = sb["statc"].ap()
    stb = bf.buf("statc")
    scrb = bf.buf("scr")
    xnb = bf.buf("xn_dram")
    xn2b = bf.buf("xn2p_dram")
    for t in range(16):
        slot = t % 2
        xs = sb["xstage"].ap()[:, 1024 * slot:1024 * (slot + 1)]
        xsb = bf.buf(f"xstage{slot}")
        p.dma("sy", lambda h, o=xs, i=x_in[128 * t:128 * (t + 1), :]: h.dma_start(out=o, in_=i),
              reads=[x_in_buf], writes=[xsb])
        p.op("v", lambda h, o=st[:, 0:1], i=xs: h.reduce_sum(o, i, axis=AX.X),
             reads=[xsb], writes=[stb])
        p.op("a", lambda h, o=sb["scr"].ap()[:, 0:1024], i=xs, a=st[:, 1:2]:
             h.activation(o, i, AF.Square, accum_out=a),
             reads=[xsb], writes=[scrb, stb])
        p.op("v", lambda h: h.tensor_scalar(st[:, 2:3], st[:, 0:1], 1.0 / 1024, None, ALU.mult),
             reads=[stb], writes=[stb])
        p.op("v", lambda h: h.tensor_scalar(st[:, 3:4], st[:, 1:2], 1.0 / 1024, None, ALU.mult),
             reads=[stb], writes=[stb])
        p.op("v", lambda h: h.tensor_tensor(st[:, 4:5], st[:, 2:3], st[:, 2:3], ALU.mult),
             reads=[stb], writes=[stb])
        p.op("v", lambda h: h.tensor_tensor(st[:, 5:6], st[:, 3:4], st[:, 4:5], ALU.subtract),
             reads=[stb], writes=[stb])
        p.op("a", lambda h: h.activation(st[:, 6:7], st[:, 5:6], AF.Sqrt, bias=sb["epsc"].ap()),
             reads=[stb, bf.buf("epsc")], writes=[stb])
        p.op("v", lambda h: h.reciprocal(st[:, 7:8], st[:, 6:7]),
             reads=[stb], writes=[stb])
        p.op("v", lambda h: h.tensor_scalar(st[:, 8:9], st[:, 2:3], st[:, 7:8], -1.0, ALU.mult, ALU.mult),
             reads=[stb], writes=[stb])
        xnt, xntb = ctx.stq(slot)
        p.op("a", lambda h, o=xnt, i=xs: h.activation(o, i, AF.Identity, bias=st[:, 8:9], scale=st[:, 7:8]),
             reads=[xsb, stb], writes=[xntb])
        p.dma("sy", lambda h, o=dr["xn_dram"][128 * t:128 * (t + 1), :], i=xnt: h.dma_start(out=o, in_=i),
              reads=[xntb], writes=[xnb])
        if li == 0:
            p.op("v", lambda h: h.tensor_tensor(st[:, 9:10], st[:, 5:6], st[:, 7:8], ALU.mult),
                 reads=[stb], writes=[stb])
            p.op("v", lambda h: h.tensor_tensor(st[:, 10:11], st[:, 9:10], st[:, 7:8], ALU.mult),
                 reads=[stb], writes=[stb])
            p.op("a", lambda h: h.activation(st[:, 11:12], st[:, 10:11], AF.Sqrt, bias=sb["epsc"].ap()),
                 reads=[stb, bf.buf("epsc")], writes=[stb])
            p.op("v", lambda h: h.reciprocal(st[:, 12:13], st[:, 11:12]),
                 reads=[stb], writes=[stb])
            p.op("v", lambda h: h.tensor_tensor(st[:, 13:14], st[:, 7:8], st[:, 12:13], ALU.mult),
                 reads=[stb], writes=[stb])
            p.op("v", lambda h: h.tensor_scalar(st[:, 14:15], st[:, 2:3], st[:, 13:14], -1.0, ALU.mult, ALU.mult),
                 reads=[stb], writes=[stb])
            xn2t, xn2tb = ctx.stq(2 + slot)
            p.op("a", lambda h, o=xn2t, i=xs: h.activation(o, i, AF.Identity, bias=st[:, 14:15], scale=st[:, 13:14]),
                 reads=[xsb, stb], writes=[xn2tb])
            p.dma("sy", lambda h, o=dr["xn2p_dram"][128 * t:128 * (t + 1), :], i=xn2t: h.dma_start(out=o, in_=i),
                  reads=[xn2tb], writes=[xn2b])


def emit_xnT(ctx):
    p, sb, dr, bf = ctx.plan, ctx.sb, ctx.dram, ctx.bufs
    for c in range(8):
        dst = sb["A1"].ap()[:, 2048 * c:2048 * (c + 1)]
        src = dr["xn_dram"][:, 128 * c:128 * (c + 1)]
        p.dma("sy", lambda h, o=dst, i=src: h.dma_start_transpose(o, i),
              reads=[bf.buf("xn_dram")], writes=[bf.buf(f"A1_{c // 4}")])



def emit_qkv(ctx, li):
    p, sb, dr, bf = ctx.plan, ctx.sb, ctx.dram, ctx.bufs
    a1 = sb["A1"].ap()
    a1r = [bf.buf("A1_0"), bf.buf("A1_1")]
    w1a, w1b2, w2a, w2b2 = bf.buf("W1a"), bf.buf("W1b"), bf.buf("W2a"), bf.buf("W2b")
    if li == 0:
        wsrc = dr["w_attn_T"]
        qs, ks, vs = (chunked(wsrc[:, 0:1024]), chunked(wsrc[:, 1024:2048]),
                      chunked(wsrc[:, 2048:3072]))
    else:
        qs, ks, vs = chunked(dr["wq"]), chunked(dr["wk"]), chunked(dr["wv"])
    def wload(c0, c1, src, arena, buf):
        dst = sb[arena].ap()[:, c0:c1].rearrange("p (c n) -> p c n", c=8)
        p.dma("g", lambda h, o=dst, i=src: h.dma_start(out=o, in_=i), writes=[buf])
    wload(0, 8192, qs, "W1", w1a)
    wload(8192, 16384, ks, "W1", w1b2)
    wload(8192, 16384, vs, "W2", w2b2)

    def proj_qk(which, wc0):
        for m in range(8):
            for n in range(4):
                i = (m * 4 + n) % 2
                bank, bb = ctx.bank(i), ctx.bankbuf(i)
                terms = []
                for k in range(8):
                    lhs = sb["W1"].ap()[:, wc0 + 1024 * k + 128 * m: wc0 + 1024 * k + 128 * (m + 1)]
                    rhs = a1[:, 2048 * k + 512 * n: 2048 * k + 512 * (n + 1)]
                    terms.append((bank, lhs, rhs, k == 0, k == 7))
                mm_group(p, terms, bb, [w1a if wc0 == 0 else w1b2] + a1r)
                ss, ssb = ctx.stq5(i)
                p.op("v", lambda h, o=ss, b=bank: h.tensor_copy(o, b), reads=[bb], writes=[ssb])
                if which == "q":
                    dst = dr["q_dram"][128 * m:128 * (m + 1), 512 * n:512 * (n + 1)]
                    p.dma("sy", lambda h, o=dst, i2=ss: h.dma_start(out=o, in_=i2),
                          reads=[ssb], writes=[bf.buf("q_dram")])
                else:
                    dst = dr["ag_in"][128 * m:128 * (m + 1), 512 * n:512 * (n + 1)]
                    p.dma("sy", lambda h, o=dst, i2=ss: h.dma_start(out=o, in_=i2),
                          reads=[ssb], writes=[bf.buf("ag_in")])

    proj_qk("k", 8192)
    ctx.q_proj_pending = proj_qk
    for tq in range(16):
        for n in range(2):
            i = (tq * 2 + n) % 2
            bank, bb = ctx.bank(i), ctx.bankbuf(i)
            terms = []
            for k in range(8):
                lhs = a1[:, 2048 * k + 128 * tq: 2048 * k + 128 * (tq + 1)]
                rhs = sb["W2"].ap()[:, 8192 + 1024 * k + 512 * n: 8192 + 1024 * k + 512 * (n + 1)]
                terms.append((bank, lhs, rhs, k == 0, k == 7))
            mm_group(p, terms, bb, [w2b2] + a1r)
            ss, ssb = ctx.stq5(i)
            p.op("v", lambda h, o=ss, b=bank: h.tensor_copy(o, b), reads=[bb], writes=[ssb])
            off = 1024 * 2048 + (128 * tq) * 1024 + 512 * n
            dst = flat_ap(dr["ag_in"], off, [[1024, 128], [1, 512]])
            p.dma("sy", lambda h, o=dst, i2=ss: h.dma_start(out=o, in_=i2),
                  reads=[ssb], writes=[bf.buf("ag_in")])


AG_SERIAL = False
FFN1_WAIT_AG = True
FLASH_PIPE = True
FLASH_YPP = False

def emit_precast_l1(ctx):
    p, dr, bf = ctx.plan, ctx.dram, ctx.bufs
    for name in ("wg_bf", "wu_bf", "wd_bf"):
        srcn = {"wg_bf": "wg_T", "wu_bf": "wu_T", "wd_bf": "wd_T"}[name]
        p.dma("g", lambda h, o=dr[name], i=dr[srcn]: h.dma_start(out=o, in_=i),
              writes=[bf.buf(name)])
    ctx.precast_done = [bf.buf("wg_bf"), bf.buf("wu_bf"), bf.buf("wd_bf")]


def emit_ag(ctx):
    p, dr, bf = ctx.plan, ctx.dram, ctx.bufs
    extra = getattr(ctx, "precast_done", None) or []
    tok = p.coll(lambda h: h.collective_compute(
        "AllGather", ALU.bypass,
        replica_groups=[[0, 1], [2, 3], [4, 5], [6, 7]],
        ins=[dr["ag_in"]], outs=[dr["ag_out"]]),
        reads=[bf.buf("ag_in")] + list(extra), writes=[bf.buf("ag_out")])
    ctx.last_ag_tok = tok
    if AG_SERIAL:
        for nm in ("W1", "W2", "A1_0", "A1_1", "A3a", "A3b", "xstage0", "xstage1",
                   "pbuf0", "pbuf1", "scr", "statc", "ffn_dram", "stage0", "stage1",
                   "stage2", "stage3"):
            b = bf.buf(nm)
            b.rs.append(tok)


def emit_kv_reorder(ctx):
    p, dr, bf = ctx.plan, ctx.dram, ctx.bufs
    blocks0 = [a // 128 for a in own_token_abs(0)[::128]]
    blocks1 = [a // 128 for a in own_token_abs(1)[::128]]
    src_of_abs = {}
    for loc, ab in enumerate(blocks0):
        src_of_abs[ab] = (0, loc)
    for loc, ab in enumerate(blocks1):
        src_of_abs[ab] = (1, loc)
    agb, kb, vb = bf.buf("ag_out"), bf.buf("kabs"), bf.buf("vabs")
    for ab in range(32):
        rank, loc = src_of_abs[ab]
        src = dr["ag_out"][2048 * rank: 2048 * rank + 1024, 128 * loc:128 * (loc + 1)]
        dst = dr["kabs"][:, 128 * ab:128 * (ab + 1)]
        p.dma("sy", lambda h, o=dst, i=src: h.dma_start(out=o, in_=i), reads=[agb], writes=[kb])
        off = (2048 * rank + 1024) * 2048 + (128 * loc) * 1024
        src = flat_ap(dr["ag_out"], off, [[1024, 128], [1, 1024]])
        dst = dr["vabs"][128 * ab:128 * (ab + 1), :]
        p.dma("sy", lambda h, o=dst, i=src: h.dma_start(out=o, in_=i), reads=[agb], writes=[vb])



def emit_flash(ctx, li):
    p, sb, dr, bf = ctx.plan, ctx.sb, ctx.dram, ctx.bufs
    a1w = [bf.buf("A1_0"), bf.buf("A1_1")]
    kbuf, vbuf, qbuf = bf.buf("kslot"), bf.buf("vslot"), bf.buf("qslot")
    mqb = bf.buf("maskq")
    iob, thb = bf.buf("iota"), bf.buf("thrT")
    kabs_b, vabs_b, qdr_b = bf.buf("kabs"), bf.buf("vabs"), bf.buf("q_dram")
    alb = bf.buf("alibi") if li == 0 else None
    zb = bf.buf("zeros")
    ofb = bf.buf("onesf")
    vsl = sb["vslot"].ap().rearrange("p (c w) -> p c w", w=65)

    p.op("v", lambda h: h.memset(sb["kslot"].ap()[64:65, :], 1.0), writes=[kbuf])
    if li == 1:
        p.op("v", lambda h: h.memset(sb["qslot"].ap()[64:65, :], -8.0 * BCONST), writes=[qbuf])

    for j in range(4):
        s = SLOTS[j]
        for i8 in range(8):
            dst = sb["maskq"].ap()[:, 512 * i8:512 * (i8 + 1)]
            thr = sb["thrT"].ap()[:, 8 * j + i8: 8 * j + i8 + 1]
            p.op("v", lambda h, o=dst, t=thr: h.tensor_scalar(o, sb["iota"].ap()[:], t, None, ALU.is_ge),
                 reads=[iob, thb], writes=[mqb])
        for hd in range(NH):
            if li == 0:
                qa_min = QB_ABS[1][j]
                cs = [c for c in range(s)
                      if c >= s - 8 or SLOPES[hd] * ((128 * c + 127) - qa_min) >= -93.0]
            else:
                cs = list(range(s))
            p.dma("sy", lambda h, o=sb["kslot"].ap()[0:64, 0:128 * s],
                  i=dr["kabs"][64 * hd:64 * (hd + 1), 0:128 * s]: h.dma_start(out=o, in_=i),
                  reads=[kabs_b], writes=[kbuf])
            src = flat_ap(dr["vabs"], 64 * hd, [[1024, 128], [128 * 1024, s], [1, 64]])
            p.dma("sy", lambda h, o=vsl[:, 0:s, 0:64], i=src: h.dma_start(out=o, in_=i),
                  reads=[vabs_b], writes=[vbuf])
            p.op("v", lambda h, o=vsl[:, 0:s, 64:65]: h.memset(o, 1.0), writes=[vbuf])
            qs0 = 512 * (hd % 2)
            qsl = sb["qslot"].ap()[0:65, qs0:qs0 + 512]
            p.dma("sy", lambda h, o=sb["qslot"].ap()[0:64, qs0:qs0 + 512],
                  i=dr["q_dram"][64 * hd:64 * (hd + 1), 512 * j:512 * (j + 1)]: h.dma_start(out=o, in_=i),
                  reads=[qdr_b], writes=[qbuf])
            if li == 0:
                p.dma("sy", lambda h, o=sb["qslot"].ap()[64:65, qs0:qs0 + 512],
                      i=dr["brow0"][hd:hd + 1, 512 * j:512 * (j + 1)]: h.dma_start(out=o, in_=i),
                      reads=[bf.buf("brow0_src")], writes=[qbuf])

            yb = 2 if (hd % 2 == 0 or not FLASH_YPP) else 4
            bcb = 3 if (hd % 2 == 0 or not FLASH_YPP) else 5
            ps_y, ps_y_buf = ctx.ps.ap()[0:65, 512 * yb:512 * (yb + 1)], ctx.bankbuf(yb)
            ps_bc, ps_bc_buf = ctx.ps.ap()[0:64, 512 * bcb:512 * (bcb + 1)], ctx.bankbuf(bcb)

            def s_mm(idx):
                c = cs[idx]
                i = idx % 2
                bank, bb = ctx.bank(i), ctx.bankbuf(i)
                p.op("t", lambda h, o=bank, l=sb["kslot"].ap()[0:65, 128 * c:128 * (c + 1)], r=qsl:
                     h.matmul(o, l, r, start=True, stop=True),
                     reads=[kbuf, qbuf], writes=[bb])

            s_mm(0)
            for idx, c in enumerate(cs):
                i = idx % 2
                bank, bb = ctx.bank(i), ctx.bankbuf(i)
                pb = sb["pbuf"].ap()[:, 512 * i:512 * (i + 1)]
                pbb = bf.buf(f"pbuf{i}")
                if li == 0:
                    bias = sb["alibi"].ap()[:, hd * 32 + c: hd * 32 + c + 1]
                    p.op("a", lambda h, o=pb, b=bank, bi=bias: h.activation(o, b, AF.Exp, bias=bi, scale=0.125),
                         reads=[bb, alb], writes=[pbb])
                else:
                    p.op("a", lambda h, o=pb, b=bank: h.activation(o, b, AF.Exp, scale=0.125),
                         reads=[bb], writes=[pbb])
                if FLASH_PIPE and idx + 1 < len(cs):
                    s_mm(idx + 1)
                if c >= s - 8:
                    msk = sb["maskq"].ap()[:, 512 * (c - (s - 8)):512 * (c - (s - 8) + 1)]
                    pm = sb["pbuf"].ap()[:, 512 * (2 + i):512 * (3 + i)]
                    pmb = bf.buf(f"pbuf{2 + i}")
                    p.op("v", lambda h, o=pm, m=msk, src=pb: h.select(o, m, src, sb["zeros"].ap()[:]),
                         reads=[mqb, zb, pbb], writes=[pmb])
                    pv_src, pv_srcb = pm, pmb
                else:
                    pv_src, pv_srcb = pb, pbb
                lhs = vsl[:, c, 0:65]
                p.op("t", lambda h, o=ps_y, l=lhs, r=pv_src, st=(idx == 0), sp=(idx == len(cs) - 1):
                     h.matmul(o, l, r, start=st, stop=sp),
                     reads=[vbuf, pv_srcb], writes=[ps_y_buf])
                if (not FLASH_PIPE) and idx + 1 < len(cs):
                    s_mm(idx + 1)

            rr = sb["reciprow"].ap()
            rrb = bf.buf("reciprow")
            p.op("v", lambda h, o=rr: h.reciprocal(o, ps_y[64:65, 0:512]),
                 reads=[ps_y_buf], writes=[rrb])
            p.op("t", lambda h, o=ps_bc: h.matmul(o, sb["onesf"].ap()[:, 0:64], rr, start=True, stop=True),
                 reads=[ofb, rrb], writes=[ps_bc_buf])
            rbb = bf.buf(f"rb{hd % 2}")
            rbap = sb["rb"].ap()[0:64, 512 * (hd % 2):512 * (hd % 2) + 512]
            p.op("v", lambda h, o=rbap: h.tensor_copy(o, ps_bc),
                 reads=[ps_bc_buf], writes=[rbb])
            dst = sb["A1"].ap()[64 * (hd % 2):64 * (hd % 2) + 64,
                                2048 * (hd // 2) + 512 * j: 2048 * (hd // 2) + 512 * (j + 1)]
            p.op("v", lambda h, o=dst, r2=rbap: h.tensor_tensor(o, ps_y[0:64, 0:512], r2, ALU.mult),
                 reads=[ps_y_buf, rbb], writes=[a1w[(hd // 2) // 4]])


def emit_attn_proj(ctx, li):
    p, sb, dr, bf = ctx.plan, ctx.sb, ctx.dram, ctx.bufs
    w1b = bf.buf("W1a")
    a1r = [bf.buf("A1_0"), bf.buf("A1_1")]
    for fl in range(3):
        p.op("v", lambda h: h.tensor_copy(sb["statc"].ap()[:, 15:16], sb["statc"].ap()[:, 14:15]),
             reads=a1r, writes=[bf.buf("statc")])
    flb = bf.buf("statc")
    a1r = a1r + [flb]
    src = chunked(dr["w_proj_T"] if li == 0 else dr["wo"])
    dst = sb["W1"].ap()[:, 0:8192].rearrange("p (c n) -> p c n", c=8)
    p.dma("g", lambda h, o=dst, i=src: h.dma_start(out=o, in_=i), writes=[w1b])

    for tq in range(16):
        for n in range(2):
            i = (tq * 2 + n) % 2
            bank, bb = ctx.bank(i), ctx.bankbuf(i)
            terms = []
            for k in range(8):
                lhs = sb["A1"].ap()[:, 2048 * k + 128 * tq:2048 * k + 128 * (tq + 1)]
                rhs = sb["W1"].ap()[:, 1024 * k + 512 * n:1024 * k + 512 * (n + 1)]
                terms.append((bank, lhs, rhs, k == 0, k == 7))
            mm_group(p, terms, bb, [w1b] + a1r)
            ss, ssb = ctx.stq5(i)
            p.op("v", lambda h, o=ss, b=bank: h.tensor_copy(o, b), reads=[bb], writes=[ssb])
            d2 = dr["attn_dram"][128 * tq:128 * (tq + 1), 512 * n:512 * (n + 1)]
            p.dma("sy", lambda h, o=d2, i2=ss: h.dma_start(out=o, in_=i2),
                  reads=[ssb], writes=[bf.buf("attn_dram")])



def emit_gate0(ctx):
    p, sb, dr, bf = ctx.plan, ctx.sb, ctx.dram, ctx.bufs
    ps_lb = ctx.bankbuf(4)
    ps_t, ps_tb = ctx.ps.ap()[0:128, 2560:3072], ctx.bankbuf(5)
    scrb = bf.buf("scr")
    st = sb["statc"].ap()
    stb = bf.buf("statc")
    for n in range(2):
        psl = ctx.ps.ap()[0:8, 2048 + 512 * n:2048 + 512 * (n + 1)]
        dst = sb["scr"].ap()[0:8, 1024 + 512 * n:1024 + 512 * (n + 1)]
        p.op("a", lambda h, o=dst, i=psl: h.activation(o, i, AF.Identity, bias=sb["gateb"].ap()[:]),
             reads=[ps_lb, bf.buf("gateb")], writes=[scrb])
    for s in range(8):
        src = sb["scr"].ap()[0:8, 1024 + 128 * s:1024 + 128 * (s + 1)]
        p.op("t", lambda h, i=src: h.transpose(ps_t[0:128, 0:8], i, sb["identf"].ap()[0:8, 0:8]),
             reads=[scrb, bf.buf("identf")], writes=[ps_tb])
        lg = sb["lgt"].ap()
        lgb = bf.buf("lgt")
        p.op("v", lambda h: h.tensor_copy(lg, ps_t[0:128, 0:8]), reads=[ps_tb], writes=[lgb])
        p.op("v", lambda h: h.reduce_max(st[:, 0:1], lg, axis=AX.X), reads=[lgb], writes=[stb])
        p.op("v", lambda h: h.tensor_scalar(st[:, 1:2], st[:, 0:1], -1.0, None, ALU.mult),
             reads=[stb], writes=[stb])
        p.op("v", lambda h: h.tensor_scalar(sb["eqm"].ap(), lg, st[:, 0:1], None, ALU.is_ge),
             reads=[lgb, stb], writes=[bf.buf("eqm")])
        p.op("v", lambda h: h.select(sb["lmt"].ap(), sb["eqm"].ap(), sb["negbig"].ap(), lg),
             reads=[bf.buf("eqm"), bf.buf("negbig"), lgb], writes=[bf.buf("lmt")])
        p.op("v", lambda h: h.reduce_max(st[:, 2:3], sb["lmt"].ap(), axis=AX.X),
             reads=[bf.buf("lmt")], writes=[stb])
        p.op("a", lambda h: h.activation(sb["e1t"].ap(), lg, AF.Exp, bias=st[:, 1:2]),
             reads=[lgb, stb], writes=[bf.buf("e1t")])
        p.op("a", lambda h: h.activation(st[:, 3:4], st[:, 2:3], AF.Exp, bias=st[:, 1:2]),
             reads=[stb], writes=[stb])
        p.op("v", lambda h: h.tensor_scalar(st[:, 4:5], st[:, 3:4], 1.0, None, ALU.add),
             reads=[stb], writes=[stb])
        p.op("v", lambda h: h.reciprocal(st[:, 5:6], st[:, 4:5]), reads=[stb], writes=[stb])
        p.op("v", lambda h: h.tensor_scalar(sb["ge2f"].ap(), lg, st[:, 2:3], None, ALU.is_ge),
             reads=[lgb, stb], writes=[bf.buf("ge2f")])
        dst = sb["wall"].ap()[:, 8 * s:8 * (s + 1)]
        p.op("v", lambda h, o=dst: h.scalar_tensor_tensor(o, sb["e1t"].ap(), st[:, 5:6],
                                                          sb["ge2f"].ap(), ALU.mult, ALU.mult),
             reads=[bf.buf("e1t"), stb, bf.buf("ge2f")], writes=[bf.buf("wall")])


def emit_E_load(ctx, r, arena, wb):
    p, sb, dr, bf = ctx.plan, ctx.sb, ctx.dram, ctx.bufs
    if ctx.mode == "l0":
        src = chunked(dr["E"][r])
        dst = sb[arena].ap()[:, 0:8192].rearrange("p (c n) -> p c n", c=8)
        p.dma("sy", lambda h, o=dst, i=src: h.dma_start(out=o, in_=i),
              reads=[bf.buf("E_src")], writes=[wb])
    else:
        for rk in range(8):
            dst = dataclasses.replace(sb[arena].ap(), offset=128 * rk,
                                      ap=[[16384, 128], [1024, 8], [1, 128]])
            src = flat_ap(dr["eg_out"], (3072 * rk + 1024 * r) * 128,
                          [[128, 128], [128 * 128, 8], [1, 128]])
            p.dma("sy", lambda h, o=dst, i=src: h.dma_start(out=o, in_=i),
                  reads=[bf.buf("eg_out")], writes=[wb])


def emit_ffn0(ctx):
    p, sb, dr, bf = ctx.plan, ctx.sb, ctx.dram, ctx.bufs
    w1l = [bf.buf("W1a"), bf.buf("W1b")]
    w2l = [bf.buf("W2a"), bf.buf("W2b")]
    a3ab, a3bb = bf.buf("A3a"), bf.buf("A3b")
    a1ab, a1bb = bf.buf("A1_0"), bf.buf("A1_1")
    onb = bf.buf("onesb")
    for th in range(2):
        for c in range(8):
            dst = sb["A3"].ap()[:, 1024 * c:1024 * (c + 1)]
            src = dr["xn2p_dram"][1024 * th:1024 * (th + 1), 128 * c:128 * (c + 1)]
            p.dma("sy", lambda h, o=dst, i=src: h.dma_start_transpose(o, i),
                  reads=[bf.buf("xn2p_dram")], writes=[a3ab])
        ps_l, ps_lb = ctx.ps.ap()[0:8, 2048:2560], ctx.bankbuf(4)
        scrb = bf.buf("scr")
        for c in range(8):
            sl = sb["A3"].ap()[:, 1024 * c:1024 * (c + 1)]
            fs = sb["scr"].ap()[:, 0:1024]
            p.op("a", lambda h, o=fs, i=sl, g=sb["lnc"].ap()[:, c:c + 1], b2=sb["lnc"].ap()[:, 8 + c:9 + c]:
                 h.activation(o, i, AF.Identity, bias=b2, scale=g),
                 reads=[bf.buf("lnc"), a3ab, scrb], writes=[scrb])
            for nb in range(2):
                p.op("t", lambda h, c=c, nb=nb: h.matmul(
                    ps_l[:, 512 * nb:512 * (nb + 1)] if False else ctx.ps.ap()[0:8, 2048 + 512 * nb:2048 + 512 * (nb + 1)],
                    sb["gatewf"].ap()[:, 8 * c:8 * (c + 1)],
                    sb["scr"].ap()[:, 512 * nb:512 * (nb + 1)],
                    start=(c == 0), stop=(c == 7)),
                    reads=[bf.buf("gatewf"), scrb], writes=[ps_lb])
            p.op("v", lambda h, o=sl, i=fs: h.tensor_copy(o, i),
                 reads=[scrb], writes=[a3ab])

        emit_gate0(ctx)

        for e in range(EXP_N):
            src = chunked(dr["ew1"][e])
            dst = sb["W1"].ap().rearrange("p (c n) -> p c n", c=8)
            p.dma("g", lambda h, o=dst, i=src: h.dma_start(out=o, in_=i), writes=w1l)
            src = chunked(dr["ew2"][e])
            dst = sb["W2"].ap().rearrange("p (c n) -> p c n", c=16)
            p.dma("g", lambda h, o=dst, i=src: h.dma_start(out=o, in_=i), writes=w2l)
            p.dma("g", lambda h, o=sb["ebrow"].ap()[:], i=dr["eb2"][e:e + 1, :]: h.dma_start(out=o, in_=i),
                  writes=[bf.buf("ebrow")])

            for tb in range(2):
                for m in range(16):
                    i = m % 2
                    bank, bb = ctx.bank(i), ctx.bankbuf(i)
                    terms = []
                    for k in range(8):
                        lhs = sb["W1"].ap()[:, 2048 * k + 128 * m:2048 * k + 128 * (m + 1)]
                        rhs = sb["A3"].ap()[:, 1024 * k + 512 * tb:1024 * k + 512 * (tb + 1)]
                        terms.append((bank, lhs, rhs, k == 0, k == 7))
                    mm_group(p, terms, bb, w1l + [a3ab])
                    hdst = sb["A1"].ap()[:, 512 * m:512 * (m + 1)]
                    bias = sb["eb1c"].ap()[:, 16 * e + m:16 * e + m + 1]
                    p.op("a", lambda h, o=hdst, b=bank, bi=bias: h.activation(o, b, AF.Gelu, bias=bi),
                         reads=[bb, bf.buf("eb1c")], writes=[a1ab])
                for t4 in range(4):
                    for dn in range(2):
                        ib = t4 * 2 + dn
                        bank, bb = ctx.bank(ib), ctx.bankbuf(ib)
                        terms = []
                        for k in range(16):
                            lhs = sb["A1"].ap()[:, 512 * k + 128 * t4:512 * k + 128 * (t4 + 1)]
                            rhs = sb["W2"].ap()[:, 1024 * k + 512 * dn:1024 * k + 512 * (dn + 1)]
                            terms.append((bank, lhs, rhs, k == 0, False))
                        terms.append((bank, sb["onesb"].ap()[:, 0:128],
                                      sb["ebrow"].ap()[:, 512 * dn:512 * (dn + 1)], False, True))
                        mm_group(p, terms, bb, [a1ab] + w2l + [onb, bf.buf("ebrow")])
                        tt = tb * 4 + t4
                        acc = sb["A3"].ap()[:, 8192 + 1024 * tt + 512 * dn:8192 + 1024 * tt + 512 * (dn + 1)]
                        wcol = sb["wall"].ap()[:, 8 * tt + e:8 * tt + e + 1]
                        if e == 0:
                            p.op("v", lambda h, o=acc, b=bank, w=wcol: h.tensor_scalar(o, b, w, None, ALU.mult),
                                 reads=[bb, bf.buf("wall")], writes=[a3bb])
                        else:
                            p.op("v", lambda h, o=acc, b=bank, w=wcol: h.scalar_tensor_tensor(
                                o, b, w, o, ALU.mult, ALU.add),
                                reads=[bb, bf.buf("wall")], writes=[a3bb])

        for tt in range(8):
            src2 = sb["A3"].ap()[:, 8192 + 1024 * tt:8192 + 1024 * (tt + 1)]
            p.dma("sy", lambda h, o=dr["moe_tok"][128 * tt:128 * (tt + 1), :], i=src2: h.dma_start(out=o, in_=i),
                  reads=[a3bb], writes=[bf.buf("moe_tok")])
        for c in range(8):
            dst = sb["A1"].ap()[:, 1024 * c:1024 * (c + 1)]
            src2 = dr["moe_tok"][:, 128 * c:128 * (c + 1)]
            p.dma("sy", lambda h, o=dst, i=src2: h.dma_start_transpose(o, i),
                  reads=[bf.buf("moe_tok")], writes=[a1ab])
        for r in range(3):
            warena = "W1" if r % 2 == 0 else "W2"
            wb = bf.buf("W1a") if r % 2 == 0 else bf.buf("W2a")
            emit_E_load(ctx, r, warena, wb)
            src_cols = 0 if r % 2 == 0 else 8192
            dst_cols = 8192 if r % 2 == 0 else 0
            srcb = a1ab if r % 2 == 0 else a1bb
            dstb = a1bb if r % 2 == 0 else a1ab
            for m in range(8):
                for n in range(2):
                    i = (m * 2 + n) % 2
                    bank, bb = ctx.bank(i), ctx.bankbuf(i)
                    terms = []
                    for k in range(8):
                        lhs = sb[warena].ap()[:, 1024 * k + 128 * m:1024 * k + 128 * (m + 1)]
                        rhs = sb["A1"].ap()[:, src_cols + 1024 * k + 512 * n:src_cols + 1024 * k + 512 * (n + 1)]
                        terms.append((bank, lhs, rhs, k == 0, k == 7))
                    mm_group(p, terms, bb, [wb, srcb])
                    dst = sb["A1"].ap()[:, dst_cols + 1024 * m + 512 * n:dst_cols + 1024 * m + 512 * (n + 1)]
                    p.op("v", lambda h, o=dst, b=bank: h.tensor_copy(o, b), reads=[bb], writes=[dstb])

        src = chunked(dr["abel_w"])
        dst = sb["W1"].ap()[:, 0:8192].rearrange("p (c n) -> p c n", c=8)
        p.dma("g", lambda h, o=dst, i=src: h.dma_start(out=o, in_=i), writes=[bf.buf("W1a")])
        for m in range(8):
            for n in range(2):
                i = (m * 2 + n) % 2
                bank, bb = ctx.bank(i), ctx.bankbuf(i)
                terms = []
                for k in range(8):
                    lhs = sb["W1"].ap()[:, 1024 * k + 128 * m:1024 * k + 128 * (m + 1)]
                    rhs = sb["A3"].ap()[:, 1024 * k + 512 * n:1024 * k + 512 * (n + 1)]
                    terms.append((bank, lhs, rhs, k == 0, k == 7))
                mm_group(p, terms, bb, [bf.buf("W1a"), a3ab])
                dst = sb["A3"].ap()[:, 8192 + 1024 * m + 512 * n:8192 + 1024 * m + 512 * (n + 1)]
                bias = sb["abelb"].ap()[:, m:m + 1]
                p.op("a", lambda h, o=dst, b=bank, bi=bias: h.activation(o, b, AF.Gelu, bias=bi),
                     reads=[bb, bf.buf("abelb")], writes=[a3bb])

        ps_g, ps_gb = ctx.ps.ap()[0:1, 2048:2560], ctx.bankbuf(4)
        for n in range(2):
            terms = []
            for k in range(8):
                lhs = sb["gcw"].ap()[:, k:k + 1]
                rhs = sb["A3"].ap()[:, 8192 + 1024 * k + 512 * n:8192 + 1024 * k + 512 * (n + 1)]
                terms.append((ps_g, lhs, rhs, k == 0, False))
            for k in range(8):
                lhs = sb["gcw"].ap()[:, 8 + k:9 + k]
                rhs = sb["A1"].ap()[:, 8192 + 1024 * k + 512 * n:8192 + 1024 * k + 512 * (n + 1)]
                terms.append((ps_g, lhs, rhs, False, k == 7))
            mm_group(p, terms, ps_gb, [bf.buf("gcw"), a3bb, a1bb])
            dst = sb["grow"].ap()[:, 512 * n:512 * (n + 1)]
            p.op("a", lambda h, o=dst: h.activation(o, ps_g, AF.Sigmoid, bias=sb["gcb"].ap()[:]),
                 reads=[ps_gb, bf.buf("gcb")], writes=[bf.buf("grow")])

        ps_bc, ps_bc_buf = ctx.ps.ap()[0:128, 1536:2048], ctx.bankbuf(3)
        for n in range(2):
            p.op("t", lambda h, r2=sb["grow"].ap()[:, 512 * n:512 * (n + 1)]:
                 h.matmul(ps_bc, sb["onesf"].ap()[:, 0:128], r2, start=True, stop=True),
                 reads=[bf.buf("grow"), bf.buf("onesf")], writes=[ps_bc_buf])
            p.op("v", lambda h: h.tensor_copy(sb["rb"].ap()[:, 0:512], ps_bc),
                 reads=[ps_bc_buf], writes=[bf.buf("rb")])
            for m in range(8):
                ab = sb["A3"].ap()[:, 8192 + 1024 * m + 512 * n:8192 + 1024 * m + 512 * (n + 1)]
                mo = sb["A1"].ap()[:, 8192 + 1024 * m + 512 * n:8192 + 1024 * m + 512 * (n + 1)]
                df = sb["pbuf"].ap()[:, 0:512]
                p.op("v", lambda h, o=df, a=mo, b2=ab: h.tensor_tensor(o, a, b2, ALU.subtract),
                     reads=[a1bb, a3bb], writes=[bf.buf("pbuf0")])
                p.op("v", lambda h, o=df: h.tensor_tensor(o, o, sb["rb"].ap()[:, 0:512], ALU.mult),
                     reads=[bf.buf("rb"), bf.buf("pbuf0")], writes=[bf.buf("pbuf0")])
                mx = sb["pbuf"].ap()[:, 512:1024]
                p.op("v", lambda h, o=mx, a=df, b2=ab: h.tensor_tensor(o, a, b2, ALU.add),
                     reads=[bf.buf("pbuf0"), a3bb], writes=[bf.buf("pbuf1")])
                dst = dr["mixed_dram"][128 * m:128 * (m + 1), 1024 * th + 512 * n:1024 * th + 512 * (n + 1)]
                p.dma("sy", lambda h, o=dst, i=mx: h.dma_start(out=o, in_=i),
                      reads=[bf.buf("pbuf1")], writes=[bf.buf("mixed_dram")])



def emit_ffn1(ctx):
    p, sb, dr, bf = ctx.plan, ctx.sb, ctx.dram, ctx.bufs
    a1r = [bf.buf("A1_0"), bf.buf("A1_1")]
    a3b = bf.buf("A3a")
    pc = list(getattr(ctx, "precast_done", None) or [])
    if FFN1_WAIT_AG:
        agtok = getattr(ctx, "last_ag_tok", None)
        if agtok is not None:
            b = Buf("_ag_done")
            b.w = agtok
            pc.append(b)
    for third in range(3):
        for (name, c0, c1, wbx) in (("wg_bf", 0, 8192, "W1a"), ("wu_bf", 8192, 16384, "W1b")):
            src = chunked(dr[name][:, 1024 * third:1024 * (third + 1)])
            dst = sb["W1"].ap()[:, c0:c1].rearrange("p (c n) -> p c n", c=8)
            p.dma("sy", lambda h, o=dst, i=src: h.dma_start(out=o, in_=i),
                  reads=list(pc), writes=[bf.buf(wbx)])
        src = chunked(dr["wd_bf"][1024 * third:1024 * (third + 1), :])
        dst = sb["W2"].ap()[:, 0:8192].rearrange("p (c n) -> p c n", c=8)
        p.dma("sy", lambda h, o=dst, i=src: h.dma_start(out=o, in_=i),
              reads=list(pc), writes=[bf.buf("W2a")])

        for m in range(8):
            for n in range(4):
                ig = (m * 4 + n) % 2
                bankg, bbg = ctx.bank(2 * ig), ctx.bankbuf(2 * ig)
                banku, bbu = ctx.bank(2 * ig + 1), ctx.bankbuf(2 * ig + 1)
                termsg, termsu = [], []
                for k in range(8):
                    rhs = sb["A1"].ap()[:, 2048 * k + 512 * n:2048 * k + 512 * (n + 1)]
                    lg = sb["W1"].ap()[:, 1024 * k + 128 * m:1024 * k + 128 * (m + 1)]
                    lu = sb["W1"].ap()[:, 8192 + 1024 * k + 128 * m:8192 + 1024 * k + 128 * (m + 1)]
                    termsg.append((bankg, lg, rhs, k == 0, k == 7))
                    termsu.append((banku, lu, rhs, k == 0, k == 7))
                mm_group(p, termsg, bbg, [bf.buf("W1a")] + a1r)
                mm_group(p, termsu, bbu, [bf.buf("W1b")] + a1r)
                sil = sb["pbuf"].ap()[:, 512 * ig:512 * (ig + 1)]
                silb = bf.buf(f"pbuf{ig}")
                p.op("a", lambda h, o=sil, b=bankg: h.activation(o, b, AF.Silu, scale=S8),
                     reads=[bbg], writes=[silb])
                dst = sb["A3"].ap()[:, 2048 * m + 512 * n:2048 * m + 512 * (n + 1)]
                p.op("v", lambda h, o=dst, b=banku, s=sil: h.scalar_tensor_tensor(
                    o, b, S8, s, ALU.mult, ALU.mult),
                    reads=[bbu, silb], writes=[a3b])
        for tq in range(16):
            for dn in range(2):
                i = (tq * 2 + dn) % 2
                bank, bb = ctx.bank(4 + i), ctx.bankbuf(4 + i)
                terms = []
                for k in range(8):
                    lhs = sb["A3"].ap()[:, 2048 * k + 128 * tq:2048 * k + 128 * (tq + 1)]
                    rhs = sb["W2"].ap()[:, 1024 * k + 512 * dn:1024 * k + 512 * (dn + 1)]
                    terms.append((bank, lhs, rhs, k == 0, k == 7))
                mm_group(p, terms, bb, [a3b, bf.buf("W2a")])
                ss, ssb = ctx.stq5(i)
                p.op("v", lambda h, o=ss, b=bank: h.tensor_scalar(o, b, S8, None, ALU.mult),
                     reads=[bb], writes=[ssb])
                dst = dr["ffn_dram"][third, 128 * tq:128 * (tq + 1), 512 * dn:512 * (dn + 1)]
                p.dma("sy", lambda h, o=dst, i2=ss: h.dma_start(out=o, in_=i2),
                      reads=[ssb], writes=[bf.buf("ffn_dram")])



XPASS_BARRIER = True

def emit_xpass(ctx, li, x_in, x_out, x_in_buf, x_out_buf):
    p, sb, dr, bf = ctx.plan, ctx.sb, ctx.dram, ctx.bufs
    if XPASS_BARRIER:
        toks = [(s, p.cnt[s]) for s in ("t", "v", "a", "g", "sy")]
        for e in ("t", "v", "a", "g", "sy"):
            p.wait_tokens(e, toks)
    for t in range(16):
        slot = t % 2
        xs = sb["xstage"].ap()[:, 1024 * slot:1024 * (slot + 1)]
        xsb = bf.buf(f"xstage{slot}")
        sc = sb["scr"].ap()[:, 1024 * slot:1024 * (slot + 1)]
        scb = bf.buf(f"scrx{slot}")
        p.dma("sy", lambda h, o=xs, i=x_in[128 * t:128 * (t + 1), :]: h.dma_start(out=o, in_=i),
              reads=[x_in_buf], writes=[xsb])
        at, atb = ctx.stq(slot)
        p.dma("sy", lambda h, o=at, i=dr["attn_dram"][128 * t:128 * (t + 1), :]: h.dma_start(out=o, in_=i),
              reads=[bf.buf("attn_dram")], writes=[atb])
        p.op("v", lambda h, o=sc, a=xs, b=at: h.tensor_tensor(o, a, b, ALU.add),
             reads=[atb, xsb], writes=[scb])
        cur, curb, nxt, nxtb = sc, scb, xs, xsb
        if li == 0:
            xn, xnb2 = ctx.stq(2 + slot)
            p.dma("sy", lambda h, o=xn, i=dr["xn_dram"][128 * t:128 * (t + 1), :]: h.dma_start(out=o, in_=i),
                  reads=[bf.buf("xn_dram")], writes=[xnb2])
            p.op("v", lambda h, o=nxt, a=cur, b=xn: h.tensor_tensor(o, a, b, ALU.add),
                 reads=[xnb2, curb], writes=[nxtb])
            cur, curb, nxt, nxtb = nxt, nxtb, cur, curb
            p.wait_tokens("sy", [("sy", p.cnt["sy"])])
            p.dma("sy", lambda h, o=xn, i=dr["mixed_dram"][:, 128 * t:128 * (t + 1)]: h.dma_start_transpose(o, i),
                  reads=[bf.buf("mixed_dram")], writes=[xnb2])
            p.wait_tokens("sy", [("sy", p.cnt["sy"])])
            p.op("v", lambda h, o=nxt, a=cur, b=xn: h.tensor_tensor(o, a, b, ALU.add),
                 reads=[xnb2, curb], writes=[nxtb])
            cur, curb, nxt, nxtb = nxt, nxtb, cur, curb
        else:
            ff, ffb = ctx.stq(2 + slot)
            for third in range(3):
                p.dma("sy", lambda h, o=ff, i=dr["ffn_dram"][third, 128 * t:128 * (t + 1), :]: h.dma_start(out=o, in_=i),
                      reads=[bf.buf("ffn_dram")], writes=[ffb])
                p.op("v", lambda h, o=nxt, a=cur, b=ff: h.tensor_tensor(o, a, b, ALU.add),
                     reads=[ffb, curb], writes=[nxtb])
                cur, curb, nxt, nxtb = nxt, nxtb, cur, curb
        p.op("v", lambda h, a=cur: h.tensor_copy(sb["statc"].ap()[:, 15:16], a[0:128, 0:1]),
             reads=[curb], writes=[bf.buf("statc")])
        p.dma("sy", lambda h, o=x_out[128 * t:128 * (t + 1), :], i=cur: h.dma_start(out=o, in_=i),
              reads=[curb], writes=[x_out_buf])



def emit_expm(ctx):
    p, sb, dr, bf = ctx.plan, ctx.sb, ctx.dram, ctx.bufs
    w1b = bf.buf("W2a")
    xa = lambda c: ctx.sb["A3"].ap()[:, 0 + 128 * c:0 + 128 * (c + 1)]
    xbq = lambda c: ctx.sb["A3"].ap()[:, 1024 + 128 * c:1024 + 128 * (c + 1)]
    x0 = lambda c: ctx.sb["A3"].ap()[:, 2048 + 128 * c:2048 + 128 * (c + 1)]
    xab, xbb, x0b = bf.buf("eXa"), bf.buf("eXb"), bf.buf("eX0")
    for c in range(8):
        p.dma("g", lambda h, o=x0(c), i=dr["eyecol"][128 * c:128 * (c + 1), :]: h.dma_start(out=o, in_=i),
              writes=[x0b])
    out_dram = dr["eslice"] if ctx.mode == "expm" else dr["eg_in"]
    outb = bf.buf("eslice" if ctx.mode == "expm" else "eg_in")
    for gi in range(3):
        src = chunked(dr["gensT"][gi])
        dst = sb["W2"].ap()[:, 0:8192].rearrange("p (c n) -> p c n", c=8)
        p.dma("g", lambda h, o=dst, i=src: h.dma_start(out=o, in_=i), writes=[w1b])
        for c in range(8):
            p.op("v", lambda h, o=xa(c), i=x0(c): h.tensor_copy(o, i), reads=[x0b], writes=[xab])
        cur, curb = xa, xab
        nxt, nxtb = xbq, xbb
        for kk in range(KTAYLOR, 0, -1):
            for m in range(8):
                i = m % 2
                bank, bb = ctx.bank(i), ctx.bankbuf(i)
                terms = []
                for k in range(8):
                    lhs = sb["W2"].ap()[:, 1024 * k + 128 * m:1024 * k + 128 * (m + 1)]
                    terms.append((bank[0:128, 0:128], lhs, cur(k), k == 0, k == 7))
                mm_group(p, terms, bb, [w1b, curb])
                p.op("v", lambda h, o=nxt(m), b=bank[0:128, 0:128], i0=x0(m), s=1.0 / kk:
                     h.scalar_tensor_tensor(o, b, s, i0, ALU.mult, ALU.add),
                     reads=[bb, x0b], writes=[nxtb])
            cur, curb, nxt, nxtb = nxt, nxtb, cur, curb
        for c in range(8):
            dst = out_dram[1024 * gi + 128 * c:1024 * gi + 128 * (c + 1), :]
            p.dma("sy", lambda h, o=dst, i=cur(c): h.dma_start(out=o, in_=i),
                  reads=[curb], writes=[outb])
    if ctx.mode == "fused":
        extra = getattr(ctx, "precast_done", None) or []
        tok = p.coll(lambda h: h.collective_compute(
            "AllGather", ALU.bypass, replica_groups=[list(range(8))],
            ins=[dr["eg_in"]], outs=[dr["eg_out"]]),
            reads=[bf.buf("eg_in")] + list(extra), writes=[bf.buf("eg_out")])
        ctx.expm_cc_tok = tok
    tok = outb.w
    for nm in ("A3a", "A3b"):
        b = bf.buf(nm)
        b.w = tok
        b.rs = []



def emit_layer(ctx, li, x_in, x_out, x_in_buf, x_out_buf):
    emit_ln(ctx, li, x_in, x_in_buf)
    emit_xnT(ctx)
    emit_qkv(ctx, li)
    emit_ag(ctx)
    ctx.q_proj_pending("q", 0)
    if li == 0:
        emit_ffn0(ctx)
    else:
        emit_ffn1(ctx)
    emit_kv_reorder(ctx)
    emit_flash(ctx, li)
    emit_attn_proj(ctx, li)
    emit_xpass(ctx, li, x_in, x_out, x_in_buf, x_out_buf)


def build(mode):
    nc = bass.Bass("TRN2", target_bir_lowering=False)
    dram = declare_dram(nc, mode)
    with ExitStack() as stack:
        sb = alloc_sbuf(nc, stack, mode)
        ps = stack.enter_context(nc.psum_tensor("ps_all", [128, 4096], F32))
        sems = {}
        for s in ("t", "v", "a", "g", "sy", "cc"):
            sems[s] = stack.enter_context(nc.semaphore(f"sem_{s}"))
        block = stack.enter_context(nc.Block())
        plan = Plan()
        ctx = Ctx(nc=nc, plan=plan, sb=sb, ps=ps, dram=dram, bufs=Bufs(), mode=mode)
        load_const_sbuf(ctx)
        if mode == "expm":
            emit_expm(ctx)
        elif mode == "l0":
            emit_layer(ctx, 0, dram["x_in"], dram["x_out"],
                       ctx.bufs.buf("x_in_src"), ctx.bufs.buf("x_out_dst"))
        elif mode == "l1":
            emit_precast_l1(ctx)
            emit_layer(ctx, 1, dram["x_in"], dram["x_out"],
                       ctx.bufs.buf("x_in_src"), ctx.bufs.buf("x_out_dst"))
        elif mode == "fused":
            emit_precast_l1(ctx)
            emit_expm(ctx)
            emit_layer(ctx, 0, dram["x0"], dram["x_mid"],
                       ctx.bufs.buf("x_in_src"), ctx.bufs.buf("x_mid_buf"))
            emit_layer(ctx, 1, dram["x_mid"], dram["x_out"],
                       ctx.bufs.buf("x_mid_buf"), ctx.bufs.buf("x_out_dst"))
        plan.wait_tokens("sy", [("sy", plan.cnt["sy"]), ("g", plan.cnt["g"]),
                                ("v", plan.cnt["v"]), ("a", plan.cnt["a"]),
                                ("t", plan.cnt["t"])])
        block.tensor(lambda h: plan.replay("t", h, sems))
        block.vector(lambda h: plan.replay("v", h, sems))
        block.scalar(lambda h: plan.replay("a", h, sems))
        block.gpsimd(lambda h: plan.replay("g", h, sems))
        block.sync(lambda h: plan.replay("sy", h, sems))
    return nc



def kernel(**inputs):
    x_full = np.asarray(inputs["L0_attn_x"], dtype=np.float32)
    w = prep_weights(inputs)
    consts = prep_consts()
    lng = _f32(np.asarray(inputs["L0_ffn_ln_gamma"]).reshape(8, 128).T)
    lnb = _f32(np.asarray(inputs["L0_ffn_ln_beta"]).reshape(8, 128).T)

    cores = list(range(NCORE))
    per_core = [prep_core(c, x_full) for c in cores]

    nc = build("fused")
    in_maps = []
    for c in cores:
        m = {k: w[k] for k in L0_W_KEYS + L1_W_KEYS}
        m.update({"ident_f": consts["ident_f"], "iota_row": consts["iota_row"],
                  "alibi_cols": consts["alibi_cols"], "thr": per_core[c]["thr"],
                  "brow0": per_core[c]["brow0"], "x0": per_core[c]["x0"],
                  "gensT": w["gensT"], "eyecol": per_core[c]["eyecol"],
                  "ln_gamma": lng, "ln_beta": lnb})
        in_maps.append(m)
    res = run_bass_kernel_spmd(nc, in_maps, core_ids=cores)

    out = np.zeros((B_, T_, D_), dtype=np.float32)
    for c in cores:
        toks = own_token_abs(c % 2)
        out[c // 2][toks] = np.asarray(res.results[c]["x_out"], dtype=np.float32)
    return out


# version 09
# speedup vs baseline: 1.4528x; 1.0081x over previous
import math
import dataclasses
from contextlib import ExitStack

import numpy as np
import ml_dtypes

import concourse.bass as bass
import concourse.mybir as mybir
from concourse.bass_utils import run_bass_kernel_spmd

F32 = mybir.dt.float32
BF16 = mybir.dt.bfloat16
I32 = mybir.dt.int32
U8 = mybir.dt.uint8
AF = mybir.ActivationFunctionType
ALU = mybir.AluOpType
AX = mybir.AxisListType

BFNP = ml_dtypes.bfloat16

B_, T_, D_ = 4, 4096, 1024
NCORE = 8
TOK = 2048
NH, HD = 16, 64
EXP_N, EHID = 8, 2048
L1HID = 3072
EPS = 1e-5
S8 = 1.0 / 127.0
KTAYLOR = 12
BCONST = 30.0
SLOTS = [8, 16, 24, 32]

def _slopes():
    start = 2.0 ** (-(2.0 ** -(math.log2(NH) - 3)))
    return [start * start ** i for i in range(NH)]

SLOPES = _slopes()

QB_ABS = {0: [512, 1536, 2560, 3584],
          1: [0, 1024, 2048, 3072]}

def own_token_abs(half):
    out = []
    for qa in QB_ABS[half]:
        out.extend(range(qa, qa + 512))
    return out


class Buf:
    __slots__ = ("w", "rs", "name")
    def __init__(self, name):
        self.name = name
        self.w = None
        self.rs = []

class Plan:
    ENGS = ("t", "v", "a", "g", "sy")

    def __init__(self):
        self.q = {e: [] for e in self.ENGS}
        self.cnt = {e: 0 for e in ("t", "v", "a", "g", "sy", "cc")}
        self.hw = {e: {} for e in self.ENGS}

    def _emit(self, eng, fn, reads, writes, sem, inc):
        waits = {}
        deps = []
        for b in reads:
            if b.w is not None:
                deps.append(b.w)
        for b in writes:
            if b.w is not None:
                deps.append(b.w)
            deps.extend(b.rs)
        for (s, v) in deps:
            if s == "t" and eng == "t":
                continue
            if waits.get(s, -1) < v:
                waits[s] = v
        hw = self.hw[eng]
        wl = []
        for s, v in sorted(waits.items()):
            if hw.get(s, -1) >= v:
                continue
            hw[s] = v
            wl.append((s, v))
        self.cnt[sem] += inc
        tok = (sem, self.cnt[sem])
        self.q[eng].append((wl, fn, (sem, inc)))
        for b in reads:
            b.rs.append(tok)
        for b in writes:
            b.w = tok
            b.rs = []
        return tok

    def op(self, eng, fn, reads=(), writes=()):
        return self._emit(eng, fn, list(reads), list(writes), eng, 1)

    def dma(self, queue, fn, reads=(), writes=()):
        return self._emit(queue, fn, list(reads), list(writes), queue, 16)

    def coll(self, fn, reads=(), writes=()):
        return self._emit("g", fn, list(reads), list(writes), "cc", 1)

    def wait_tokens(self, eng, toks):
        wl = []
        hw = self.hw[eng]
        for (s, v) in toks:
            if hw.get(s, -1) < v:
                hw[s] = v
                wl.append((s, v))
        if wl:
            self.q[eng].append((wl, None, None))

    def replay(self, eng, handle, sems):
        for waits, fn, inc in self.q[eng]:
            for (s, v) in waits:
                handle.wait_ge(sems[s], v)
            if fn is not None:
                ins = fn(handle)
                if inc is not None:
                    ins.then_inc(sems[inc[0]], inc[1])


def mm_group(plan, terms, bank_buf, read_bufs):
    def fn(h, terms=terms):
        last = None
        for (o, l, r, st, sp) in terms:
            last = h.matmul(o, l, r, start=st, stop=sp)
        return last
    return plan.op("t", fn, reads=list(read_bufs), writes=[bank_buf])



def _bf(x):
    return np.ascontiguousarray(np.asarray(x).astype(BFNP))

def _f32(x):
    return np.ascontiguousarray(np.asarray(x), dtype=np.float32)

def prep_consts():
    c = {}
    c["ident_f"] = _f32(np.eye(128))
    c["iota_row"] = _f32(np.broadcast_to(np.arange(512, dtype=np.float32), (128, 512)))
    al = np.zeros((128, NH * 32), dtype=np.float32)
    p = np.arange(128, dtype=np.float32)
    for h in range(NH):
        for cc in range(32):
            al[:, h * 32 + cc] = SLOPES[h] * (128 * cc + p)
    c["alibi_cols"] = al
    return c

def prep_core(core, x_full):
    half = core % 2
    seq = core // 2
    toks = own_token_abs(half)
    d = {}
    d["x0"] = _f32(x_full[seq][toks])
    qabs = np.array(toks, dtype=np.float32)
    brow0 = np.zeros((NH, TOK), dtype=np.float32)
    for h in range(NH):
        brow0[h] = -8.0 * (BCONST + SLOPES[h] * qabs)
    d["brow0"] = _bf(brow0)
    thr = np.zeros((128, 32), dtype=np.float32)
    pp = np.arange(128, dtype=np.float32)
    for j in range(4):
        qa = QB_ABS[half][j]
        for i in range(8):
            cch = SLOTS[j] - 8 + i
            thr[:, 8 * j + i] = pp + 128.0 * cch - qa
    d["thr"] = thr
    ey = np.zeros((1024, 128), dtype=np.float32)
    ey[128 * core:128 * (core + 1), :] = np.eye(128)
    d["eyecol"] = ey
    return d

def prep_weights(inp):
    w = {}
    w["w_attn_T"] = _f32(np.asarray(inp["L0_attn_w_attn"]).T)
    w["w_proj_T"] = _f32(np.asarray(inp["L0_attn_w_proj"]).T)
    w["abel_w"] = _f32(inp["L0_ffn_abel_w"])
    w["abel_b"] = _f32(np.asarray(inp["L0_ffn_abel_b"]).reshape(8, 128).T)
    w["gate_w"] = _f32(inp["L0_ffn_moe_gate_w"])
    w["gate_b"] = _f32(np.asarray(inp["L0_ffn_moe_gate_b"]).reshape(8, 1))
    w["ew1"] = _f32(inp["L0_ffn_ew1"])
    eb1 = np.asarray(inp["L0_ffn_eb1"])
    w["eb1c"] = _f32(np.concatenate([np.asarray(eb1[e]).reshape(16, 128).T for e in range(8)], axis=1))
    w["ew2"] = _f32(inp["L0_ffn_ew2"])
    w["eb2"] = _f32(inp["L0_ffn_eb2"])
    w["gensT"] = _f32(np.transpose(np.asarray(inp["L0_ffn_gens"]), (0, 2, 1)))
    w["gc_w"] = _f32(np.asarray(inp["L0_ffn_gc_w"]).reshape(16, 128).T)
    w["gc_b"] = _f32(np.asarray(inp["L0_ffn_gc_b"]).reshape(1, 1))
    w["wq"] = _f32(inp["L1_attn_Wq"])
    w["wk"] = _f32(inp["L1_attn_Wk"])
    w["wv"] = _f32(inp["L1_attn_Wv"])
    w["wo"] = _f32(inp["L1_attn_Wo"])
    w["wg_T"] = np.ascontiguousarray(np.asarray(inp["L1_ffn_w_gate"]).T)
    w["wu_T"] = np.ascontiguousarray(np.asarray(inp["L1_ffn_w_up"]).T)
    w["wd_T"] = np.ascontiguousarray(np.asarray(inp["L1_ffn_w_down"]).T)
    return w

L0_W_KEYS = ["w_attn_T", "w_proj_T", "abel_w", "abel_b", "gate_w", "gate_b",
             "ew1", "eb1c", "ew2", "eb2", "gc_w", "gc_b"]
L1_W_KEYS = ["wq", "wk", "wv", "wo", "wg_T", "wu_T", "wd_T"]



class Bufs(dict):
    def buf(self, name):
        if name not in self:
            self[name] = Buf(name)
        return self[name]

@dataclasses.dataclass
class Ctx:
    nc: object
    plan: Plan
    sb: dict
    ps: object
    dram: dict
    bufs: Bufs
    mode: str

    def bank(self, i, rows=128):
        return self.ps.ap()[0:rows, 512 * i:512 * (i + 1)]

    def bankbuf(self, i):
        return self.bufs.buf(f"ps{i}")

    def stq(self, i):
        return (self.sb["stage"].ap()[:, 1024 * i:1024 * (i + 1)],
                self.bufs.buf(f"stage{i}"))

    def stq5(self, i):
        return (self.sb["stage"].ap()[:, 1024 * i:1024 * i + 512],
                self.bufs.buf(f"stage{i}"))


def flat_ap(ap2d, offset, dims):
    return dataclasses.replace(ap2d, offset=offset, ap=[list(x) for x in dims])


def chunked(src2d, p=128):
    return src2d.rearrange("(c p) n -> p c n", p=p)


DEBUG_OUTPUTS = ()

def declare_dram(nc, mode):
    d = {}
    def ein(name, shape, dt):
        d[name] = nc.dram_tensor(name, list(shape), dt, kind="ExternalInput").ap()
    def eout(name, shape, dt):
        d[name] = nc.dram_tensor(name, list(shape), dt, kind="ExternalOutput").ap()
    def internal(name, shape, dt):
        if name in DEBUG_OUTPUTS:
            eout(name, shape, dt)
        else:
            d[name] = nc.dram_tensor(name, list(shape), dt).ap()

    ein("ident_f", (128, 128), F32)
    ein("iota_row", (128, 512), F32)
    ein("thr", (128, 32), F32)

    if mode in ("expm", "fused"):
        ein("gensT", (3, 1024, 1024), F32)
        ein("eyecol", (1024, 128), F32)
    if mode == "expm":
        eout("eslice", (3 * 1024, 128), BF16)
    if mode in ("l0", "fused"):
        ein("alibi_cols", (128, NH * 32), F32)
        ein("brow0", (NH, TOK), BF16)
        ein("ln_gamma", (128, 8), F32)
        ein("ln_beta", (128, 8), F32)
        for k, shape, dt in [
            ("w_attn_T", (1024, 3072), F32), ("w_proj_T", (1024, 1024), F32),
            ("abel_w", (1024, 1024), F32), ("abel_b", (128, 8), F32),
            ("gate_w", (1024, 8), F32), ("gate_b", (8, 1), F32),
            ("ew1", (8, 1024, 2048), F32), ("eb1c", (128, 128), F32),
            ("ew2", (8, 2048, 1024), F32), ("eb2", (8, 1024), F32),
            ("gc_w", (128, 16), F32), ("gc_b", (1, 1), F32),
        ]:
            ein(k, shape, dt)
    if mode == "l0":
        ein("E", (3, 1024, 1024), BF16)
    if mode in ("l1", "fused"):
        for k, shape, dt in [
            ("wq", (1024, 1024), F32), ("wk", (1024, 1024), F32),
            ("wv", (1024, 1024), F32), ("wo", (1024, 1024), F32),
            ("wg_T", (1024, 3072), I32), ("wu_T", (1024, 3072), I32),
            ("wd_T", (3072, 1024), I32),
        ]:
            ein(k, shape, dt)

    if mode in ("l0", "l1"):
        ein("x_in", (TOK, 1024), F32)
        eout("x_out", (TOK, 1024), F32)
    if mode == "fused":
        ein("x0", (TOK, 1024), F32)
        eout("x_out", (TOK, 1024), F32)
        internal("x_mid", (TOK, 1024), F32)
        internal("eg_in", (3 * 1024, 128), BF16)
        internal("eg_out", (NCORE * 3 * 1024, 128), BF16)

    if mode in ("l0", "l1", "fused"):
        internal("xn_dram", (TOK, 1024), BF16)
        internal("q_dram", (1024, TOK), BF16)
        internal("ag_in", (2048, 2048), BF16)
        internal("ag_out", (4096, 2048), BF16)
        internal("kabs", (1024, 4096), BF16)
        internal("vabs", (4096, 1024), BF16)
        internal("attn_dram", (TOK, 1024), BF16)
    if mode in ("l0", "fused"):
        internal("xn2p_dram", (TOK, 1024), BF16)
        internal("mixed_dram", (1024, TOK), BF16)
        internal("moe_tok", (1024, 1024), BF16)
    if mode in ("l1", "fused"):
        internal("ffn_dram", (3, TOK, 1024), BF16)
        internal("wg_bf", (1024, 3072), BF16)
        internal("wu_bf", (1024, 3072), BF16)
        internal("wd_bf", (3072, 1024), BF16)
    return d


def alloc_sbuf(nc, stack, mode):
    en = stack.enter_context
    sb = {}
    def t(name, shape, dt):
        sb[name] = en(nc.sbuf_tensor("sb_" + name, list(shape), dt))
    t("W1", (128, 16384), BF16)
    t("W2", (128, 16384), BF16)
    t("A1", (128, 16384), BF16)
    t("A3", (128, 16384), BF16)
    t("kslot", (65, 4096), BF16)
    t("vslot", (128, 32 * 65), BF16)
    t("qslot", (65, 2 * 512), BF16)
    t("xstage", (128, 2 * 1024), F32)
    t("stage", (128, 4 * 1024), BF16)
    t("scr", (128, 2048), F32)
    t("pbuf", (128, 4 * 512), BF16)
    t("rb", (128, 1024), F32)
    t("maskq", (128, 8 * 512), mybir.dt.uint16)
    t("iota", (128, 512), F32)
    t("thrT", (128, 32), F32)
    t("identf", (128, 128), F32)
    t("zeros", (128, 512), BF16)
    t("onesb", (1, 512), BF16)
    t("onesf", (1, 128), F32)
    t("browst", (1, 512), BF16)
    t("ebrow", (1, 1024), BF16)
    t("statc", (128, 16), F32)
    t("lnst", (128, 128), F32)
    t("epsc", (128, 1), F32)
    t("lgt", (128, 8), F32)
    t("negbig", (128, 8), F32)
    t("e1t", (128, 8), F32)
    t("lmt", (128, 8), F32)
    t("ge2f", (128, 8), F32)
    t("eqm", (128, 8), U8)
    t("wall", (128, 64), F32)
    t("reciprow", (1, 512), F32)
    t("lnc", (128, 16), F32)
    if mode in ("l0", "fused"):
        t("alibi", (128, NH * 32), F32)
        t("eb1c", (128, 128), F32)
        t("gatew", (128, 64), BF16)
        t("gatewf", (128, 64), F32)
        t("gateb", (8, 1), F32)
        t("gcw", (128, 16), BF16)
        t("gcb", (1, 1), F32)
        t("abelb", (128, 8), F32)
        t("grow", (1, 1024), F32)
    return sb



def load_const_sbuf(ctx):
    p, sb, dr, bf = ctx.plan, ctx.sb, ctx.dram, ctx.bufs
    def ld(dst_name, src_name, queue="sy"):
        dst = sb[dst_name]
        src = dr[src_name]
        p.dma(queue, lambda h, d=dst, s=src: h.dma_start(out=d.ap()[:], in_=s),
              writes=[bf.buf(dst_name)])
    ld("iota", "iota_row")
    ld("thrT", "thr")
    ld("identf", "ident_f")
    if ctx.mode in ("l0", "fused"):
        ld("alibi", "alibi_cols")
        ld("eb1c", "eb1c")
        ld("gateb", "gate_b")
        ld("gcb", "gc_b")
        ld("abelb", "abel_b")
        gw = dr["gate_w"].rearrange("(c p) e -> p c e", p=128)
        p.dma("g", lambda h, gw=gw: h.dma_start(
            out=ctx.sb["gatew"].ap().rearrange("p (c e) -> p c e", c=8), in_=gw),
            writes=[bf.buf("gatew")])
        p.dma("sy", lambda h, gw=gw: h.dma_start(
            out=ctx.sb["gatewf"].ap().rearrange("p (c e) -> p c e", c=8), in_=gw),
            writes=[bf.buf("gatewf")])
        p.dma("g", lambda h: h.dma_start(out=ctx.sb["gcw"].ap()[:], in_=ctx.dram["gc_w"]),
              writes=[bf.buf("gcw")])
        p.dma("sy", lambda h: h.dma_start(out=sb["lnc"].ap()[:, 0:8], in_=dr["ln_gamma"]),
              writes=[bf.buf("lnc")])
        p.dma("sy", lambda h: h.dma_start(out=sb["lnc"].ap()[:, 8:16], in_=dr["ln_beta"]),
              writes=[bf.buf("lnc")])
    p.op("v", lambda h: h.memset(sb["zeros"].ap()[:], 0.0), writes=[bf.buf("zeros")])
    p.op("v", lambda h: h.memset(sb["epsc"].ap()[:], EPS), writes=[bf.buf("epsc")])
    p.op("v", lambda h: h.memset(sb["onesb"].ap()[:], 1.0), writes=[bf.buf("onesb")])
    p.op("v", lambda h: h.memset(sb["onesf"].ap()[:], 1.0), writes=[bf.buf("onesf")])
    p.op("v", lambda h: h.memset(sb["negbig"].ap()[:], -1e30), writes=[bf.buf("negbig")])



def emit_ln(ctx, li, x_in, x_in_buf):
    p, sb, dr, bf = ctx.plan, ctx.sb, ctx.dram, ctx.bufs
    L = sb["lnst"].ap()
    lb = bf.buf("lnst")
    scrb = bf.buf("scr")
    xnb = bf.buf("xn_dram")
    xn2b = bf.buf("xn2p_dram")
    epsb = bf.buf("epsc")
    g = lambda i: L[:, 16 * i:16 * (i + 1)]
    for t in range(16):
        slot = t % 2
        xs = sb["xstage"].ap()[:, 1024 * slot:1024 * (slot + 1)]
        xsb = bf.buf(f"xstage{slot}")
        p.dma("sy", lambda h, o=xs, i=x_in[128 * t:128 * (t + 1), :]: h.dma_start(out=o, in_=i),
              reads=[x_in_buf], writes=[xsb])
        p.op("v", lambda h, o=L[:, t:t + 1], i=xs: h.reduce_sum(o, i, axis=AX.X),
             reads=[xsb], writes=[lb])
        p.op("a", lambda h, o=sb["scr"].ap()[:, 0:1024], i=xs, a=L[:, 16 + t:17 + t]:
             h.activation(o, i, AF.Square, accum_out=a),
             reads=[xsb], writes=[scrb, lb])
    p.op("v", lambda h: h.tensor_scalar(g(2), g(0), 1.0 / 1024, None, ALU.mult), reads=[lb], writes=[lb])
    p.op("v", lambda h: h.tensor_scalar(g(6), g(1), 1.0 / 1024, None, ALU.mult), reads=[lb], writes=[lb])
    p.op("v", lambda h: h.tensor_tensor(g(7), g(2), g(2), ALU.mult), reads=[lb], writes=[lb])
    p.op("v", lambda h: h.tensor_tensor(g(3), g(6), g(7), ALU.subtract), reads=[lb], writes=[lb])
    p.op("a", lambda h: h.activation(g(6), g(3), AF.Sqrt, bias=sb["epsc"].ap()), reads=[lb, epsb], writes=[lb])
    p.op("v", lambda h: h.reciprocal(g(4), g(6)), reads=[lb], writes=[lb])
    p.op("v", lambda h: h.tensor_tensor(g(7), g(2), g(4), ALU.mult), reads=[lb], writes=[lb])
    p.op("v", lambda h: h.tensor_scalar(g(5), g(7), -1.0, None, ALU.mult), reads=[lb], writes=[lb])
    if li == 0:
        p.op("v", lambda h: h.tensor_tensor(g(6), g(3), g(4), ALU.mult), reads=[lb], writes=[lb])
        p.op("v", lambda h: h.tensor_tensor(g(7), g(6), g(4), ALU.mult), reads=[lb], writes=[lb])
        p.op("a", lambda h: h.activation(g(6), g(7), AF.Sqrt, bias=sb["epsc"].ap()), reads=[lb, epsb], writes=[lb])
        p.op("v", lambda h: h.reciprocal(g(7), g(6)), reads=[lb], writes=[lb])
        p.op("v", lambda h: h.tensor_tensor(g(6), g(4), g(7), ALU.mult), reads=[lb], writes=[lb])
        p.op("v", lambda h: h.tensor_tensor(g(1), g(2), g(6), ALU.mult), reads=[lb], writes=[lb])
        p.op("v", lambda h: h.tensor_scalar(g(7), g(1), -1.0, None, ALU.mult), reads=[lb], writes=[lb])
    for t in range(16):
        slot = t % 2
        xs = sb["xstage"].ap()[:, 1024 * slot:1024 * (slot + 1)]
        xsb = bf.buf(f"xstage{slot}")
        p.dma("sy", lambda h, o=xs, i=x_in[128 * t:128 * (t + 1), :]: h.dma_start(out=o, in_=i),
              reads=[x_in_buf], writes=[xsb])
        xnt, xntb = ctx.stq(slot)
        p.op("a", lambda h, o=xnt, i=xs, s=L[:, 64 + t:65 + t], b2=L[:, 80 + t:81 + t]:
             h.activation(o, i, AF.Identity, bias=b2, scale=s),
             reads=[xsb, lb], writes=[xntb])
        p.dma("sy", lambda h, o=dr["xn_dram"][128 * t:128 * (t + 1), :], i=xnt: h.dma_start(out=o, in_=i),
              reads=[xntb], writes=[xnb])
        if li == 0:
            xn2t, xn2tb = ctx.stq(2 + slot)
            p.op("a", lambda h, o=xn2t, i=xs, s=L[:, 96 + t:97 + t], b2=L[:, 112 + t:113 + t]:
                 h.activation(o, i, AF.Identity, bias=b2, scale=s),
                 reads=[xsb, lb], writes=[xn2tb])
            p.dma("sy", lambda h, o=dr["xn2p_dram"][128 * t:128 * (t + 1), :], i=xn2t: h.dma_start(out=o, in_=i),
                  reads=[xn2tb], writes=[xn2b])


def emit_xnT(ctx):
    p, sb, dr, bf = ctx.plan, ctx.sb, ctx.dram, ctx.bufs
    for c in range(8):
        dst = sb["A1"].ap()[:, 2048 * c:2048 * (c + 1)]
        src = dr["xn_dram"][:, 128 * c:128 * (c + 1)]
        p.dma("sy", lambda h, o=dst, i=src: h.dma_start_transpose(o, i),
              reads=[bf.buf("xn_dram")], writes=[bf.buf(f"A1_{c // 4}")])



def emit_qkv(ctx, li):
    p, sb, dr, bf = ctx.plan, ctx.sb, ctx.dram, ctx.bufs
    a1 = sb["A1"].ap()
    a1r = [bf.buf("A1_0"), bf.buf("A1_1")]
    w1a, w1b2, w2a, w2b2 = bf.buf("W1a"), bf.buf("W1b"), bf.buf("W2a"), bf.buf("W2b")
    if li == 0:
        wsrc = dr["w_attn_T"]
        qs, ks, vs = (chunked(wsrc[:, 0:1024]), chunked(wsrc[:, 1024:2048]),
                      chunked(wsrc[:, 2048:3072]))
    else:
        qs, ks, vs = chunked(dr["wq"]), chunked(dr["wk"]), chunked(dr["wv"])
    def wload(c0, c1, src, arena, buf):
        dst = sb[arena].ap()[:, c0:c1].rearrange("p (c n) -> p c n", c=8)
        p.dma("g", lambda h, o=dst, i=src: h.dma_start(out=o, in_=i), writes=[buf])
    wload(0, 8192, qs, "W1", w1a)
    wload(8192, 16384, ks, "W1", w1b2)
    wload(8192, 16384, vs, "W2", w2b2)

    def proj_qk(which, wc0):
        for m in range(8):
            for n in range(4):
                i = (m * 4 + n) % 2
                bank, bb = ctx.bank(i), ctx.bankbuf(i)
                terms = []
                for k in range(8):
                    lhs = sb["W1"].ap()[:, wc0 + 1024 * k + 128 * m: wc0 + 1024 * k + 128 * (m + 1)]
                    rhs = a1[:, 2048 * k + 512 * n: 2048 * k + 512 * (n + 1)]
                    terms.append((bank, lhs, rhs, k == 0, k == 7))
                mm_group(p, terms, bb, [w1a if wc0 == 0 else w1b2] + a1r)
                ss, ssb = ctx.stq5(i)
                p.op("v", lambda h, o=ss, b=bank: h.tensor_copy(o, b), reads=[bb], writes=[ssb])
                if which == "q":
                    dst = dr["q_dram"][128 * m:128 * (m + 1), 512 * n:512 * (n + 1)]
                    p.dma("sy", lambda h, o=dst, i2=ss: h.dma_start(out=o, in_=i2),
                          reads=[ssb], writes=[bf.buf("q_dram")])
                else:
                    dst = dr["ag_in"][128 * m:128 * (m + 1), 512 * n:512 * (n + 1)]
                    p.dma("sy", lambda h, o=dst, i2=ss: h.dma_start(out=o, in_=i2),
                          reads=[ssb], writes=[bf.buf("ag_in")])

    proj_qk("k", 8192)
    ctx.q_proj_pending = proj_qk
    for tq in range(16):
        for n in range(2):
            i = (tq * 2 + n) % 2
            bank, bb = ctx.bank(i), ctx.bankbuf(i)
            terms = []
            for k in range(8):
                lhs = a1[:, 2048 * k + 128 * tq: 2048 * k + 128 * (tq + 1)]
                rhs = sb["W2"].ap()[:, 8192 + 1024 * k + 512 * n: 8192 + 1024 * k + 512 * (n + 1)]
                terms.append((bank, lhs, rhs, k == 0, k == 7))
            mm_group(p, terms, bb, [w2b2] + a1r)
            ss, ssb = ctx.stq5(i)
            p.op("v", lambda h, o=ss, b=bank: h.tensor_copy(o, b), reads=[bb], writes=[ssb])
            off = 1024 * 2048 + (128 * tq) * 1024 + 512 * n
            dst = flat_ap(dr["ag_in"], off, [[1024, 128], [1, 512]])
            p.dma("sy", lambda h, o=dst, i2=ss: h.dma_start(out=o, in_=i2),
                  reads=[ssb], writes=[bf.buf("ag_in")])


AG_SERIAL = False
FFN1_WAIT_AG = True
FLASH_PIPE = True
FLASH_YPP = False

def emit_precast_l1(ctx):
    p, dr, bf = ctx.plan, ctx.dram, ctx.bufs
    for name in ("wg_bf", "wu_bf", "wd_bf"):
        srcn = {"wg_bf": "wg_T", "wu_bf": "wu_T", "wd_bf": "wd_T"}[name]
        p.dma("g", lambda h, o=dr[name], i=dr[srcn]: h.dma_start(out=o, in_=i),
              writes=[bf.buf(name)])
    ctx.precast_done = [bf.buf("wg_bf"), bf.buf("wu_bf"), bf.buf("wd_bf")]


def emit_ag(ctx):
    p, dr, bf = ctx.plan, ctx.dram, ctx.bufs
    extra = getattr(ctx, "precast_done", None) or []
    tok = p.coll(lambda h: h.collective_compute(
        "AllGather", ALU.bypass,
        replica_groups=[[0, 1], [2, 3], [4, 5], [6, 7]],
        ins=[dr["ag_in"]], outs=[dr["ag_out"]]),
        reads=[bf.buf("ag_in")] + list(extra), writes=[bf.buf("ag_out")])
    ctx.last_ag_tok = tok
    if AG_SERIAL:
        for nm in ("W1", "W2", "A1_0", "A1_1", "A3a", "A3b", "xstage0", "xstage1",
                   "pbuf0", "pbuf1", "scr", "statc", "ffn_dram", "stage0", "stage1",
                   "stage2", "stage3"):
            b = bf.buf(nm)
            b.rs.append(tok)


def emit_kv_reorder(ctx):
    p, dr, bf = ctx.plan, ctx.dram, ctx.bufs
    blocks0 = [a // 128 for a in own_token_abs(0)[::128]]
    blocks1 = [a // 128 for a in own_token_abs(1)[::128]]
    src_of_abs = {}
    for loc, ab in enumerate(blocks0):
        src_of_abs[ab] = (0, loc)
    for loc, ab in enumerate(blocks1):
        src_of_abs[ab] = (1, loc)
    agb, kb, vb = bf.buf("ag_out"), bf.buf("kabs"), bf.buf("vabs")
    for ab in range(32):
        rank, loc = src_of_abs[ab]
        src = dr["ag_out"][2048 * rank: 2048 * rank + 1024, 128 * loc:128 * (loc + 1)]
        dst = dr["kabs"][:, 128 * ab:128 * (ab + 1)]
        p.dma("sy", lambda h, o=dst, i=src: h.dma_start(out=o, in_=i), reads=[agb], writes=[kb])
        off = (2048 * rank + 1024) * 2048 + (128 * loc) * 1024
        src = flat_ap(dr["ag_out"], off, [[1024, 128], [1, 1024]])
        dst = dr["vabs"][128 * ab:128 * (ab + 1), :]
        p.dma("sy", lambda h, o=dst, i=src: h.dma_start(out=o, in_=i), reads=[agb], writes=[vb])



def emit_flash(ctx, li):
    p, sb, dr, bf = ctx.plan, ctx.sb, ctx.dram, ctx.bufs
    a1w = [bf.buf("A1_0"), bf.buf("A1_1")]
    kbuf, vbuf, qbuf = bf.buf("kslot"), bf.buf("vslot"), bf.buf("qslot")
    mqb = bf.buf("maskq")
    iob, thb = bf.buf("iota"), bf.buf("thrT")
    kabs_b, vabs_b, qdr_b = bf.buf("kabs"), bf.buf("vabs"), bf.buf("q_dram")
    alb = bf.buf("alibi") if li == 0 else None
    zb = bf.buf("zeros")
    ofb = bf.buf("onesf")
    vsl = sb["vslot"].ap().rearrange("p (c w) -> p c w", w=65)

    p.op("v", lambda h: h.memset(sb["kslot"].ap()[64:65, :], 1.0), writes=[kbuf])
    if li == 1:
        p.op("v", lambda h: h.memset(sb["qslot"].ap()[64:65, :], -8.0 * BCONST), writes=[qbuf])

    for j in range(4):
        s = SLOTS[j]
        for i8 in range(8):
            dst = sb["maskq"].ap()[:, 512 * i8:512 * (i8 + 1)]
            thr = sb["thrT"].ap()[:, 8 * j + i8: 8 * j + i8 + 1]
            p.op("v", lambda h, o=dst, t=thr: h.tensor_scalar(o, sb["iota"].ap()[:], t, None, ALU.is_ge),
                 reads=[iob, thb], writes=[mqb])
        for hd in range(NH):
            if li == 0:
                qa_min = QB_ABS[1][j]
                cs = [c for c in range(s)
                      if c >= s - 8 or SLOPES[hd] * ((128 * c + 127) - qa_min) >= -93.0]
            else:
                cs = list(range(s))
            p.dma("sy", lambda h, o=sb["kslot"].ap()[0:64, 0:128 * s],
                  i=dr["kabs"][64 * hd:64 * (hd + 1), 0:128 * s]: h.dma_start(out=o, in_=i),
                  reads=[kabs_b], writes=[kbuf])
            src = flat_ap(dr["vabs"], 64 * hd, [[1024, 128], [128 * 1024, s], [1, 64]])
            p.dma("sy", lambda h, o=vsl[:, 0:s, 0:64], i=src: h.dma_start(out=o, in_=i),
                  reads=[vabs_b], writes=[vbuf])
            p.op("v", lambda h, o=vsl[:, 0:s, 64:65]: h.memset(o, 1.0), writes=[vbuf])
            qs0 = 512 * (hd % 2)
            qsl = sb["qslot"].ap()[0:65, qs0:qs0 + 512]
            p.dma("sy", lambda h, o=sb["qslot"].ap()[0:64, qs0:qs0 + 512],
                  i=dr["q_dram"][64 * hd:64 * (hd + 1), 512 * j:512 * (j + 1)]: h.dma_start(out=o, in_=i),
                  reads=[qdr_b], writes=[qbuf])
            if li == 0:
                p.dma("sy", lambda h, o=sb["qslot"].ap()[64:65, qs0:qs0 + 512],
                      i=dr["brow0"][hd:hd + 1, 512 * j:512 * (j + 1)]: h.dma_start(out=o, in_=i),
                      reads=[bf.buf("brow0_src")], writes=[qbuf])

            yb = 2 if (hd % 2 == 0 or not FLASH_YPP) else 4
            bcb = 3 if (hd % 2 == 0 or not FLASH_YPP) else 5
            ps_y, ps_y_buf = ctx.ps.ap()[0:65, 512 * yb:512 * (yb + 1)], ctx.bankbuf(yb)
            ps_bc, ps_bc_buf = ctx.ps.ap()[0:64, 512 * bcb:512 * (bcb + 1)], ctx.bankbuf(bcb)

            def s_mm(idx):
                c = cs[idx]
                i = idx % 2
                bank, bb = ctx.bank(i), ctx.bankbuf(i)
                p.op("t", lambda h, o=bank, l=sb["kslot"].ap()[0:65, 128 * c:128 * (c + 1)], r=qsl:
                     h.matmul(o, l, r, start=True, stop=True),
                     reads=[kbuf, qbuf], writes=[bb])

            s_mm(0)
            for idx, c in enumerate(cs):
                i = idx % 2
                bank, bb = ctx.bank(i), ctx.bankbuf(i)
                pb = sb["pbuf"].ap()[:, 512 * i:512 * (i + 1)]
                pbb = bf.buf(f"pbuf{i}")
                if li == 0:
                    bias = sb["alibi"].ap()[:, hd * 32 + c: hd * 32 + c + 1]
                    p.op("a", lambda h, o=pb, b=bank, bi=bias: h.activation(o, b, AF.Exp, bias=bi, scale=0.125),
                         reads=[bb, alb], writes=[pbb])
                else:
                    p.op("a", lambda h, o=pb, b=bank: h.activation(o, b, AF.Exp, scale=0.125),
                         reads=[bb], writes=[pbb])
                if FLASH_PIPE and idx + 1 < len(cs):
                    s_mm(idx + 1)
                if c >= s - 8:
                    msk = sb["maskq"].ap()[:, 512 * (c - (s - 8)):512 * (c - (s - 8) + 1)]
                    pm = sb["pbuf"].ap()[:, 512 * (2 + i):512 * (3 + i)]
                    pmb = bf.buf(f"pbuf{2 + i}")
                    p.op("v", lambda h, o=pm, m=msk, src=pb: h.select(o, m, src, sb["zeros"].ap()[:]),
                         reads=[mqb, zb, pbb], writes=[pmb])
                    pv_src, pv_srcb = pm, pmb
                else:
                    pv_src, pv_srcb = pb, pbb
                lhs = vsl[:, c, 0:65]
                p.op("t", lambda h, o=ps_y, l=lhs, r=pv_src, st=(idx == 0), sp=(idx == len(cs) - 1):
                     h.matmul(o, l, r, start=st, stop=sp),
                     reads=[vbuf, pv_srcb], writes=[ps_y_buf])
                if (not FLASH_PIPE) and idx + 1 < len(cs):
                    s_mm(idx + 1)

            rr = sb["reciprow"].ap()
            rrb = bf.buf("reciprow")
            p.op("v", lambda h, o=rr: h.reciprocal(o, ps_y[64:65, 0:512]),
                 reads=[ps_y_buf], writes=[rrb])
            p.op("t", lambda h, o=ps_bc: h.matmul(o, sb["onesf"].ap()[:, 0:64], rr, start=True, stop=True),
                 reads=[ofb, rrb], writes=[ps_bc_buf])
            rbb = bf.buf(f"rb{hd % 2}")
            rbap = sb["rb"].ap()[0:64, 512 * (hd % 2):512 * (hd % 2) + 512]
            p.op("v", lambda h, o=rbap: h.tensor_copy(o, ps_bc),
                 reads=[ps_bc_buf], writes=[rbb])
            dst = sb["A1"].ap()[64 * (hd % 2):64 * (hd % 2) + 64,
                                2048 * (hd // 2) + 512 * j: 2048 * (hd // 2) + 512 * (j + 1)]
            p.op("v", lambda h, o=dst, r2=rbap: h.tensor_tensor(o, ps_y[0:64, 0:512], r2, ALU.mult),
                 reads=[ps_y_buf, rbb], writes=[a1w[(hd // 2) // 4]])


def emit_attn_proj(ctx, li):
    p, sb, dr, bf = ctx.plan, ctx.sb, ctx.dram, ctx.bufs
    w1b = bf.buf("W1a")
    a1r = [bf.buf("A1_0"), bf.buf("A1_1")]
    for fl in range(3):
        p.op("v", lambda h: h.tensor_copy(sb["statc"].ap()[:, 15:16], sb["statc"].ap()[:, 14:15]),
             reads=a1r, writes=[bf.buf("statc")])
    flb = bf.buf("statc")
    a1r = a1r + [flb]
    src = chunked(dr["w_proj_T"] if li == 0 else dr["wo"])
    dst = sb["W1"].ap()[:, 0:8192].rearrange("p (c n) -> p c n", c=8)
    p.dma("g", lambda h, o=dst, i=src: h.dma_start(out=o, in_=i), writes=[w1b])

    for tq in range(16):
        for n in range(2):
            i = (tq * 2 + n) % 2
            bank, bb = ctx.bank(i), ctx.bankbuf(i)
            terms = []
            for k in range(8):
                lhs = sb["A1"].ap()[:, 2048 * k + 128 * tq:2048 * k + 128 * (tq + 1)]
                rhs = sb["W1"].ap()[:, 1024 * k + 512 * n:1024 * k + 512 * (n + 1)]
                terms.append((bank, lhs, rhs, k == 0, k == 7))
            mm_group(p, terms, bb, [w1b] + a1r)
            ss, ssb = ctx.stq5(i)
            p.op("v", lambda h, o=ss, b=bank: h.tensor_copy(o, b), reads=[bb], writes=[ssb])
            d2 = dr["attn_dram"][128 * tq:128 * (tq + 1), 512 * n:512 * (n + 1)]
            p.dma("sy", lambda h, o=d2, i2=ss: h.dma_start(out=o, in_=i2),
                  reads=[ssb], writes=[bf.buf("attn_dram")])



def emit_gate0(ctx):
    p, sb, dr, bf = ctx.plan, ctx.sb, ctx.dram, ctx.bufs
    ps_lb = ctx.bankbuf(4)
    ps_t, ps_tb = ctx.ps.ap()[0:128, 2560:3072], ctx.bankbuf(5)
    scrb = bf.buf("scr")
    st = sb["statc"].ap()
    stb = bf.buf("statc")
    for n in range(2):
        psl = ctx.ps.ap()[0:8, 2048 + 512 * n:2048 + 512 * (n + 1)]
        dst = sb["scr"].ap()[0:8, 1024 + 512 * n:1024 + 512 * (n + 1)]
        p.op("a", lambda h, o=dst, i=psl: h.activation(o, i, AF.Identity, bias=sb["gateb"].ap()[:]),
             reads=[ps_lb, bf.buf("gateb")], writes=[scrb])
    for s in range(8):
        src = sb["scr"].ap()[0:8, 1024 + 128 * s:1024 + 128 * (s + 1)]
        p.op("t", lambda h, i=src: h.transpose(ps_t[0:128, 0:8], i, sb["identf"].ap()[0:8, 0:8]),
             reads=[scrb, bf.buf("identf")], writes=[ps_tb])
        lg = sb["lgt"].ap()
        lgb = bf.buf("lgt")
        p.op("v", lambda h: h.tensor_copy(lg, ps_t[0:128, 0:8]), reads=[ps_tb], writes=[lgb])
        p.op("v", lambda h: h.reduce_max(st[:, 0:1], lg, axis=AX.X), reads=[lgb], writes=[stb])
        p.op("v", lambda h: h.tensor_scalar(st[:, 1:2], st[:, 0:1], -1.0, None, ALU.mult),
             reads=[stb], writes=[stb])
        p.op("v", lambda h: h.tensor_scalar(sb["eqm"].ap(), lg, st[:, 0:1], None, ALU.is_ge),
             reads=[lgb, stb], writes=[bf.buf("eqm")])
        p.op("v", lambda h: h.select(sb["lmt"].ap(), sb["eqm"].ap(), sb["negbig"].ap(), lg),
             reads=[bf.buf("eqm"), bf.buf("negbig"), lgb], writes=[bf.buf("lmt")])
        p.op("v", lambda h: h.reduce_max(st[:, 2:3], sb["lmt"].ap(), axis=AX.X),
             reads=[bf.buf("lmt")], writes=[stb])
        p.op("a", lambda h: h.activation(sb["e1t"].ap(), lg, AF.Exp, bias=st[:, 1:2]),
             reads=[lgb, stb], writes=[bf.buf("e1t")])
        p.op("a", lambda h: h.activation(st[:, 3:4], st[:, 2:3], AF.Exp, bias=st[:, 1:2]),
             reads=[stb], writes=[stb])
        p.op("v", lambda h: h.tensor_scalar(st[:, 4:5], st[:, 3:4], 1.0, None, ALU.add),
             reads=[stb], writes=[stb])
        p.op("v", lambda h: h.reciprocal(st[:, 5:6], st[:, 4:5]), reads=[stb], writes=[stb])
        p.op("v", lambda h: h.tensor_scalar(sb["ge2f"].ap(), lg, st[:, 2:3], None, ALU.is_ge),
             reads=[lgb, stb], writes=[bf.buf("ge2f")])
        dst = sb["wall"].ap()[:, 8 * s:8 * (s + 1)]
        p.op("v", lambda h, o=dst: h.scalar_tensor_tensor(o, sb["e1t"].ap(), st[:, 5:6],
                                                          sb["ge2f"].ap(), ALU.mult, ALU.mult),
             reads=[bf.buf("e1t"), stb, bf.buf("ge2f")], writes=[bf.buf("wall")])


def emit_E_load(ctx, r, arena, wb):
    p, sb, dr, bf = ctx.plan, ctx.sb, ctx.dram, ctx.bufs
    if ctx.mode == "l0":
        src = chunked(dr["E"][r])
        dst = sb[arena].ap()[:, 0:8192].rearrange("p (c n) -> p c n", c=8)
        p.dma("sy", lambda h, o=dst, i=src: h.dma_start(out=o, in_=i),
              reads=[bf.buf("E_src")], writes=[wb])
    else:
        for rk in range(8):
            dst = dataclasses.replace(sb[arena].ap(), offset=128 * rk,
                                      ap=[[16384, 128], [1024, 8], [1, 128]])
            src = flat_ap(dr["eg_out"], (3072 * rk + 1024 * r) * 128,
                          [[128, 128], [128 * 128, 8], [1, 128]])
            p.dma("sy", lambda h, o=dst, i=src: h.dma_start(out=o, in_=i),
                  reads=[bf.buf("eg_out")], writes=[wb])


def emit_ffn0(ctx):
    p, sb, dr, bf = ctx.plan, ctx.sb, ctx.dram, ctx.bufs
    w1l = [bf.buf("W1a"), bf.buf("W1b")]
    w2l = [bf.buf("W2a"), bf.buf("W2b")]
    a3ab, a3bb = bf.buf("A3a"), bf.buf("A3b")
    a1ab, a1bb = bf.buf("A1_0"), bf.buf("A1_1")
    onb = bf.buf("onesb")
    for th in range(2):
        for c in range(8):
            dst = sb["A3"].ap()[:, 1024 * c:1024 * (c + 1)]
            src = dr["xn2p_dram"][1024 * th:1024 * (th + 1), 128 * c:128 * (c + 1)]
            p.dma("sy", lambda h, o=dst, i=src: h.dma_start_transpose(o, i),
                  reads=[bf.buf("xn2p_dram")], writes=[a3ab])
        ps_l, ps_lb = ctx.ps.ap()[0:8, 2048:2560], ctx.bankbuf(4)
        scrb = bf.buf("scr")
        for c in range(8):
            sl = sb["A3"].ap()[:, 1024 * c:1024 * (c + 1)]
            fs = sb["scr"].ap()[:, 0:1024]
            p.op("a", lambda h, o=fs, i=sl, g=sb["lnc"].ap()[:, c:c + 1], b2=sb["lnc"].ap()[:, 8 + c:9 + c]:
                 h.activation(o, i, AF.Identity, bias=b2, scale=g),
                 reads=[bf.buf("lnc"), a3ab, scrb], writes=[scrb])
            for nb in range(2):
                p.op("t", lambda h, c=c, nb=nb: h.matmul(
                    ps_l[:, 512 * nb:512 * (nb + 1)] if False else ctx.ps.ap()[0:8, 2048 + 512 * nb:2048 + 512 * (nb + 1)],
                    sb["gatewf"].ap()[:, 8 * c:8 * (c + 1)],
                    sb["scr"].ap()[:, 512 * nb:512 * (nb + 1)],
                    start=(c == 0), stop=(c == 7)),
                    reads=[bf.buf("gatewf"), scrb], writes=[ps_lb])
            p.op("v", lambda h, o=sl, i=fs: h.tensor_copy(o, i),
                 reads=[scrb], writes=[a3ab])

        emit_gate0(ctx)

        for e in range(EXP_N):
            src = chunked(dr["ew1"][e])
            dst = sb["W1"].ap().rearrange("p (c n) -> p c n", c=8)
            p.dma("g", lambda h, o=dst, i=src: h.dma_start(out=o, in_=i), writes=w1l)
            src = chunked(dr["ew2"][e])
            dst = sb["W2"].ap().rearrange("p (c n) -> p c n", c=16)
            p.dma("g", lambda h, o=dst, i=src: h.dma_start(out=o, in_=i), writes=w2l)
            p.dma("g", lambda h, o=sb["ebrow"].ap()[:], i=dr["eb2"][e:e + 1, :]: h.dma_start(out=o, in_=i),
                  writes=[bf.buf("ebrow")])

            for tb in range(2):
                for m in range(16):
                    i = m % 2
                    bank, bb = ctx.bank(i), ctx.bankbuf(i)
                    terms = []
                    for k in range(8):
                        lhs = sb["W1"].ap()[:, 2048 * k + 128 * m:2048 * k + 128 * (m + 1)]
                        rhs = sb["A3"].ap()[:, 1024 * k + 512 * tb:1024 * k + 512 * (tb + 1)]
                        terms.append((bank, lhs, rhs, k == 0, k == 7))
                    mm_group(p, terms, bb, w1l + [a3ab])
                    hdst = sb["A1"].ap()[:, 512 * m:512 * (m + 1)]
                    bias = sb["eb1c"].ap()[:, 16 * e + m:16 * e + m + 1]
                    p.op("a", lambda h, o=hdst, b=bank, bi=bias: h.activation(o, b, AF.Gelu, bias=bi),
                         reads=[bb, bf.buf("eb1c")], writes=[a1ab])
                for t4 in range(4):
                    for dn in range(2):
                        ib = t4 * 2 + dn
                        bank, bb = ctx.bank(ib), ctx.bankbuf(ib)
                        terms = []
                        for k in range(16):
                            lhs = sb["A1"].ap()[:, 512 * k + 128 * t4:512 * k + 128 * (t4 + 1)]
                            rhs = sb["W2"].ap()[:, 1024 * k + 512 * dn:1024 * k + 512 * (dn + 1)]
                            terms.append((bank, lhs, rhs, k == 0, False))
                        terms.append((bank, sb["onesb"].ap()[:, 0:128],
                                      sb["ebrow"].ap()[:, 512 * dn:512 * (dn + 1)], False, True))
                        mm_group(p, terms, bb, [a1ab] + w2l + [onb, bf.buf("ebrow")])
                        tt = tb * 4 + t4
                        acc = sb["A3"].ap()[:, 8192 + 1024 * tt + 512 * dn:8192 + 1024 * tt + 512 * (dn + 1)]
                        wcol = sb["wall"].ap()[:, 8 * tt + e:8 * tt + e + 1]
                        if e == 0:
                            p.op("v", lambda h, o=acc, b=bank, w=wcol: h.tensor_scalar(o, b, w, None, ALU.mult),
                                 reads=[bb, bf.buf("wall")], writes=[a3bb])
                        else:
                            p.op("v", lambda h, o=acc, b=bank, w=wcol: h.scalar_tensor_tensor(
                                o, b, w, o, ALU.mult, ALU.add),
                                reads=[bb, bf.buf("wall")], writes=[a3bb])

        for tt in range(8):
            src2 = sb["A3"].ap()[:, 8192 + 1024 * tt:8192 + 1024 * (tt + 1)]
            p.dma("sy", lambda h, o=dr["moe_tok"][128 * tt:128 * (tt + 1), :], i=src2: h.dma_start(out=o, in_=i),
                  reads=[a3bb], writes=[bf.buf("moe_tok")])
        for c in range(8):
            dst = sb["A1"].ap()[:, 1024 * c:1024 * (c + 1)]
            src2 = dr["moe_tok"][:, 128 * c:128 * (c + 1)]
            p.dma("sy", lambda h, o=dst, i=src2: h.dma_start_transpose(o, i),
                  reads=[bf.buf("moe_tok")], writes=[a1ab])
        for r in range(3):
            warena = "W1" if r % 2 == 0 else "W2"
            wb = bf.buf("W1a") if r % 2 == 0 else bf.buf("W2a")
            emit_E_load(ctx, r, warena, wb)
            src_cols = 0 if r % 2 == 0 else 8192
            dst_cols = 8192 if r % 2 == 0 else 0
            srcb = a1ab if r % 2 == 0 else a1bb
            dstb = a1bb if r % 2 == 0 else a1ab
            for m in range(8):
                for n in range(2):
                    i = (m * 2 + n) % 2
                    bank, bb = ctx.bank(i), ctx.bankbuf(i)
                    terms = []
                    for k in range(8):
                        lhs = sb[warena].ap()[:, 1024 * k + 128 * m:1024 * k + 128 * (m + 1)]
                        rhs = sb["A1"].ap()[:, src_cols + 1024 * k + 512 * n:src_cols + 1024 * k + 512 * (n + 1)]
                        terms.append((bank, lhs, rhs, k == 0, k == 7))
                    mm_group(p, terms, bb, [wb, srcb])
                    dst = sb["A1"].ap()[:, dst_cols + 1024 * m + 512 * n:dst_cols + 1024 * m + 512 * (n + 1)]
                    p.op("v", lambda h, o=dst, b=bank: h.tensor_copy(o, b), reads=[bb], writes=[dstb])

        src = chunked(dr["abel_w"])
        dst = sb["W1"].ap()[:, 0:8192].rearrange("p (c n) -> p c n", c=8)
        p.dma("g", lambda h, o=dst, i=src: h.dma_start(out=o, in_=i), writes=[bf.buf("W1a")])
        for m in range(8):
            for n in range(2):
                i = (m * 2 + n) % 2
                bank, bb = ctx.bank(i), ctx.bankbuf(i)
                terms = []
                for k in range(8):
                    lhs = sb["W1"].ap()[:, 1024 * k + 128 * m:1024 * k + 128 * (m + 1)]
                    rhs = sb["A3"].ap()[:, 1024 * k + 512 * n:1024 * k + 512 * (n + 1)]
                    terms.append((bank, lhs, rhs, k == 0, k == 7))
                mm_group(p, terms, bb, [bf.buf("W1a"), a3ab])
                dst = sb["A3"].ap()[:, 8192 + 1024 * m + 512 * n:8192 + 1024 * m + 512 * (n + 1)]
                bias = sb["abelb"].ap()[:, m:m + 1]
                p.op("a", lambda h, o=dst, b=bank, bi=bias: h.activation(o, b, AF.Gelu, bias=bi),
                     reads=[bb, bf.buf("abelb")], writes=[a3bb])

        ps_g, ps_gb = ctx.ps.ap()[0:1, 2048:2560], ctx.bankbuf(4)
        for n in range(2):
            terms = []
            for k in range(8):
                lhs = sb["gcw"].ap()[:, k:k + 1]
                rhs = sb["A3"].ap()[:, 8192 + 1024 * k + 512 * n:8192 + 1024 * k + 512 * (n + 1)]
                terms.append((ps_g, lhs, rhs, k == 0, False))
            for k in range(8):
                lhs = sb["gcw"].ap()[:, 8 + k:9 + k]
                rhs = sb["A1"].ap()[:, 8192 + 1024 * k + 512 * n:8192 + 1024 * k + 512 * (n + 1)]
                terms.append((ps_g, lhs, rhs, False, k == 7))
            mm_group(p, terms, ps_gb, [bf.buf("gcw"), a3bb, a1bb])
            dst = sb["grow"].ap()[:, 512 * n:512 * (n + 1)]
            p.op("a", lambda h, o=dst: h.activation(o, ps_g, AF.Sigmoid, bias=sb["gcb"].ap()[:]),
                 reads=[ps_gb, bf.buf("gcb")], writes=[bf.buf("grow")])

        ps_bc, ps_bc_buf = ctx.ps.ap()[0:128, 1536:2048], ctx.bankbuf(3)
        for n in range(2):
            p.op("t", lambda h, r2=sb["grow"].ap()[:, 512 * n:512 * (n + 1)]:
                 h.matmul(ps_bc, sb["onesf"].ap()[:, 0:128], r2, start=True, stop=True),
                 reads=[bf.buf("grow"), bf.buf("onesf")], writes=[ps_bc_buf])
            p.op("v", lambda h: h.tensor_copy(sb["rb"].ap()[:, 0:512], ps_bc),
                 reads=[ps_bc_buf], writes=[bf.buf("rb")])
            for m in range(8):
                ab = sb["A3"].ap()[:, 8192 + 1024 * m + 512 * n:8192 + 1024 * m + 512 * (n + 1)]
                mo = sb["A1"].ap()[:, 8192 + 1024 * m + 512 * n:8192 + 1024 * m + 512 * (n + 1)]
                df = sb["pbuf"].ap()[:, 0:512]
                p.op("v", lambda h, o=df, a=mo, b2=ab: h.tensor_tensor(o, a, b2, ALU.subtract),
                     reads=[a1bb, a3bb], writes=[bf.buf("pbuf0")])
                p.op("v", lambda h, o=df: h.tensor_tensor(o, o, sb["rb"].ap()[:, 0:512], ALU.mult),
                     reads=[bf.buf("rb"), bf.buf("pbuf0")], writes=[bf.buf("pbuf0")])
                mx = sb["pbuf"].ap()[:, 512:1024]
                p.op("v", lambda h, o=mx, a=df, b2=ab: h.tensor_tensor(o, a, b2, ALU.add),
                     reads=[bf.buf("pbuf0"), a3bb], writes=[bf.buf("pbuf1")])
                dst = dr["mixed_dram"][128 * m:128 * (m + 1), 1024 * th + 512 * n:1024 * th + 512 * (n + 1)]
                p.dma("sy", lambda h, o=dst, i=mx: h.dma_start(out=o, in_=i),
                      reads=[bf.buf("pbuf1")], writes=[bf.buf("mixed_dram")])



def emit_ffn1(ctx):
    p, sb, dr, bf = ctx.plan, ctx.sb, ctx.dram, ctx.bufs
    a1r = [bf.buf("A1_0"), bf.buf("A1_1")]
    a3b = bf.buf("A3a")
    pc = list(getattr(ctx, "precast_done", None) or [])
    if FFN1_WAIT_AG:
        agtok = getattr(ctx, "last_ag_tok", None)
        if agtok is not None:
            b = Buf("_ag_done")
            b.w = agtok
            pc.append(b)
    for third in range(3):
        for (name, c0, c1, wbx) in (("wg_bf", 0, 8192, "W1a"), ("wu_bf", 8192, 16384, "W1b")):
            src = chunked(dr[name][:, 1024 * third:1024 * (third + 1)])
            dst = sb["W1"].ap()[:, c0:c1].rearrange("p (c n) -> p c n", c=8)
            p.dma("sy", lambda h, o=dst, i=src: h.dma_start(out=o, in_=i),
                  reads=list(pc), writes=[bf.buf(wbx)])
        src = chunked(dr["wd_bf"][1024 * third:1024 * (third + 1), :])
        dst = sb["W2"].ap()[:, 0:8192].rearrange("p (c n) -> p c n", c=8)
        p.dma("sy", lambda h, o=dst, i=src: h.dma_start(out=o, in_=i),
              reads=list(pc), writes=[bf.buf("W2a")])

        for m in range(8):
            for n in range(4):
                ig = (m * 4 + n) % 2
                bankg, bbg = ctx.bank(2 * ig), ctx.bankbuf(2 * ig)
                banku, bbu = ctx.bank(2 * ig + 1), ctx.bankbuf(2 * ig + 1)
                termsg, termsu = [], []
                for k in range(8):
                    rhs = sb["A1"].ap()[:, 2048 * k + 512 * n:2048 * k + 512 * (n + 1)]
                    lg = sb["W1"].ap()[:, 1024 * k + 128 * m:1024 * k + 128 * (m + 1)]
                    lu = sb["W1"].ap()[:, 8192 + 1024 * k + 128 * m:8192 + 1024 * k + 128 * (m + 1)]
                    termsg.append((bankg, lg, rhs, k == 0, k == 7))
                    termsu.append((banku, lu, rhs, k == 0, k == 7))
                mm_group(p, termsg, bbg, [bf.buf("W1a")] + a1r)
                mm_group(p, termsu, bbu, [bf.buf("W1b")] + a1r)
                sil = sb["pbuf"].ap()[:, 512 * ig:512 * (ig + 1)]
                silb = bf.buf(f"pbuf{ig}")
                p.op("a", lambda h, o=sil, b=bankg: h.activation(o, b, AF.Silu, scale=S8),
                     reads=[bbg], writes=[silb])
                dst = sb["A3"].ap()[:, 2048 * m + 512 * n:2048 * m + 512 * (n + 1)]
                p.op("v", lambda h, o=dst, b=banku, s=sil: h.scalar_tensor_tensor(
                    o, b, S8, s, ALU.mult, ALU.mult),
                    reads=[bbu, silb], writes=[a3b])
        for tq in range(16):
            for dn in range(2):
                i = (tq * 2 + dn) % 2
                bank, bb = ctx.bank(4 + i), ctx.bankbuf(4 + i)
                terms = []
                for k in range(8):
                    lhs = sb["A3"].ap()[:, 2048 * k + 128 * tq:2048 * k + 128 * (tq + 1)]
                    rhs = sb["W2"].ap()[:, 1024 * k + 512 * dn:1024 * k + 512 * (dn + 1)]
                    terms.append((bank, lhs, rhs, k == 0, k == 7))
                mm_group(p, terms, bb, [a3b, bf.buf("W2a")])
                ss, ssb = ctx.stq5(i)
                p.op("v", lambda h, o=ss, b=bank: h.tensor_scalar(o, b, S8, None, ALU.mult),
                     reads=[bb], writes=[ssb])
                dst = dr["ffn_dram"][third, 128 * tq:128 * (tq + 1), 512 * dn:512 * (dn + 1)]
                p.dma("sy", lambda h, o=dst, i2=ss: h.dma_start(out=o, in_=i2),
                      reads=[ssb], writes=[bf.buf("ffn_dram")])



XPASS_BARRIER = True

def emit_xpass(ctx, li, x_in, x_out, x_in_buf, x_out_buf):
    p, sb, dr, bf = ctx.plan, ctx.sb, ctx.dram, ctx.bufs
    if XPASS_BARRIER:
        toks = [(s, p.cnt[s]) for s in ("t", "v", "a", "g", "sy")]
        for e in ("t", "v", "a", "g", "sy"):
            p.wait_tokens(e, toks)
    for t in range(16):
        slot = t % 2
        xs = sb["xstage"].ap()[:, 1024 * slot:1024 * (slot + 1)]
        xsb = bf.buf(f"xstage{slot}")
        sc = sb["scr"].ap()[:, 1024 * slot:1024 * (slot + 1)]
        scb = bf.buf(f"scrx{slot}")
        p.dma("sy", lambda h, o=xs, i=x_in[128 * t:128 * (t + 1), :]: h.dma_start(out=o, in_=i),
              reads=[x_in_buf], writes=[xsb])
        at, atb = ctx.stq(slot)
        p.dma("sy", lambda h, o=at, i=dr["attn_dram"][128 * t:128 * (t + 1), :]: h.dma_start(out=o, in_=i),
              reads=[bf.buf("attn_dram")], writes=[atb])
        p.op("v", lambda h, o=sc, a=xs, b=at: h.tensor_tensor(o, a, b, ALU.add),
             reads=[atb, xsb], writes=[scb])
        cur, curb, nxt, nxtb = sc, scb, xs, xsb
        if li == 0:
            xn, xnb2 = ctx.stq(2 + slot)
            p.dma("sy", lambda h, o=xn, i=dr["xn_dram"][128 * t:128 * (t + 1), :]: h.dma_start(out=o, in_=i),
                  reads=[bf.buf("xn_dram")], writes=[xnb2])
            p.op("v", lambda h, o=nxt, a=cur, b=xn: h.tensor_tensor(o, a, b, ALU.add),
                 reads=[xnb2, curb], writes=[nxtb])
            cur, curb, nxt, nxtb = nxt, nxtb, cur, curb
            p.wait_tokens("sy", [("sy", p.cnt["sy"])])
            p.dma("sy", lambda h, o=xn, i=dr["mixed_dram"][:, 128 * t:128 * (t + 1)]: h.dma_start_transpose(o, i),
                  reads=[bf.buf("mixed_dram")], writes=[xnb2])
            p.wait_tokens("sy", [("sy", p.cnt["sy"])])
            p.op("v", lambda h, o=nxt, a=cur, b=xn: h.tensor_tensor(o, a, b, ALU.add),
                 reads=[xnb2, curb], writes=[nxtb])
            cur, curb, nxt, nxtb = nxt, nxtb, cur, curb
        else:
            ff, ffb = ctx.stq(2 + slot)
            for third in range(3):
                p.dma("sy", lambda h, o=ff, i=dr["ffn_dram"][third, 128 * t:128 * (t + 1), :]: h.dma_start(out=o, in_=i),
                      reads=[bf.buf("ffn_dram")], writes=[ffb])
                p.op("v", lambda h, o=nxt, a=cur, b=ff: h.tensor_tensor(o, a, b, ALU.add),
                     reads=[ffb, curb], writes=[nxtb])
                cur, curb, nxt, nxtb = nxt, nxtb, cur, curb
        p.op("v", lambda h, a=cur: h.tensor_copy(sb["statc"].ap()[:, 15:16], a[0:128, 0:1]),
             reads=[curb], writes=[bf.buf("statc")])
        p.dma("sy", lambda h, o=x_out[128 * t:128 * (t + 1), :], i=cur: h.dma_start(out=o, in_=i),
              reads=[curb], writes=[x_out_buf])



def emit_expm(ctx):
    p, sb, dr, bf = ctx.plan, ctx.sb, ctx.dram, ctx.bufs
    w1b = bf.buf("W2a")
    xa = lambda c: ctx.sb["A3"].ap()[:, 0 + 128 * c:0 + 128 * (c + 1)]
    xbq = lambda c: ctx.sb["A3"].ap()[:, 1024 + 128 * c:1024 + 128 * (c + 1)]
    x0 = lambda c: ctx.sb["A3"].ap()[:, 2048 + 128 * c:2048 + 128 * (c + 1)]
    xab, xbb, x0b = bf.buf("eXa"), bf.buf("eXb"), bf.buf("eX0")
    for c in range(8):
        p.dma("g", lambda h, o=x0(c), i=dr["eyecol"][128 * c:128 * (c + 1), :]: h.dma_start(out=o, in_=i),
              writes=[x0b])
    out_dram = dr["eslice"] if ctx.mode == "expm" else dr["eg_in"]
    outb = bf.buf("eslice" if ctx.mode == "expm" else "eg_in")
    for gi in range(3):
        src = chunked(dr["gensT"][gi])
        dst = sb["W2"].ap()[:, 0:8192].rearrange("p (c n) -> p c n", c=8)
        p.dma("g", lambda h, o=dst, i=src: h.dma_start(out=o, in_=i), writes=[w1b])
        for c in range(8):
            p.op("v", lambda h, o=xa(c), i=x0(c): h.tensor_copy(o, i), reads=[x0b], writes=[xab])
        cur, curb = xa, xab
        nxt, nxtb = xbq, xbb
        for kk in range(KTAYLOR, 0, -1):
            for m in range(8):
                i = m % 2
                bank, bb = ctx.bank(i), ctx.bankbuf(i)
                terms = []
                for k in range(8):
                    lhs = sb["W2"].ap()[:, 1024 * k + 128 * m:1024 * k + 128 * (m + 1)]
                    terms.append((bank[0:128, 0:128], lhs, cur(k), k == 0, k == 7))
                mm_group(p, terms, bb, [w1b, curb])
                p.op("v", lambda h, o=nxt(m), b=bank[0:128, 0:128], i0=x0(m), s=1.0 / kk:
                     h.scalar_tensor_tensor(o, b, s, i0, ALU.mult, ALU.add),
                     reads=[bb, x0b], writes=[nxtb])
            cur, curb, nxt, nxtb = nxt, nxtb, cur, curb
        for c in range(8):
            dst = out_dram[1024 * gi + 128 * c:1024 * gi + 128 * (c + 1), :]
            p.dma("sy", lambda h, o=dst, i=cur(c): h.dma_start(out=o, in_=i),
                  reads=[curb], writes=[outb])
    if ctx.mode == "fused":
        extra = getattr(ctx, "precast_done", None) or []
        tok = p.coll(lambda h: h.collective_compute(
            "AllGather", ALU.bypass, replica_groups=[list(range(8))],
            ins=[dr["eg_in"]], outs=[dr["eg_out"]]),
            reads=[bf.buf("eg_in")] + list(extra), writes=[bf.buf("eg_out")])
        ctx.expm_cc_tok = tok
    tok = outb.w
    for nm in ("A3a", "A3b"):
        b = bf.buf(nm)
        b.w = tok
        b.rs = []



def emit_layer(ctx, li, x_in, x_out, x_in_buf, x_out_buf):
    emit_ln(ctx, li, x_in, x_in_buf)
    emit_xnT(ctx)
    emit_qkv(ctx, li)
    emit_ag(ctx)
    ctx.q_proj_pending("q", 0)
    if li == 0:
        emit_ffn0(ctx)
    else:
        emit_ffn1(ctx)
    emit_kv_reorder(ctx)
    emit_flash(ctx, li)
    emit_attn_proj(ctx, li)
    emit_xpass(ctx, li, x_in, x_out, x_in_buf, x_out_buf)


def build(mode):
    nc = bass.Bass("TRN2", target_bir_lowering=False)
    dram = declare_dram(nc, mode)
    with ExitStack() as stack:
        sb = alloc_sbuf(nc, stack, mode)
        ps = stack.enter_context(nc.psum_tensor("ps_all", [128, 4096], F32))
        sems = {}
        for s in ("t", "v", "a", "g", "sy", "cc"):
            sems[s] = stack.enter_context(nc.semaphore(f"sem_{s}"))
        block = stack.enter_context(nc.Block())
        plan = Plan()
        ctx = Ctx(nc=nc, plan=plan, sb=sb, ps=ps, dram=dram, bufs=Bufs(), mode=mode)
        load_const_sbuf(ctx)
        if mode == "expm":
            emit_expm(ctx)
        elif mode == "l0":
            emit_layer(ctx, 0, dram["x_in"], dram["x_out"],
                       ctx.bufs.buf("x_in_src"), ctx.bufs.buf("x_out_dst"))
        elif mode == "l1":
            emit_precast_l1(ctx)
            emit_layer(ctx, 1, dram["x_in"], dram["x_out"],
                       ctx.bufs.buf("x_in_src"), ctx.bufs.buf("x_out_dst"))
        elif mode == "fused":
            emit_precast_l1(ctx)
            emit_expm(ctx)
            emit_layer(ctx, 0, dram["x0"], dram["x_mid"],
                       ctx.bufs.buf("x_in_src"), ctx.bufs.buf("x_mid_buf"))
            emit_layer(ctx, 1, dram["x_mid"], dram["x_out"],
                       ctx.bufs.buf("x_mid_buf"), ctx.bufs.buf("x_out_dst"))
        plan.wait_tokens("sy", [("sy", plan.cnt["sy"]), ("g", plan.cnt["g"]),
                                ("v", plan.cnt["v"]), ("a", plan.cnt["a"]),
                                ("t", plan.cnt["t"])])
        block.tensor(lambda h: plan.replay("t", h, sems))
        block.vector(lambda h: plan.replay("v", h, sems))
        block.scalar(lambda h: plan.replay("a", h, sems))
        block.gpsimd(lambda h: plan.replay("g", h, sems))
        block.sync(lambda h: plan.replay("sy", h, sems))
    return nc



def kernel(**inputs):
    x_full = np.asarray(inputs["L0_attn_x"], dtype=np.float32)
    w = prep_weights(inputs)
    consts = prep_consts()
    lng = _f32(np.asarray(inputs["L0_ffn_ln_gamma"]).reshape(8, 128).T)
    lnb = _f32(np.asarray(inputs["L0_ffn_ln_beta"]).reshape(8, 128).T)

    cores = list(range(NCORE))
    per_core = [prep_core(c, x_full) for c in cores]

    nc = build("fused")
    in_maps = []
    for c in cores:
        m = {k: w[k] for k in L0_W_KEYS + L1_W_KEYS}
        m.update({"ident_f": consts["ident_f"], "iota_row": consts["iota_row"],
                  "alibi_cols": consts["alibi_cols"], "thr": per_core[c]["thr"],
                  "brow0": per_core[c]["brow0"], "x0": per_core[c]["x0"],
                  "gensT": w["gensT"], "eyecol": per_core[c]["eyecol"],
                  "ln_gamma": lng, "ln_beta": lnb})
        in_maps.append(m)
    res = run_bass_kernel_spmd(nc, in_maps, core_ids=cores)

    out = np.zeros((B_, T_, D_), dtype=np.float32)
    for c in cores:
        toks = own_token_abs(c % 2)
        out[c // 2][toks] = np.asarray(res.results[c]["x_out"], dtype=np.float32)
    return out


# version 11
# speedup vs baseline: 1.4589x; 1.0042x over previous
import math
import dataclasses
from contextlib import ExitStack

import numpy as np
import ml_dtypes

import concourse.bass as bass
import concourse.mybir as mybir
from concourse.bass_utils import run_bass_kernel_spmd

F32 = mybir.dt.float32
BF16 = mybir.dt.bfloat16
I32 = mybir.dt.int32
U8 = mybir.dt.uint8
AF = mybir.ActivationFunctionType
ALU = mybir.AluOpType
AX = mybir.AxisListType

BFNP = ml_dtypes.bfloat16

B_, T_, D_ = 4, 4096, 1024
NCORE = 8
TOK = 2048
NH, HD = 16, 64
EXP_N, EHID = 8, 2048
L1HID = 3072
EPS = 1e-5
S8 = 1.0 / 127.0
KTAYLOR = 12
BCONST = 30.0
SLOTS = [8, 16, 24, 32]

def _slopes():
    start = 2.0 ** (-(2.0 ** -(math.log2(NH) - 3)))
    return [start * start ** i for i in range(NH)]

SLOPES = _slopes()

QB_ABS = {0: [512, 1536, 2560, 3584],
          1: [0, 1024, 2048, 3072]}

def own_token_abs(half):
    out = []
    for qa in QB_ABS[half]:
        out.extend(range(qa, qa + 512))
    return out


class Buf:
    __slots__ = ("w", "rs", "name")
    def __init__(self, name):
        self.name = name
        self.w = None
        self.rs = []

class Plan:
    ENGS = ("t", "v", "a", "g", "sy")

    def __init__(self):
        self.q = {e: [] for e in self.ENGS}
        self.cnt = {e: 0 for e in ("t", "v", "a", "g", "sy", "cc")}
        self.hw = {e: {} for e in self.ENGS}

    def _emit(self, eng, fn, reads, writes, sem, inc):
        waits = {}
        deps = []
        for b in reads:
            if b.w is not None:
                deps.append(b.w)
        for b in writes:
            if b.w is not None:
                deps.append(b.w)
            deps.extend(b.rs)
        for (s, v) in deps:
            if s == "t" and eng == "t":
                continue
            if waits.get(s, -1) < v:
                waits[s] = v
        hw = self.hw[eng]
        wl = []
        for s, v in sorted(waits.items()):
            if hw.get(s, -1) >= v:
                continue
            hw[s] = v
            wl.append((s, v))
        self.cnt[sem] += inc
        tok = (sem, self.cnt[sem])
        self.q[eng].append((wl, fn, (sem, inc)))
        for b in reads:
            b.rs.append(tok)
        for b in writes:
            b.w = tok
            b.rs = []
        return tok

    def op(self, eng, fn, reads=(), writes=()):
        return self._emit(eng, fn, list(reads), list(writes), eng, 1)

    def dma(self, queue, fn, reads=(), writes=()):
        return self._emit(queue, fn, list(reads), list(writes), queue, 16)

    def coll(self, fn, reads=(), writes=()):
        return self._emit("g", fn, list(reads), list(writes), "cc", 1)

    def wait_tokens(self, eng, toks):
        wl = []
        hw = self.hw[eng]
        for (s, v) in toks:
            if hw.get(s, -1) < v:
                hw[s] = v
                wl.append((s, v))
        if wl:
            self.q[eng].append((wl, None, None))

    def replay(self, eng, handle, sems):
        for waits, fn, inc in self.q[eng]:
            for (s, v) in waits:
                handle.wait_ge(sems[s], v)
            if fn is not None:
                ins = fn(handle)
                if inc is not None:
                    ins.then_inc(sems[inc[0]], inc[1])


def mm_group(plan, terms, bank_buf, read_bufs):
    def fn(h, terms=terms):
        last = None
        for (o, l, r, st, sp) in terms:
            last = h.matmul(o, l, r, start=st, stop=sp)
        return last
    return plan.op("t", fn, reads=list(read_bufs), writes=[bank_buf])



def _bf(x):
    return np.ascontiguousarray(np.asarray(x).astype(BFNP))

def _f32(x):
    return np.ascontiguousarray(np.asarray(x), dtype=np.float32)

def prep_consts():
    c = {}
    c["ident_f"] = _f32(np.eye(128))
    c["iota_row"] = _f32(np.broadcast_to(np.arange(512, dtype=np.float32), (128, 512)))
    al = np.zeros((128, NH * 32), dtype=np.float32)
    p = np.arange(128, dtype=np.float32)
    for h in range(NH):
        for cc in range(32):
            al[:, h * 32 + cc] = SLOPES[h] * (128 * cc + p)
    c["alibi_cols"] = al
    return c

def prep_core(core, x_full):
    half = core % 2
    seq = core // 2
    toks = own_token_abs(half)
    d = {}
    d["x0"] = _f32(x_full[seq][toks])
    qabs = np.array(toks, dtype=np.float32)
    brow0 = np.zeros((NH, TOK), dtype=np.float32)
    for h in range(NH):
        brow0[h] = -8.0 * (BCONST + SLOPES[h] * qabs)
    d["brow0"] = _bf(brow0)
    thr = np.zeros((128, 32), dtype=np.float32)
    pp = np.arange(128, dtype=np.float32)
    for j in range(4):
        qa = QB_ABS[half][j]
        for i in range(8):
            cch = SLOTS[j] - 8 + i
            thr[:, 8 * j + i] = pp + 128.0 * cch - qa
    d["thr"] = thr
    ey = np.zeros((1024, 128), dtype=np.float32)
    ey[128 * core:128 * (core + 1), :] = np.eye(128)
    d["eyecol"] = ey
    return d

def prep_weights(inp):
    w = {}
    w["w_attn_T"] = _f32(np.asarray(inp["L0_attn_w_attn"]).T)
    w["w_proj_T"] = _f32(np.asarray(inp["L0_attn_w_proj"]).T)
    w["abel_w"] = _f32(inp["L0_ffn_abel_w"])
    w["abel_b"] = _f32(np.asarray(inp["L0_ffn_abel_b"]).reshape(8, 128).T)
    w["gate_w"] = _f32(inp["L0_ffn_moe_gate_w"])
    w["gate_b"] = _f32(np.asarray(inp["L0_ffn_moe_gate_b"]).reshape(8, 1))
    w["ew1"] = _f32(inp["L0_ffn_ew1"])
    eb1 = np.asarray(inp["L0_ffn_eb1"])
    w["eb1c"] = _f32(np.concatenate([np.asarray(eb1[e]).reshape(16, 128).T for e in range(8)], axis=1))
    w["ew2"] = _f32(inp["L0_ffn_ew2"])
    w["eb2"] = _f32(inp["L0_ffn_eb2"])
    w["gensT"] = _f32(np.transpose(np.asarray(inp["L0_ffn_gens"]), (0, 2, 1)))
    w["gc_w"] = _f32(np.asarray(inp["L0_ffn_gc_w"]).reshape(16, 128).T)
    w["gc_b"] = _f32(np.asarray(inp["L0_ffn_gc_b"]).reshape(1, 1))
    w["wq"] = _f32(inp["L1_attn_Wq"])
    w["wk"] = _f32(inp["L1_attn_Wk"])
    w["wv"] = _f32(inp["L1_attn_Wv"])
    w["wo"] = _f32(inp["L1_attn_Wo"])
    w["wg_T"] = np.ascontiguousarray(np.asarray(inp["L1_ffn_w_gate"]).T)
    w["wu_T"] = np.ascontiguousarray(np.asarray(inp["L1_ffn_w_up"]).T)
    w["wd_T"] = np.ascontiguousarray(np.asarray(inp["L1_ffn_w_down"]).T)
    return w

L0_W_KEYS = ["w_attn_T", "w_proj_T", "abel_w", "abel_b", "gate_w", "gate_b",
             "ew1", "eb1c", "ew2", "eb2", "gc_w", "gc_b"]
L1_W_KEYS = ["wq", "wk", "wv", "wo", "wg_T", "wu_T", "wd_T"]



class Bufs(dict):
    def buf(self, name):
        if name not in self:
            self[name] = Buf(name)
        return self[name]

@dataclasses.dataclass
class Ctx:
    nc: object
    plan: Plan
    sb: dict
    ps: object
    dram: dict
    bufs: Bufs
    mode: str

    def bank(self, i, rows=128):
        return self.ps.ap()[0:rows, 512 * i:512 * (i + 1)]

    def bankbuf(self, i):
        return self.bufs.buf(f"ps{i}")

    def stq(self, i):
        return (self.sb["stage"].ap()[:, 1024 * i:1024 * (i + 1)],
                self.bufs.buf(f"stage{i}"))

    def stq5(self, i):
        return (self.sb["stage"].ap()[:, 1024 * i:1024 * i + 512],
                self.bufs.buf(f"stage{i}"))


def flat_ap(ap2d, offset, dims):
    return dataclasses.replace(ap2d, offset=offset, ap=[list(x) for x in dims])


def chunked(src2d, p=128):
    return src2d.rearrange("(c p) n -> p c n", p=p)


DEBUG_OUTPUTS = ()

def declare_dram(nc, mode):
    d = {}
    def ein(name, shape, dt):
        d[name] = nc.dram_tensor(name, list(shape), dt, kind="ExternalInput").ap()
    def eout(name, shape, dt):
        d[name] = nc.dram_tensor(name, list(shape), dt, kind="ExternalOutput").ap()
    def internal(name, shape, dt):
        if name in DEBUG_OUTPUTS:
            eout(name, shape, dt)
        else:
            d[name] = nc.dram_tensor(name, list(shape), dt).ap()

    ein("ident_f", (128, 128), F32)
    ein("iota_row", (128, 512), F32)
    ein("thr", (128, 32), F32)

    if mode in ("expm", "fused"):
        ein("gensT", (3, 1024, 1024), F32)
        ein("eyecol", (1024, 128), F32)
    if mode == "expm":
        eout("eslice", (3 * 1024, 128), BF16)
    if mode in ("l0", "fused"):
        ein("alibi_cols", (128, NH * 32), F32)
        ein("brow0", (NH, TOK), BF16)
        ein("ln_gamma", (128, 8), F32)
        ein("ln_beta", (128, 8), F32)
        for k, shape, dt in [
            ("w_attn_T", (1024, 3072), F32), ("w_proj_T", (1024, 1024), F32),
            ("abel_w", (1024, 1024), F32), ("abel_b", (128, 8), F32),
            ("gate_w", (1024, 8), F32), ("gate_b", (8, 1), F32),
            ("ew1", (8, 1024, 2048), F32), ("eb1c", (128, 128), F32),
            ("ew2", (8, 2048, 1024), F32), ("eb2", (8, 1024), F32),
            ("gc_w", (128, 16), F32), ("gc_b", (1, 1), F32),
        ]:
            ein(k, shape, dt)
    if mode == "l0":
        ein("E", (3, 1024, 1024), BF16)
    if mode in ("l1", "fused"):
        for k, shape, dt in [
            ("wq", (1024, 1024), F32), ("wk", (1024, 1024), F32),
            ("wv", (1024, 1024), F32), ("wo", (1024, 1024), F32),
            ("wg_T", (1024, 3072), I32), ("wu_T", (1024, 3072), I32),
            ("wd_T", (3072, 1024), I32),
        ]:
            ein(k, shape, dt)

    if mode in ("l0", "l1"):
        ein("x_in", (TOK, 1024), F32)
        eout("x_out", (TOK, 1024), F32)
    if mode == "fused":
        ein("x0", (TOK, 1024), F32)
        eout("x_out", (TOK, 1024), F32)
        internal("x_mid", (TOK, 1024), F32)
        internal("eg_in", (3 * 1024, 128), BF16)
        internal("eg_out", (NCORE * 3 * 1024, 128), BF16)

    if mode in ("l0", "l1", "fused"):
        internal("xn_dram", (TOK, 1024), BF16)
        internal("q_dram", (1024, TOK), BF16)
        internal("ag_in", (2048, 2048), BF16)
        internal("ag_out", (4096, 2048), BF16)
        internal("kabs", (1024, 4096), BF16)
        internal("vabs", (4096, 1024), BF16)
        internal("attn_dram", (TOK, 1024), BF16)
    if mode in ("l0", "fused"):
        internal("xn2p_dram", (TOK, 1024), BF16)
        internal("mixed_dram", (1024, TOK), BF16)
        internal("moe_tok", (1024, 1024), BF16)
    if mode in ("l1", "fused"):
        internal("ffn_dram", (3, TOK, 1024), BF16)
        internal("wg_bf", (1024, 3072), BF16)
        internal("wu_bf", (1024, 3072), BF16)
        internal("wd_bf", (3072, 1024), BF16)
    return d


def alloc_sbuf(nc, stack, mode):
    en = stack.enter_context
    sb = {}
    def t(name, shape, dt):
        sb[name] = en(nc.sbuf_tensor("sb_" + name, list(shape), dt))
    t("W1", (128, 16384), BF16)
    t("W2", (128, 16384), BF16)
    t("A1", (128, 16384), BF16)
    t("A3", (128, 16384), BF16)
    t("kslot", (65, 4096), BF16)
    t("vslot", (128, 32 * 65), BF16)
    t("qslot", (65, 2 * 512), BF16)
    t("xstage", (128, 2 * 1024), F32)
    t("stage", (128, 4 * 1024), BF16)
    t("scr", (128, 2048), F32)
    t("pbuf", (128, 4 * 512), BF16)
    t("rb", (128, 1024), F32)
    t("maskq", (128, 8 * 512), mybir.dt.uint16)
    t("iota", (128, 512), F32)
    t("thrT", (128, 32), F32)
    t("identf", (128, 128), F32)
    t("zeros", (128, 512), BF16)
    t("onesb", (1, 512), BF16)
    t("onesf", (1, 128), F32)
    t("browst", (1, 512), BF16)
    t("ebrow", (1, 1024), BF16)
    t("statc", (128, 16), F32)
    t("lnst", (128, 128), F32)
    t("epsc", (128, 1), F32)
    t("lgt", (128, 8), F32)
    t("negbig", (128, 8), F32)
    t("e1t", (128, 8), F32)
    t("lmt", (128, 8), F32)
    t("ge2f", (128, 8), F32)
    t("eqm", (128, 8), U8)
    t("wall", (128, 64), F32)
    t("reciprow", (1, 512), F32)
    t("lnc", (128, 16), F32)
    if mode in ("l0", "fused"):
        t("alibi", (128, NH * 32), F32)
        t("eb1c", (128, 128), F32)
        t("gatew", (128, 64), BF16)
        t("gatewf", (128, 64), F32)
        t("gateb", (8, 1), F32)
        t("gcw", (128, 16), BF16)
        t("gcb", (1, 1), F32)
        t("abelb", (128, 8), F32)
        t("grow", (1, 1024), F32)
    return sb



def load_const_sbuf(ctx):
    p, sb, dr, bf = ctx.plan, ctx.sb, ctx.dram, ctx.bufs
    def ld(dst_name, src_name, queue="sy"):
        dst = sb[dst_name]
        src = dr[src_name]
        p.dma(queue, lambda h, d=dst, s=src: h.dma_start(out=d.ap()[:], in_=s),
              writes=[bf.buf(dst_name)])
    ld("iota", "iota_row")
    ld("thrT", "thr")
    ld("identf", "ident_f")
    if ctx.mode in ("l0", "fused"):
        ld("alibi", "alibi_cols")
        ld("eb1c", "eb1c")
        ld("gateb", "gate_b")
        ld("gcb", "gc_b")
        ld("abelb", "abel_b")
        gw = dr["gate_w"].rearrange("(c p) e -> p c e", p=128)
        p.dma("g", lambda h, gw=gw: h.dma_start(
            out=ctx.sb["gatew"].ap().rearrange("p (c e) -> p c e", c=8), in_=gw),
            writes=[bf.buf("gatew")])
        p.dma("sy", lambda h, gw=gw: h.dma_start(
            out=ctx.sb["gatewf"].ap().rearrange("p (c e) -> p c e", c=8), in_=gw),
            writes=[bf.buf("gatewf")])
        p.dma("g", lambda h: h.dma_start(out=ctx.sb["gcw"].ap()[:], in_=ctx.dram["gc_w"]),
              writes=[bf.buf("gcw")])
        p.dma("sy", lambda h: h.dma_start(out=sb["lnc"].ap()[:, 0:8], in_=dr["ln_gamma"]),
              writes=[bf.buf("lnc")])
        p.dma("sy", lambda h: h.dma_start(out=sb["lnc"].ap()[:, 8:16], in_=dr["ln_beta"]),
              writes=[bf.buf("lnc")])
    p.op("v", lambda h: h.memset(sb["zeros"].ap()[:], 0.0), writes=[bf.buf("zeros")])
    p.op("v", lambda h: h.memset(sb["epsc"].ap()[:], EPS), writes=[bf.buf("epsc")])
    p.op("v", lambda h: h.memset(sb["onesb"].ap()[:], 1.0), writes=[bf.buf("onesb")])
    p.op("v", lambda h: h.memset(sb["onesf"].ap()[:], 1.0), writes=[bf.buf("onesf")])
    p.op("v", lambda h: h.memset(sb["negbig"].ap()[:], -1e30), writes=[bf.buf("negbig")])



def emit_ln(ctx, li, x_in, x_in_buf):
    p, sb, dr, bf = ctx.plan, ctx.sb, ctx.dram, ctx.bufs
    L = sb["lnst"].ap()
    lb = bf.buf("lnst")
    scrb = bf.buf("scr")
    xnb = bf.buf("xn_dram")
    xn2b = bf.buf("xn2p_dram")
    epsb = bf.buf("epsc")
    g = lambda i: L[:, 16 * i:16 * (i + 1)]
    for t in range(16):
        slot = t % 2
        xs = sb["xstage"].ap()[:, 1024 * slot:1024 * (slot + 1)]
        xsb = bf.buf(f"xstage{slot}")
        p.dma("sy", lambda h, o=xs, i=x_in[128 * t:128 * (t + 1), :]: h.dma_start(out=o, in_=i),
              reads=[x_in_buf], writes=[xsb])
        p.op("v", lambda h, o=L[:, t:t + 1], i=xs: h.reduce_sum(o, i, axis=AX.X),
             reads=[xsb], writes=[lb])
        p.op("a", lambda h, o=sb["scr"].ap()[:, 0:1024], i=xs, a=L[:, 16 + t:17 + t]:
             h.activation(o, i, AF.Square, accum_out=a),
             reads=[xsb], writes=[scrb, lb])
    p.op("v", lambda h: h.tensor_scalar(g(2), g(0), 1.0 / 1024, None, ALU.mult), reads=[lb], writes=[lb])
    p.op("v", lambda h: h.tensor_scalar(g(6), g(1), 1.0 / 1024, None, ALU.mult), reads=[lb], writes=[lb])
    p.op("v", lambda h: h.tensor_tensor(g(7), g(2), g(2), ALU.mult), reads=[lb], writes=[lb])
    p.op("v", lambda h: h.tensor_tensor(g(3), g(6), g(7), ALU.subtract), reads=[lb], writes=[lb])
    p.op("a", lambda h: h.activation(g(6), g(3), AF.Sqrt, bias=sb["epsc"].ap()), reads=[lb, epsb], writes=[lb])
    p.op("v", lambda h: h.reciprocal(g(4), g(6)), reads=[lb], writes=[lb])
    p.op("v", lambda h: h.tensor_tensor(g(7), g(2), g(4), ALU.mult), reads=[lb], writes=[lb])
    p.op("v", lambda h: h.tensor_scalar(g(5), g(7), -1.0, None, ALU.mult), reads=[lb], writes=[lb])
    if li == 0:
        p.op("v", lambda h: h.tensor_tensor(g(6), g(3), g(4), ALU.mult), reads=[lb], writes=[lb])
        p.op("v", lambda h: h.tensor_tensor(g(7), g(6), g(4), ALU.mult), reads=[lb], writes=[lb])
        p.op("a", lambda h: h.activation(g(6), g(7), AF.Sqrt, bias=sb["epsc"].ap()), reads=[lb, epsb], writes=[lb])
        p.op("v", lambda h: h.reciprocal(g(7), g(6)), reads=[lb], writes=[lb])
        p.op("v", lambda h: h.tensor_tensor(g(6), g(4), g(7), ALU.mult), reads=[lb], writes=[lb])
        p.op("v", lambda h: h.tensor_tensor(g(1), g(2), g(6), ALU.mult), reads=[lb], writes=[lb])
        p.op("v", lambda h: h.tensor_scalar(g(7), g(1), -1.0, None, ALU.mult), reads=[lb], writes=[lb])
    for t in range(16):
        slot = t % 2
        xs = sb["xstage"].ap()[:, 1024 * slot:1024 * (slot + 1)]
        xsb = bf.buf(f"xstage{slot}")
        p.dma("sy", lambda h, o=xs, i=x_in[128 * t:128 * (t + 1), :]: h.dma_start(out=o, in_=i),
              reads=[x_in_buf], writes=[xsb])
        xnt, xntb = ctx.stq(slot)
        p.op("a", lambda h, o=xnt, i=xs, s=L[:, 64 + t:65 + t], b2=L[:, 80 + t:81 + t]:
             h.activation(o, i, AF.Identity, bias=b2, scale=s),
             reads=[xsb, lb], writes=[xntb])
        p.dma("sy", lambda h, o=dr["xn_dram"][128 * t:128 * (t + 1), :], i=xnt: h.dma_start(out=o, in_=i),
              reads=[xntb], writes=[xnb])
        if li == 0:
            xn2t, xn2tb = ctx.stq(2 + slot)
            p.op("a", lambda h, o=xn2t, i=xs, s=L[:, 96 + t:97 + t], b2=L[:, 112 + t:113 + t]:
                 h.activation(o, i, AF.Identity, bias=b2, scale=s),
                 reads=[xsb, lb], writes=[xn2tb])
            p.dma("sy", lambda h, o=dr["xn2p_dram"][128 * t:128 * (t + 1), :], i=xn2t: h.dma_start(out=o, in_=i),
                  reads=[xn2tb], writes=[xn2b])


def emit_xnT(ctx):
    p, sb, dr, bf = ctx.plan, ctx.sb, ctx.dram, ctx.bufs
    for c in range(8):
        dst = sb["A1"].ap()[:, 2048 * c:2048 * (c + 1)]
        src = dr["xn_dram"][:, 128 * c:128 * (c + 1)]
        p.dma("sy", lambda h, o=dst, i=src: h.dma_start_transpose(o, i),
              reads=[bf.buf("xn_dram")], writes=[bf.buf(f"A1_{c // 4}")])



def emit_qkv(ctx, li):
    p, sb, dr, bf = ctx.plan, ctx.sb, ctx.dram, ctx.bufs
    a1 = sb["A1"].ap()
    a1r = [bf.buf("A1_0"), bf.buf("A1_1")]
    w1a, w1b2, w2a, w2b2 = bf.buf("W1a"), bf.buf("W1b"), bf.buf("W2a"), bf.buf("W2b")
    if li == 0:
        wsrc = dr["w_attn_T"]
        qs, ks, vs = (chunked(wsrc[:, 0:1024]), chunked(wsrc[:, 1024:2048]),
                      chunked(wsrc[:, 2048:3072]))
    else:
        qs, ks, vs = chunked(dr["wq"]), chunked(dr["wk"]), chunked(dr["wv"])
    def wload(c0, c1, src, arena, buf):
        dst = sb[arena].ap()[:, c0:c1].rearrange("p (c n) -> p c n", c=8)
        p.dma("g", lambda h, o=dst, i=src: h.dma_start(out=o, in_=i), writes=[buf])
    wload(0, 8192, qs, "W1", w1a)
    wload(8192, 16384, ks, "W1", w1b2)
    wload(8192, 16384, vs, "W2", w2b2)

    def proj_qk(which, wc0):
        for m in range(8):
            for n in range(4):
                i = (m * 4 + n) % 4
                bank, bb = ctx.bank(i), ctx.bankbuf(i)
                terms = []
                for k in range(8):
                    lhs = sb["W1"].ap()[:, wc0 + 1024 * k + 128 * m: wc0 + 1024 * k + 128 * (m + 1)]
                    rhs = a1[:, 2048 * k + 512 * n: 2048 * k + 512 * (n + 1)]
                    terms.append((bank, lhs, rhs, k == 0, k == 7))
                mm_group(p, terms, bb, [w1a if wc0 == 0 else w1b2] + a1r)
                ss, ssb = ctx.stq5(i)
                p.op("v", lambda h, o=ss, b=bank: h.tensor_copy(o, b), reads=[bb], writes=[ssb])
                if which == "q":
                    dst = dr["q_dram"][128 * m:128 * (m + 1), 512 * n:512 * (n + 1)]
                    p.dma("sy", lambda h, o=dst, i2=ss: h.dma_start(out=o, in_=i2),
                          reads=[ssb], writes=[bf.buf("q_dram")])
                else:
                    dst = dr["ag_in"][128 * m:128 * (m + 1), 512 * n:512 * (n + 1)]
                    p.dma("sy", lambda h, o=dst, i2=ss: h.dma_start(out=o, in_=i2),
                          reads=[ssb], writes=[bf.buf("ag_in")])

    proj_qk("k", 8192)
    ctx.q_proj_pending = proj_qk
    for tq in range(16):
        for n in range(2):
            i = (tq * 2 + n) % 4
            bank, bb = ctx.bank(i), ctx.bankbuf(i)
            terms = []
            for k in range(8):
                lhs = a1[:, 2048 * k + 128 * tq: 2048 * k + 128 * (tq + 1)]
                rhs = sb["W2"].ap()[:, 8192 + 1024 * k + 512 * n: 8192 + 1024 * k + 512 * (n + 1)]
                terms.append((bank, lhs, rhs, k == 0, k == 7))
            mm_group(p, terms, bb, [w2b2] + a1r)
            ss, ssb = ctx.stq5(i)
            p.op("v", lambda h, o=ss, b=bank: h.tensor_copy(o, b), reads=[bb], writes=[ssb])
            off = 1024 * 2048 + (128 * tq) * 1024 + 512 * n
            dst = flat_ap(dr["ag_in"], off, [[1024, 128], [1, 512]])
            p.dma("sy", lambda h, o=dst, i2=ss: h.dma_start(out=o, in_=i2),
                  reads=[ssb], writes=[bf.buf("ag_in")])


AG_SERIAL = False
FFN1_WAIT_AG = True
FLASH_PIPE = True
FLASH_YPP = False

def emit_precast_l1(ctx):
    p, dr, bf = ctx.plan, ctx.dram, ctx.bufs
    for name in ("wg_bf", "wu_bf", "wd_bf"):
        srcn = {"wg_bf": "wg_T", "wu_bf": "wu_T", "wd_bf": "wd_T"}[name]
        p.dma("g", lambda h, o=dr[name], i=dr[srcn]: h.dma_start(out=o, in_=i),
              writes=[bf.buf(name)])
    ctx.precast_done = [bf.buf("wg_bf"), bf.buf("wu_bf"), bf.buf("wd_bf")]


def emit_ag(ctx):
    p, dr, bf = ctx.plan, ctx.dram, ctx.bufs
    extra = getattr(ctx, "precast_done", None) or []
    tok = p.coll(lambda h: h.collective_compute(
        "AllGather", ALU.bypass,
        replica_groups=[[0, 1], [2, 3], [4, 5], [6, 7]],
        ins=[dr["ag_in"]], outs=[dr["ag_out"]]),
        reads=[bf.buf("ag_in")] + list(extra), writes=[bf.buf("ag_out")])
    ctx.last_ag_tok = tok
    if AG_SERIAL:
        for nm in ("W1", "W2", "A1_0", "A1_1", "A3a", "A3b", "xstage0", "xstage1",
                   "pbuf0", "pbuf1", "scr", "statc", "ffn_dram", "stage0", "stage1",
                   "stage2", "stage3"):
            b = bf.buf(nm)
            b.rs.append(tok)


def emit_kv_reorder(ctx):
    p, dr, bf = ctx.plan, ctx.dram, ctx.bufs
    blocks0 = [a // 128 for a in own_token_abs(0)[::128]]
    blocks1 = [a // 128 for a in own_token_abs(1)[::128]]
    src_of_abs = {}
    for loc, ab in enumerate(blocks0):
        src_of_abs[ab] = (0, loc)
    for loc, ab in enumerate(blocks1):
        src_of_abs[ab] = (1, loc)
    agb, kb, vb = bf.buf("ag_out"), bf.buf("kabs"), bf.buf("vabs")
    for ab in range(32):
        rank, loc = src_of_abs[ab]
        src = dr["ag_out"][2048 * rank: 2048 * rank + 1024, 128 * loc:128 * (loc + 1)]
        dst = dr["kabs"][:, 128 * ab:128 * (ab + 1)]
        p.dma("sy", lambda h, o=dst, i=src: h.dma_start(out=o, in_=i), reads=[agb], writes=[kb])
        off = (2048 * rank + 1024) * 2048 + (128 * loc) * 1024
        src = flat_ap(dr["ag_out"], off, [[1024, 128], [1, 1024]])
        dst = dr["vabs"][128 * ab:128 * (ab + 1), :]
        p.dma("sy", lambda h, o=dst, i=src: h.dma_start(out=o, in_=i), reads=[agb], writes=[vb])



def emit_flash(ctx, li):
    p, sb, dr, bf = ctx.plan, ctx.sb, ctx.dram, ctx.bufs
    a1w = [bf.buf("A1_0"), bf.buf("A1_1")]
    kbuf, vbuf, qbuf = bf.buf("kslot"), bf.buf("vslot"), bf.buf("qslot")
    mqb = bf.buf("maskq")
    iob, thb = bf.buf("iota"), bf.buf("thrT")
    kabs_b, vabs_b, qdr_b = bf.buf("kabs"), bf.buf("vabs"), bf.buf("q_dram")
    alb = bf.buf("alibi") if li == 0 else None
    zb = bf.buf("zeros")
    ofb = bf.buf("onesf")
    vsl = sb["vslot"].ap().rearrange("p (c w) -> p c w", w=65)

    p.op("v", lambda h: h.memset(sb["kslot"].ap()[64:65, :], 1.0), writes=[kbuf])
    if li == 1:
        p.op("v", lambda h: h.memset(sb["qslot"].ap()[64:65, :], -8.0 * BCONST), writes=[qbuf])

    for j in range(4):
        s = SLOTS[j]
        for i8 in range(8):
            dst = sb["maskq"].ap()[:, 512 * i8:512 * (i8 + 1)]
            thr = sb["thrT"].ap()[:, 8 * j + i8: 8 * j + i8 + 1]
            p.op("v", lambda h, o=dst, t=thr: h.tensor_scalar(o, sb["iota"].ap()[:], t, None, ALU.is_ge),
                 reads=[iob, thb], writes=[mqb])
        for hd in range(NH):
            if li == 0:
                qa_min = QB_ABS[1][j]
                cs = [c for c in range(s)
                      if c >= s - 8 or SLOPES[hd] * ((128 * c + 127) - qa_min) >= -93.0]
            else:
                cs = list(range(s))
            p.dma("sy", lambda h, o=sb["kslot"].ap()[0:64, 0:128 * s],
                  i=dr["kabs"][64 * hd:64 * (hd + 1), 0:128 * s]: h.dma_start(out=o, in_=i),
                  reads=[kabs_b], writes=[kbuf])
            src = flat_ap(dr["vabs"], 64 * hd, [[1024, 128], [128 * 1024, s], [1, 64]])
            p.dma("sy", lambda h, o=vsl[:, 0:s, 0:64], i=src: h.dma_start(out=o, in_=i),
                  reads=[vabs_b], writes=[vbuf])
            p.op("v", lambda h, o=vsl[:, 0:s, 64:65]: h.memset(o, 1.0), writes=[vbuf])
            qs0 = 512 * (hd % 2)
            qsl = sb["qslot"].ap()[0:65, qs0:qs0 + 512]
            p.dma("sy", lambda h, o=sb["qslot"].ap()[0:64, qs0:qs0 + 512],
                  i=dr["q_dram"][64 * hd:64 * (hd + 1), 512 * j:512 * (j + 1)]: h.dma_start(out=o, in_=i),
                  reads=[qdr_b], writes=[qbuf])
            if li == 0:
                p.dma("sy", lambda h, o=sb["qslot"].ap()[64:65, qs0:qs0 + 512],
                      i=dr["brow0"][hd:hd + 1, 512 * j:512 * (j + 1)]: h.dma_start(out=o, in_=i),
                      reads=[bf.buf("brow0_src")], writes=[qbuf])

            yb = 2 if (hd % 2 == 0 or not FLASH_YPP) else 4
            bcb = 3 if (hd % 2 == 0 or not FLASH_YPP) else 5
            ps_y, ps_y_buf = ctx.ps.ap()[0:65, 512 * yb:512 * (yb + 1)], ctx.bankbuf(yb)
            ps_bc, ps_bc_buf = ctx.ps.ap()[0:64, 512 * bcb:512 * (bcb + 1)], ctx.bankbuf(bcb)

            SBANKS = (0, 1, 6, 7)
            def s_mm(idx):
                c = cs[idx]
                i = SBANKS[idx % 4]
                bank, bb = ctx.bank(i), ctx.bankbuf(i)
                p.op("t", lambda h, o=bank, l=sb["kslot"].ap()[0:65, 128 * c:128 * (c + 1)], r=qsl:
                     h.matmul(o, l, r, start=True, stop=True),
                     reads=[kbuf, qbuf], writes=[bb])

            s_mm(0)
            for idx, c in enumerate(cs):
                i = SBANKS[idx % 4]
                bank, bb = ctx.bank(i), ctx.bankbuf(i)
                pi = idx % 2
                pb = sb["pbuf"].ap()[:, 512 * pi:512 * (pi + 1)]
                pbb = bf.buf(f"pbuf{pi}")
                if li == 0:
                    bias = sb["alibi"].ap()[:, hd * 32 + c: hd * 32 + c + 1]
                    p.op("a", lambda h, o=pb, b=bank, bi=bias: h.activation(o, b, AF.Exp, bias=bi, scale=0.125),
                         reads=[bb, alb], writes=[pbb])
                else:
                    p.op("a", lambda h, o=pb, b=bank: h.activation(o, b, AF.Exp, scale=0.125),
                         reads=[bb], writes=[pbb])
                if FLASH_PIPE and idx + 1 < len(cs):
                    s_mm(idx + 1)
                if c >= s - 8:
                    msk = sb["maskq"].ap()[:, 512 * (c - (s - 8)):512 * (c - (s - 8) + 1)]
                    pm = sb["pbuf"].ap()[:, 512 * (2 + pi):512 * (3 + pi)]
                    pmb = bf.buf(f"pbuf{2 + pi}")
                    p.op("v", lambda h, o=pm, m=msk, src=pb: h.select(o, m, src, sb["zeros"].ap()[:]),
                         reads=[mqb, zb, pbb], writes=[pmb])
                    pv_src, pv_srcb = pm, pmb
                else:
                    pv_src, pv_srcb = pb, pbb
                lhs = vsl[:, c, 0:65]
                p.op("t", lambda h, o=ps_y, l=lhs, r=pv_src, st=(idx == 0), sp=(idx == len(cs) - 1):
                     h.matmul(o, l, r, start=st, stop=sp),
                     reads=[vbuf, pv_srcb], writes=[ps_y_buf])
                if (not FLASH_PIPE) and idx + 1 < len(cs):
                    s_mm(idx + 1)

            rr = sb["reciprow"].ap()
            rrb = bf.buf("reciprow")
            p.op("v", lambda h, o=rr: h.reciprocal(o, ps_y[64:65, 0:512]),
                 reads=[ps_y_buf], writes=[rrb])
            p.op("t", lambda h, o=ps_bc: h.matmul(o, sb["onesf"].ap()[:, 0:64], rr, start=True, stop=True),
                 reads=[ofb, rrb], writes=[ps_bc_buf])
            rbb = bf.buf(f"rb{hd % 2}")
            rbap = sb["rb"].ap()[0:64, 512 * (hd % 2):512 * (hd % 2) + 512]
            p.op("v", lambda h, o=rbap: h.tensor_copy(o, ps_bc),
                 reads=[ps_bc_buf], writes=[rbb])
            dst = sb["A1"].ap()[64 * (hd % 2):64 * (hd % 2) + 64,
                                2048 * (hd // 2) + 512 * j: 2048 * (hd // 2) + 512 * (j + 1)]
            p.op("v", lambda h, o=dst, r2=rbap: h.tensor_tensor(o, ps_y[0:64, 0:512], r2, ALU.mult),
                 reads=[ps_y_buf, rbb], writes=[a1w[(hd // 2) // 4]])


def emit_attn_proj(ctx, li):
    p, sb, dr, bf = ctx.plan, ctx.sb, ctx.dram, ctx.bufs
    w1b = bf.buf("W1a")
    a1r = [bf.buf("A1_0"), bf.buf("A1_1")]
    for fl in range(3):
        p.op("v", lambda h: h.tensor_copy(sb["statc"].ap()[:, 15:16], sb["statc"].ap()[:, 14:15]),
             reads=a1r, writes=[bf.buf("statc")])
    flb = bf.buf("statc")
    a1r = a1r + [flb]
    src = chunked(dr["w_proj_T"] if li == 0 else dr["wo"])
    dst = sb["W1"].ap()[:, 0:8192].rearrange("p (c n) -> p c n", c=8)
    p.dma("g", lambda h, o=dst, i=src: h.dma_start(out=o, in_=i), writes=[w1b])

    for tq in range(16):
        for n in range(2):
            i = (tq * 2 + n) % 4
            bank, bb = ctx.bank(i), ctx.bankbuf(i)
            terms = []
            for k in range(8):
                lhs = sb["A1"].ap()[:, 2048 * k + 128 * tq:2048 * k + 128 * (tq + 1)]
                rhs = sb["W1"].ap()[:, 1024 * k + 512 * n:1024 * k + 512 * (n + 1)]
                terms.append((bank, lhs, rhs, k == 0, k == 7))
            mm_group(p, terms, bb, [w1b] + a1r)
            ss, ssb = ctx.stq5(i)
            p.op("v", lambda h, o=ss, b=bank: h.tensor_copy(o, b), reads=[bb], writes=[ssb])
            d2 = dr["attn_dram"][128 * tq:128 * (tq + 1), 512 * n:512 * (n + 1)]
            p.dma("sy", lambda h, o=d2, i2=ss: h.dma_start(out=o, in_=i2),
                  reads=[ssb], writes=[bf.buf("attn_dram")])



def emit_gate0(ctx):
    p, sb, dr, bf = ctx.plan, ctx.sb, ctx.dram, ctx.bufs
    ps_lb = ctx.bankbuf(4)
    ps_t, ps_tb = ctx.ps.ap()[0:128, 2560:3072], ctx.bankbuf(5)
    scrb = bf.buf("scr")
    st = sb["statc"].ap()
    stb = bf.buf("statc")
    for n in range(2):
        psl = ctx.ps.ap()[0:8, 2048 + 512 * n:2048 + 512 * (n + 1)]
        dst = sb["scr"].ap()[0:8, 1024 + 512 * n:1024 + 512 * (n + 1)]
        p.op("a", lambda h, o=dst, i=psl: h.activation(o, i, AF.Identity, bias=sb["gateb"].ap()[:]),
             reads=[ps_lb, bf.buf("gateb")], writes=[scrb])
    for s in range(8):
        src = sb["scr"].ap()[0:8, 1024 + 128 * s:1024 + 128 * (s + 1)]
        p.op("t", lambda h, i=src: h.transpose(ps_t[0:128, 0:8], i, sb["identf"].ap()[0:8, 0:8]),
             reads=[scrb, bf.buf("identf")], writes=[ps_tb])
        lg = sb["lgt"].ap()
        lgb = bf.buf("lgt")
        p.op("v", lambda h: h.tensor_copy(lg, ps_t[0:128, 0:8]), reads=[ps_tb], writes=[lgb])
        p.op("v", lambda h: h.reduce_max(st[:, 0:1], lg, axis=AX.X), reads=[lgb], writes=[stb])
        p.op("v", lambda h: h.tensor_scalar(st[:, 1:2], st[:, 0:1], -1.0, None, ALU.mult),
             reads=[stb], writes=[stb])
        p.op("v", lambda h: h.tensor_scalar(sb["eqm"].ap(), lg, st[:, 0:1], None, ALU.is_ge),
             reads=[lgb, stb], writes=[bf.buf("eqm")])
        p.op("v", lambda h: h.select(sb["lmt"].ap(), sb["eqm"].ap(), sb["negbig"].ap(), lg),
             reads=[bf.buf("eqm"), bf.buf("negbig"), lgb], writes=[bf.buf("lmt")])
        p.op("v", lambda h: h.reduce_max(st[:, 2:3], sb["lmt"].ap(), axis=AX.X),
             reads=[bf.buf("lmt")], writes=[stb])
        p.op("a", lambda h: h.activation(sb["e1t"].ap(), lg, AF.Exp, bias=st[:, 1:2]),
             reads=[lgb, stb], writes=[bf.buf("e1t")])
        p.op("a", lambda h: h.activation(st[:, 3:4], st[:, 2:3], AF.Exp, bias=st[:, 1:2]),
             reads=[stb], writes=[stb])
        p.op("v", lambda h: h.tensor_scalar(st[:, 4:5], st[:, 3:4], 1.0, None, ALU.add),
             reads=[stb], writes=[stb])
        p.op("v", lambda h: h.reciprocal(st[:, 5:6], st[:, 4:5]), reads=[stb], writes=[stb])
        p.op("v", lambda h: h.tensor_scalar(sb["ge2f"].ap(), lg, st[:, 2:3], None, ALU.is_ge),
             reads=[lgb, stb], writes=[bf.buf("ge2f")])
        dst = sb["wall"].ap()[:, 8 * s:8 * (s + 1)]
        p.op("v", lambda h, o=dst: h.scalar_tensor_tensor(o, sb["e1t"].ap(), st[:, 5:6],
                                                          sb["ge2f"].ap(), ALU.mult, ALU.mult),
             reads=[bf.buf("e1t"), stb, bf.buf("ge2f")], writes=[bf.buf("wall")])


def emit_E_load(ctx, r, arena, wb):
    p, sb, dr, bf = ctx.plan, ctx.sb, ctx.dram, ctx.bufs
    if ctx.mode == "l0":
        src = chunked(dr["E"][r])
        dst = sb[arena].ap()[:, 0:8192].rearrange("p (c n) -> p c n", c=8)
        p.dma("sy", lambda h, o=dst, i=src: h.dma_start(out=o, in_=i),
              reads=[bf.buf("E_src")], writes=[wb])
    else:
        for rk in range(8):
            dst = dataclasses.replace(sb[arena].ap(), offset=128 * rk,
                                      ap=[[16384, 128], [1024, 8], [1, 128]])
            src = flat_ap(dr["eg_out"], (3072 * rk + 1024 * r) * 128,
                          [[128, 128], [128 * 128, 8], [1, 128]])
            p.dma("sy", lambda h, o=dst, i=src: h.dma_start(out=o, in_=i),
                  reads=[bf.buf("eg_out")], writes=[wb])


def emit_ffn0(ctx):
    p, sb, dr, bf = ctx.plan, ctx.sb, ctx.dram, ctx.bufs
    w1l = [bf.buf("W1a"), bf.buf("W1b")]
    w2l = [bf.buf("W2a"), bf.buf("W2b")]
    a3ab, a3bb = bf.buf("A3a"), bf.buf("A3b")
    a1ab, a1bb = bf.buf("A1_0"), bf.buf("A1_1")
    onb = bf.buf("onesb")
    for th in range(2):
        for c in range(8):
            dst = sb["A3"].ap()[:, 1024 * c:1024 * (c + 1)]
            src = dr["xn2p_dram"][1024 * th:1024 * (th + 1), 128 * c:128 * (c + 1)]
            p.dma("sy", lambda h, o=dst, i=src: h.dma_start_transpose(o, i),
                  reads=[bf.buf("xn2p_dram")], writes=[a3ab])
        ps_l, ps_lb = ctx.ps.ap()[0:8, 2048:2560], ctx.bankbuf(4)
        scrb = bf.buf("scr")
        for c in range(8):
            sl = sb["A3"].ap()[:, 1024 * c:1024 * (c + 1)]
            fs = sb["scr"].ap()[:, 0:1024]
            p.op("a", lambda h, o=fs, i=sl, g=sb["lnc"].ap()[:, c:c + 1], b2=sb["lnc"].ap()[:, 8 + c:9 + c]:
                 h.activation(o, i, AF.Identity, bias=b2, scale=g),
                 reads=[bf.buf("lnc"), a3ab, scrb], writes=[scrb])
            for nb in range(2):
                p.op("t", lambda h, c=c, nb=nb: h.matmul(
                    ps_l[:, 512 * nb:512 * (nb + 1)] if False else ctx.ps.ap()[0:8, 2048 + 512 * nb:2048 + 512 * (nb + 1)],
                    sb["gatewf"].ap()[:, 8 * c:8 * (c + 1)],
                    sb["scr"].ap()[:, 512 * nb:512 * (nb + 1)],
                    start=(c == 0), stop=(c == 7)),
                    reads=[bf.buf("gatewf"), scrb], writes=[ps_lb])
            p.op("v", lambda h, o=sl, i=fs: h.tensor_copy(o, i),
                 reads=[scrb], writes=[a3ab])

        emit_gate0(ctx)

        for e in range(EXP_N):
            src = chunked(dr["ew1"][e])
            dst = sb["W1"].ap().rearrange("p (c n) -> p c n", c=8)
            p.dma("g", lambda h, o=dst, i=src: h.dma_start(out=o, in_=i), writes=w1l)
            src = chunked(dr["ew2"][e])
            dst = sb["W2"].ap().rearrange("p (c n) -> p c n", c=16)
            p.dma("g", lambda h, o=dst, i=src: h.dma_start(out=o, in_=i), writes=w2l)
            p.dma("g", lambda h, o=sb["ebrow"].ap()[:], i=dr["eb2"][e:e + 1, :]: h.dma_start(out=o, in_=i),
                  writes=[bf.buf("ebrow")])

            for tb in range(2):
                for m in range(16):
                    i = m % 4
                    bank, bb = ctx.bank(i), ctx.bankbuf(i)
                    terms = []
                    for k in range(8):
                        lhs = sb["W1"].ap()[:, 2048 * k + 128 * m:2048 * k + 128 * (m + 1)]
                        rhs = sb["A3"].ap()[:, 1024 * k + 512 * tb:1024 * k + 512 * (tb + 1)]
                        terms.append((bank, lhs, rhs, k == 0, k == 7))
                    mm_group(p, terms, bb, w1l + [a3ab])
                    hdst = sb["A1"].ap()[:, 512 * m:512 * (m + 1)]
                    bias = sb["eb1c"].ap()[:, 16 * e + m:16 * e + m + 1]
                    p.op("a", lambda h, o=hdst, b=bank, bi=bias: h.activation(o, b, AF.Gelu, bias=bi),
                         reads=[bb, bf.buf("eb1c")], writes=[a1ab])
                for t4 in range(4):
                    for dn in range(2):
                        ib = t4 * 2 + dn
                        bank, bb = ctx.bank(ib), ctx.bankbuf(ib)
                        terms = []
                        for k in range(16):
                            lhs = sb["A1"].ap()[:, 512 * k + 128 * t4:512 * k + 128 * (t4 + 1)]
                            rhs = sb["W2"].ap()[:, 1024 * k + 512 * dn:1024 * k + 512 * (dn + 1)]
                            terms.append((bank, lhs, rhs, k == 0, False))
                        terms.append((bank, sb["onesb"].ap()[:, 0:128],
                                      sb["ebrow"].ap()[:, 512 * dn:512 * (dn + 1)], False, True))
                        mm_group(p, terms, bb, [a1ab] + w2l + [onb, bf.buf("ebrow")])
                        tt = tb * 4 + t4
                        acc = sb["A3"].ap()[:, 8192 + 1024 * tt + 512 * dn:8192 + 1024 * tt + 512 * (dn + 1)]
                        wcol = sb["wall"].ap()[:, 8 * tt + e:8 * tt + e + 1]
                        if e == 0:
                            p.op("v", lambda h, o=acc, b=bank, w=wcol: h.tensor_scalar(o, b, w, None, ALU.mult),
                                 reads=[bb, bf.buf("wall")], writes=[a3bb])
                        else:
                            p.op("v", lambda h, o=acc, b=bank, w=wcol: h.scalar_tensor_tensor(
                                o, b, w, o, ALU.mult, ALU.add),
                                reads=[bb, bf.buf("wall")], writes=[a3bb])

        for tt in range(8):
            src2 = sb["A3"].ap()[:, 8192 + 1024 * tt:8192 + 1024 * (tt + 1)]
            p.dma("sy", lambda h, o=dr["moe_tok"][128 * tt:128 * (tt + 1), :], i=src2: h.dma_start(out=o, in_=i),
                  reads=[a3bb], writes=[bf.buf("moe_tok")])
        for c in range(8):
            dst = sb["A1"].ap()[:, 1024 * c:1024 * (c + 1)]
            src2 = dr["moe_tok"][:, 128 * c:128 * (c + 1)]
            p.dma("sy", lambda h, o=dst, i=src2: h.dma_start_transpose(o, i),
                  reads=[bf.buf("moe_tok")], writes=[a1ab])
        for r in range(3):
            warena = "W1" if r % 2 == 0 else "W2"
            wb = bf.buf("W1a") if r % 2 == 0 else bf.buf("W2a")
            emit_E_load(ctx, r, warena, wb)
            src_cols = 0 if r % 2 == 0 else 8192
            dst_cols = 8192 if r % 2 == 0 else 0
            srcb = a1ab if r % 2 == 0 else a1bb
            dstb = a1bb if r % 2 == 0 else a1ab
            for m in range(8):
                for n in range(2):
                    i = (m * 2 + n) % 4
                    bank, bb = ctx.bank(i), ctx.bankbuf(i)
                    terms = []
                    for k in range(8):
                        lhs = sb[warena].ap()[:, 1024 * k + 128 * m:1024 * k + 128 * (m + 1)]
                        rhs = sb["A1"].ap()[:, src_cols + 1024 * k + 512 * n:src_cols + 1024 * k + 512 * (n + 1)]
                        terms.append((bank, lhs, rhs, k == 0, k == 7))
                    mm_group(p, terms, bb, [wb, srcb])
                    dst = sb["A1"].ap()[:, dst_cols + 1024 * m + 512 * n:dst_cols + 1024 * m + 512 * (n + 1)]
                    p.op("v", lambda h, o=dst, b=bank: h.tensor_copy(o, b), reads=[bb], writes=[dstb])

        src = chunked(dr["abel_w"])
        dst = sb["W1"].ap()[:, 0:8192].rearrange("p (c n) -> p c n", c=8)
        p.dma("g", lambda h, o=dst, i=src: h.dma_start(out=o, in_=i), writes=[bf.buf("W1a")])
        for m in range(8):
            for n in range(2):
                i = (m * 2 + n) % 4
                bank, bb = ctx.bank(i), ctx.bankbuf(i)
                terms = []
                for k in range(8):
                    lhs = sb["W1"].ap()[:, 1024 * k + 128 * m:1024 * k + 128 * (m + 1)]
                    rhs = sb["A3"].ap()[:, 1024 * k + 512 * n:1024 * k + 512 * (n + 1)]
                    terms.append((bank, lhs, rhs, k == 0, k == 7))
                mm_group(p, terms, bb, [bf.buf("W1a"), a3ab])
                dst = sb["A3"].ap()[:, 8192 + 1024 * m + 512 * n:8192 + 1024 * m + 512 * (n + 1)]
                bias = sb["abelb"].ap()[:, m:m + 1]
                p.op("a", lambda h, o=dst, b=bank, bi=bias: h.activation(o, b, AF.Gelu, bias=bi),
                     reads=[bb, bf.buf("abelb")], writes=[a3bb])

        ps_g, ps_gb = ctx.ps.ap()[0:1, 2048:2560], ctx.bankbuf(4)
        for n in range(2):
            terms = []
            for k in range(8):
                lhs = sb["gcw"].ap()[:, k:k + 1]
                rhs = sb["A3"].ap()[:, 8192 + 1024 * k + 512 * n:8192 + 1024 * k + 512 * (n + 1)]
                terms.append((ps_g, lhs, rhs, k == 0, False))
            for k in range(8):
                lhs = sb["gcw"].ap()[:, 8 + k:9 + k]
                rhs = sb["A1"].ap()[:, 8192 + 1024 * k + 512 * n:8192 + 1024 * k + 512 * (n + 1)]
                terms.append((ps_g, lhs, rhs, False, k == 7))
            mm_group(p, terms, ps_gb, [bf.buf("gcw"), a3bb, a1bb])
            dst = sb["grow"].ap()[:, 512 * n:512 * (n + 1)]
            p.op("a", lambda h, o=dst: h.activation(o, ps_g, AF.Sigmoid, bias=sb["gcb"].ap()[:]),
                 reads=[ps_gb, bf.buf("gcb")], writes=[bf.buf("grow")])

        ps_bc, ps_bc_buf = ctx.ps.ap()[0:128, 1536:2048], ctx.bankbuf(3)
        for n in range(2):
            p.op("t", lambda h, r2=sb["grow"].ap()[:, 512 * n:512 * (n + 1)]:
                 h.matmul(ps_bc, sb["onesf"].ap()[:, 0:128], r2, start=True, stop=True),
                 reads=[bf.buf("grow"), bf.buf("onesf")], writes=[ps_bc_buf])
            p.op("v", lambda h: h.tensor_copy(sb["rb"].ap()[:, 0:512], ps_bc),
                 reads=[ps_bc_buf], writes=[bf.buf("rb")])
            for m in range(8):
                ab = sb["A3"].ap()[:, 8192 + 1024 * m + 512 * n:8192 + 1024 * m + 512 * (n + 1)]
                mo = sb["A1"].ap()[:, 8192 + 1024 * m + 512 * n:8192 + 1024 * m + 512 * (n + 1)]
                df = sb["pbuf"].ap()[:, 0:512]
                p.op("v", lambda h, o=df, a=mo, b2=ab: h.tensor_tensor(o, a, b2, ALU.subtract),
                     reads=[a1bb, a3bb], writes=[bf.buf("pbuf0")])
                p.op("v", lambda h, o=df: h.tensor_tensor(o, o, sb["rb"].ap()[:, 0:512], ALU.mult),
                     reads=[bf.buf("rb"), bf.buf("pbuf0")], writes=[bf.buf("pbuf0")])
                mx = sb["pbuf"].ap()[:, 512:1024]
                p.op("v", lambda h, o=mx, a=df, b2=ab: h.tensor_tensor(o, a, b2, ALU.add),
                     reads=[bf.buf("pbuf0"), a3bb], writes=[bf.buf("pbuf1")])
                dst = dr["mixed_dram"][128 * m:128 * (m + 1), 1024 * th + 512 * n:1024 * th + 512 * (n + 1)]
                p.dma("sy", lambda h, o=dst, i=mx: h.dma_start(out=o, in_=i),
                      reads=[bf.buf("pbuf1")], writes=[bf.buf("mixed_dram")])



def emit_ffn1(ctx):
    p, sb, dr, bf = ctx.plan, ctx.sb, ctx.dram, ctx.bufs
    a1r = [bf.buf("A1_0"), bf.buf("A1_1")]
    a3b = bf.buf("A3a")
    pc = list(getattr(ctx, "precast_done", None) or [])
    if FFN1_WAIT_AG:
        agtok = getattr(ctx, "last_ag_tok", None)
        if agtok is not None:
            b = Buf("_ag_done")
            b.w = agtok
            pc.append(b)
    for third in range(3):
        for (name, c0, c1, wbx) in (("wg_bf", 0, 8192, "W1a"), ("wu_bf", 8192, 16384, "W1b")):
            src = chunked(dr[name][:, 1024 * third:1024 * (third + 1)])
            dst = sb["W1"].ap()[:, c0:c1].rearrange("p (c n) -> p c n", c=8)
            p.dma("sy", lambda h, o=dst, i=src: h.dma_start(out=o, in_=i),
                  reads=list(pc), writes=[bf.buf(wbx)])
        src = chunked(dr["wd_bf"][1024 * third:1024 * (third + 1), :])
        dst = sb["W2"].ap()[:, 0:8192].rearrange("p (c n) -> p c n", c=8)
        p.dma("sy", lambda h, o=dst, i=src: h.dma_start(out=o, in_=i),
              reads=list(pc), writes=[bf.buf("W2a")])

        for m in range(8):
            for n in range(4):
                ig = (m * 4 + n) % 2
                bankg, bbg = ctx.bank(2 * ig), ctx.bankbuf(2 * ig)
                banku, bbu = ctx.bank(2 * ig + 1), ctx.bankbuf(2 * ig + 1)
                termsg, termsu = [], []
                for k in range(8):
                    rhs = sb["A1"].ap()[:, 2048 * k + 512 * n:2048 * k + 512 * (n + 1)]
                    lg = sb["W1"].ap()[:, 1024 * k + 128 * m:1024 * k + 128 * (m + 1)]
                    lu = sb["W1"].ap()[:, 8192 + 1024 * k + 128 * m:8192 + 1024 * k + 128 * (m + 1)]
                    termsg.append((bankg, lg, rhs, k == 0, k == 7))
                    termsu.append((banku, lu, rhs, k == 0, k == 7))
                mm_group(p, termsg, bbg, [bf.buf("W1a")] + a1r)
                mm_group(p, termsu, bbu, [bf.buf("W1b")] + a1r)
                sil = sb["pbuf"].ap()[:, 512 * ig:512 * (ig + 1)]
                silb = bf.buf(f"pbuf{ig}")
                p.op("a", lambda h, o=sil, b=bankg: h.activation(o, b, AF.Silu, scale=S8),
                     reads=[bbg], writes=[silb])
                dst = sb["A3"].ap()[:, 2048 * m + 512 * n:2048 * m + 512 * (n + 1)]
                p.op("v", lambda h, o=dst, b=banku, s=sil: h.scalar_tensor_tensor(
                    o, b, S8, s, ALU.mult, ALU.mult),
                    reads=[bbu, silb], writes=[a3b])
        for tq in range(16):
            for dn in range(2):
                i = (tq * 2 + dn) % 4
                bank, bb = ctx.bank(4 + i), ctx.bankbuf(4 + i)
                terms = []
                for k in range(8):
                    lhs = sb["A3"].ap()[:, 2048 * k + 128 * tq:2048 * k + 128 * (tq + 1)]
                    rhs = sb["W2"].ap()[:, 1024 * k + 512 * dn:1024 * k + 512 * (dn + 1)]
                    terms.append((bank, lhs, rhs, k == 0, k == 7))
                mm_group(p, terms, bb, [a3b, bf.buf("W2a")])
                ss, ssb = ctx.stq5(i)
                p.op("v", lambda h, o=ss, b=bank: h.tensor_scalar(o, b, S8, None, ALU.mult),
                     reads=[bb], writes=[ssb])
                dst = dr["ffn_dram"][third, 128 * tq:128 * (tq + 1), 512 * dn:512 * (dn + 1)]
                p.dma("sy", lambda h, o=dst, i2=ss: h.dma_start(out=o, in_=i2),
                      reads=[ssb], writes=[bf.buf("ffn_dram")])



XPASS_BARRIER = True

def emit_xpass(ctx, li, x_in, x_out, x_in_buf, x_out_buf):
    p, sb, dr, bf = ctx.plan, ctx.sb, ctx.dram, ctx.bufs
    if XPASS_BARRIER:
        toks = [(s, p.cnt[s]) for s in ("t", "v", "a", "g", "sy")]
        for e in ("t", "v", "a", "g", "sy"):
            p.wait_tokens(e, toks)
    for t in range(16):
        slot = t % 2
        xs = sb["xstage"].ap()[:, 1024 * slot:1024 * (slot + 1)]
        xsb = bf.buf(f"xstage{slot}")
        sc = sb["scr"].ap()[:, 1024 * slot:1024 * (slot + 1)]
        scb = bf.buf(f"scrx{slot}")
        p.dma("sy", lambda h, o=xs, i=x_in[128 * t:128 * (t + 1), :]: h.dma_start(out=o, in_=i),
              reads=[x_in_buf], writes=[xsb])
        at, atb = ctx.stq(slot)
        p.dma("sy", lambda h, o=at, i=dr["attn_dram"][128 * t:128 * (t + 1), :]: h.dma_start(out=o, in_=i),
              reads=[bf.buf("attn_dram")], writes=[atb])
        p.op("v", lambda h, o=sc, a=xs, b=at: h.tensor_tensor(o, a, b, ALU.add),
             reads=[atb, xsb], writes=[scb])
        cur, curb, nxt, nxtb = sc, scb, xs, xsb
        if li == 0:
            xn, xnb2 = ctx.stq(2 + slot)
            p.dma("sy", lambda h, o=xn, i=dr["xn_dram"][128 * t:128 * (t + 1), :]: h.dma_start(out=o, in_=i),
                  reads=[bf.buf("xn_dram")], writes=[xnb2])
            p.op("v", lambda h, o=nxt, a=cur, b=xn: h.tensor_tensor(o, a, b, ALU.add),
                 reads=[xnb2, curb], writes=[nxtb])
            cur, curb, nxt, nxtb = nxt, nxtb, cur, curb
            p.wait_tokens("sy", [("sy", p.cnt["sy"])])
            p.dma("sy", lambda h, o=xn, i=dr["mixed_dram"][:, 128 * t:128 * (t + 1)]: h.dma_start_transpose(o, i),
                  reads=[bf.buf("mixed_dram")], writes=[xnb2])
            p.wait_tokens("sy", [("sy", p.cnt["sy"])])
            p.op("v", lambda h, o=nxt, a=cur, b=xn: h.tensor_tensor(o, a, b, ALU.add),
                 reads=[xnb2, curb], writes=[nxtb])
            cur, curb, nxt, nxtb = nxt, nxtb, cur, curb
        else:
            ff, ffb = ctx.stq(2 + slot)
            for third in range(3):
                p.dma("sy", lambda h, o=ff, i=dr["ffn_dram"][third, 128 * t:128 * (t + 1), :]: h.dma_start(out=o, in_=i),
                      reads=[bf.buf("ffn_dram")], writes=[ffb])
                p.op("v", lambda h, o=nxt, a=cur, b=ff: h.tensor_tensor(o, a, b, ALU.add),
                     reads=[ffb, curb], writes=[nxtb])
                cur, curb, nxt, nxtb = nxt, nxtb, cur, curb
        p.op("v", lambda h, a=cur: h.tensor_copy(sb["statc"].ap()[:, 15:16], a[0:128, 0:1]),
             reads=[curb], writes=[bf.buf("statc")])
        p.dma("sy", lambda h, o=x_out[128 * t:128 * (t + 1), :], i=cur: h.dma_start(out=o, in_=i),
              reads=[curb], writes=[x_out_buf])



def emit_expm(ctx):
    p, sb, dr, bf = ctx.plan, ctx.sb, ctx.dram, ctx.bufs
    w1b = bf.buf("W2a")
    xa = lambda c: ctx.sb["A3"].ap()[:, 0 + 128 * c:0 + 128 * (c + 1)]
    xbq = lambda c: ctx.sb["A3"].ap()[:, 1024 + 128 * c:1024 + 128 * (c + 1)]
    x0 = lambda c: ctx.sb["A3"].ap()[:, 2048 + 128 * c:2048 + 128 * (c + 1)]
    xab, xbb, x0b = bf.buf("eXa"), bf.buf("eXb"), bf.buf("eX0")
    for c in range(8):
        p.dma("g", lambda h, o=x0(c), i=dr["eyecol"][128 * c:128 * (c + 1), :]: h.dma_start(out=o, in_=i),
              writes=[x0b])
    out_dram = dr["eslice"] if ctx.mode == "expm" else dr["eg_in"]
    outb = bf.buf("eslice" if ctx.mode == "expm" else "eg_in")
    for gi in range(3):
        src = chunked(dr["gensT"][gi])
        dst = sb["W2"].ap()[:, 0:8192].rearrange("p (c n) -> p c n", c=8)
        p.dma("g", lambda h, o=dst, i=src: h.dma_start(out=o, in_=i), writes=[w1b])
        for c in range(8):
            p.op("v", lambda h, o=xa(c), i=x0(c): h.tensor_copy(o, i), reads=[x0b], writes=[xab])
        cur, curb = xa, xab
        nxt, nxtb = xbq, xbb
        for kk in range(KTAYLOR, 0, -1):
            for m in range(8):
                i = m % 4
                bank, bb = ctx.bank(i), ctx.bankbuf(i)
                terms = []
                for k in range(8):
                    lhs = sb["W2"].ap()[:, 1024 * k + 128 * m:1024 * k + 128 * (m + 1)]
                    terms.append((bank[0:128, 0:128], lhs, cur(k), k == 0, k == 7))
                mm_group(p, terms, bb, [w1b, curb])
                p.op("v", lambda h, o=nxt(m), b=bank[0:128, 0:128], i0=x0(m), s=1.0 / kk:
                     h.scalar_tensor_tensor(o, b, s, i0, ALU.mult, ALU.add),
                     reads=[bb, x0b], writes=[nxtb])
            cur, curb, nxt, nxtb = nxt, nxtb, cur, curb
        for c in range(8):
            dst = out_dram[1024 * gi + 128 * c:1024 * gi + 128 * (c + 1), :]
            p.dma("sy", lambda h, o=dst, i=cur(c): h.dma_start(out=o, in_=i),
                  reads=[curb], writes=[outb])
    if ctx.mode == "fused":
        extra = getattr(ctx, "precast_done", None) or []
        tok = p.coll(lambda h: h.collective_compute(
            "AllGather", ALU.bypass, replica_groups=[list(range(8))],
            ins=[dr["eg_in"]], outs=[dr["eg_out"]]),
            reads=[bf.buf("eg_in")] + list(extra), writes=[bf.buf("eg_out")])
        ctx.expm_cc_tok = tok
    tok = outb.w
    for nm in ("A3a", "A3b"):
        b = bf.buf(nm)
        b.w = tok
        b.rs = []



def emit_layer(ctx, li, x_in, x_out, x_in_buf, x_out_buf):
    emit_ln(ctx, li, x_in, x_in_buf)
    emit_xnT(ctx)
    emit_qkv(ctx, li)
    emit_ag(ctx)
    ctx.q_proj_pending("q", 0)
    if li == 0:
        emit_ffn0(ctx)
    else:
        emit_ffn1(ctx)
    emit_kv_reorder(ctx)
    emit_flash(ctx, li)
    emit_attn_proj(ctx, li)
    emit_xpass(ctx, li, x_in, x_out, x_in_buf, x_out_buf)


def build(mode):
    nc = bass.Bass("TRN2", target_bir_lowering=False)
    dram = declare_dram(nc, mode)
    with ExitStack() as stack:
        sb = alloc_sbuf(nc, stack, mode)
        ps = stack.enter_context(nc.psum_tensor("ps_all", [128, 4096], F32))
        sems = {}
        for s in ("t", "v", "a", "g", "sy", "cc"):
            sems[s] = stack.enter_context(nc.semaphore(f"sem_{s}"))
        block = stack.enter_context(nc.Block())
        plan = Plan()
        ctx = Ctx(nc=nc, plan=plan, sb=sb, ps=ps, dram=dram, bufs=Bufs(), mode=mode)
        load_const_sbuf(ctx)
        if mode == "expm":
            emit_expm(ctx)
        elif mode == "l0":
            emit_layer(ctx, 0, dram["x_in"], dram["x_out"],
                       ctx.bufs.buf("x_in_src"), ctx.bufs.buf("x_out_dst"))
        elif mode == "l1":
            emit_precast_l1(ctx)
            emit_layer(ctx, 1, dram["x_in"], dram["x_out"],
                       ctx.bufs.buf("x_in_src"), ctx.bufs.buf("x_out_dst"))
        elif mode == "fused":
            emit_precast_l1(ctx)
            emit_expm(ctx)
            emit_layer(ctx, 0, dram["x0"], dram["x_mid"],
                       ctx.bufs.buf("x_in_src"), ctx.bufs.buf("x_mid_buf"))
            emit_layer(ctx, 1, dram["x_mid"], dram["x_out"],
                       ctx.bufs.buf("x_mid_buf"), ctx.bufs.buf("x_out_dst"))
        plan.wait_tokens("sy", [("sy", plan.cnt["sy"]), ("g", plan.cnt["g"]),
                                ("v", plan.cnt["v"]), ("a", plan.cnt["a"]),
                                ("t", plan.cnt["t"])])
        block.tensor(lambda h: plan.replay("t", h, sems))
        block.vector(lambda h: plan.replay("v", h, sems))
        block.scalar(lambda h: plan.replay("a", h, sems))
        block.gpsimd(lambda h: plan.replay("g", h, sems))
        block.sync(lambda h: plan.replay("sy", h, sems))
    return nc



def kernel(**inputs):
    x_full = np.asarray(inputs["L0_attn_x"], dtype=np.float32)
    w = prep_weights(inputs)
    consts = prep_consts()
    lng = _f32(np.asarray(inputs["L0_ffn_ln_gamma"]).reshape(8, 128).T)
    lnb = _f32(np.asarray(inputs["L0_ffn_ln_beta"]).reshape(8, 128).T)

    cores = list(range(NCORE))
    per_core = [prep_core(c, x_full) for c in cores]

    nc = build("fused")
    in_maps = []
    for c in cores:
        m = {k: w[k] for k in L0_W_KEYS + L1_W_KEYS}
        m.update({"ident_f": consts["ident_f"], "iota_row": consts["iota_row"],
                  "alibi_cols": consts["alibi_cols"], "thr": per_core[c]["thr"],
                  "brow0": per_core[c]["brow0"], "x0": per_core[c]["x0"],
                  "gensT": w["gensT"], "eyecol": per_core[c]["eyecol"],
                  "ln_gamma": lng, "ln_beta": lnb})
        in_maps.append(m)
    res = run_bass_kernel_spmd(nc, in_maps, core_ids=cores)

    out = np.zeros((B_, T_, D_), dtype=np.float32)
    for c in cores:
        toks = own_token_abs(c % 2)
        out[c // 2][toks] = np.asarray(res.results[c]["x_out"], dtype=np.float32)
    return out
